# version 03
# baseline (speedup 1.0000x reference)
from contextlib import ExitStack
import numpy as np
import concourse.bass as bass
import concourse.mybir as mybir

F32 = mybir.dt.float32
F32R = mybir.dt.float32r
BF16 = mybir.dt.bfloat16
I32 = mybir.dt.int32
AF = mybir.ActivationFunctionType
ALU = mybir.AluOpType
AX = mybir.AxisListType

STREAMS = {
    "tensor": ("tensor", False),
    "vector": ("vector", False),
    "scalar": ("scalar", False),
    "gpsimd": ("gpsimd", False),
    "sync": ("sync", True),
    "gdma": ("gpsimd", True),
    "sdma": ("scalar", True),
}
SHARED_QUEUE = {"gpsimd": "gp", "gdma": "gp", "scalar": "sc", "sdma": "sc"}


class Step:
    __slots__ = ("stream", "count_after", "sid", "slot", "target", "epoch")

    def __init__(self, stream, count_after, sid, slot=None, target=0, epoch=0):
        self.stream = stream
        self.count_after = count_after
        self.sid = sid
        self.slot = slot
        self.target = target
        self.epoch = epoch


class Sched:
    DMA_POOL = 6

    def __init__(self, nc):
        self.nc = nc
        self.ctx = ExitStack()
        self.steps = []
        self._semcount = 0
        self._sid = 0
        self.epoch = -1
        self.esems = []
        self.edma_sems = []
        self.ecounts = []
        self.eslot_counts = []
        self._new_epoch()

    def _new_epoch(self):
        self.epoch += 1
        self.esems.append({k: self.sem(f"e{self.epoch}_{k}") for k in STREAMS if not STREAMS[k][1]})
        self.edma_sems.append({k: [self.sem(f"e{self.epoch}_{k}{j}") for j in range(self.DMA_POOL)]
                               for k in STREAMS if STREAMS[k][1]})
        self.ecounts.append({k: 0 for k in STREAMS})
        self.eslot_counts.append({k: [0] * self.DMA_POOL for k in STREAMS if STREAMS[k][1]})
        self.dma_rr = {k: 0 for k in STREAMS if STREAMS[k][1]}

    def sem(self, name):
        self._semcount += 1
        return self.ctx.enter_context(self.nc.semaphore(f"{name}_{self._semcount}"))

    def use_arenas(self, bf16_cols, f32_cols, i32_cols=0):
        self._arena = {}
        for dt, cols, nm in ((BF16, bf16_cols, "bf16"), (F32, f32_cols, "f32"), (I32, i32_cols, "i32")):
            if cols:
                t = self.ctx.enter_context(self.nc.sbuf_tensor(f"arena_{nm}", [128, cols], dt))
                self._arena[dt] = [t, cols, 0]

    def sbuf_reset(self):
        for v in self._arena.values():
            v[2] = 0

    def sbuf(self, name, shape, dtype, **kw):
        ar = getattr(self, "_arena", None)
        if ar and dtype in ar:
            t, cap, off = ar[dtype]
            bpe = 2 if dtype == BF16 else 4
            align = 64 // bpe
            off = (off + align - 1) // align * align
            need = shape[1]
            assert off + need <= cap, f"arena {dtype} overflow at {name}: {off}+{need}>{cap}"
            ar[dtype][2] = off + need
            return t[0:shape[0], off:off + need]
        return self.ctx.enter_context(self.nc.sbuf_tensor(name, shape, dtype, **kw))

    def barrier(self, dep_steps):
        self._pending_bar = {k: list(dep_steps) for k in STREAMS}
        self._new_epoch()

    def psum(self, name, shape, dtype=F32):
        return self.ctx.enter_context(self.nc.psum_tensor(name, shape, dtype))

    def global_psums(self):
        if not hasattr(self, "_gps"):
            self._gps = {k: self.psum("gps_" + k, [128, 512]) for k in
                         ["q", "p1", "p2", "sc0", "sc1", "ctx", "bc", "d"]}
            self._gps["rs"] = self._gps["bc"]
        return self._gps

    def psum_open(self, name, shape, dtype=F32):
        cm = self.nc.psum_tensor(name, shape, dtype)
        h = cm.__enter__()
        return h, (lambda: cm.__exit__(None, None, None))

    def step(self, stream, deps, fn, weight=None):
        if weight is None:
            weight = 16 if STREAMS[stream][1] else 1
        deps = [d for d in deps if d is not None]
        pb = getattr(self, "_pending_bar", None)
        if pb and pb.get(stream):
            deps = deps + pb.pop(stream)
        self._sid += 1
        ep = self.epoch
        if STREAMS[stream][1]:
            slot = self.dma_rr[stream]
            self.dma_rr[stream] = (slot + 1) % self.DMA_POOL
            self.eslot_counts[ep][stream][slot] += weight
            st = Step(stream, 0, self._sid, slot,
                      self.eslot_counts[ep][stream][slot], ep)
        else:
            self.ecounts[ep][stream] += weight
            st = Step(stream, self.ecounts[ep][stream], self._sid, epoch=ep)
        self.steps.append((stream, deps, fn, weight, st))
        return st

    def dma_fence(self, stream):
        ep = self.epoch
        return [Step(stream, 0, 0, slot=j, target=self.eslot_counts[ep][stream][j], epoch=ep)
                for j in range(self.DMA_POOL) if self.eslot_counts[ep][stream][j] > 0]

    def dma(self, stream, deps, out, in_, **kw):
        return self.step(stream, deps, lambda e: e.dma_start(out=out, in_=in_, **kw), 16)

    def dmaT(self, stream, deps, out, in_):
        return self.step(stream, deps, lambda e: e.dma_start_transpose(out, in_), 16)

    def finish(self, final_waits=()):
        for k in STREAMS:
            self.sems[k] = self.sem("st_" + k)
        waited = {k: {j: 0 for j in STREAMS} for k in STREAMS}
        by_stream = {k: [] for k in STREAMS}
        for stream, deps, fn, weight, st in self.steps:
            by_stream[stream].append((deps, fn, weight, st))

        def run_stream(stream, eng):
            w = waited[stream]
            for deps, fn, weight, st in by_stream[stream]:
                for dp in deps:
                    if dp.stream == stream:
                        if dp.count_after > w[stream]:
                            eng.wait_ge(self.sems[stream], dp.count_after)
                            w[stream] = dp.count_after
                    else:
                        if dp.count_after > w[dp.stream]:
                            eng.wait_ge(self.sems[dp.stream], dp.count_after)
                            w[dp.stream] = dp.count_after
                ins = fn(eng)
                assert ins is not None, f"step fn returned None (stream {stream})"
                ins.then_inc(self.sems[stream], weight)

        with self.nc.Block() as block:
            @block.sync
            def _(e):
                run_stream("sync", e)

            @block.scalar
            def _(e):
                run_stream("scalar", e)
                run_stream("sdma", e)

            @block.vector
            def _(e):
                run_stream("vector", e)

            @block.tensor
            def _(e):
                run_stream("tensor", e)

            @block.gpsimd
            def _(e):
                run_stream("gpsimd", e)
                run_stream("gdma", e)
                for st in final_waits:
                    e.wait_ge(self.sems[st.stream], st.count_after)

    def finish_interleaved(self, final_waits=()):
        waited = {}
        for k in set(list(STREAMS) + list(SHARED_QUEUE.values())):
            waited[k] = {}
            waited[k, "dma"] = {}

        def run(eng, streams_here):
            wkey = SHARED_QUEUE.get(streams_here[0], streams_here[0])
            w = waited[wkey]
            wd = waited[wkey, "dma"]
            own = set(streams_here)
            for stream, deps, fn, weight, st in self.steps:
                if stream not in own:
                    continue
                for dp in deps:
                    if STREAMS[dp.stream][1]:
                        key = (dp.epoch, dp.stream, dp.slot)
                        if dp.target > wd.get(key, 0):
                            eng.wait_ge(self.edma_sems[dp.epoch][dp.stream][dp.slot], dp.target)
                            wd[key] = dp.target
                    else:
                        key = (dp.epoch, dp.stream)
                        if dp.count_after > w.get(key, 0):
                            eng.wait_ge(self.esems[dp.epoch][dp.stream], dp.count_after)
                            w[key] = dp.count_after
                if STREAMS[stream][1]:
                    prev_target = st.target - weight
                    key = (st.epoch, stream, st.slot)
                    if prev_target > 0 and prev_target > wd.get(key, 0):
                        eng.wait_ge(self.edma_sems[st.epoch][stream][st.slot], prev_target)
                        wd[key] = prev_target
                ins = fn(eng)
                assert ins is not None
                if STREAMS[stream][1]:
                    if not isinstance(ins, (list, tuple)):
                        ins = [ins]
                    assert weight == 16 * len(ins), f"dma weight {weight} != 16*{len(ins)}"
                    for one in ins:
                        one.then_inc(self.edma_sems[st.epoch][stream][st.slot], 16)
                else:
                    if isinstance(ins, (list, tuple)):
                        ins = ins[-1]
                    ins.then_inc(self.esems[st.epoch][stream], weight)

        with self.nc.Block() as block:
            @block.sync
            def _(e):
                run(e, ["sync"])

            @block.scalar
            def _(e):
                run(e, ["scalar", "sdma"])

            @block.vector
            def _(e):
                run(e, ["vector"])

            @block.tensor
            def _(e):
                run(e, ["tensor"])

            @block.gpsimd
            def _(e):
                run(e, ["gpsimd", "gdma"])
                done = {}
                for st in final_waits:
                    key = (st.epoch, st.stream, st.slot)
                    tgt = st.target if STREAMS[st.stream][1] else st.count_after
                    if done.get(key, 0) >= tgt:
                        continue
                    done[key] = tgt
                    if STREAMS[st.stream][1]:
                        e.wait_ge(self.edma_sems[st.epoch][st.stream][st.slot], tgt)
                    else:
                        e.wait_ge(self.esems[st.epoch][st.stream], st.count_after)

        self.ctx.close()



def rope_consts_l0(T):
    f = np.arange(64, dtype=np.float64)
    freqs = 10000.0 ** (-2.0 * f / 2048.0)
    t = np.arange(T, dtype=np.float64)
    ang = np.outer(t, freqs)
    C = np.zeros((128, T), np.float32)
    S2 = np.zeros((128, T), np.float32)
    for i in range(64):
        C[2 * i] = np.cos(ang[:, i])
        C[2 * i + 1] = np.sin(ang[:, i])
        S2[2 * i] = -np.sin(ang[:, i])
        S2[2 * i + 1] = np.cos(ang[:, i])
    P1T = np.zeros((128, 128), np.float32)
    P2T = np.zeros((128, 128), np.float32)
    for i in range(64):
        P1T[2 * i, 2 * i] = 1.0
        P1T[2 * i, 2 * i + 1] = 1.0
        P2T[2 * i + 1, 2 * i] = 1.0
        P2T[2 * i + 1, 2 * i + 1] = 1.0
    return C, S2, P1T, P2T


def rope_consts_l1(T):
    half = 32
    inv_freq = 1.0 / 10000.0 ** (np.arange(half, dtype=np.float64) / half)
    t = np.arange(T, dtype=np.float64)
    fr = np.outer(t, inv_freq)
    C = np.zeros((128, T), np.float32)
    S2 = np.zeros((128, T), np.float32)
    P1T = np.zeros((128, 128), np.float32)
    P2T = np.zeros((128, 128), np.float32)
    for blk in range(2):
        b = blk * 64
        for i in range(half):
            C[b + 2 * i] = np.cos(fr[:, i])
            C[b + 2 * i + 1] = np.sin(fr[:, i])
            S2[b + 2 * i] = -np.sin(fr[:, i])
            S2[b + 2 * i + 1] = np.cos(fr[:, i])
            P1T[b + i, b + 2 * i] = 1.0
            P1T[b + i, b + 2 * i + 1] = 1.0
            P2T[b + i + 32, b + 2 * i] = 1.0
            P2T[b + i + 32, b + 2 * i + 1] = 1.0
    return C, S2, P1T, P2T


def causal_masks(TCH=512):
    m = np.zeros((128, 4 * TCH), np.float32)
    for r in range(4):
        for sl in range(128):
            lo = r * 128 + sl
            if lo < TCH:
                m[sl, r * TCH + lo:(r + 1) * TCH] = 1.0
    return m


NHEAD = 16
NKV = 4
GRP = 4
D = 2048
SCALE_OUT = 2048.0 ** -0.5


def emit_l0_attn(S, T, NB, d):
    NT = T // 128
    TC = T // 512
    sb = S.sbuf

    xst0 = sb("l0a_xst0", [128, D], F32)
    xst = [xst0, xst0]
    xn = [sb(f"l0a_xn{i}", [128, D], BF16) for i in range(2)]
    stats = sb("l0a_stats", [128, 24], F32)
    mv = [sb(f"l0a_mv{i}", [128, 2], F32) for i in range(2)]
    rstd = [sb(f"l0a_rstd{i}", [128, 1], F32) for i in range(2)]
    bigA = sb("l0a_bigA", [128, 16 * 2048], BF16)
    ct = sb("l0a_ct", [128, T], F32)
    st = sb("l0a_st", [128, T], F32)
    p1t = sb("l0a_p1t", [128, 128], F32)
    p2t = sb("l0a_p2t", [128, 128], F32)
    qf32 = sb("l0a_qf32", [128, 512], F32)
    dmask = sb("l0a_dmask", [128, 4 * 512], BF16)
    w_sl = [sb(f"l0a_w{i}", [128, D], BF16) for i in range(4)]
    bqo_row = sb("l0a_bqo", [1, D], F32)
    bk_row = sb("l0a_bk", [1, 512], F32)
    bv_row = sb("l0a_bv", [1, 512], F32)
    ones_r = sb("l0a_ones_r", [1, 512], F32)
    ones_c = sb("l0a_ones_c", [128, 1], BF16)
    eps_c = sb("l0a_eps", [128, 1], F32)
    qT = sb("l0a_qT", [128, T], BF16)
    kT = sb("l0a_kT", [128, T], BF16)
    v_tm = sb("l0a_vtm", [128, NT * 512], BF16)
    tmp1 = sb("l0a_tmp1", [128, 512], F32)
    tmp2 = sb("l0a_tmp2", [128, 512], F32)
    expT = sb("l0a_expT", [128, NT * 512], BF16)
    recip = sb("l0a_recip", [1, 512], F32)
    recipB = sb("l0a_recipB", [128, 512], F32)
    ctx_sb = [sb(f"l0a_ctxsb{i}", [128, 512], BF16) for i in range(2)]
    ctxg = [sb(f"l0a_ctxg{i}", [128, 2048], BF16) for i in range(2)]

    scr = d["ctx_scr"]

    c_tab = S.dma("gdma", [], ct[:], d["ct"])
    S.dma("gdma", [], st[:], d["st"])
    S.dma("gdma", [], p1t[:], d["p1t"])
    S.dma("gdma", [], p2t[:], d["p2t"])
    c_msk = S.dma("gdma", [], dmask[:], d["dmask"])
    c_bk = S.dma("gdma", [], bk_row[:], d["bk"].rearrange("(a m) -> a m", a=1))
    c_bv = S.dma("gdma", [], bv_row[:], d["bv"].rearrange("(a m) -> a m", a=1))
    c_ones = S.step("vector", [], lambda e: (e.memset(ones_r[:], 1.0), e.memset(ones_c[:], 1.0),
                                             e.memset(eps_c[:], 1e-5))[-1])

    last_out = []
    pv = {}

    for b in range(NB):
        c_bq = S.dma("gdma", [pv.get("po_last")], bqo_row[:], d["bq"].rearrange("(a m) -> a m", a=1))

        G = S.global_psums()
        ps_q, ps_p1, ps_p2 = G["q"], G["p1"], G["p2"]
        ps_ctx, ps_bc = G["ctx"], G["bc"]
        ps_rs = ps_bc
        ps_sc = [G["sc0"], G["sc1"]]
        ps_out4 = [G["sc0"], G["sc1"], G["ctx"], G["bc"]]

        st_norm = [None] * NT
        st_tr = [None] * NT
        st_ag = [None] * NT
        for i in range(NT):
            dep_war = st_norm[i - 1] if i >= 1 else pv.get("out_last1")
            ld = S.dma("sync", [dep_war], xst[i % 2][:],
                       d["x"][b * T + i * 128: b * T + (i + 1) * 128, :])
            bs = S.step("vector", [ld, st_ag[i - 1] if i >= 1 else pv.get("ag_last")],
                        lambda e, i=i: [e.bn_stats(stats[:, 6 * k:6 * k + 6],
                                                   xst[i % 2][:, 512 * k:512 * (k + 1)]) for k in range(4)][-1])
            st_ag[i] = S.step("vector", [bs], lambda e, i=i: e.bn_aggr(mv[i % 2][:], stats[:]))
            ln1 = S.step("scalar", [st_ag[i], st_norm[i - 2] if i >= 2 else pv.get("norm_last"), c_ones],
                         lambda e, i=i: e.activation(rstd[i % 2][:], mv[i % 2][:, 1:2], AF.Ln, bias=eps_c[:, 0:1]))
            ln2 = S.step("scalar", [ln1], lambda e, i=i: e.activation(rstd[i % 2][:], rstd[i % 2][:], AF.Exp, scale=-0.5))
            st_norm[i] = S.step("vector", [ln2, st_tr[i - 2] if i >= 2 else pv.get("tr_last")],
                                lambda e, i=i: e.tensor_scalar(xn[i % 2][:], xst[i % 2][:], mv[i % 2][:, 0:1],
                                                               rstd[i % 2][:, 0:1], ALU.subtract, ALU.mult))
            deps_tr = [st_norm[i]] + ([pv.get("po_last")] if (b > 0 and i == 0) else [])
            st_tr[i] = S.step("sync", deps_tr,
                              lambda e, i=i: [e.dma_start_transpose(
                                  bigA[:, dt * T + i * 128: dt * T + (i + 1) * 128],
                                  xn[i % 2][:, dt * 128:(dt + 1) * 128]) for dt in range(16)],
                              weight=16 * 16)

        units = [("v", m) for m in range(4)]
        for h in range(NHEAD):
            if h % GRP == 0:
                units.append(("k", h // GRP))
            units.append(("q", h))

        def bufof(kind, idx):
            return w_sl[(2 + idx % 2) if kind != "q" else (idx % 2)]

        wsrc_of = {"q": d["wq"], "k": d["wk"], "v": d["wv"]}
        wld = {}
        last_proj_of = {}

        def ensure_wld(ui):
            if ui >= len(units) or units[ui] in wld:
                return
            kind, idx = units[ui]
            buf = bufof(kind, idx)
            wsrc = wsrc_of[kind][:, idx * 128:(idx + 1) * 128]
            key = ("q", idx % 2) if kind == "q" else ("kv", idx % 2)
            dep = last_proj_of.get(key, pv.get("proj_last") if b > 0 else None)
            wld[(kind, idx)] = S.dma("gdma", [dep],
                                     buf[:].rearrange("p (dt m) -> p dt m", dt=16),
                                     wsrc.rearrange("(dt p) m -> p dt m", p=128))
        ensure_wld(0)
        ensure_wld(1)

        st_ropeB = {}
        prev_pm = pv.get("pm_last")
        prev_ropeA = None
        prev_ropeB = None
        last_reader_of = {}
        prev_stt = pv.get("stt_last")
        prev_rec = pv.get("rec_last")
        prev_bcp = pv.get("bcp_last")
        prev_ctx_dma = {0: pv.get("ctxd_m2"), 1: pv.get("ctxd_m1")}
        st_ctx_dma = []
        psq_prev = pv.get("psq_free")
        exp_slot_free = pv.get("avs_last")
        sc_bank_prev = [pv.get("avs_last"), pv.get("avs_last")]
        proj_last = None
        vT_done = {}
        prev_vtr = pv.get("vtr_last")

        for ui, (kind, idx) in enumerate(units):
            ensure_wld(ui)
            wbuf = bufof(kind, idx)
            if kind == "q":
                brow, boff = bqo_row, idx * 128
                bdep = c_bq
            elif kind == "k":
                brow, boff = bk_row, idx * 128
                bdep = c_bk
            else:
                brow, boff = bv_row, idx * 128
                bdep = c_bv
            dstT = qT if kind == "q" else kT
            for tc in range(TC):
                deps = [wld[(kind, idx)], psq_prev, bdep, c_ones] + [st_tr[min(4 * tc + j, NT - 1)] for j in range(4)]
                if b > 0 and ui == 0 and tc == 0:
                    deps.append(pv.get("radd_last"))
                def pr_fn(e, wbuf=wbuf, brow=brow, boff=boff, tc=tc):
                    e.matmul(ps_q[:], brow[0:1, boff:boff + 128], ones_r[0:1, :], start=True, stop=False)
                    last = None
                    for dt in range(16):
                        last = e.matmul(ps_q[:], wbuf[:, dt * 128:(dt + 1) * 128],
                                        bigA[:, dt * T + tc * 512: dt * T + tc * 512 + 512],
                                        start=False, stop=(dt == 15))
                    return last
                stp = S.step("tensor", deps, pr_fn)
                proj_last = stp
                if tc == TC - 1:
                    last_proj_of[("q", idx % 2) if kind == "q" else ("kv", idx % 2)] = stp
                    ensure_wld(ui + 1)
                    ensure_wld(ui + 2)
                if kind == "v":
                    ev_war = vT_done.get(idx - 1, pv.get("avs_last"))
                    ev = S.step("scalar", [stp, ev_war] if ev_war else [stp],
                                lambda e, dstT=dstT, tc=tc: e.copy(dstT[:, tc * 512:(tc + 1) * 512], ps_q[:]))
                    psq_prev = ev
                    continue
                ev = S.step("scalar", [stp] + ([prev_pm] if prev_pm else []),
                            lambda e: e.copy(qf32[:], ps_q[:]))
                psq_prev = ev
                pm = S.step("tensor", [ev, prev_ropeA],
                            lambda e: (
                                e.matmul(ps_p1[:], p1t[:], qf32[:], start=True, stop=True),
                                e.matmul(ps_p2[:], p2t[:], qf32[:], start=True, stop=True))[1])
                prev_pm = pm
                ra = S.step("vector", [pm, prev_ropeB, c_tab],
                            lambda e, tc=tc: (
                                e.tensor_tensor(tmp1[:], ps_p1[:], ct[:, tc * 512:(tc + 1) * 512], ALU.mult),
                                e.tensor_tensor(tmp2[:], ps_p2[:], st[:, tc * 512:(tc + 1) * 512], ALU.mult))[1])
                prev_ropeA = ra
                if kind == "k":
                    rb_war = last_reader_of.get(("k", idx - 1), vT_done.get(3))
                else:
                    rb_war = last_reader_of.get(("q", idx - 1), pv.get("avs_last"))
                rb_ = S.step("vector", [ra] + ([rb_war] if rb_war else []),
                             lambda e, dstT=dstT, tc=tc: e.tensor_tensor(
                                 dstT[:, tc * 512:(tc + 1) * 512], tmp1[:], tmp2[:], ALU.add))
                prev_ropeB = rb_
                st_ropeB[(kind, idx, tc)] = rb_

            if kind == "v":
                war_vtm = pv.get("avs_last") if (b > 0 and idx == 0) else None
                vt = S.step("sync", [psq_prev] + ([war_vtm] if war_vtm else []),
                            lambda e, m=idx: [e.dma_start_transpose(
                                v_tm[:, i * 512 + m * 128: i * 512 + (m + 1) * 128],
                                kT[:, i * 128:(i + 1) * 128]) for i in range(NT)],
                            weight=16 * NT)
                vT_done[idx] = vt
                continue
            if kind == "k":
                continue

            h = idx
            g = h // GRP
            for tc in range(TC):
                smax = min(4 * (tc + 1), NT)
                rdy = [None] * smax
                for s_ in range(smax):
                    deps = [st_ropeB[("q", h, tc)], st_ropeB[("k", g, s_ // 4)],
                            sc_bank_prev[s_ % 2]]
                    scs = S.step("tensor", deps,
                                 lambda e, s_=s_, tc=tc: e.matmul(
                                     ps_sc[s_ % 2][:], kT[:, s_ * 128:(s_ + 1) * 128],
                                     qT[:, tc * 512:(tc + 1) * 512], start=True, stop=True))
                    ex = S.step("scalar", [scs, exp_slot_free],
                                lambda e, s_=s_: e.activation(expT[:, s_ * 512:(s_ + 1) * 512],
                                                              ps_sc[s_ % 2][:], AF.Exp))
                    sc_bank_prev[s_ % 2] = ex
                    if s_ >= 4 * tc:
                        r = s_ - 4 * tc
                        rdy[s_] = S.step("vector", [ex, c_msk],
                                         lambda e, s_=s_, r=r: e.tensor_tensor(
                                             expT[:, s_ * 512:(s_ + 1) * 512], expT[:, s_ * 512:(s_ + 1) * 512],
                                             dmask[:, r * 512:(r + 1) * 512], ALU.mult))
                    else:
                        rdy[s_] = ex
                avs = None
                for s_ in range(smax):
                    deps = [rdy[s_], vT_done[g]]
                    if s_ == 0:
                        deps += [prev_rec, prev_stt, prev_bcp, c_ones]
                    def av_fn(e, s_=s_, g=g, first=(s_ == 0), last=(s_ == smax - 1)):
                        e.matmul(ps_rs[0:1, :], ones_c[:], expT[:, s_ * 512:(s_ + 1) * 512],
                                 start=first, stop=last)
                        return e.matmul(ps_ctx[:], v_tm[:, s_ * 512 + g * 128: s_ * 512 + g * 128 + 128],
                                        expT[:, s_ * 512:(s_ + 1) * 512], start=first, stop=last)
                    avs = S.step("tensor", deps, av_fn)
                exp_slot_free = avs
                last_reader_of[("q", h)] = avs
                if h % GRP == GRP - 1:
                    last_reader_of[("k", g)] = avs
                rec = S.step("vector", [avs], lambda e: e.reciprocal(recip[:], ps_rs[0:1, :]))
                prev_rec = rec
                bcm = S.step("tensor", [rec, prev_bcp],
                             lambda e: e.matmul(ps_bc[:], ones_r[0:1, 0:128], recip[0:1, :],
                                                start=True, stop=True))
                rbc = S.step("scalar", [bcm, prev_stt],
                             lambda e: e.copy(recipB[:], ps_bc[:]))
                prev_bcp = rbc
                j = (h * TC + tc) % 2
                stt = S.step("vector", [rbc, avs, prev_ctx_dma[j]],
                             lambda e, j=j: e.scalar_tensor_tensor(
                                 ctx_sb[j][:], ps_ctx[:], SCALE_OUT, recipB[:], ALU.mult, ALU.mult))
                prev_stt = stt
                cd = S.dma("sdma", [stt], scr[:, h * T + tc * 512: h * T + (tc + 1) * 512], ctx_sb[j][:])
                prev_ctx_dma[j] = cd
                st_ctx_dma.append(cd)

        xnT_free = psq_prev
        wo_lds = []
        for dt in range(16):
            wo_lds.append(S.dma("gdma", [xnT_free] if dt == 0 else [],
                                bigA[:, dt * 2048:(dt + 1) * 2048],
                                d["wo"][dt * 128:(dt + 1) * 128, :]))
        c_bo = S.dma("gdma", [proj_last], bqo_row[:], d["bo"].rearrange("(a m) -> a m", a=1))
        st_radd = [None] * NT
        st_out = [None] * NT
        st_cg = [None] * NT
        st_po = [None] * NT
        po_last = None
        scr3 = scr.rearrange("p (dt t) -> p dt t", dt=16)
        for i in range(NT):
            cg_deps = (S.dma_fence("sdma") if i == 0 else []) + \
                      [st_po[i - 2] if i >= 2 else pv.get("po_last")]
            st_cg[i] = S.dma("sync", cg_deps,
                             ctxg[i % 2][:].rearrange("p (dt t) -> p dt t", dt=16),
                             scr3[:, :, i * 128:(i + 1) * 128])
            x2 = S.dma("sync", [st_out[i - 1] if i >= 1 else pv.get("out_last1"),
                                st_norm[NT - 1] if i < 2 else None],
                       xst[i % 2][:], d["x"][b * T + i * 128: b * T + (i + 1) * 128, :])
            deps = wo_lds + [st_cg[i], c_bo, st_radd[i - 1] if i >= 1 else pv.get("radd_last")]
            if i == 0:
                deps.append(prev_stt)
            def po_fn(e, i=i):
                for c in range(4):
                    e.matmul(ps_out4[c][:], ones_r[0:1, 0:128],
                             bqo_row[0:1, c * 512:(c + 1) * 512], start=True, stop=False)
                last = None
                for dt in range(16):
                    for c in range(4):
                        last = e.matmul(ps_out4[c][:],
                                        ctxg[i % 2][:, dt * 128:(dt + 1) * 128],
                                        bigA[:, dt * 2048 + c * 512: dt * 2048 + (c + 1) * 512],
                                        start=False, stop=(dt == 15))
                return last
            po = S.step("tensor", deps, po_fn)
            po_last = po
            st_po[i] = po_last
            ra = S.step("vector", [po_last, x2],
                        lambda e, i=i: [e.tensor_tensor(xst[i % 2][:, c * 512:(c + 1) * 512],
                                                        ps_out4[c][:], xst[i % 2][:, c * 512:(c + 1) * 512],
                                                        ALU.add) for c in range(4)][-1])
            st_radd[i] = ra
            st_out[i] = S.dma("sync", [ra], d["out"][b * T + i * 128: b * T + (i + 1) * 128, :], xst[i % 2][:])
            last_out.append(st_out[i])

        pv = dict(norm_last=st_norm[NT - 1], tr_last=st_tr[NT - 1], ag_last=st_ag[NT - 1],
                  po_last=po_last, psq_free=psq_prev, radd_last=st_radd[NT - 1],
                  stt_last=prev_stt, rec_last=prev_rec, avs_last=exp_slot_free,
                  proj_last=proj_last, bcp_last=prev_bcp, pm_last=prev_pm,
                  out_last2=st_out[NT - 2] if NT >= 2 else None,
                  out_last1=st_out[NT - 1], ctxd_m2=prev_ctx_dma[0], ctxd_m1=prev_ctx_dma[1],
                  vtr_last=vT_done.get(3))

    return last_out


D = 2048
HID = 5632
NH = HID // 128


def emit_l0_ffn(S, NTOK, d, SLAB=1024):
    NSL = NTOK // SLAB
    NT = SLAB // 128
    TC = SLAB // 512
    sb = S.sbuf
    G = S.global_psums()
    ps_g = [G["q"], G["p1"]][:TC]
    ps_u = [G["p2"], G["sc0"]][:TC]
    ps_d0 = [G["sc1"], G["ctx"], G["bc"], G["d"]]
    ps_d1 = [G["q"], G["p1"], G["p2"], G["sc0"]]

    xst = [sb(f"l0f_xst{i}", [128, D], F32) for i in range(2)]
    xn = [sb(f"l0f_xn{i}", [128, D], BF16) for i in range(2)]
    stats = sb("l0f_stats", [128, 24], F32)
    mv = [sb(f"l0f_mv{i}", [128, 2], F32) for i in range(2)]
    rstd = [sb(f"l0f_rstd{i}", [128, 1], F32) for i in range(2)]
    eps_c = sb("l0f_eps", [128, 1], F32)
    xnT = sb("l0f_xnT", [128, 16 * SLAB], BF16)
    wbf = [sb(f"l0f_wbf{i}", [128, D], BF16) for i in range(2)]
    wtp = [sb(f"l0f_wtp{i}", [128, D], BF16) for i in range(2)]
    sgu = sb("l0f_sgu", [128, 2 * NH], F32)
    sdc = sb("l0f_sdc", [128, 16], F32)
    wg = [sb(f"l0f_wg{i}", [128, D], BF16) for i in range(2)]
    wu = [sb(f"l0f_wu{i}", [128, D], BF16) for i in range(2)]
    wd = [sb(f"l0f_wd{i}", [128, D], BF16) for i in range(2)]
    g_sb = sb("l0f_gsb", [128, SLAB], BF16)
    hT = sb("l0f_hT", [128, NH * SLAB], BF16)

    scr_g, scr_u, scr_d = d["scr_g"], d["scr_u"], d["scr_d"]

    c_eps = S.step("vector", [], lambda e: e.memset(eps_c[:], 1e-5))
    c_sg = S.dma("gdma", [], sgu[:, 0:NH], d["s_gate_t"])
    c_su = S.dma("gdma", [], sgu[:, NH:2 * NH], d["s_up_t"])
    c_sd = S.dma("gdma", [], sdc[:], d["s_down_t"])

    prep_units = [(k, ht) for k in ("g", "u") for ht in range(NH)] + [("d", ht) for ht in range(NH)]
    prep_done = {}
    pu_prev = {0: None, 1: None}
    dq_prev = {0: None, 1: None}
    tp_prev = {0: None, 1: None}
    for n, (k, ht) in enumerate(prep_units):
        par = n % 2
        if k == "d":
            src = d["wq_down"][:, ht * 128:(ht + 1) * 128].rearrange("(dt p) m -> p dt m", p=128)
            scr = scr_d
            ld = S.dma("gdma", [tp_prev[par]], wbf[par][:].rearrange("p (dt m) -> p dt m", dt=16), src)
            dq = S.step("vector", [ld, tp_prev[par], c_sd],
                        lambda e, par=par: [e.tensor_scalar(
                            wbf[par][:, dt * 128:(dt + 1) * 128], wbf[par][:, dt * 128:(dt + 1) * 128],
                            -128.0, sdc[:, dt:dt + 1], ALU.add, ALU.mult) for dt in range(16)][-1])
        else:
            src = (d["wq_gate"] if k == "g" else d["wq_up"])[ht * 128:(ht + 1) * 128, :]
            scr = scr_g if k == "g" else scr_u
            col = ht if k == "g" else NH + ht
            ld = S.dma("gdma", [tp_prev[par]], wbf[par][:], src)
            dq = S.step("vector", [ld, tp_prev[par], c_sg if k == "g" else c_su],
                        lambda e, par=par, col=col: e.tensor_scalar(
                            wbf[par][:], wbf[par][:], -128.0, sgu[:, col:col + 1], ALU.add, ALU.mult))
        dq_prev[par] = dq
        tp = S.step("sync", [dq, pu_prev[par]],
                    lambda e, par=par: [e.dma_start_transpose(
                        wtp[par][:, dt * 128:(dt + 1) * 128],
                        wbf[par][:, dt * 128:(dt + 1) * 128]) for dt in range(16)],
                    weight=256)
        tp_prev[par] = tp
        stc = S.dma("sdma", [tp], scr[ht * 128:(ht + 1) * 128, :], wtp[par][:])
        pu_prev[par] = stc
        prep_done[(k, ht)] = stc

    last_out = []
    pv = {}

    for sl in range(NSL):
        base = sl * SLAB
        st_norm = [None] * NT
        st_tr = [None] * NT
        st_ag = [None] * NT
        for i in range(NT):
            dep_war = st_norm[i - 2] if i >= 2 else (pv.get("out_last2") if i == 0 else pv.get("out_last1"))
            ld = S.dma("sync", [dep_war], xst[i % 2][:],
                       d["x"][base + i * 128: base + (i + 1) * 128, :])
            bs = S.step("vector", [ld, st_ag[i - 1] if i >= 1 else pv.get("ag_last")],
                        lambda e, i=i: [e.bn_stats(stats[:, 6 * kk:6 * kk + 6],
                                                   xst[i % 2][:, 512 * kk:512 * (kk + 1)]) for kk in range(4)][-1])
            st_ag[i] = S.step("vector", [bs], lambda e, i=i: e.bn_aggr(mv[i % 2][:], stats[:]))
            ln1 = S.step("scalar", [st_ag[i], st_norm[i - 2] if i >= 2 else pv.get("norm_last"), c_eps],
                         lambda e, i=i: e.activation(rstd[i % 2][:], mv[i % 2][:, 1:2], AF.Ln, bias=eps_c[:, 0:1]))
            ln2 = S.step("scalar", [ln1], lambda e, i=i: e.activation(rstd[i % 2][:], rstd[i % 2][:], AF.Exp, scale=-0.5))
            st_norm[i] = S.step("vector", [ln2, st_tr[i - 2] if i >= 2 else pv.get("tr_last")],
                                lambda e, i=i: e.tensor_scalar(xn[i % 2][:], xst[i % 2][:], mv[i % 2][:, 0:1],
                                                               rstd[i % 2][:, 0:1], ALU.subtract, ALU.mult))
            deps_tr = [st_norm[i]] + ([pv.get("xnT_free")] if i < 2 else [])
            st_tr[i] = S.step("sync", deps_tr,
                              lambda e, i=i: [e.dma_start_transpose(
                                  xnT[:, dt * SLAB + i * 128: dt * SLAB + (i + 1) * 128],
                                  xn[i % 2][:, dt * 128:(dt + 1) * 128]) for dt in range(16)],
                              weight=256)

        st_g = {}
        st_u = {}
        wld_g = {}
        wld_u = {}
        hread_prev = pv.get("hread", {})

        def ensure_gu(ht):
            if ht in wld_g or ht >= NH:
                return
            par = ht % 2
            depg = st_g.get(ht - 2, pv.get("xnT_free"))
            depu = st_u.get(ht - 2, pv.get("xnT_free"))
            wld_g[ht] = S.dma("gdma", [prep_done[("g", ht)], depg], wg[par][:],
                              scr_g[ht * 128:(ht + 1) * 128, :])
            wld_u[ht] = S.dma("gdma", [prep_done[("u", ht)], depu], wu[par][:],
                              scr_u[ht * 128:(ht + 1) * 128, :])
        ensure_gu(0)
        ensure_gu(1)

        prev_gev = pv.get("gev_last")
        prev_hvv = pv.get("hvv_last")
        h_ready = [None] * NH
        for ht in range(NH):
            par = ht % 2
            deps = [wld_g[ht], prev_gev] + [st_tr[j] for j in range(NT)]
            if ht == 0:
                deps.append(pv.get("radd_last"))
            gp = S.step("tensor", deps,
                        lambda e, par=par: [e.matmul(ps_g[c][:], wg[par][:, dt * 128:(dt + 1) * 128],
                                                     xnT[:, dt * SLAB + c * 512: dt * SLAB + (c + 1) * 512],
                                                     start=(dt == 0), stop=(dt == 15))
                                            for c in range(TC) for dt in range(16)][-1])
            st_g[ht] = gp
            gev = S.step("scalar", [gp, prev_hvv],
                         lambda e: [e.activation(g_sb[:, c * 512:(c + 1) * 512], ps_g[c][:], AF.Silu)
                                    for c in range(TC)][-1])
            prev_gev = gev
            up = S.step("tensor", [wld_u[ht], gev],
                        lambda e, par=par: [e.matmul(ps_u[c][:], wu[par][:, dt * 128:(dt + 1) * 128],
                                                     xnT[:, dt * SLAB + c * 512: dt * SLAB + (c + 1) * 512],
                                                     start=(dt == 0), stop=(dt == 15))
                                            for c in range(TC) for dt in range(16)][-1])
            st_u[ht] = up
            hv = S.step("vector", [up, gev, hread_prev.get(ht)],
                        lambda e, ht=ht: [e.tensor_tensor(hT[:, ht * SLAB + c * 512: ht * SLAB + (c + 1) * 512],
                                                          ps_u[c][:], g_sb[:, c * 512:(c + 1) * 512], ALU.mult)
                                          for c in range(TC)][-1])
            prev_hvv = hv
            h_ready[ht] = hv
            ensure_gu(ht + 2)

        st_d = {}
        wldd = {}
        st_radd = {}
        st_out_l = {}

        def ensure_d(j, ht):
            if (j, ht) in wldd or ht >= NH:
                return
            if j >= NT // 2:
                return
            par = (j * NH + ht) % 2
            dep = st_d.get((j, ht - 2)) if ht >= 2 else (st_d.get((j - 1, NH - 2 + ht)) or pv.get("dlast"))
            wldd[(j, ht)] = S.dma("gdma", [prep_done[("d", ht)], dep], wd[par][:],
                                  scr_d[ht * 128:(ht + 1) * 128, :])
        ensure_d(0, 0)
        ensure_d(0, 1)
        for j in range(NT // 2):
            po = None
            for ht in range(NH):
                par = (j * NH + ht) % 2
                deps = [wldd[(j, ht)], h_ready[ht]]
                if ht == 0:
                    deps += [st_radd.get(j - 1, pv.get("radd_last")), h_ready[NH - 1], prev_gev]
                def dp_fn(e, j=j, ht=ht, par=par, first=(ht == 0), last=(ht == NH - 1)):
                    out = None
                    for tt in range(2):
                        pset = ps_d0 if tt == 0 else ps_d1
                        for c in range(4):
                            out = e.matmul(pset[c][:],
                                           hT[:, ht * SLAB + (2 * j + tt) * 128: ht * SLAB + (2 * j + tt + 1) * 128],
                                           wd[par][:, c * 512:(c + 1) * 512],
                                           start=first, stop=last)
                    return out
                po = S.step("tensor", deps, dp_fn)
                st_d[(j, ht)] = po
                if ht + 2 < NH:
                    ensure_d(j, ht + 2)
                else:
                    ensure_d(j + 1, ht + 2 - NH)
            for tt in range(2):
                i = 2 * j + tt
                x2 = S.dma("sync", [st_norm[NT - 1],
                                    st_out_l.get(i - 2, pv.get("out_last2") if tt == 0 else pv.get("out_last1"))],
                           xst[i % 2][:], d["x"][base + i * 128: base + (i + 1) * 128, :])
                pset = ps_d0 if tt == 0 else ps_d1
                ra = S.step("vector", [po, x2],
                            lambda e, i=i, pset=pset: [
                                e.tensor_tensor(xst[i % 2][:, c * 512:(c + 1) * 512], pset[c][:],
                                                xst[i % 2][:, c * 512:(c + 1) * 512], ALU.add)
                                for c in range(4)][-1])
                st_radd[j] = ra
                so = S.dma("sync", [ra], d["out"][base + i * 128: base + (i + 1) * 128, :], xst[i % 2][:])
                st_out_l[i] = so
                last_out.append(so)

        pv = dict(
            norm_last=st_norm[NT - 1], tr_last=st_tr[NT - 1], ag_last=st_ag[NT - 1],
            gev_last=prev_gev, hvv_last=prev_hvv, radd_last=st_radd[NT // 2 - 1],
            out_last2=st_out_l[NT - 2], out_last1=st_out_l[NT - 1],
            xnT_free=st_u[NH - 1], dlast=st_d[(NT // 2 - 1, NH - 1)],
            hread={ht: st_d[(NT // 2 - 1, ht)] for ht in range(NH)},
        )

    return last_out


NHEAD = 16
D = 2048
DC = 512
DR = 64
SCL = 1.0 / np.sqrt(192.0)


def emit_l1_attn(S, T, NB, d):
    NT = T // 128
    TC = T // 512
    sb = S.sbuf
    G = S.global_psums()
    ps_q, ps_p1, ps_p2 = G["q"], G["p1"], G["p2"]
    ps_ctx, ps_bc, ps_v = G["ctx"], G["bc"], G["d"]
    ps_rs = ps_bc
    ps_sc = [G["sc0"], G["sc1"]]
    ps_out4 = [G["sc0"], G["sc1"], G["ctx"], G["d"]]

    xst = sb("l1a_xst", [128, D], F32)
    xn = sb("l1a_xn", [128, D], BF16)
    stats = sb("l1a_stats", [128, 24], F32)
    mv = [sb(f"l1a_mv{i}", [128, 2], F32) for i in range(2)]
    rstd = [sb(f"l1a_rstd{i}", [128, 1], F32) for i in range(2)]
    eps_c = sb("l1a_eps", [128, 1], F32)
    bigA = sb("l1a_bigA", [128, 16 * 2048], BF16)
    ct = sb("l1a_ct", [128, T], F32)
    st = sb("l1a_st", [128, T], F32)
    p1t = sb("l1a_p1t", [128, 128], F32)
    p2t = sb("l1a_p2t", [128, 128], F32)
    qf32 = sb("l1a_qf32", [128, 512], F32)
    w_sl = [sb(f"l1a_w{i}", [128, 2048], BF16) for i in range(2)]
    bo_row = sb("l1a_bo", [1, D], F32)
    ones_r = sb("l1a_ones_r", [1, 512], F32)
    ones_c = sb("l1a_ones_c", [128, 1], BF16)
    ckvT = sb("l1a_ckvT", [128, 4 * T], BF16)
    cqT = sb("l1a_cqT", [128, 4 * T], BF16)
    kRT = sb("l1a_kRT", [128, T], BF16)
    qRT = sb("l1a_qRT", [128, T], BF16)
    qT = sb("l1a_qT", [128, T], BF16)
    kT = sb("l1a_kT", [128, T], BF16)
    v_tm = sb("l1a_vtm", [128, NT * 128], BF16)
    tmp1 = sb("l1a_tmp1", [128, 512], F32)
    tmp2 = sb("l1a_tmp2", [128, 512], F32)
    expT = sb("l1a_expT", [128, NT * 512], BF16)
    recip = sb("l1a_recip", [1, 512], F32)
    recipB = sb("l1a_recipB", [128, 512], F32)
    ctx_sb = [sb(f"l1a_ctxsb{i}", [128, 512], BF16) for i in range(2)]
    ctxg0 = sb("l1a_ctxg0", [128, 2048], BF16)
    ctxg = [ctxg0, ctxg0]

    scr = d["ctx_scr"]

    c_tab = S.dma("gdma", [], ct[:], d["ct"])
    S.dma("gdma", [], st[:], d["st"])
    S.dma("gdma", [], p1t[:], d["p1t"])
    c_p2 = S.dma("gdma", [], p2t[:], d["p2t"])
    c_ones = S.step("vector", [], lambda e: (e.memset(ones_r[:], 1.0), e.memset(ones_c[:], 1.0),
                                             e.memset(eps_c[:], 1e-5))[-1])

    last_out = []
    pv = {}

    for b in range(NB):
        c_bo = S.dma("gdma", [pv.get("po_last")], bo_row[:], d["b_o"].rearrange("(a m) -> a m", a=1))

        st_norm = [None] * NT
        st_tr = [None] * NT
        st_ag = [None] * NT
        for i in range(NT):
            dep_war = st_norm[i - 1] if i >= 1 else pv.get("out_last1")
            ld = S.dma("sync", [dep_war], xst[:],
                       d["x"][b * T + i * 128: b * T + (i + 1) * 128, :])
            bs = S.step("vector", [ld, st_ag[i - 1] if i >= 1 else pv.get("ag_last")],
                        lambda e: [e.bn_stats(stats[:, 6 * kk:6 * kk + 6],
                                              xst[:, 512 * kk:512 * (kk + 1)]) for kk in range(4)][-1])
            st_ag[i] = S.step("vector", [bs], lambda e, i=i: e.bn_aggr(mv[i % 2][:], stats[:]))
            ln1 = S.step("scalar", [st_ag[i], st_norm[i - 2] if i >= 2 else pv.get("norm_last"), c_ones],
                         lambda e, i=i: e.activation(rstd[i % 2][:], mv[i % 2][:, 1:2], AF.Ln, bias=eps_c[:, 0:1]))
            ln2 = S.step("scalar", [ln1], lambda e, i=i: e.activation(rstd[i % 2][:], rstd[i % 2][:], AF.Exp, scale=-0.5))
            st_norm[i] = S.step("vector", [ln2, st_tr[i - 1] if i >= 1 else pv.get("tr_last")],
                                lambda e, i=i: e.tensor_scalar(xn[:], xst[:], mv[i % 2][:, 0:1],
                                                               rstd[i % 2][:, 0:1], ALU.subtract, ALU.mult))
            deps_tr = [st_norm[i]] + ([pv.get("po_last")] if (b > 0 and i == 0) else [])
            st_tr[i] = S.step("sync", deps_tr,
                              lambda e, i=i: [e.dma_start_transpose(
                                  bigA[:, dt * T + i * 128: dt * T + (i + 1) * 128],
                                  xn[:, dt * 128:(dt + 1) * 128]) for dt in range(16)],
                              weight=256)
        tr_all = st_tr[NT - 1]

        units = [("ckv", m) for m in range(4)] + [("cq", m) for m in range(4)] + [("kr", 0)]
        for h in range(NHEAD):
            if h % 2 == 0:
                units.append(("qr", h // 2))
            units.append(("k", h))
            units.append(("q", h))
            units.append(("v", h))

        def wspec(kind, idx):
            if kind == "ckv":
                return d["w_dkv"][:, idx * 128:(idx + 1) * 128], 16, 128
            if kind == "cq":
                return d["w_dq"][:, idx * 128:(idx + 1) * 128], 16, 128
            if kind == "kr":
                return d["w_kr"], 16, 128
            if kind == "qr":
                return d["w_qr"][:, idx * 128:(idx + 1) * 128], 4, 128
            if kind == "k":
                return d["w_uk"][:, idx * 128:(idx + 1) * 128], 4, 128
            if kind == "q":
                return d["w_uq"][:, idx * 128:(idx + 1) * 128], 4, 128
            if kind == "v":
                return d["w_uv"][:, idx * 128:(idx + 1) * 128], 4, 128

        wld = {}
        wld_reader = {}

        def ensure_wld(ui):
            if ui >= len(units) or units[ui] in wld:
                return
            kind, idx = units[ui]
            wsrc, nk, mc = wspec(kind, idx)
            par = ui % 2
            if kind == "kr":
                v3 = w_sl[par][:].rearrange("p (dt m) -> p dt m", m=128)[:, 0:nk, :]
                d1 = S.dma("gdma", [wld_reader.get(par)], v3[:, :, 0:64],
                           wsrc.rearrange("(dt p) m -> p dt m", p=128))
                wld[(kind, idx)] = S.dma("gdma", [d1], v3[:, :, 64:128],
                                         wsrc.rearrange("(dt p) m -> p dt m", p=128))
            else:
                wld[(kind, idx)] = S.dma("gdma", [wld_reader.get(par)],
                                         w_sl[par][:].rearrange("p (dt m) -> p dt m", m=mc)[:, 0:nk, :],
                                         wsrc.rearrange("(dt p) m -> p dt m", p=128))
        ensure_wld(0)
        ensure_wld(1)

        st_ropeB = {}
        prev_pm = pv.get("pm_last")
        prev_ropeA = None
        prev_ropeB = None
        last_reader_of = {}
        prev_stt = pv.get("stt_last")
        prev_rec = pv.get("rec_last")
        prev_bcp = pv.get("bcp_last")
        prev_ctx_dma = {0: pv.get("ctxd_m2"), 1: pv.get("ctxd_m1")}
        psq_prev = pv.get("psq_free")
        psv_prev = pv.get("psv_free")
        exp_slot_free = pv.get("avs_last")
        sc_bank_prev = [pv.get("avs_last"), pv.get("avs_last")]
        proj_last = None
        kr_done = None

        for ui, (kind, idx) in enumerate(units):
            ensure_wld(ui)
            par = ui % 2
            wbuf = w_sl[par]
            wsrc, nk, mc = wspec(kind, idx)

            if kind == "v":
                h = idx
                deps = [wld[(kind, idx)], tr_all, psv_prev,
                        last_reader_of.get(("v", h - 1), pv.get("avs_last"))]
                vev = None
                for i in range(NT):
                    vp = S.step("tensor", deps if i == 0 else [vev],
                                lambda e, wbuf=wbuf, i=i: [e.matmul(
                                    ps_v[:, 0:128], ckvT[:, kt * T + i * 128: kt * T + (i + 1) * 128],
                                    wbuf[:, kt * 128:(kt + 1) * 128], start=(kt == 0), stop=(kt == 3))
                                    for kt in range(4)][-1])
                    vev = S.step("scalar", [vp],
                                 lambda e, i=i: e.copy(v_tm[:, i * 128:(i + 1) * 128], ps_v[:, 0:128]))
                wld_reader[par] = vp
                psv_prev = vev
                vT_ready = vev
                ensure_wld(ui + 1)
                ensure_wld(ui + 2)
                g2 = h % 2
                for tc in range(TC):
                    rdy = [None] * NT
                    st_sc = None
                    for s_ in range(NT):
                        deps = [st_ropeB[("q", h)], st_ropeB[("k", h)],
                                st_ropeB[("qr", h // 2)], kr_done,
                                sc_bank_prev[s_ % 2]]
                        st_sc = S.step("tensor", deps,
                                       lambda e, s_=s_, tc=tc, g2=g2: (
                                           e.matmul(ps_sc[s_ % 2][:], kT[:, s_ * 128:(s_ + 1) * 128],
                                                    qT[:, tc * 512:(tc + 1) * 512], start=True, stop=False),
                                           e.matmul(ps_sc[s_ % 2][:], kRT[64 * g2:64 * g2 + 64, s_ * 128:(s_ + 1) * 128],
                                                    qRT[64 * g2:64 * g2 + 64, tc * 512:(tc + 1) * 512],
                                                    start=False, stop=True))[1])
                        ex = S.step("scalar", [st_sc, exp_slot_free],
                                    lambda e, s_=s_: e.activation(expT[:, s_ * 512:(s_ + 1) * 512],
                                                                  ps_sc[s_ % 2][:], AF.Exp, scale=SCL))
                        sc_bank_prev[s_ % 2] = ex
                        rdy[s_] = ex
                    avs = None
                    for s_ in range(NT):
                        deps = [rdy[s_], vT_ready]
                        if s_ == 0:
                            deps += [prev_rec, prev_stt, prev_bcp, c_ones]
                        def av_fn(e, s_=s_, first=(s_ == 0), last=(s_ == NT - 1)):
                            e.matmul(ps_rs[0:1, :], ones_c[:], expT[:, s_ * 512:(s_ + 1) * 512],
                                     start=first, stop=last)
                            return e.matmul(ps_ctx[:], v_tm[:, s_ * 128:(s_ + 1) * 128],
                                            expT[:, s_ * 512:(s_ + 1) * 512], start=first, stop=last)
                        avs = S.step("tensor", deps, av_fn)
                    exp_slot_free = avs
                    last_reader_of[("v", h)] = avs
                    last_reader_of[("q", h)] = avs
                    last_reader_of[("k", h)] = avs
                    if h % 2 == 1:
                        last_reader_of[("qr", h // 2)] = avs
                    rec = S.step("vector", [avs], lambda e: e.reciprocal(recip[:], ps_rs[0:1, :]))
                    prev_rec = rec
                    bcm = S.step("tensor", [rec, prev_bcp],
                                 lambda e: e.matmul(ps_bc[:], ones_r[0:1, 0:128], recip[0:1, :],
                                                    start=True, stop=True))
                    rbc = S.step("scalar", [bcm, prev_stt],
                                 lambda e: e.copy(recipB[:], ps_bc[:]))
                    prev_bcp = rbc
                    j = (h * TC + tc) % 2
                    stt = S.step("vector", [rbc, avs, prev_ctx_dma[j]],
                                 lambda e, j=j: e.scalar_tensor_tensor(
                                     ctx_sb[j][:], ps_ctx[:], 1.0, recipB[:], ALU.mult, ALU.mult))
                    prev_stt = stt
                    cd = S.dma("sdma", [stt], scr[:, h * T + tc * 512: h * T + (tc + 1) * 512], ctx_sb[j][:])
                    prev_ctx_dma[j] = cd
                continue

            if kind in ("ckv", "cq", "kr"):
                mov, movT = bigA, T
                mov_nk = 16
            else:
                mov, movT = (cqT, T) if kind in ("qr",) else (ckvT, T) if kind == "k" else (cqT, T)
                mov_nk = 4
            for tc in range(TC):
                deps = [wld[(kind, idx)], psq_prev, tr_all]
                if b > 0 and ui == 0 and tc == 0:
                    deps.append(pv.get("radd_last"))
                mrows = mc
                def pr_fn(e, wbuf=wbuf, tc=tc, nk=nk, mov=mov, movT=movT, mc=mc):
                    last = None
                    for kt in range(nk):
                        last = e.matmul(ps_q[0:mc, :], wbuf[:, kt * mc:(kt + 1) * mc],
                                        mov[:, kt * movT + tc * 512: kt * movT + tc * 512 + 512],
                                        start=(kt == 0), stop=(kt == nk - 1))
                    return last
                stp = S.step("tensor", deps, pr_fn)
                proj_last = stp
                wld_reader[par] = stp
                if tc == TC - 1:
                    ensure_wld(ui + 1)
                    ensure_wld(ui + 2)
                if kind in ("ckv", "cq", "q", "k"):
                    dst, doff = {
                        "ckv": (ckvT, idx * T), "cq": (cqT, idx * T),
                        "q": (qT, 0), "k": (kT, 0)}[kind]
                    ev_war = None
                    if kind == "q":
                        ev_war = last_reader_of.get(("q", idx - 1), pv.get("avs_last"))
                    elif kind == "k":
                        ev_war = last_reader_of.get(("k", idx - 1), pv.get("avs_last"))
                    elif b > 0 and kind in ("ckv", "cq"):
                        ev_war = pv.get("avs_last" if False else "ckv_free")
                    ev = S.step("scalar", [stp, ev_war],
                                lambda e, dst=dst, doff=doff, tc=tc, mc=mc: e.copy(
                                    dst[:, doff + tc * 512: doff + (tc + 1) * 512], ps_q[:]))
                    psq_prev = ev
                    if kind in ("ckv", "cq"):
                        st_ropeB[(kind, idx)] = ev
                    else:
                        st_ropeB[(kind, idx)] = ev
                else:
                    ev = S.step("scalar", [stp, prev_pm],
                                lambda e, mc=mc: e.copy(qf32[0:mc, :], ps_q[0:mc, :]))
                    psq_prev = ev
                    pm = S.step("tensor", [ev, prev_ropeA],
                                lambda e, mc=mc: (
                                    e.matmul(ps_p1[0:mc, :], p1t[0:mc, 0:mc], qf32[0:mc, :], start=True, stop=True),
                                    e.matmul(ps_p2[0:mc, :], p2t[0:mc, 0:mc], qf32[0:mc, :], start=True, stop=True))[1])
                    prev_pm = pm
                    ra = S.step("vector", [pm, prev_ropeB, c_tab],
                                lambda e, tc=tc, mc=mc: (
                                    e.tensor_tensor(tmp1[0:mc, :], ps_p1[0:mc, :], ct[0:mc, tc * 512:(tc + 1) * 512], ALU.mult),
                                    e.tensor_tensor(tmp2[0:mc, :], ps_p2[0:mc, :], st[0:mc, tc * 512:(tc + 1) * 512], ALU.mult))[1])
                    prev_ropeA = ra
                    if kind == "kr":
                        rb_war = pv.get("kr_free") if b > 0 else None
                        dstT, doff2 = kRT, 0
                    else:
                        rb_war = last_reader_of.get(("qr", idx - 1), pv.get("avs_last"))
                        dstT, doff2 = qRT, 0
                    rb_ = S.step("vector", [ra] + ([rb_war] if rb_war else []),
                                 lambda e, dstT=dstT, tc=tc, mc=mc: e.tensor_tensor(
                                     dstT[0:mc, tc * 512:(tc + 1) * 512], tmp1[0:mc, :], tmp2[0:mc, :], ALU.add))
                    prev_ropeB = rb_
                    if kind == "kr":
                        kr_done = rb_ if tc == TC - 1 else kr_done
                    else:
                        st_ropeB[("qr", idx)] = rb_

        xnT_free = psq_prev
        wo_lds = []
        for dt in range(16):
            wo_lds.append(S.dma("gdma", [xnT_free, proj_last] if dt == 0 else [],
                                bigA[:, dt * 2048:(dt + 1) * 2048],
                                d["w_o"][dt * 128:(dt + 1) * 128, :]))
        st_radd = [None] * NT
        st_out = [None] * NT
        st_po = [None] * NT
        st_cg = [None] * NT
        po_last = None
        scr3 = scr.rearrange("p (dt t) -> p dt t", dt=16)
        for i in range(NT):
            cg_deps = (S.dma_fence("sdma") if i == 0 else []) + \
                      [st_po[i - 1] if i >= 1 else pv.get("po_last")]
            st_cg[i] = S.dma("sync", cg_deps,
                             ctxg[i % 2][:].rearrange("p (dt t) -> p dt t", dt=16),
                             scr3[:, :, i * 128:(i + 1) * 128])
            x2 = S.dma("sync", [st_out[i - 1] if i >= 1 else pv.get("out_last1"), st_norm[NT - 1]],
                       xst[:], d["x"][b * T + i * 128: b * T + (i + 1) * 128, :])
            deps = wo_lds + [st_cg[i], c_bo, st_radd[i - 1] if i >= 1 else pv.get("radd_last")]
            if i == 0:
                deps.append(prev_stt)
            def po_fn(e, i=i):
                for c in range(4):
                    e.matmul(ps_out4[c][:], ones_r[0:1, 0:128],
                             bo_row[0:1, c * 512:(c + 1) * 512], start=True, stop=False)
                last = None
                for dt in range(16):
                    for c in range(4):
                        last = e.matmul(ps_out4[c][:],
                                        ctxg[i % 2][:, dt * 128:(dt + 1) * 128],
                                        bigA[:, dt * 2048 + c * 512: dt * 2048 + (c + 1) * 512],
                                        start=False, stop=(dt == 15))
                return last
            po = S.step("tensor", deps, po_fn)
            po_last = po
            st_po[i] = po
            ra = S.step("vector", [po, x2],
                        lambda e: [e.tensor_tensor(xst[:, c * 512:(c + 1) * 512],
                                                   ps_out4[c][:], xst[:, c * 512:(c + 1) * 512],
                                                   ALU.add) for c in range(4)][-1])
            st_radd[i] = ra
            st_out[i] = S.dma("sync", [ra], d["out"][b * T + i * 128: b * T + (i + 1) * 128, :], xst[:])
            last_out.append(st_out[i])

        pv = dict(norm_last=st_norm[NT - 1], tr_last=st_tr[NT - 1], ag_last=st_ag[NT - 1],
                  po_last=po_last, psq_free=psq_prev, psv_free=psv_prev,
                  radd_last=st_radd[NT - 1],
                  stt_last=prev_stt, rec_last=prev_rec, bcp_last=prev_bcp,
                  avs_last=exp_slot_free, proj_last=proj_last, pm_last=prev_pm,
                  out_last1=st_out[NT - 1],
                  ctxd_m2=prev_ctx_dma[0], ctxd_m1=prev_ctx_dma[1],
                  ckv_free=psv_prev, kr_free=exp_slot_free)

    return last_out


D = 2048
LAT = 1024
SU = 4096
E = 32
NSU = SU // 128
NLT = LAT // 128
BIG = 30000.0


def emit_l1_ffn(S, NTOK, d, SLAB=1024, CH=1024, debug_stop=False):
    NSL = NTOK // SLAB
    NCH = NTOK // CH
    NT = SLAB // 128
    TC = SLAB // 512
    CT = CH // 128
    CC = CH // 512
    sb = S.sbuf
    G = S.global_psums()
    ps_a = [G["q"], G["p1"]][:TC]
    ps_d0 = [G["sc1"], G["ctx"], G["bc"], G["d"]]
    ps_d1 = [G["q"], G["p1"], G["p2"], G["sc0"]]
    ps_h = [G["q"], G["p1"]][:CC]
    ps_y = [G["sc1"], G["ctx"]][:CC]
    ps_l = G["p2"]
    ps_bcast = G["sc0"]

    xst0 = sb("l1f_xst0", [128, D], F32)
    xst = [xst0, xst0]
    xn = [sb(f"l1f_xn{i}", [128, D], BF16) for i in range(2)]
    xn32 = sb("l1f_xn32", [128, D], F32)
    xlo = sb("l1f_xlo", [128, D], BF16)
    xloT = sb("l1f_xloT", [128, D], BF16)
    stats = sb("l1f_stats", [128, 24], F32)
    mv = [sb(f"l1f_mv{i}", [128, 2], F32) for i in range(2)]
    rstd = [sb(f"l1f_rstd{i}", [128, 1], F32) for i in range(2)]
    eps_c = sb("l1f_eps", [128, 1], F32)
    ones_r = sb("l1f_ones_r", [1, 512], F32)
    ones8 = sb("l1f_ones8", [128, 8], F32)
    onesb = sb("l1f_onesb", [1, 128], BF16)
    AB1 = sb("l1f_AB1", [128, 32768], BF16)
    AB2 = sb("l1f_AB2", [128, 16384], BF16)
    WS = sb("l1f_WS", [128, 4096], BF16)
    YSB = sb("l1f_ysb", [128, 8 * 1024], F32)
    wbf = [sb(f"l1f_wb{i}", [128, D], BF16) for i in range(2)]
    gw32 = xst[0]
    wtp = [sb(f"l1f_wt{i}", [128, D], BF16) for i in range(2)]
    gbB = sb("l1f_gbB", [128, 32], F32)
    scores = sb("l1f_scores", [128, 32], F32)
    sbv = sb("l1f_sbv", [128, 32], F32)
    gs = sb("l1f_gs", [128, 4], F32)
    rt1 = sb("l1f_rt1", [128, 32], F32)
    rt2 = sb("l1f_rt2", [128, 32], F32)
    sel = sb("l1f_sel", [128, 32], F32)
    rcol = sb("l1f_rcol", [128, 4], F32)
    cwpad = sb("l1f_cwpad", [128, 128], BF16)
    rt2i = sb("l1f_rt2i", [128, 32], I32)
    cwrow = sb("l1f_cwrow", [1, CH], BF16)
    cwB = sb("l1f_cwB", [128, CH], BF16)
    gb_row = sb("l1f_gbrow", [1, 32], F32)
    negB = sb("l1f_negB", [128, 32], F32)

    def A_hsu(s_):
        return AB1[:, s_ * 1024:(s_ + 1) * 1024]

    def B_xlat(lk):
        return AB1[:, lk * 1024:(lk + 1) * 1024]

    def B_h(it):
        return AB1[:, 8192 + it * 1024:8192 + (it + 1) * 1024]

    def B_fc2(lt):
        return AB1[:, lt * 2048:(lt + 1) * 2048]

    def B_ybf(lt):
        return AB1[:, 16384 + lt * CH:16384 + (lt + 1) * CH]

    def A_xnT(dt):
        return AB2[:, dt * SLAB:(dt + 1) * SLAB]

    def B_xnTtc(dt):
        return AB2[:, dt * 512:(dt + 1) * 512]

    def B_gwT(dt):
        return AB2[:, 8192 + dt * 32:8192 + dt * 32 + 32]

    def B_gwTlo(dt):
        return AB2[:, 8832 + dt * 32:8832 + dt * 32 + 32]

    def B_xnlo(dt):
        return AB1[:, 24576 + dt * 512:24576 + (dt + 1) * 512]

    B_cwT = AB2[:, 9472:9472 + 128]

    def A_wbuf(par):
        return WS[:, par * 2048:(par + 1) * 2048]

    def B_fc1buf(par):
        return WS[:, par * 2048:(par + 1) * 2048]

    def B_w1(par):
        return WS[:, par * 1024:(par + 1) * 1024]

    def B_w2(par):
        return WS[:, 2048 + par * 1024:2048 + (par + 1) * 1024]

    scr_su, scr_sd, scr_fc1, scr_fc2, scr_w1, scr_w2 = (
        d["scr_su"], d["scr_sd"], d["scr_fc1"], d["scr_fc2"], d["scr_w1"], d["scr_w2"])
    xnT_scr = d["xnT_scr"]
    xnT_lo_scr = d["xnT_lo_scr"]
    cw_scr = d["cw_scr"]

    c_ones = S.step("vector", [], lambda e: (e.memset(eps_c[:], 1e-5), e.memset(ones_r[:], 1.0),
                                             e.memset(ones8[:], 1.0), e.memset(onesb[:], 1.0),
                                             e.memset(cwpad[:], 0.0), e.memset(negB[:], -BIG))[-1])
    c_gb = S.dma("gdma", [], gb_row[:], d["gate_bias"].rearrange("(a m) -> a m", a=1))
    c_gbm = S.step("tensor", [c_gb, c_ones],
                   lambda e: e.matmul(G["d"][:, 0:32], ones_r[0:1, 0:128], gb_row[0:1, :],
                                      start=True, stop=True))
    c_gbB = S.step("scalar", [c_gbm], lambda e: e.copy(gbB[:], G["d"][:, 0:32]))

    prep_units = ([("su", i) for i in range(NSU)] + [("sd", i) for i in range(NSU)] +
                  [("fc1", i) for i in range(NLT)] + [("fc2", i) for i in range(NLT)] +
                  [("w1", i) for i in range(E * NLT)] + [("w2", i) for i in range(E * NLT)])
    SRC = {"su": (d["shared_up_w"], "row", 2048), "sd": (d["shared_down_w"], "col", 2048),
           "fc1": (d["fc1_w"], "row", 2048), "fc2": (d["fc2_w"], "col", 2048),
           "w1": (d["w1f"], "row", 1024), "w2": (d["w2f"], "row", 1024)}
    SCRS = {"su": scr_su, "sd": scr_sd, "fc1": scr_fc1, "fc2": scr_fc2, "w1": scr_w1, "w2": scr_w2}
    prep_done = {}
    pu_prev = {0: None, 1: None}
    dq_prev = {0: None, 1: None}
    tp_prev = {0: None, 1: None}
    for n, (k, i) in enumerate(prep_units):
        par = n % 2
        srcM, mode, W = SRC[k]
        nblk = W // 128
        if mode == "row":
            ld = S.dma("gdma", [tp_prev[par]], wbf[par][:, 0:W], srcM[i * 128:(i + 1) * 128, :])
        else:
            ld = S.dma("gdma", [tp_prev[par]],
                       wbf[par][:, 0:W].rearrange("p (dt m) -> p dt m", dt=nblk),
                       srcM[:, i * 128:(i + 1) * 128].rearrange("(dt p) m -> p dt m", p=128))
        dq = ld
        dq_prev[par] = dq
        tp = S.step("sync", [dq, pu_prev[par]],
                    lambda e, par=par, nblk=nblk: [e.dma_start_transpose(
                        wtp[par][:, dt * 128:(dt + 1) * 128],
                        wbf[par][:, dt * 128:(dt + 1) * 128]) for dt in range(nblk)],
                    weight=16 * nblk)
        tp_prev[par] = tp
        stc = S.dma("sdma", [tp], SCRS[k][i * 128:(i + 1) * 128, 0:W], wtp[par][:, 0:W])
        pu_prev[par] = stc
        prep_done[(k, i)] = stc

    last_out = []
    pv = {}
    a_out = {}

    for sl in range(NSL):
        base = sl * SLAB
        st_norm = [None] * NT
        st_tr = [None] * NT
        st_ag = [None] * NT
        st_xscr = [None] * NT
        prev_n32 = pv.get("n32_last")
        for i in range(NT):
            dep_war = prev_n32 if i >= 1 else pv.get("out_last1")
            ld = S.dma("sync", [dep_war], xst[i % 2][:],
                       d["x"][base + i * 128: base + (i + 1) * 128, :])
            bs = S.step("vector", [ld, st_ag[i - 1] if i >= 1 else pv.get("ag_last")],
                        lambda e, i=i: [e.bn_stats(stats[:, 6 * kk:6 * kk + 6],
                                                   xst[i % 2][:, 512 * kk:512 * (kk + 1)]) for kk in range(4)][-1])
            st_ag[i] = S.step("vector", [bs], lambda e, i=i: e.bn_aggr(mv[i % 2][:], stats[:]))
            ln1 = S.step("scalar", [st_ag[i], st_norm[i - 2] if i >= 2 else pv.get("norm_last"), c_ones],
                         lambda e, i=i: e.activation(rstd[i % 2][:], mv[i % 2][:, 1:2], AF.Ln, bias=eps_c[:, 0:1]))
            ln2 = S.step("scalar", [ln1], lambda e, i=i: e.activation(rstd[i % 2][:], rstd[i % 2][:], AF.Exp, scale=-0.5))
            n32 = S.step("vector", [ln2, pv.get(("lo", i - 1)) if i >= 1 else pv.get("lo_last")],
                         lambda e, i=i: e.tensor_scalar(xn32[:], xst[i % 2][:], mv[i % 2][:, 0:1],
                                                        rstd[i % 2][:, 0:1], ALU.subtract, ALU.mult))
            st_norm[i] = S.step("vector", [n32, st_tr[i - 2] if i >= 2 else pv.get("tr_last")],
                                lambda e, i=i: e.tensor_copy(xn[i % 2][:], xn32[:]))
            lo = S.step("vector", [st_norm[i], pv.get(("trlo", i - 1)) if i >= 1 else pv.get("trlo_last")],
                        lambda e, i=i: e.tensor_tensor(xlo[:], xn32[:], xn[i % 2][:], ALU.subtract))
            pv[("lo", i)] = lo
            prev_n32 = n32
            deps_tr = [st_norm[i]] + ([pv.get("xnT_free")] if i < 2 else [])
            prev_xscr = pv.get("xscr_list")
            if prev_xscr:
                deps_tr.append(prev_xscr[i])
            st_tr[i] = S.step("sync", deps_tr,
                              lambda e, i=i: [e.dma_start_transpose(
                                  A_xnT(dt)[:, i * 128:(i + 1) * 128],
                                  xn[i % 2][:, dt * 128:(dt + 1) * 128]) for dt in range(16)],
                              weight=256)
            st_xscr[i] = S.dma("sdma", [st_tr[i]],
                               xnT_scr.rearrange("p (dt t) -> p dt t", dt=16)[:, :, base + i * 128: base + (i + 1) * 128],
                               AB2[:, 0:16 * SLAB].rearrange("p (dt t) -> p dt t", dt=16)[:, :, i * 128:(i + 1) * 128])
            trlo = S.step("sync", [lo, pv.get("xlscr_last")],
                          lambda e: [e.dma_start_transpose(
                              xloT[:, dt * 128:(dt + 1) * 128],
                              xlo[:, dt * 128:(dt + 1) * 128]) for dt in range(16)],
                          weight=256)
            pv[("trlo", i)] = trlo
            pv["xlscr_last"] = S.dma("sdma", [trlo],
                                     xnT_lo_scr.rearrange("p (dt t) -> p dt t", dt=16)[:, :, base + i * 128: base + (i + 1) * 128],
                                     xloT[:].rearrange("p (dt m) -> p dt m", dt=16))

        pv["n32_last"] = prev_n32
        st_g = {}
        wld_g = {}

        def ensure_su(sidx):
            if sidx in wld_g or sidx >= NSU:
                return
            par = sidx % 2
            depg = st_g.get(sidx - 2, pv.get("dlast"))
            wld_g[sidx] = S.dma("gdma", [prep_done[("su", sidx)], depg], A_wbuf(par),
                                scr_su[sidx * 128:(sidx + 1) * 128, :])
        ensure_su(0)
        ensure_su(1)
        prev_gev = pv.get("gev_last")
        prev_sq = pv.get("sq_last")
        hread_prev = pv.get("hread", {})
        h_ready = [None] * NSU
        for s_ in range(NSU):
            par = s_ % 2
            deps = [wld_g[s_], prev_gev] + [st_tr[j] for j in range(NT)]
            if s_ == 0:
                deps.append(pv.get("radd_last"))
            gp = S.step("tensor", deps,
                        lambda e, par=par: [e.matmul(ps_a[c][:], A_wbuf(par)[:, dt * 128:(dt + 1) * 128],
                                                     A_xnT(dt)[:, c * 512:(c + 1) * 512],
                                                     start=(dt == 0), stop=(dt == 15))
                                            for c in range(TC) for dt in range(16)][-1])
            st_g[s_] = gp
            gev = S.step("vector", [gp, prev_sq, hread_prev.get(s_)],
                         lambda e, s_=s_: [e.tensor_scalar_max(A_hsu(s_)[:, c * 512:(c + 1) * 512],
                                                               ps_a[c][:], 0.0) for c in range(TC)][-1])
            prev_gev = gev
            sq = S.step("scalar", [gev],
                        lambda e, s_=s_: e.activation(A_hsu(s_)[:, 0:SLAB], A_hsu(s_)[:, 0:SLAB], AF.Square))
            prev_sq = sq
            h_ready[s_] = sq
            ensure_su(s_ + 2)

        st_d = {}
        wldd = {}
        st_radd = {}
        st_out_l = {}

        def ensure_sd(j, ht):
            if (j, ht) in wldd or ht >= NSU or j >= NT // 2:
                return
            par = (j * NSU + ht) % 2
            dep = (st_d.get((j, ht - 2)) if ht >= 2 else
                   (st_d.get((j - 1, NSU - 2 + ht)) or st_g.get(NSU - 2 + ht) or pv.get("dlast")))
            wldd[(j, ht)] = S.dma("gdma", [prep_done[("sd", ht)], dep], A_wbuf(par),
                                  scr_sd[ht * 128:(ht + 1) * 128, :])
        ensure_sd(0, 0)
        ensure_sd(0, 1)
        for j in range(NT // 2):
            po = None
            for ht in range(NSU):
                par = (j * NSU + ht) % 2
                deps = [wldd[(j, ht)], h_ready[ht]]
                if ht == 0:
                    deps += [st_radd.get(j - 1, pv.get("radd_last")), prev_gev, prev_sq]
                def dp_fn(e, j=j, par=par, ht=ht, first=(ht == 0), last=(ht == NSU - 1)):
                    out = None
                    for tt in range(2):
                        pset = ps_d0 if tt == 0 else ps_d1
                        for c in range(4):
                            out = e.matmul(pset[c][:],
                                           A_hsu(ht)[:, (2 * j + tt) * 128:(2 * j + tt + 1) * 128],
                                           A_wbuf(par)[:, c * 512:(c + 1) * 512],
                                           start=first, stop=last)
                    return out
                po = S.step("tensor", deps, dp_fn)
                st_d[(j, ht)] = po
                if ht + 2 < NSU:
                    ensure_sd(j, ht + 2)
                else:
                    ensure_sd(j + 1, ht + 2 - NSU)
            for tt in range(2):
                i = 2 * j + tt
                x2 = S.dma("sync", [pv.get("n32_last"),
                                    st_out_l.get(i - 1, pv.get("out_last1"))],
                           xst[i % 2][:], d["x"][base + i * 128: base + (i + 1) * 128, :])
                pset = ps_d0 if tt == 0 else ps_d1
                ra = S.step("vector", [po, x2],
                            lambda e, i=i, pset=pset: [
                                e.tensor_tensor(xst[i % 2][:, c * 512:(c + 1) * 512], pset[c][:],
                                                xst[i % 2][:, c * 512:(c + 1) * 512], ALU.add)
                                for c in range(4)][-1])
                st_radd[j] = ra
                so = S.dma("sync", [ra], d["out"][base + i * 128: base + (i + 1) * 128, :], xst[i % 2][:])
                st_out_l[i] = so
                a_out[base // 128 + i] = so
                last_out.append(so)

        pv = dict(
            norm_last=st_norm[NT - 1], tr_last=st_tr[NT - 1], ag_last=st_ag[NT - 1],
            n32_last=prev_n32,
            gev_last=prev_gev, sq_last=prev_sq, radd_last=st_radd[NT // 2 - 1],
            out_last2=st_out_l[NT - 2], out_last1=st_out_l[NT - 1],
            xnT_free=st_g[NSU - 1], dlast=st_d[(NT // 2 - 1, NSU - 1)],
            xscr_last=st_xscr[NT - 1], xscr_list=st_xscr,
            lo_last=pv.get(("lo", NT - 1)), trlo_last=pv.get(("trlo", NT - 1)),
            xlscr_last=pv.get("xlscr_last"),
            hread={s_: st_d[(NT // 2 - 1, s_)] for s_ in range(NSU)},
        )

    a_radd_last = pv["radd_last"]
    a_dlast = pv["dlast"]
    a_xnT_free = pv["xnT_free"]
    a_gev = pv["gev_last"]
    a_sq = pv["sq_last"]

    b_fence = S.dma_fence("sdma")
    gld = S.dma("gdma", [pv.get("out_last1")], gw32[0:32, :], d["gate_w"])
    gcv = S.step("vector", [gld, tp_prev[0]], lambda e: e.tensor_copy(wbf[0][0:32, :], gw32[0:32, :]))
    gcl = S.step("vector", [gcv, tp_prev[1]],
                 lambda e: e.tensor_tensor(wbf[1][0:32, :], gw32[0:32, :], wbf[0][0:32, :], ALU.subtract))
    gtp = S.step("sync", [gcl, tp_prev[0], tp_prev[1], a_xnT_free] + b_fence,
                 lambda e: [e.dma_start_transpose(AB2[:, 8192 + dt * 32:8192 + dt * 32 + 32],
                                                  wbf[0][0:32, dt * 128:(dt + 1) * 128]) for dt in range(16)] +
                           [e.dma_start_transpose(AB2[:, 8832 + dt * 32:8832 + dt * 32 + 32],
                                                  wbf[1][0:32, dt * 128:(dt + 1) * 128]) for dt in range(16)],
                 weight=512)

    st_b_out = {}
    bpv = {}
    for ch in range(NCH):
        cbase = ch * CH
        wfc1 = {}

        def ensure_fc1(lt):
            if lt in wfc1 or lt >= NLT:
                return
            par = lt % 2
            dep = bpv.get(("fp", lt - 2)) or (bpv.get("fp_last") if ch > 0 else a_dlast)
            wfc1[lt] = S.dma("gdma", [prep_done[("fc1", lt)], dep], B_fc1buf(par),
                             scr_fc1[lt * 128:(lt + 1) * 128, :])
        ensure_fc1(0)
        ensure_fc1(1)
        prev_xl_ev = bpv.get("xl_ev")
        prev_logit_ev = bpv.get("sg_last")
        prev_rt = bpv.get("rt_last")
        st_xnld = {}
        rt_done = {}
        fp = None
        slab_readers = [bpv.get("xn_free0"), bpv.get("xn_free1")]
        for cc in range(CC):
            st_xnld[cc] = S.dma("sync", [gtp if (ch == 0 and cc == 0) else None]
                                + (b_fence if (ch == 0 and cc == 0) else [])
                                + slab_readers,
                                AB2[:, 0:8192].rearrange("p (dt t) -> p dt t", dt=16),
                                xnT_scr.rearrange("p (dt t) -> p dt t", dt=16)[:, :, cbase + cc * 512: cbase + (cc + 1) * 512])
            lo_deps = [pv["xlscr_last"] if (ch == 0 and cc == 0) else None] + slab_readers + \
                      ([bpv.get("hread_all")] if True else [])
            st_xnlo = S.dma("sync", lo_deps,
                            AB1[:, 24576:24576 + 8192].rearrange("p (dt t) -> p dt t", dt=16),
                            xnT_lo_scr.rearrange("p (dt t) -> p dt t", dt=16)[:, :, cbase + cc * 512: cbase + (cc + 1) * 512])
            for lt in range(NLT):
                par = lt % 2
                deps = [st_xnld[cc], wfc1[lt], prev_xl_ev]
                if ch == 0 and cc == 0 and lt == 0:
                    deps += [a_radd_last, a_gev, a_sq]
                fp = S.step("tensor", deps,
                            lambda e, par=par, cc=cc: [e.matmul(ps_h[cc % len(ps_h)][:],
                                                                B_fc1buf(par)[:, dt * 128:(dt + 1) * 128],
                                                                B_xnTtc(dt), start=(dt == 0), stop=(dt == 15))
                                                       for dt in range(16)][-1])
                bpv[("fp", lt)] = fp
                if cc == CC - 1:
                    ensure_fc1(lt + 2)
                ev = S.step("scalar", [fp, prev_logit_ev],
                            lambda e, lt=lt, cc=cc: e.copy(B_xlat(lt)[:, cc * 512:(cc + 1) * 512],
                                                           ps_h[cc % len(ps_h)][:]))
                prev_xl_ev = ev
            for w in range(4):
                i = cc * 4 + w
                lg = S.step("tensor", [st_xnld[cc], st_xnlo, gtp, prev_logit_ev, c_gbm],
                            lambda e, w=w: [e.matmul(ps_l[:, 0:32],
                                                     (B_xnTtc(dt), B_xnTtc(dt), B_xnlo(dt))[z][:, w * 128:(w + 1) * 128],
                                                     (B_gwT(dt), B_gwTlo(dt), B_gwT(dt))[z],
                                                     start=(dt == 0 and z == 0), stop=(dt == 15 and z == 2))
                                            for dt in range(16) for z in range(3)][-1])
                sg = S.step("scalar", [lg, prev_rt, c_gbB],
                            lambda e: e.activation(scores[:], ps_l[:, 0:32], AF.Sigmoid))
                prev_logit_ev = sg
                fns = [
                    lambda e: e.tensor_tensor(sbv[:], scores[:], gbB[:], ALU.add),
                    lambda e: [e.reduce_max(rcol[:, g:g + 1], sbv[:, 8 * g:8 * g + 8], axis=AX.X) for g in range(4)][-1],
                    lambda e: [e.tensor_scalar(rt2[:, 8 * g:8 * g + 8], sbv[:, 8 * g:8 * g + 8],
                                               rcol[:, g:g + 1], None, ALU.is_equal) for g in range(4)][-1],
                    lambda e: e.scalar_tensor_tensor(rt1[:], rt2[:], -BIG, sbv[:], ALU.mult, ALU.add),
                    lambda e: [e.reduce_max(gs[:, g:g + 1], rt1[:, 8 * g:8 * g + 8], axis=AX.X) for g in range(4)][-1],
                    lambda e: e.tensor_tensor(gs[:], gs[:], rcol[:, 0:4], ALU.add),
                    lambda e: e.reduce_max(rcol[:, 0:1], gs[:], axis=AX.X),
                    lambda e: e.tensor_scalar(rt2[:, 0:4], gs[:], rcol[:, 0:1], None, ALU.is_equal),
                    lambda e: e.scalar_tensor_tensor(rt1[:, 0:4], rt2[:, 0:4], -BIG, gs[:], ALU.mult, ALU.add),
                    lambda e: e.reduce_max(rcol[:, 1:2], rt1[:, 0:4], axis=AX.X),
                    lambda e: e.tensor_scalar(rcol[:, 0:4], gs[:], rcol[:, 1:2], None, ALU.is_ge),
                    lambda e: [e.tensor_scalar(rt2[:, 8 * g:8 * g + 8], ones8[:], rcol[:, g:g + 1],
                                               None, ALU.mult) for g in range(4)][-1],
                    lambda e: e.tensor_copy(rt1[:], negB[:]),
                    lambda e: e.tensor_copy(rt2i[:], rt2[:]),
                    lambda e: e.copy_predicated(rt1[:], rt2i[:], sbv[:]),
                    lambda e: e.memset(sel[:], 0.0),
                ]
                for _ in range(8):
                    fns += [
                        lambda e: e.reduce_max(rcol[:, 0:1], rt1[:], axis=AX.X),
                        lambda e: e.tensor_scalar(rt2[:], rt1[:], rcol[:, 0:1], None, ALU.is_equal),
                        lambda e: e.scalar_tensor_tensor(rt1[:], rt2[:], -BIG, rt1[:], ALU.mult, ALU.add),
                        lambda e: e.tensor_tensor(sel[:], sel[:], rt2[:], ALU.add),
                    ]
                fns += [
                    lambda e: e.tensor_tensor(rt2[:], sel[:], scores[:], ALU.mult),
                    lambda e: e.reduce_sum(rcol[:, 0:1], rt2[:], axis=AX.X),
                    lambda e: e.reciprocal(rcol[:, 1:2], rcol[:, 0:1]),
                    lambda e: e.tensor_scalar(rcol[:, 1:2], rcol[:, 1:2], 2.5, None, ALU.mult),
                    lambda e: e.tensor_scalar(rt2[:], rt2[:], rcol[:, 1:2], None, ALU.mult),
                ]
                prev = None
                for fn in fns:
                    prev = S.step("vector", [sg, c_gbB, c_ones, prev_rt] if prev is None else [prev], fn)
                rt = prev
                sqr = S.step("scalar", [rt], lambda e: e.activation(cwpad[:, 0:32], rt2[:], AF.Sqrt))
                prev_rt = sqr
                tr = S.step("sync", [sqr, bpv.get("cw_free")] +
                            (b_fence if ch == 0 else []),
                            lambda e: e.dma_start_transpose(B_cwT, cwpad[:]), weight=16)
                cwst = S.dma("sdma", [tr], cw_scr[:, cbase + i * 128: cbase + (i + 1) * 128],
                             B_cwT[0:32, 0:128])
                bpv["cw_free"] = cwst
                rt_done[i] = cwst
            slab_readers = [fp, lg]
            bpv["hread_all"] = None
        bpv["fp_last"] = fp
        bpv["xl_ev"] = prev_xl_ev
        bpv["sg_last"] = prev_logit_ev
        bpv["rt_last"] = prev_rt
        bpv["xn_free0"] = fp
        bpv["xn_free1"] = lg

        wld1 = {}
        wld2 = {}

        def ensure_w(kind, eidx, it):
            if eidx >= E:
                return
            dic = wld1 if kind == "w1" else wld2
            if (eidx, it) in dic:
                return
            n = eidx * NLT + it
            par = n % 2
            buf = B_w1(par) if kind == "w1" else B_w2(par)
            scr = scr_w1 if kind == "w1" else scr_w2
            dic[(eidx, it)] = S.dma("gdma", [prep_done[(kind, n)],
                                             bpv.get(("wread", kind, par)), bpv.get("fp_last")],
                                    buf, scr[n * 128:(n + 1) * 128, :])
        for it0 in range(2):
            ensure_w("w1", 0, it0)
            ensure_w("w2", 0, it0)
        prev_hev = bpv.get("hev")
        prev_sqh = bpv.get("sqh")
        prev_yadd = bpv.get("yadd")
        cwld_prev = bpv.get("cwld")
        prev_bv = bpv.get("bv_last")
        for eidx in range(E):
            cwld = S.dma("gdma", [rt_done[CT - 1] if eidx == 0 else None, cwld_prev],
                         cwrow[:], cw_scr[eidx:eidx + 1, cbase:cbase + CH])
            cwld_prev = cwld
            cwB_ev = None
            for c in range(CC):
                bm = S.step("tensor", [cwld, prev_bv],
                            lambda e, c=c: e.matmul(ps_bcast[:], onesb[0:1, :],
                                                    cwrow[0:1, c * 512:(c + 1) * 512], start=True, stop=True))
                bv = S.step("scalar", [bm, bpv.get("cwB_read") if c == 0 else None],
                            lambda e, c=c: e.copy(cwB[:, c * 512:(c + 1) * 512], ps_bcast[:]))
                prev_bv = bv
                cwB_ev = bv
            for it in range(NLT):
                n = eidx * NLT + it
                par = n % 2
                deps = [wld1[(eidx, it)], bpv["xl_ev"], prev_hev]
                def h_fn(e, par=par):
                    out = None
                    for lk in range(NLT):
                        for c in range(CC):
                            out = e.matmul(ps_h[c][:], B_w1(par)[:, lk * 128:(lk + 1) * 128],
                                           B_xlat(lk)[:, c * 512:(c + 1) * 512],
                                           start=(lk == 0), stop=(lk == NLT - 1))
                    return out
                hp = S.step("tensor", deps, h_fn)
                bpv[("wread", "w1", par)] = hp
                if it + 2 < NLT:
                    ensure_w("w1", eidx, it + 2)
                else:
                    ensure_w("w1", eidx + 1, it + 2 - NLT)
                hev = S.step("vector", [hp, cwB_ev, prev_sqh],
                             lambda e, it=it: [e.scalar_tensor_tensor(
                                 B_h(it)[:, c * 512:(c + 1) * 512], ps_h[c][:], 0.0,
                                 cwB[:, c * 512:(c + 1) * 512], ALU.max, ALU.mult) for c in range(CC)][-1])
                prev_hev = hev
                sqh = S.step("scalar", [hev],
                             lambda e, it=it: e.activation(B_h(it)[:, 0:CH], B_h(it)[:, 0:CH], AF.Square))
                prev_sqh = sqh
                bpv[("hready", it)] = sqh
            for lt in range(NLT):
                n = eidx * NLT + lt
                par = n % 2
                deps = [wld2[(eidx, lt)], prev_yadd] + [bpv[("hready", k2)] for k2 in range(NLT)]
                def y_fn(e, par=par):
                    out = None
                    for it in range(NLT):
                        for c in range(CC):
                            out = e.matmul(ps_y[c][:], B_w2(par)[:, it * 128:(it + 1) * 128],
                                           B_h(it)[:, c * 512:(c + 1) * 512],
                                           start=(it == 0), stop=(it == NLT - 1))
                    return out
                yp = S.step("tensor", deps, y_fn)
                bpv[("wread", "w2", par)] = yp
                if lt + 2 < NLT:
                    ensure_w("w2", eidx, lt + 2)
                else:
                    ensure_w("w2", eidx + 1, lt + 2 - NLT)
                if eidx == 0:
                    ya = S.step("vector", [yp, bpv.get("ysb_read"), prev_yadd],
                                lambda e, lt=lt: [e.tensor_copy(
                                    YSB[:, lt * CH + c * 512: lt * CH + (c + 1) * 512],
                                    ps_y[c][:]) for c in range(CC)][-1])
                else:
                    ya = S.step("vector", [yp, prev_yadd],
                                lambda e, lt=lt: [e.tensor_tensor(
                                    YSB[:, lt * CH + c * 512: lt * CH + (c + 1) * 512], ps_y[c][:],
                                    YSB[:, lt * CH + c * 512: lt * CH + (c + 1) * 512], ALU.add)
                                    for c in range(CC)][-1])
                prev_yadd = ya
        bpv["hev"] = prev_hev
        bpv["sqh"] = prev_sqh
        bpv["cwld"] = cwld_prev
        bpv["bv_last"] = prev_bv
        bpv["cwB_read"] = prev_hev

        ybf = {}
        for lt in range(NLT):
            ybf[lt] = S.step("vector", [prev_yadd, prev_sqh, bpv.get(("hread2", lt))],
                             lambda e, lt=lt: e.tensor_copy(B_ybf(lt)[:, 0:CH], YSB[:, lt * CH:(lt + 1) * CH]))
        if debug_stop:
            return last_out
        wfc2 = {}
        for lt in range(NLT):
            wfc2[lt] = S.dma("gdma", [prep_done[("fc2", lt)],
                                      bpv[("wread", "w1", 0)], bpv[("wread", "w1", 1)],
                                      bpv[("wread", "w2", 0)], bpv[("wread", "w2", 1)],
                                      bpv.get(("fc2read", lt))],
                             B_fc2(lt), scr_fc2[lt * 128:(lt + 1) * 128, :])
        prev_po = None
        prev_racc = bpv.get("racc")
        for i in range(CT):
            gi = cbase // 128 + i
            acc = S.dma("sync", [a_out[gi], gcl if (ch == 0 and i == 0) else None,
                                 st_b_out.get(gi - 1, bpv.get("b_out_m1") or pv.get("out_last1"))],
                        xst[i % 2][:], d["out"][gi * 128:(gi + 1) * 128, :])
            deps = [ybf[lt] for lt in range(NLT)] + list(wfc2.values()) + \
                   [prev_racc, bpv.get("rt_last")]
            def o_fn(e, i=i):
                out = None
                for lt in range(NLT):
                    for c in range(4):
                        out = e.matmul(ps_d0[c][:], B_ybf(lt)[:, i * 128:(i + 1) * 128],
                                       B_fc2(lt)[:, c * 512:(c + 1) * 512],
                                       start=(lt == 0), stop=(lt == NLT - 1))
                return out
            po = S.step("tensor", deps, o_fn)
            prev_po = po
            ra = S.step("vector", [po, acc],
                        lambda e, i=i: [e.tensor_tensor(xst[i % 2][:, c * 512:(c + 1) * 512], ps_d0[c][:],
                                                        xst[i % 2][:, c * 512:(c + 1) * 512], ALU.add)
                                        for c in range(4)][-1])
            prev_racc = ra
            so = S.dma("sync", [ra], d["out"][gi * 128:(gi + 1) * 128, :], xst[i % 2][:])
            st_b_out[gi] = so
            last_out.append(so)
        bpv["racc"] = prev_racc
        bpv["yadd"] = prev_po
        bpv["ysb_read"] = ybf[NLT - 1]
        for lt in range(NLT):
            bpv[("hread2", lt)] = prev_po
            bpv[("fc2read", lt)] = prev_po
        bpv["b_out_m2"] = st_b_out.get(cbase // 128 + CT - 2)
        bpv["b_out_m1"] = st_b_out.get(cbase // 128 + CT - 1)
        bpv["xl_ev"] = None
        bpv["fp_last"] = prev_po
        prev_xl_ev = None

    return last_out


AR_F32 = 13312
AR_BF16 = 77056
AR_I32 = 32


def build_fused(T, NB, SLAB=512, CH=512, debug_mid=False):
    NTOK = NB * T
    nc = bass.Bass("TRN2", target_bir_lowering=False)
    ins = {}

    def din(name, shape, dt=F32):
        ins[name] = (shape, dt)
        return nc.dram_tensor(name, shape, dt, kind="ExternalInput").ap()

    def dint(name, shape, dt=BF16):
        return nc.dram_tensor(name, shape, dt, kind="Internal").ap()

    x_in = din("x", [NTOK, 2048])
    out = nc.dram_tensor("out", [NTOK, 2048], F32, kind="ExternalOutput").ap()
    mid_kind = "ExternalOutput" if debug_mid else "Internal"
    x1 = nc.dram_tensor("x1", [NTOK, 2048], F32, kind=mid_kind).ap()
    x2 = nc.dram_tensor("x2", [NTOK, 2048], F32, kind=mid_kind).ap()
    x3 = nc.dram_tensor("x3", [NTOK, 2048], F32, kind=mid_kind).ap()
    ctx_scr = dint("ctx_scr", [128, 16 * T])

    d1 = dict(x=x_in, out=x1, ctx_scr=ctx_scr,
              wq=din("l0a_wq", [2048, 2048]), bq=din("l0a_bq", [2048]),
              wk=din("l0a_wk", [2048, 512]), bk=din("l0a_bk", [512]),
              wv=din("l0a_wv", [2048, 512]), bv=din("l0a_bv", [512]),
              wo=din("l0a_wo", [2048, 2048]), bo=din("l0a_bo", [2048]),
              ct=din("l0a_ct", [128, T]), st=din("l0a_st", [128, T]),
              p1t=din("l0a_p1t", [128, 128]), p2t=din("l0a_p2t", [128, 128]),
              dmask=din("l0a_dmask", [128, 2048]))
    d2 = dict(x=x1, out=x2,
              wq_gate=din("l0f_wq_gate", [5632, 2048], I32),
              wq_up=din("l0f_wq_up", [5632, 2048], I32),
              wq_down=din("l0f_wq_down", [2048, 5632], I32),
              s_gate_t=din("l0f_s_gate_t", [128, 44]), s_up_t=din("l0f_s_up_t", [128, 44]),
              s_down_t=din("l0f_s_down_t", [128, 16]),
              scr_g=dint("scr_g", [5632, 2048]), scr_u=dint("scr_u", [5632, 2048]),
              scr_d=dint("scr_d", [5632, 2048]))
    d3 = dict(x=x2, out=x3, ctx_scr=ctx_scr,
              w_dkv=din("l1a_w_dkv", [2048, 512]), w_dq=din("l1a_w_dq", [2048, 512]),
              w_uk=din("l1a_w_uk", [512, 2048]), w_uv=din("l1a_w_uv", [512, 2048]),
              w_uq=din("l1a_w_uq", [512, 2048]), w_kr=din("l1a_w_kr", [2048, 64]),
              w_qr=din("l1a_w_qr", [512, 1024]), w_o=din("l1a_w_o", [2048, 2048]),
              b_o=din("l1a_b_o", [2048]),
              ct=din("l1a_ct", [128, T]), st=din("l1a_st", [128, T]),
              p1t=din("l1a_p1t", [128, 128]), p2t=din("l1a_p2t", [128, 128]))
    d4 = dict(x=x3, out=out,
              gate_w=din("l1f_gate_w", [32, 2048]), gate_bias=din("l1f_gate_bias", [32]),
              fc1_w=din("l1f_fc1_w", [1024, 2048]), fc2_w=din("l1f_fc2_w", [2048, 1024]),
              w1f=din("l1f_w1f", [32768, 1024]), w2f=din("l1f_w2f", [32768, 1024]),
              shared_up_w=din("l1f_shared_up_w", [4096, 2048]),
              shared_down_w=din("l1f_shared_down_w", [2048, 4096]),
              scr_su=dint("scr_su", [4096, 2048]), scr_sd=dint("scr_sd", [4096, 2048]),
              scr_fc1=dint("scr_fc1", [1024, 2048]), scr_fc2=dint("scr_fc2", [1024, 2048]),
              scr_w1=dint("scr_w1", [32768, 1024]), scr_w2=dint("scr_w2", [32768, 1024]),
              xnT_scr=dint("xnT_scr", [128, 16 * NTOK]),
              xnT_lo_scr=dint("xnT_lo_scr", [128, 16 * NTOK]),
              cw_scr=dint("cw_scr", [32, NTOK]))

    S = Sched(nc)
    S.use_arenas(AR_BF16, AR_F32, AR_I32)
    S.sbuf_reset()
    o1 = emit_l0_attn(S, T, NB, d1)
    S.barrier(o1)
    S.sbuf_reset()
    o2 = emit_l0_ffn(S, NTOK, d2, SLAB=SLAB)
    S.barrier(o2)
    S.sbuf_reset()
    o3 = emit_l1_attn(S, T, NB, d3)
    S.barrier(o3)
    S.sbuf_reset()
    o4 = emit_l1_ffn(S, NTOK, d4, SLAB=SLAB, CH=CH)
    S.finish_interleaved(final_waits=o4)
    return nc, ins


def make_inputs(T, weights):
    C0, S20, P1T0, P2T0 = rope_consts_l0(T)
    C1, S21, P1T1, P2T1 = rope_consts_l1(T)
    w = weights
    return dict(
        l0a_wq=w["L0_attn_Wq"], l0a_bq=w["L0_attn_bq"], l0a_wk=w["L0_attn_Wk"],
        l0a_bk=w["L0_attn_bk"], l0a_wv=w["L0_attn_Wv"], l0a_bv=w["L0_attn_bv"],
        l0a_wo=w["L0_attn_Wo"], l0a_bo=w["L0_attn_bo"],
        l0a_ct=C0, l0a_st=S20, l0a_p1t=P1T0, l0a_p2t=P2T0, l0a_dmask=causal_masks(512),
        l0f_wq_gate=w["L0_ffn_wq_gate"], l0f_wq_up=w["L0_ffn_wq_up"],
        l0f_wq_down=w["L0_ffn_wq_down"],
        l0f_s_gate_t=np.ascontiguousarray(w["L0_ffn_s_gate"].reshape(44, 128).T),
        l0f_s_up_t=np.ascontiguousarray(w["L0_ffn_s_up"].reshape(44, 128).T),
        l0f_s_down_t=np.ascontiguousarray(w["L0_ffn_s_down"].reshape(16, 128).T),
        l1a_w_dkv=w["L1_attn_W_DKV"], l1a_w_dq=w["L1_attn_W_DQ"], l1a_w_uk=w["L1_attn_W_UK"],
        l1a_w_uv=w["L1_attn_W_UV"], l1a_w_uq=w["L1_attn_W_UQ"], l1a_w_kr=w["L1_attn_W_KR"],
        l1a_w_qr=w["L1_attn_W_QR"], l1a_w_o=w["L1_attn_W_O"], l1a_b_o=w["L1_attn_b_O"],
        l1a_ct=C1, l1a_st=S21, l1a_p1t=P1T1, l1a_p2t=P2T1,
        l1f_gate_w=w["L1_ffn_gate_w"], l1f_gate_bias=w["L1_ffn_gate_bias"],
        l1f_fc1_w=w["L1_ffn_fc1_w"], l1f_fc2_w=w["L1_ffn_fc2_w"],
        l1f_w1f=np.ascontiguousarray(w["L1_ffn_w1"].reshape(32768, 1024)),
        l1f_w2f=np.ascontiguousarray(w["L1_ffn_w2"].reshape(32768, 1024)),
        l1f_shared_up_w=w["L1_ffn_shared_up_w"], l1f_shared_down_w=w["L1_ffn_shared_down_w"],
    )


from concourse.bass_utils import run_bass_kernel_spmd

NCORES = 8
TT = 2048
DD = 2048


def kernel(L0_attn_x, **kw):
    kw["L0_attn_x"] = L0_attn_x
    x = np.ascontiguousarray(kw["L0_attn_x"].astype(np.float32).reshape(16, TT, DD))
    xs = [np.ascontiguousarray(x[2 * c:2 * c + 2].reshape(2 * TT, DD)) for c in range(NCORES)]
    nc, ins_spec = build_fused(TT, 2, SLAB=512, CH=512)
    shared = make_inputs(TT, kw)
    in_maps = []
    for c in range(NCORES):
        m = dict(shared)
        m["x"] = xs[c]
        in_maps.append(m)
    res = run_bass_kernel_spmd(nc, in_maps, core_ids=list(range(NCORES)))
    out = np.concatenate([res.results[c]["out"].reshape(2, TT, DD) for c in range(NCORES)], axis=0)
    return out.astype(np.float32)


# version 04
# speedup vs baseline: 1.0202x; 1.0202x over previous
from contextlib import ExitStack
import numpy as np
import concourse.bass as bass
import concourse.mybir as mybir

F32 = mybir.dt.float32
F32R = mybir.dt.float32r
BF16 = mybir.dt.bfloat16
I32 = mybir.dt.int32
AF = mybir.ActivationFunctionType
ALU = mybir.AluOpType
AX = mybir.AxisListType

STREAMS = {
    "tensor": ("tensor", False),
    "vector": ("vector", False),
    "scalar": ("scalar", False),
    "gpsimd": ("gpsimd", False),
    "sync": ("sync", True),
    "gdma": ("gpsimd", True),
    "sdma": ("scalar", True),
}
SHARED_QUEUE = {"gpsimd": "gp", "gdma": "gp", "scalar": "sc", "sdma": "sc"}


class Step:
    __slots__ = ("stream", "count_after", "sid", "slot", "target", "epoch")

    def __init__(self, stream, count_after, sid, slot=None, target=0, epoch=0):
        self.stream = stream
        self.count_after = count_after
        self.sid = sid
        self.slot = slot
        self.target = target
        self.epoch = epoch


class Sched:
    DMA_POOL = 6

    def __init__(self, nc):
        self.nc = nc
        self.ctx = ExitStack()
        self.steps = []
        self._semcount = 0
        self._sid = 0
        self.epoch = -1
        self.esems = []
        self.edma_sems = []
        self.ecounts = []
        self.eslot_counts = []
        self._new_epoch()

    def _new_epoch(self):
        self.epoch += 1
        self.esems.append({k: self.sem(f"e{self.epoch}_{k}") for k in STREAMS if not STREAMS[k][1]})
        self.edma_sems.append({k: [self.sem(f"e{self.epoch}_{k}{j}") for j in range(self.DMA_POOL)]
                               for k in STREAMS if STREAMS[k][1]})
        self.ecounts.append({k: 0 for k in STREAMS})
        self.eslot_counts.append({k: [0] * self.DMA_POOL for k in STREAMS if STREAMS[k][1]})
        self.dma_rr = {k: 0 for k in STREAMS if STREAMS[k][1]}

    def sem(self, name):
        self._semcount += 1
        return self.ctx.enter_context(self.nc.semaphore(f"{name}_{self._semcount}"))

    def use_arenas(self, bf16_cols, f32_cols, i32_cols=0):
        self._arena = {}
        for dt, cols, nm in ((BF16, bf16_cols, "bf16"), (F32, f32_cols, "f32"), (I32, i32_cols, "i32")):
            if cols:
                t = self.ctx.enter_context(self.nc.sbuf_tensor(f"arena_{nm}", [128, cols], dt))
                self._arena[dt] = [t, cols, 0]

    def sbuf_reset(self):
        for v in self._arena.values():
            v[2] = 0

    def sbuf(self, name, shape, dtype, **kw):
        ar = getattr(self, "_arena", None)
        if ar and dtype in ar:
            t, cap, off = ar[dtype]
            bpe = 2 if dtype == BF16 else 4
            align = 64 // bpe
            off = (off + align - 1) // align * align
            need = shape[1]
            assert off + need <= cap, f"arena {dtype} overflow at {name}: {off}+{need}>{cap}"
            ar[dtype][2] = off + need
            return t[0:shape[0], off:off + need]
        return self.ctx.enter_context(self.nc.sbuf_tensor(name, shape, dtype, **kw))

    def barrier(self, dep_steps):
        self._pending_bar = {k: list(dep_steps) for k in STREAMS}
        self._new_epoch()

    def psum(self, name, shape, dtype=F32):
        return self.ctx.enter_context(self.nc.psum_tensor(name, shape, dtype))

    def global_psums(self):
        if not hasattr(self, "_gps"):
            self._gps = {k: self.psum("gps_" + k, [128, 512]) for k in
                         ["q", "p1", "p2", "sc0", "sc1", "ctx", "bc", "d"]}
            self._gps["rs"] = self._gps["bc"]
        return self._gps

    def psum_open(self, name, shape, dtype=F32):
        cm = self.nc.psum_tensor(name, shape, dtype)
        h = cm.__enter__()
        return h, (lambda: cm.__exit__(None, None, None))

    def step(self, stream, deps, fn, weight=None):
        if weight is None:
            weight = 16 if STREAMS[stream][1] else 1
        deps = [d for d in deps if d is not None]
        pb = getattr(self, "_pending_bar", None)
        if pb and pb.get(stream):
            deps = deps + pb.pop(stream)
        self._sid += 1
        ep = self.epoch
        if STREAMS[stream][1]:
            slot = self.dma_rr[stream]
            self.dma_rr[stream] = (slot + 1) % self.DMA_POOL
            self.eslot_counts[ep][stream][slot] += weight
            st = Step(stream, 0, self._sid, slot,
                      self.eslot_counts[ep][stream][slot], ep)
        else:
            self.ecounts[ep][stream] += weight
            st = Step(stream, self.ecounts[ep][stream], self._sid, epoch=ep)
        self.steps.append((stream, deps, fn, weight, st))
        return st

    def dma_fence(self, stream):
        ep = self.epoch
        return [Step(stream, 0, 0, slot=j, target=self.eslot_counts[ep][stream][j], epoch=ep)
                for j in range(self.DMA_POOL) if self.eslot_counts[ep][stream][j] > 0]

    def dma(self, stream, deps, out, in_, **kw):
        return self.step(stream, deps, lambda e: e.dma_start(out=out, in_=in_, **kw), 16)

    def dmaT(self, stream, deps, out, in_):
        return self.step(stream, deps, lambda e: e.dma_start_transpose(out, in_), 16)

    def finish(self, final_waits=()):
        for k in STREAMS:
            self.sems[k] = self.sem("st_" + k)
        waited = {k: {j: 0 for j in STREAMS} for k in STREAMS}
        by_stream = {k: [] for k in STREAMS}
        for stream, deps, fn, weight, st in self.steps:
            by_stream[stream].append((deps, fn, weight, st))

        def run_stream(stream, eng):
            w = waited[stream]
            for deps, fn, weight, st in by_stream[stream]:
                for dp in deps:
                    if dp.stream == stream:
                        if dp.count_after > w[stream]:
                            eng.wait_ge(self.sems[stream], dp.count_after)
                            w[stream] = dp.count_after
                    else:
                        if dp.count_after > w[dp.stream]:
                            eng.wait_ge(self.sems[dp.stream], dp.count_after)
                            w[dp.stream] = dp.count_after
                ins = fn(eng)
                assert ins is not None, f"step fn returned None (stream {stream})"
                ins.then_inc(self.sems[stream], weight)

        with self.nc.Block() as block:
            @block.sync
            def _(e):
                run_stream("sync", e)

            @block.scalar
            def _(e):
                run_stream("scalar", e)
                run_stream("sdma", e)

            @block.vector
            def _(e):
                run_stream("vector", e)

            @block.tensor
            def _(e):
                run_stream("tensor", e)

            @block.gpsimd
            def _(e):
                run_stream("gpsimd", e)
                run_stream("gdma", e)
                for st in final_waits:
                    e.wait_ge(self.sems[st.stream], st.count_after)

    def finish_interleaved(self, final_waits=()):
        waited = {}
        for k in set(list(STREAMS) + list(SHARED_QUEUE.values())):
            waited[k] = {}
            waited[k, "dma"] = {}

        def run(eng, streams_here):
            wkey = SHARED_QUEUE.get(streams_here[0], streams_here[0])
            w = waited[wkey]
            wd = waited[wkey, "dma"]
            own = set(streams_here)
            for stream, deps, fn, weight, st in self.steps:
                if stream not in own:
                    continue
                for dp in deps:
                    if STREAMS[dp.stream][1]:
                        key = (dp.epoch, dp.stream, dp.slot)
                        if dp.target > wd.get(key, 0):
                            eng.wait_ge(self.edma_sems[dp.epoch][dp.stream][dp.slot], dp.target)
                            wd[key] = dp.target
                    else:
                        key = (dp.epoch, dp.stream)
                        if dp.count_after > w.get(key, 0):
                            eng.wait_ge(self.esems[dp.epoch][dp.stream], dp.count_after)
                            w[key] = dp.count_after
                if STREAMS[stream][1]:
                    prev_target = st.target - weight
                    key = (st.epoch, stream, st.slot)
                    if prev_target > 0 and prev_target > wd.get(key, 0):
                        eng.wait_ge(self.edma_sems[st.epoch][stream][st.slot], prev_target)
                        wd[key] = prev_target
                ins = fn(eng)
                assert ins is not None
                if STREAMS[stream][1]:
                    if not isinstance(ins, (list, tuple)):
                        ins = [ins]
                    assert weight == 16 * len(ins), f"dma weight {weight} != 16*{len(ins)}"
                    for one in ins:
                        one.then_inc(self.edma_sems[st.epoch][stream][st.slot], 16)
                else:
                    if isinstance(ins, (list, tuple)):
                        ins = ins[-1]
                    ins.then_inc(self.esems[st.epoch][stream], weight)

        with self.nc.Block() as block:
            @block.sync
            def _(e):
                run(e, ["sync"])

            @block.scalar
            def _(e):
                run(e, ["scalar", "sdma"])

            @block.vector
            def _(e):
                run(e, ["vector"])

            @block.tensor
            def _(e):
                run(e, ["tensor"])

            @block.gpsimd
            def _(e):
                run(e, ["gpsimd", "gdma"])
                done = {}
                for st in final_waits:
                    key = (st.epoch, st.stream, st.slot)
                    tgt = st.target if STREAMS[st.stream][1] else st.count_after
                    if done.get(key, 0) >= tgt:
                        continue
                    done[key] = tgt
                    if STREAMS[st.stream][1]:
                        e.wait_ge(self.edma_sems[st.epoch][st.stream][st.slot], tgt)
                    else:
                        e.wait_ge(self.esems[st.epoch][st.stream], st.count_after)

        self.ctx.close()



def rope_consts_l0(T):
    f = np.arange(64, dtype=np.float64)
    freqs = 10000.0 ** (-2.0 * f / 2048.0)
    t = np.arange(T, dtype=np.float64)
    ang = np.outer(t, freqs)
    C = np.zeros((128, T), np.float32)
    S2 = np.zeros((128, T), np.float32)
    for i in range(64):
        C[2 * i] = np.cos(ang[:, i])
        C[2 * i + 1] = np.sin(ang[:, i])
        S2[2 * i] = -np.sin(ang[:, i])
        S2[2 * i + 1] = np.cos(ang[:, i])
    P1T = np.zeros((128, 128), np.float32)
    P2T = np.zeros((128, 128), np.float32)
    for i in range(64):
        P1T[2 * i, 2 * i] = 1.0
        P1T[2 * i, 2 * i + 1] = 1.0
        P2T[2 * i + 1, 2 * i] = 1.0
        P2T[2 * i + 1, 2 * i + 1] = 1.0
    return C, S2, P1T, P2T


def rope_consts_l1(T):
    half = 32
    inv_freq = 1.0 / 10000.0 ** (np.arange(half, dtype=np.float64) / half)
    t = np.arange(T, dtype=np.float64)
    fr = np.outer(t, inv_freq)
    C = np.zeros((128, T), np.float32)
    S2 = np.zeros((128, T), np.float32)
    P1T = np.zeros((128, 128), np.float32)
    P2T = np.zeros((128, 128), np.float32)
    for blk in range(2):
        b = blk * 64
        for i in range(half):
            C[b + 2 * i] = np.cos(fr[:, i])
            C[b + 2 * i + 1] = np.sin(fr[:, i])
            S2[b + 2 * i] = -np.sin(fr[:, i])
            S2[b + 2 * i + 1] = np.cos(fr[:, i])
            P1T[b + i, b + 2 * i] = 1.0
            P1T[b + i, b + 2 * i + 1] = 1.0
            P2T[b + i + 32, b + 2 * i] = 1.0
            P2T[b + i + 32, b + 2 * i + 1] = 1.0
    return C, S2, P1T, P2T


def causal_masks(TCH=512):
    m = np.zeros((128, 4 * TCH), np.float32)
    for r in range(4):
        for sl in range(128):
            lo = r * 128 + sl
            if lo < TCH:
                m[sl, r * TCH + lo:(r + 1) * TCH] = 1.0
    return m


NHEAD = 16
NKV = 4
GRP = 4
D = 2048
SCALE_OUT = 2048.0 ** -0.5


def emit_l0_attn(S, T, NB, d):
    NT = T // 128
    TC = T // 512
    sb = S.sbuf

    xst0 = sb("l0a_xst0", [128, D], F32)
    xst = [xst0, xst0]
    xn = [sb(f"l0a_xn{i}", [128, D], BF16) for i in range(2)]
    stats = sb("l0a_stats", [128, 24], F32)
    mv = [sb(f"l0a_mv{i}", [128, 2], F32) for i in range(2)]
    rstd = [sb(f"l0a_rstd{i}", [128, 1], F32) for i in range(2)]
    bigA = sb("l0a_bigA", [128, 16 * 2048], BF16)
    ct = sb("l0a_ct", [128, T], F32)
    st = sb("l0a_st", [128, T], F32)
    p1t = sb("l0a_p1t", [128, 128], F32)
    p2t = sb("l0a_p2t", [128, 128], F32)
    qf32 = sb("l0a_qf32", [128, 512], F32)
    dmask = sb("l0a_dmask", [128, 4 * 512], BF16)
    w_sl = [sb(f"l0a_w{i}", [128, D], BF16) for i in range(4)]
    bqo_row = sb("l0a_bqo", [1, D], F32)
    bk_row = sb("l0a_bk", [1, 512], F32)
    bv_row = sb("l0a_bv", [1, 512], F32)
    ones_r = sb("l0a_ones_r", [1, 512], F32)
    ones_c = sb("l0a_ones_c", [128, 1], BF16)
    eps_c = sb("l0a_eps", [128, 1], F32)
    qT = sb("l0a_qT", [128, T], BF16)
    kT = sb("l0a_kT", [128, T], BF16)
    v_tm = sb("l0a_vtm", [128, NT * 512], BF16)
    tmp1 = sb("l0a_tmp1", [128, 512], F32)
    tmp2 = sb("l0a_tmp2", [128, 512], F32)
    expT = sb("l0a_expT", [128, NT * 512], BF16)
    recip = sb("l0a_recip", [1, 512], F32)
    recipB = sb("l0a_recipB", [128, 512], F32)
    ctx_sb = [sb(f"l0a_ctxsb{i}", [128, 512], BF16) for i in range(2)]
    ctxg = [sb(f"l0a_ctxg{i}", [128, 2048], BF16) for i in range(2)]

    scr = d["ctx_scr"]

    c_tab = S.dma("gdma", [], ct[:], d["ct"])
    S.dma("gdma", [], st[:], d["st"])
    S.dma("gdma", [], p1t[:], d["p1t"])
    S.dma("gdma", [], p2t[:], d["p2t"])
    c_msk = S.dma("gdma", [], dmask[:], d["dmask"])
    c_bk = S.dma("gdma", [], bk_row[:], d["bk"].rearrange("(a m) -> a m", a=1))
    c_bv = S.dma("gdma", [], bv_row[:], d["bv"].rearrange("(a m) -> a m", a=1))
    c_ones = S.step("vector", [], lambda e: (e.memset(ones_r[:], 1.0), e.memset(ones_c[:], 1.0),
                                             e.memset(eps_c[:], 1e-5))[-1])

    last_out = []
    pv = {}

    for b in range(NB):
        c_bq = S.dma("gdma", [pv.get("po_last")], bqo_row[:], d["bq"].rearrange("(a m) -> a m", a=1))

        G = S.global_psums()
        ps_q, ps_p1, ps_p2 = G["q"], G["p1"], G["p2"]
        ps_ctx, ps_bc = G["ctx"], G["bc"]
        ps_rs = ps_bc
        ps_sc = [G["sc0"], G["sc1"]]
        ps_out4 = [G["sc0"], G["sc1"], G["ctx"], G["bc"]]

        st_norm = [None] * NT
        st_tr = [None] * NT
        st_ag = [None] * NT
        for i in range(NT):
            dep_war = st_norm[i - 1] if i >= 1 else pv.get("out_last1")
            ld = S.dma("sync", [dep_war], xst[i % 2][:],
                       d["x"][b * T + i * 128: b * T + (i + 1) * 128, :])
            bs = S.step("vector", [ld, st_ag[i - 1] if i >= 1 else pv.get("ag_last")],
                        lambda e, i=i: [e.bn_stats(stats[:, 6 * k:6 * k + 6],
                                                   xst[i % 2][:, 512 * k:512 * (k + 1)]) for k in range(4)][-1])
            st_ag[i] = S.step("vector", [bs], lambda e, i=i: e.bn_aggr(mv[i % 2][:], stats[:]))
            ln1 = S.step("scalar", [st_ag[i], st_norm[i - 2] if i >= 2 else pv.get("norm_last"), c_ones],
                         lambda e, i=i: e.activation(rstd[i % 2][:], mv[i % 2][:, 1:2], AF.Ln, bias=eps_c[:, 0:1]))
            ln2 = S.step("scalar", [ln1], lambda e, i=i: e.activation(rstd[i % 2][:], rstd[i % 2][:], AF.Exp, scale=-0.5))
            st_norm[i] = S.step("vector", [ln2, st_tr[i - 2] if i >= 2 else pv.get("tr_last")],
                                lambda e, i=i: e.tensor_scalar(xn[i % 2][:], xst[i % 2][:], mv[i % 2][:, 0:1],
                                                               rstd[i % 2][:, 0:1], ALU.subtract, ALU.mult))
            deps_tr = [st_norm[i]] + ([pv.get("po_last")] if (b > 0 and i == 0) else [])
            st_tr[i] = S.step("sync", deps_tr,
                              lambda e, i=i: [e.dma_start_transpose(
                                  bigA[:, dt * T + i * 128: dt * T + (i + 1) * 128],
                                  xn[i % 2][:, dt * 128:(dt + 1) * 128]) for dt in range(16)],
                              weight=16 * 16)

        units = [("v", m) for m in range(4)]
        for h in range(NHEAD):
            if h % GRP == 0:
                units.append(("k", h // GRP))
            units.append(("q", h))

        def bufof(kind, idx):
            return w_sl[(2 + idx % 2) if kind != "q" else (idx % 2)]

        wsrc_of = {"q": d["wq"], "k": d["wk"], "v": d["wv"]}
        wld = {}
        last_proj_of = {}

        def ensure_wld(ui):
            if ui >= len(units) or units[ui] in wld:
                return
            kind, idx = units[ui]
            buf = bufof(kind, idx)
            wsrc = wsrc_of[kind][:, idx * 128:(idx + 1) * 128]
            key = ("q", idx % 2) if kind == "q" else ("kv", idx % 2)
            dep = last_proj_of.get(key, pv.get("proj_last") if b > 0 else None)
            wld[(kind, idx)] = S.dma("gdma", [dep],
                                     buf[:].rearrange("p (dt m) -> p dt m", dt=16),
                                     wsrc.rearrange("(dt p) m -> p dt m", p=128))
        ensure_wld(0)
        ensure_wld(1)

        st_ropeB = {}
        prev_pm = pv.get("pm_last")
        prev_ropeA = None
        prev_ropeB = None
        last_reader_of = {}
        prev_stt = pv.get("stt_last")
        prev_rec = pv.get("rec_last")
        prev_bcp = pv.get("bcp_last")
        prev_ctx_dma = {0: pv.get("ctxd_m2"), 1: pv.get("ctxd_m1")}
        st_ctx_dma = []
        psq_prev = pv.get("psq_free")
        exp_slot_free = pv.get("avs_last")
        sc_bank_prev = [pv.get("avs_last"), pv.get("avs_last")]
        proj_last = None
        vT_done = {}
        prev_vtr = pv.get("vtr_last")

        for ui, (kind, idx) in enumerate(units):
            ensure_wld(ui)
            wbuf = bufof(kind, idx)
            if kind == "q":
                brow, boff = bqo_row, idx * 128
                bdep = c_bq
            elif kind == "k":
                brow, boff = bk_row, idx * 128
                bdep = c_bk
            else:
                brow, boff = bv_row, idx * 128
                bdep = c_bv
            dstT = qT if kind == "q" else kT
            for tc in range(TC):
                deps = [wld[(kind, idx)], psq_prev, bdep, c_ones] + [st_tr[min(4 * tc + j, NT - 1)] for j in range(4)]
                if b > 0 and ui == 0 and tc == 0:
                    deps.append(pv.get("radd_last"))
                def pr_fn(e, wbuf=wbuf, brow=brow, boff=boff, tc=tc):
                    e.matmul(ps_q[:], brow[0:1, boff:boff + 128], ones_r[0:1, :], start=True, stop=False)
                    last = None
                    for dt in range(16):
                        last = e.matmul(ps_q[:], wbuf[:, dt * 128:(dt + 1) * 128],
                                        bigA[:, dt * T + tc * 512: dt * T + tc * 512 + 512],
                                        start=False, stop=(dt == 15))
                    return last
                stp = S.step("tensor", deps, pr_fn)
                proj_last = stp
                if tc == TC - 1:
                    last_proj_of[("q", idx % 2) if kind == "q" else ("kv", idx % 2)] = stp
                    ensure_wld(ui + 1)
                    ensure_wld(ui + 2)
                if kind == "v":
                    ev_war = vT_done.get(idx - 1, pv.get("avs_last"))
                    ev = S.step("scalar", [stp, ev_war] if ev_war else [stp],
                                lambda e, dstT=dstT, tc=tc: e.copy(dstT[:, tc * 512:(tc + 1) * 512], ps_q[:]))
                    psq_prev = ev
                    continue
                ev = S.step("scalar", [stp] + ([prev_pm] if prev_pm else []),
                            lambda e: e.copy(qf32[:], ps_q[:]))
                psq_prev = ev
                pm = S.step("tensor", [ev, prev_ropeA],
                            lambda e: (
                                e.matmul(ps_p1[:], p1t[:], qf32[:], start=True, stop=True),
                                e.matmul(ps_p2[:], p2t[:], qf32[:], start=True, stop=True))[1])
                prev_pm = pm
                ra = S.step("vector", [pm, prev_ropeB, c_tab],
                            lambda e, tc=tc: (
                                e.tensor_tensor(tmp1[:], ps_p1[:], ct[:, tc * 512:(tc + 1) * 512], ALU.mult),
                                e.tensor_tensor(tmp2[:], ps_p2[:], st[:, tc * 512:(tc + 1) * 512], ALU.mult))[1])
                prev_ropeA = ra
                if kind == "k":
                    rb_war = last_reader_of.get(("k", idx - 1), vT_done.get(3))
                else:
                    rb_war = last_reader_of.get(("q", idx - 1), pv.get("avs_last"))
                rb_ = S.step("vector", [ra] + ([rb_war] if rb_war else []),
                             lambda e, dstT=dstT, tc=tc: e.tensor_tensor(
                                 dstT[:, tc * 512:(tc + 1) * 512], tmp1[:], tmp2[:], ALU.add))
                prev_ropeB = rb_
                st_ropeB[(kind, idx, tc)] = rb_

            if kind == "v":
                war_vtm = pv.get("avs_last") if (b > 0 and idx == 0) else None
                vt = S.step("sync", [psq_prev] + ([war_vtm] if war_vtm else []),
                            lambda e, m=idx: [e.dma_start_transpose(
                                v_tm[:, i * 512 + m * 128: i * 512 + (m + 1) * 128],
                                kT[:, i * 128:(i + 1) * 128]) for i in range(NT)],
                            weight=16 * NT)
                vT_done[idx] = vt
                continue
            if kind == "k":
                continue

            h = idx
            g = h // GRP
            for tc in range(TC):
                smax = min(4 * (tc + 1), NT)
                rdy = [None] * smax
                for s_ in range(smax):
                    deps = [st_ropeB[("q", h, tc)], st_ropeB[("k", g, s_ // 4)],
                            sc_bank_prev[s_ % 2]]
                    scs = S.step("tensor", deps,
                                 lambda e, s_=s_, tc=tc: e.matmul(
                                     ps_sc[s_ % 2][:], kT[:, s_ * 128:(s_ + 1) * 128],
                                     qT[:, tc * 512:(tc + 1) * 512], start=True, stop=True))
                    ex = S.step("scalar", [scs, exp_slot_free],
                                lambda e, s_=s_: e.activation(expT[:, s_ * 512:(s_ + 1) * 512],
                                                              ps_sc[s_ % 2][:], AF.Exp))
                    sc_bank_prev[s_ % 2] = ex
                    if s_ >= 4 * tc:
                        r = s_ - 4 * tc
                        rdy[s_] = S.step("vector", [ex, c_msk],
                                         lambda e, s_=s_, r=r: e.tensor_tensor(
                                             expT[:, s_ * 512:(s_ + 1) * 512], expT[:, s_ * 512:(s_ + 1) * 512],
                                             dmask[:, r * 512:(r + 1) * 512], ALU.mult))
                    else:
                        rdy[s_] = ex
                avs = None
                for s_ in range(smax):
                    deps = [rdy[s_], vT_done[g]]
                    if s_ == 0:
                        deps += [prev_rec, prev_stt, prev_bcp, c_ones]
                    def av_fn(e, s_=s_, g=g, first=(s_ == 0), last=(s_ == smax - 1)):
                        e.matmul(ps_rs[0:1, :], ones_c[:], expT[:, s_ * 512:(s_ + 1) * 512],
                                 start=first, stop=last)
                        return e.matmul(ps_ctx[:], v_tm[:, s_ * 512 + g * 128: s_ * 512 + g * 128 + 128],
                                        expT[:, s_ * 512:(s_ + 1) * 512], start=first, stop=last)
                    avs = S.step("tensor", deps, av_fn)
                exp_slot_free = avs
                last_reader_of[("q", h)] = avs
                if h % GRP == GRP - 1:
                    last_reader_of[("k", g)] = avs
                rec = S.step("vector", [avs], lambda e: e.reciprocal(recip[:], ps_rs[0:1, :]))
                prev_rec = rec
                bcm = S.step("tensor", [rec, prev_bcp],
                             lambda e: e.matmul(ps_bc[:], ones_r[0:1, 0:128], recip[0:1, :],
                                                start=True, stop=True))
                rbc = S.step("scalar", [bcm, prev_stt],
                             lambda e: e.copy(recipB[:], ps_bc[:]))
                prev_bcp = rbc
                j = (h * TC + tc) % 2
                stt = S.step("vector", [rbc, avs, prev_ctx_dma[j]],
                             lambda e, j=j: e.scalar_tensor_tensor(
                                 ctx_sb[j][:], ps_ctx[:], SCALE_OUT, recipB[:], ALU.mult, ALU.mult))
                prev_stt = stt
                cd = S.dma("sdma", [stt], scr[:, h * T + tc * 512: h * T + (tc + 1) * 512], ctx_sb[j][:])
                prev_ctx_dma[j] = cd
                st_ctx_dma.append(cd)

        xnT_free = psq_prev
        wo_lds = []
        for dt in range(16):
            wo_lds.append(S.dma("gdma", [xnT_free] if dt == 0 else [],
                                bigA[:, dt * 2048:(dt + 1) * 2048],
                                d["wo"][dt * 128:(dt + 1) * 128, :]))
        c_bo = S.dma("gdma", [proj_last], bqo_row[:], d["bo"].rearrange("(a m) -> a m", a=1))
        st_radd = [None] * NT
        st_out = [None] * NT
        st_cg = [None] * NT
        st_po = [None] * NT
        po_last = None
        scr3 = scr.rearrange("p (dt t) -> p dt t", dt=16)
        for i in range(NT):
            cg_deps = (S.dma_fence("sdma") if i == 0 else []) + \
                      [st_po[i - 2] if i >= 2 else pv.get("po_last")]
            st_cg[i] = S.dma("sync", cg_deps,
                             ctxg[i % 2][:].rearrange("p (dt t) -> p dt t", dt=16),
                             scr3[:, :, i * 128:(i + 1) * 128])
            x2 = S.dma("sync", [st_out[i - 1] if i >= 1 else pv.get("out_last1"),
                                st_norm[NT - 1] if i < 2 else None],
                       xst[i % 2][:], d["x"][b * T + i * 128: b * T + (i + 1) * 128, :])
            deps = wo_lds + [st_cg[i], c_bo, st_radd[i - 1] if i >= 1 else pv.get("radd_last")]
            if i == 0:
                deps.append(prev_stt)
            def po_fn(e, i=i):
                for c in range(4):
                    e.matmul(ps_out4[c][:], ones_r[0:1, 0:128],
                             bqo_row[0:1, c * 512:(c + 1) * 512], start=True, stop=False)
                last = None
                for dt in range(16):
                    for c in range(4):
                        last = e.matmul(ps_out4[c][:],
                                        ctxg[i % 2][:, dt * 128:(dt + 1) * 128],
                                        bigA[:, dt * 2048 + c * 512: dt * 2048 + (c + 1) * 512],
                                        start=False, stop=(dt == 15))
                return last
            po = S.step("tensor", deps, po_fn)
            po_last = po
            st_po[i] = po_last
            ra = S.step("vector", [po_last, x2],
                        lambda e, i=i: [e.tensor_tensor(xst[i % 2][:, c * 512:(c + 1) * 512],
                                                        ps_out4[c][:], xst[i % 2][:, c * 512:(c + 1) * 512],
                                                        ALU.add) for c in range(4)][-1])
            st_radd[i] = ra
            st_out[i] = S.dma("sync", [ra], d["out"][b * T + i * 128: b * T + (i + 1) * 128, :], xst[i % 2][:])
            last_out.append(st_out[i])

        pv = dict(norm_last=st_norm[NT - 1], tr_last=st_tr[NT - 1], ag_last=st_ag[NT - 1],
                  po_last=po_last, psq_free=psq_prev, radd_last=st_radd[NT - 1],
                  stt_last=prev_stt, rec_last=prev_rec, avs_last=exp_slot_free,
                  proj_last=proj_last, bcp_last=prev_bcp, pm_last=prev_pm,
                  out_last2=st_out[NT - 2] if NT >= 2 else None,
                  out_last1=st_out[NT - 1], ctxd_m2=prev_ctx_dma[0], ctxd_m1=prev_ctx_dma[1],
                  vtr_last=vT_done.get(3))

    return last_out


D = 2048
HID = 5632
NH = HID // 128


def emit_l0_ffn(S, NTOK, d, SLAB=1024):
    NSL = NTOK // SLAB
    NT = SLAB // 128
    TC = SLAB // 512
    sb = S.sbuf
    G = S.global_psums()
    assert TC == 1
    ps_g = [G["q"], G["p1"]]
    ps_u = [G["p2"], G["sc0"]]
    ps_d0 = [G["sc1"], G["ctx"], G["bc"], G["d"]]
    ps_d1 = [G["q"], G["p1"], G["p2"], G["sc0"]]

    xst = [sb(f"l0f_xst{i}", [128, D], F32) for i in range(2)]
    xn = [sb(f"l0f_xn{i}", [128, D], BF16) for i in range(2)]
    stats = sb("l0f_stats", [128, 24], F32)
    mv = [sb(f"l0f_mv{i}", [128, 2], F32) for i in range(2)]
    rstd = [sb(f"l0f_rstd{i}", [128, 1], F32) for i in range(2)]
    eps_c = sb("l0f_eps", [128, 1], F32)
    xnT = sb("l0f_xnT", [128, 16 * SLAB], BF16)
    wbf = [sb(f"l0f_wbf{i}", [128, D], BF16) for i in range(2)]
    wtp = [sb(f"l0f_wtp{i}", [128, D], BF16) for i in range(2)]
    sgu = sb("l0f_sgu", [128, 2 * NH], F32)
    sdc = sb("l0f_sdc", [128, 16], F32)
    wg = [sb(f"l0f_wg{i}", [128, D], BF16) for i in range(2)]
    wu = [sb(f"l0f_wu{i}", [128, D], BF16) for i in range(2)]
    wd = [sb(f"l0f_wd{i}", [128, D], BF16) for i in range(2)]
    g_sb = sb("l0f_gsb", [128, 1024], BF16)
    hT = sb("l0f_hT", [128, NH * SLAB], BF16)

    scr_g, scr_u, scr_d = d["scr_g"], d["scr_u"], d["scr_d"]

    c_eps = S.step("vector", [], lambda e: e.memset(eps_c[:], 1e-5))
    c_sg = S.dma("gdma", [], sgu[:, 0:NH], d["s_gate_t"])
    c_su = S.dma("gdma", [], sgu[:, NH:2 * NH], d["s_up_t"])
    c_sd = S.dma("gdma", [], sdc[:], d["s_down_t"])

    prep_units = [(k, ht) for k in ("g", "u") for ht in range(NH)] + [("d", ht) for ht in range(NH)]
    prep_done = {}
    pu_prev = {0: None, 1: None}
    dq_prev = {0: None, 1: None}
    tp_prev = {0: None, 1: None}
    for n, (k, ht) in enumerate(prep_units):
        par = n % 2
        if k == "d":
            src = d["wq_down"][:, ht * 128:(ht + 1) * 128].rearrange("(dt p) m -> p dt m", p=128)
            scr = scr_d
            ld = S.dma("gdma", [tp_prev[par]], wbf[par][:].rearrange("p (dt m) -> p dt m", dt=16), src)
            dq = S.step("vector", [ld, tp_prev[par], c_sd],
                        lambda e, par=par: [e.tensor_scalar(
                            wbf[par][:, dt * 128:(dt + 1) * 128], wbf[par][:, dt * 128:(dt + 1) * 128],
                            -128.0, sdc[:, dt:dt + 1], ALU.add, ALU.mult) for dt in range(16)][-1])
        else:
            src = (d["wq_gate"] if k == "g" else d["wq_up"])[ht * 128:(ht + 1) * 128, :]
            scr = scr_g if k == "g" else scr_u
            col = ht if k == "g" else NH + ht
            ld = S.dma("gdma", [tp_prev[par]], wbf[par][:], src)
            dq = S.step("vector", [ld, tp_prev[par], c_sg if k == "g" else c_su],
                        lambda e, par=par, col=col: e.tensor_scalar(
                            wbf[par][:], wbf[par][:], -128.0, sgu[:, col:col + 1], ALU.add, ALU.mult))
        dq_prev[par] = dq
        tp = S.step("sync", [dq, pu_prev[par]],
                    lambda e, par=par: [e.dma_start_transpose(
                        wtp[par][:, dt * 128:(dt + 1) * 128],
                        wbf[par][:, dt * 128:(dt + 1) * 128]) for dt in range(16)],
                    weight=256)
        tp_prev[par] = tp
        stc = S.dma("sdma", [tp], scr[ht * 128:(ht + 1) * 128, :], wtp[par][:])
        pu_prev[par] = stc
        prep_done[(k, ht)] = stc

    last_out = []
    pv = {}

    for sl in range(NSL):
        base = sl * SLAB
        st_norm = [None] * NT
        st_tr = [None] * NT
        st_ag = [None] * NT
        for i in range(NT):
            dep_war = st_norm[i - 2] if i >= 2 else (pv.get("out_last2") if i == 0 else pv.get("out_last1"))
            ld = S.dma("sync", [dep_war], xst[i % 2][:],
                       d["x"][base + i * 128: base + (i + 1) * 128, :])
            bs = S.step("vector", [ld, st_ag[i - 1] if i >= 1 else pv.get("ag_last")],
                        lambda e, i=i: [e.bn_stats(stats[:, 6 * kk:6 * kk + 6],
                                                   xst[i % 2][:, 512 * kk:512 * (kk + 1)]) for kk in range(4)][-1])
            st_ag[i] = S.step("vector", [bs], lambda e, i=i: e.bn_aggr(mv[i % 2][:], stats[:]))
            ln1 = S.step("scalar", [st_ag[i], st_norm[i - 2] if i >= 2 else pv.get("norm_last"), c_eps],
                         lambda e, i=i: e.activation(rstd[i % 2][:], mv[i % 2][:, 1:2], AF.Ln, bias=eps_c[:, 0:1]))
            ln2 = S.step("scalar", [ln1], lambda e, i=i: e.activation(rstd[i % 2][:], rstd[i % 2][:], AF.Exp, scale=-0.5))
            st_norm[i] = S.step("vector", [ln2, st_tr[i - 2] if i >= 2 else pv.get("tr_last")],
                                lambda e, i=i: e.tensor_scalar(xn[i % 2][:], xst[i % 2][:], mv[i % 2][:, 0:1],
                                                               rstd[i % 2][:, 0:1], ALU.subtract, ALU.mult))
            deps_tr = [st_norm[i]] + ([pv.get("xnT_free")] if i < 2 else [])
            st_tr[i] = S.step("sync", deps_tr,
                              lambda e, i=i: [e.dma_start_transpose(
                                  xnT[:, dt * SLAB + i * 128: dt * SLAB + (i + 1) * 128],
                                  xn[i % 2][:, dt * 128:(dt + 1) * 128]) for dt in range(16)],
                              weight=256)

        st_g = {}
        st_u = {}
        wld_g = {}
        wld_u = {}
        hread_prev = pv.get("hread", {})

        def ensure_gu(ht):
            if ht in wld_g or ht >= NH:
                return
            par = ht % 2
            depg = st_g.get(ht - 2, pv.get("xnT_free"))
            depu = st_u.get(ht - 2, pv.get("xnT_free"))
            wld_g[ht] = S.dma("gdma", [prep_done[("g", ht)], depg], wg[par][:],
                              scr_g[ht * 128:(ht + 1) * 128, :])
            wld_u[ht] = S.dma("gdma", [prep_done[("u", ht)], depu], wu[par][:],
                              scr_u[ht * 128:(ht + 1) * 128, :])
        ensure_gu(0)
        ensure_gu(1)

        prev_gev = pv.get("gev_last")
        prev_hvv = pv.get("hvv_last")
        gev_bank = {0: pv.get("gev_b0"), 1: pv.get("gev_b1")}
        hv_bank = {0: pv.get("hv_b0"), 1: pv.get("hv_b1")}
        h_ready = [None] * NH
        for ht in range(NH):
            par = ht % 2
            bk = ht % 2
            deps = [wld_g[ht], gev_bank[bk]] + [st_tr[j] for j in range(NT)]
            if ht == 0:
                deps.append(pv.get("radd_last"))
            gp = S.step("tensor", deps,
                        lambda e, par=par, bk=bk: [e.matmul(ps_g[bk][:], wg[par][:, dt * 128:(dt + 1) * 128],
                                                            xnT[:, dt * SLAB: dt * SLAB + 512],
                                                            start=(dt == 0), stop=(dt == 15))
                                                   for dt in range(16)][-1])
            st_g[ht] = gp
            gev = S.step("scalar", [gp, hv_bank[bk], prev_gev],
                         lambda e, bk=bk: e.activation(g_sb[:, bk * 512:(bk + 1) * 512], ps_g[bk][:], AF.Silu))
            gev_bank[bk] = gev
            prev_gev = gev
            up = S.step("tensor", [wld_u[ht], hv_bank[bk]],
                        lambda e, par=par, bk=bk: [e.matmul(ps_u[bk][:], wu[par][:, dt * 128:(dt + 1) * 128],
                                                            xnT[:, dt * SLAB: dt * SLAB + 512],
                                                            start=(dt == 0), stop=(dt == 15))
                                                   for dt in range(16)][-1])
            st_u[ht] = up
            hv = S.step("vector", [up, gev, hread_prev.get(ht), prev_hvv],
                        lambda e, ht=ht, bk=bk: e.tensor_tensor(hT[:, ht * SLAB: ht * SLAB + 512],
                                                                ps_u[bk][:], g_sb[:, bk * 512:(bk + 1) * 512],
                                                                ALU.mult))
            hv_bank[bk] = hv
            prev_hvv = hv
            h_ready[ht] = hv
            ensure_gu(ht + 2)

        st_d = {}
        wldd = {}
        st_radd = {}
        st_out_l = {}

        def ensure_d(j, ht):
            if (j, ht) in wldd or ht >= NH:
                return
            if j >= NT // 2:
                return
            par = (j * NH + ht) % 2
            dep = st_d.get((j, ht - 2)) if ht >= 2 else (st_d.get((j - 1, NH - 2 + ht)) or pv.get("dlast"))
            wldd[(j, ht)] = S.dma("gdma", [prep_done[("d", ht)], dep], wd[par][:],
                                  scr_d[ht * 128:(ht + 1) * 128, :])
        ensure_d(0, 0)
        ensure_d(0, 1)
        for j in range(NT // 2):
            po = None
            for ht in range(NH):
                par = (j * NH + ht) % 2
                deps = [wldd[(j, ht)], h_ready[ht]]
                if ht == 0:
                    deps += [st_radd.get(j - 1, pv.get("radd_last")), h_ready[NH - 1], prev_gev]
                def dp_fn(e, j=j, ht=ht, par=par, first=(ht == 0), last=(ht == NH - 1)):
                    out = None
                    for tt in range(2):
                        pset = ps_d0 if tt == 0 else ps_d1
                        for c in range(4):
                            out = e.matmul(pset[c][:],
                                           hT[:, ht * SLAB + (2 * j + tt) * 128: ht * SLAB + (2 * j + tt + 1) * 128],
                                           wd[par][:, c * 512:(c + 1) * 512],
                                           start=first, stop=last)
                    return out
                po = S.step("tensor", deps, dp_fn)
                st_d[(j, ht)] = po
                if ht + 2 < NH:
                    ensure_d(j, ht + 2)
                else:
                    ensure_d(j + 1, ht + 2 - NH)
            for tt in range(2):
                i = 2 * j + tt
                x2 = S.dma("sync", [st_norm[NT - 1],
                                    st_out_l.get(i - 2, pv.get("out_last2") if tt == 0 else pv.get("out_last1"))],
                           xst[i % 2][:], d["x"][base + i * 128: base + (i + 1) * 128, :])
                pset = ps_d0 if tt == 0 else ps_d1
                ra = S.step("vector", [po, x2],
                            lambda e, i=i, pset=pset: [
                                e.tensor_tensor(xst[i % 2][:, c * 512:(c + 1) * 512], pset[c][:],
                                                xst[i % 2][:, c * 512:(c + 1) * 512], ALU.add)
                                for c in range(4)][-1])
                st_radd[j] = ra
                so = S.dma("sync", [ra], d["out"][base + i * 128: base + (i + 1) * 128, :], xst[i % 2][:])
                st_out_l[i] = so
                last_out.append(so)

        pv = dict(
            norm_last=st_norm[NT - 1], tr_last=st_tr[NT - 1], ag_last=st_ag[NT - 1],
            gev_last=prev_gev, hvv_last=prev_hvv, radd_last=st_radd[NT // 2 - 1],
            gev_b0=gev_bank[0], gev_b1=gev_bank[1], hv_b0=hv_bank[0], hv_b1=hv_bank[1],
            out_last2=st_out_l[NT - 2], out_last1=st_out_l[NT - 1],
            xnT_free=st_u[NH - 1], dlast=st_d[(NT // 2 - 1, NH - 1)],
            hread={ht: st_d[(NT // 2 - 1, ht)] for ht in range(NH)},
        )

    return last_out


NHEAD = 16
D = 2048
DC = 512
DR = 64
SCL = 1.0 / np.sqrt(192.0)


def emit_l1_attn(S, T, NB, d):
    NT = T // 128
    TC = T // 512
    sb = S.sbuf
    G = S.global_psums()
    ps_q, ps_p1, ps_p2 = G["q"], G["p1"], G["p2"]
    ps_ctx, ps_bc, ps_v = G["ctx"], G["bc"], G["d"]
    ps_rs = ps_bc
    ps_sc = [G["sc0"], G["sc1"]]
    ps_out4 = [G["sc0"], G["sc1"], G["ctx"], G["d"]]

    xst = sb("l1a_xst", [128, D], F32)
    xn = sb("l1a_xn", [128, D], BF16)
    stats = sb("l1a_stats", [128, 24], F32)
    mv = [sb(f"l1a_mv{i}", [128, 2], F32) for i in range(2)]
    rstd = [sb(f"l1a_rstd{i}", [128, 1], F32) for i in range(2)]
    eps_c = sb("l1a_eps", [128, 1], F32)
    bigA = sb("l1a_bigA", [128, 16 * 2048], BF16)
    ct = sb("l1a_ct", [128, T], F32)
    st = sb("l1a_st", [128, T], F32)
    p1t = sb("l1a_p1t", [128, 128], F32)
    p2t = sb("l1a_p2t", [128, 128], F32)
    qf32 = sb("l1a_qf32", [128, 512], F32)
    w_sl = [sb(f"l1a_w{i}", [128, 2048], BF16) for i in range(2)]
    bo_row = sb("l1a_bo", [1, D], F32)
    ones_r = sb("l1a_ones_r", [1, 512], F32)
    ones_c = sb("l1a_ones_c", [128, 1], BF16)
    ckvT = sb("l1a_ckvT", [128, 4 * T], BF16)
    cqT = sb("l1a_cqT", [128, 4 * T], BF16)
    kRT = sb("l1a_kRT", [128, T], BF16)
    qRT = sb("l1a_qRT", [128, T], BF16)
    qT = sb("l1a_qT", [128, T], BF16)
    kT = sb("l1a_kT", [128, T], BF16)
    v_tm = sb("l1a_vtm", [128, NT * 128], BF16)
    tmp1 = sb("l1a_tmp1", [128, 512], F32)
    tmp2 = sb("l1a_tmp2", [128, 512], F32)
    expT = sb("l1a_expT", [128, NT * 512], BF16)
    recip = sb("l1a_recip", [1, 512], F32)
    recipB = sb("l1a_recipB", [128, 512], F32)
    ctx_sb = [sb(f"l1a_ctxsb{i}", [128, 512], BF16) for i in range(2)]
    ctxg0 = sb("l1a_ctxg0", [128, 2048], BF16)
    ctxg = [ctxg0, ctxg0]

    scr = d["ctx_scr"]

    c_tab = S.dma("gdma", [], ct[:], d["ct"])
    S.dma("gdma", [], st[:], d["st"])
    S.dma("gdma", [], p1t[:], d["p1t"])
    c_p2 = S.dma("gdma", [], p2t[:], d["p2t"])
    c_ones = S.step("vector", [], lambda e: (e.memset(ones_r[:], 1.0), e.memset(ones_c[:], 1.0),
                                             e.memset(eps_c[:], 1e-5))[-1])

    last_out = []
    pv = {}

    for b in range(NB):
        c_bo = S.dma("gdma", [pv.get("po_last")], bo_row[:], d["b_o"].rearrange("(a m) -> a m", a=1))

        st_norm = [None] * NT
        st_tr = [None] * NT
        st_ag = [None] * NT
        for i in range(NT):
            dep_war = st_norm[i - 1] if i >= 1 else pv.get("out_last1")
            ld = S.dma("sync", [dep_war], xst[:],
                       d["x"][b * T + i * 128: b * T + (i + 1) * 128, :])
            bs = S.step("vector", [ld, st_ag[i - 1] if i >= 1 else pv.get("ag_last")],
                        lambda e: [e.bn_stats(stats[:, 6 * kk:6 * kk + 6],
                                              xst[:, 512 * kk:512 * (kk + 1)]) for kk in range(4)][-1])
            st_ag[i] = S.step("vector", [bs], lambda e, i=i: e.bn_aggr(mv[i % 2][:], stats[:]))
            ln1 = S.step("scalar", [st_ag[i], st_norm[i - 2] if i >= 2 else pv.get("norm_last"), c_ones],
                         lambda e, i=i: e.activation(rstd[i % 2][:], mv[i % 2][:, 1:2], AF.Ln, bias=eps_c[:, 0:1]))
            ln2 = S.step("scalar", [ln1], lambda e, i=i: e.activation(rstd[i % 2][:], rstd[i % 2][:], AF.Exp, scale=-0.5))
            st_norm[i] = S.step("vector", [ln2, st_tr[i - 1] if i >= 1 else pv.get("tr_last")],
                                lambda e, i=i: e.tensor_scalar(xn[:], xst[:], mv[i % 2][:, 0:1],
                                                               rstd[i % 2][:, 0:1], ALU.subtract, ALU.mult))
            deps_tr = [st_norm[i]] + ([pv.get("po_last")] if (b > 0 and i == 0) else [])
            st_tr[i] = S.step("sync", deps_tr,
                              lambda e, i=i: [e.dma_start_transpose(
                                  bigA[:, dt * T + i * 128: dt * T + (i + 1) * 128],
                                  xn[:, dt * 128:(dt + 1) * 128]) for dt in range(16)],
                              weight=256)
        tr_all = st_tr[NT - 1]

        units = [("ckv", m) for m in range(4)] + [("cq", m) for m in range(4)] + [("kr", 0)]
        for h in range(NHEAD):
            if h % 2 == 0:
                units.append(("qr", h // 2))
            units.append(("k", h))
            units.append(("q", h))
            units.append(("v", h))

        def wspec(kind, idx):
            if kind == "ckv":
                return d["w_dkv"][:, idx * 128:(idx + 1) * 128], 16, 128
            if kind == "cq":
                return d["w_dq"][:, idx * 128:(idx + 1) * 128], 16, 128
            if kind == "kr":
                return d["w_kr"], 16, 128
            if kind == "qr":
                return d["w_qr"][:, idx * 128:(idx + 1) * 128], 4, 128
            if kind == "k":
                return d["w_uk"][:, idx * 128:(idx + 1) * 128], 4, 128
            if kind == "q":
                return d["w_uq"][:, idx * 128:(idx + 1) * 128], 4, 128
            if kind == "v":
                return d["w_uv"][:, idx * 128:(idx + 1) * 128], 4, 128

        wld = {}
        wld_reader = {}

        def ensure_wld(ui):
            if ui >= len(units) or units[ui] in wld:
                return
            kind, idx = units[ui]
            wsrc, nk, mc = wspec(kind, idx)
            par = ui % 2
            if kind == "kr":
                v3 = w_sl[par][:].rearrange("p (dt m) -> p dt m", m=128)[:, 0:nk, :]
                d1 = S.dma("gdma", [wld_reader.get(par)], v3[:, :, 0:64],
                           wsrc.rearrange("(dt p) m -> p dt m", p=128))
                wld[(kind, idx)] = S.dma("gdma", [d1], v3[:, :, 64:128],
                                         wsrc.rearrange("(dt p) m -> p dt m", p=128))
            else:
                wld[(kind, idx)] = S.dma("gdma", [wld_reader.get(par)],
                                         w_sl[par][:].rearrange("p (dt m) -> p dt m", m=mc)[:, 0:nk, :],
                                         wsrc.rearrange("(dt p) m -> p dt m", p=128))
        ensure_wld(0)
        ensure_wld(1)

        st_ropeB = {}
        prev_pm = pv.get("pm_last")
        prev_ropeA = None
        prev_ropeB = None
        last_reader_of = {}
        prev_stt = pv.get("stt_last")
        prev_rec = pv.get("rec_last")
        prev_bcp = pv.get("bcp_last")
        prev_ctx_dma = {0: pv.get("ctxd_m2"), 1: pv.get("ctxd_m1")}
        psq_prev = pv.get("psq_free")
        psv_prev = pv.get("psv_free")
        exp_slot_free = pv.get("avs_last")
        sc_bank_prev = [pv.get("avs_last"), pv.get("avs_last")]
        proj_last = None
        kr_done = None

        for ui, (kind, idx) in enumerate(units):
            ensure_wld(ui)
            par = ui % 2
            wbuf = w_sl[par]
            wsrc, nk, mc = wspec(kind, idx)

            if kind == "v":
                h = idx
                deps = [wld[(kind, idx)], tr_all, psv_prev,
                        last_reader_of.get(("v", h - 1), pv.get("avs_last"))]
                vev = None
                for i in range(NT):
                    vp = S.step("tensor", deps if i == 0 else [vev],
                                lambda e, wbuf=wbuf, i=i: [e.matmul(
                                    ps_v[:, 0:128], ckvT[:, kt * T + i * 128: kt * T + (i + 1) * 128],
                                    wbuf[:, kt * 128:(kt + 1) * 128], start=(kt == 0), stop=(kt == 3))
                                    for kt in range(4)][-1])
                    vev = S.step("scalar", [vp],
                                 lambda e, i=i: e.copy(v_tm[:, i * 128:(i + 1) * 128], ps_v[:, 0:128]))
                wld_reader[par] = vp
                psv_prev = vev
                vT_ready = vev
                ensure_wld(ui + 1)
                ensure_wld(ui + 2)
                g2 = h % 2
                for tc in range(TC):
                    rdy = [None] * NT
                    st_sc = None
                    for s_ in range(NT):
                        deps = [st_ropeB[("q", h)], st_ropeB[("k", h)],
                                st_ropeB[("qr", h // 2)], kr_done,
                                sc_bank_prev[s_ % 2]]
                        st_sc = S.step("tensor", deps,
                                       lambda e, s_=s_, tc=tc, g2=g2: (
                                           e.matmul(ps_sc[s_ % 2][:], kT[:, s_ * 128:(s_ + 1) * 128],
                                                    qT[:, tc * 512:(tc + 1) * 512], start=True, stop=False),
                                           e.matmul(ps_sc[s_ % 2][:], kRT[64 * g2:64 * g2 + 64, s_ * 128:(s_ + 1) * 128],
                                                    qRT[64 * g2:64 * g2 + 64, tc * 512:(tc + 1) * 512],
                                                    start=False, stop=True))[1])
                        ex = S.step("scalar", [st_sc, exp_slot_free],
                                    lambda e, s_=s_: e.activation(expT[:, s_ * 512:(s_ + 1) * 512],
                                                                  ps_sc[s_ % 2][:], AF.Exp, scale=SCL))
                        sc_bank_prev[s_ % 2] = ex
                        rdy[s_] = ex
                    avs = None
                    for s_ in range(NT):
                        deps = [rdy[s_], vT_ready]
                        if s_ == 0:
                            deps += [prev_rec, prev_stt, prev_bcp, c_ones]
                        def av_fn(e, s_=s_, first=(s_ == 0), last=(s_ == NT - 1)):
                            e.matmul(ps_rs[0:1, :], ones_c[:], expT[:, s_ * 512:(s_ + 1) * 512],
                                     start=first, stop=last)
                            return e.matmul(ps_ctx[:], v_tm[:, s_ * 128:(s_ + 1) * 128],
                                            expT[:, s_ * 512:(s_ + 1) * 512], start=first, stop=last)
                        avs = S.step("tensor", deps, av_fn)
                    exp_slot_free = avs
                    last_reader_of[("v", h)] = avs
                    last_reader_of[("q", h)] = avs
                    last_reader_of[("k", h)] = avs
                    if h % 2 == 1:
                        last_reader_of[("qr", h // 2)] = avs
                    rec = S.step("vector", [avs], lambda e: e.reciprocal(recip[:], ps_rs[0:1, :]))
                    prev_rec = rec
                    bcm = S.step("tensor", [rec, prev_bcp],
                                 lambda e: e.matmul(ps_bc[:], ones_r[0:1, 0:128], recip[0:1, :],
                                                    start=True, stop=True))
                    rbc = S.step("scalar", [bcm, prev_stt],
                                 lambda e: e.copy(recipB[:], ps_bc[:]))
                    prev_bcp = rbc
                    j = (h * TC + tc) % 2
                    stt = S.step("vector", [rbc, avs, prev_ctx_dma[j]],
                                 lambda e, j=j: e.scalar_tensor_tensor(
                                     ctx_sb[j][:], ps_ctx[:], 1.0, recipB[:], ALU.mult, ALU.mult))
                    prev_stt = stt
                    cd = S.dma("sdma", [stt], scr[:, h * T + tc * 512: h * T + (tc + 1) * 512], ctx_sb[j][:])
                    prev_ctx_dma[j] = cd
                continue

            if kind in ("ckv", "cq", "kr"):
                mov, movT = bigA, T
                mov_nk = 16
            else:
                mov, movT = (cqT, T) if kind in ("qr",) else (ckvT, T) if kind == "k" else (cqT, T)
                mov_nk = 4
            for tc in range(TC):
                deps = [wld[(kind, idx)], psq_prev, tr_all]
                if b > 0 and ui == 0 and tc == 0:
                    deps.append(pv.get("radd_last"))
                mrows = mc
                def pr_fn(e, wbuf=wbuf, tc=tc, nk=nk, mov=mov, movT=movT, mc=mc):
                    last = None
                    for kt in range(nk):
                        last = e.matmul(ps_q[0:mc, :], wbuf[:, kt * mc:(kt + 1) * mc],
                                        mov[:, kt * movT + tc * 512: kt * movT + tc * 512 + 512],
                                        start=(kt == 0), stop=(kt == nk - 1))
                    return last
                stp = S.step("tensor", deps, pr_fn)
                proj_last = stp
                wld_reader[par] = stp
                if tc == TC - 1:
                    ensure_wld(ui + 1)
                    ensure_wld(ui + 2)
                if kind in ("ckv", "cq", "q", "k"):
                    dst, doff = {
                        "ckv": (ckvT, idx * T), "cq": (cqT, idx * T),
                        "q": (qT, 0), "k": (kT, 0)}[kind]
                    ev_war = None
                    if kind == "q":
                        ev_war = last_reader_of.get(("q", idx - 1), pv.get("avs_last"))
                    elif kind == "k":
                        ev_war = last_reader_of.get(("k", idx - 1), pv.get("avs_last"))
                    elif b > 0 and kind in ("ckv", "cq"):
                        ev_war = pv.get("avs_last" if False else "ckv_free")
                    ev = S.step("scalar", [stp, ev_war],
                                lambda e, dst=dst, doff=doff, tc=tc, mc=mc: e.copy(
                                    dst[:, doff + tc * 512: doff + (tc + 1) * 512], ps_q[:]))
                    psq_prev = ev
                    if kind in ("ckv", "cq"):
                        st_ropeB[(kind, idx)] = ev
                    else:
                        st_ropeB[(kind, idx)] = ev
                else:
                    ev = S.step("scalar", [stp, prev_pm],
                                lambda e, mc=mc: e.copy(qf32[0:mc, :], ps_q[0:mc, :]))
                    psq_prev = ev
                    pm = S.step("tensor", [ev, prev_ropeA],
                                lambda e, mc=mc: (
                                    e.matmul(ps_p1[0:mc, :], p1t[0:mc, 0:mc], qf32[0:mc, :], start=True, stop=True),
                                    e.matmul(ps_p2[0:mc, :], p2t[0:mc, 0:mc], qf32[0:mc, :], start=True, stop=True))[1])
                    prev_pm = pm
                    ra = S.step("vector", [pm, prev_ropeB, c_tab],
                                lambda e, tc=tc, mc=mc: (
                                    e.tensor_tensor(tmp1[0:mc, :], ps_p1[0:mc, :], ct[0:mc, tc * 512:(tc + 1) * 512], ALU.mult),
                                    e.tensor_tensor(tmp2[0:mc, :], ps_p2[0:mc, :], st[0:mc, tc * 512:(tc + 1) * 512], ALU.mult))[1])
                    prev_ropeA = ra
                    if kind == "kr":
                        rb_war = pv.get("kr_free") if b > 0 else None
                        dstT, doff2 = kRT, 0
                    else:
                        rb_war = last_reader_of.get(("qr", idx - 1), pv.get("avs_last"))
                        dstT, doff2 = qRT, 0
                    rb_ = S.step("vector", [ra] + ([rb_war] if rb_war else []),
                                 lambda e, dstT=dstT, tc=tc, mc=mc: e.tensor_tensor(
                                     dstT[0:mc, tc * 512:(tc + 1) * 512], tmp1[0:mc, :], tmp2[0:mc, :], ALU.add))
                    prev_ropeB = rb_
                    if kind == "kr":
                        kr_done = rb_ if tc == TC - 1 else kr_done
                    else:
                        st_ropeB[("qr", idx)] = rb_

        xnT_free = psq_prev
        wo_lds = []
        for dt in range(16):
            wo_lds.append(S.dma("gdma", [xnT_free, proj_last] if dt == 0 else [],
                                bigA[:, dt * 2048:(dt + 1) * 2048],
                                d["w_o"][dt * 128:(dt + 1) * 128, :]))
        st_radd = [None] * NT
        st_out = [None] * NT
        st_po = [None] * NT
        st_cg = [None] * NT
        po_last = None
        scr3 = scr.rearrange("p (dt t) -> p dt t", dt=16)
        for i in range(NT):
            cg_deps = (S.dma_fence("sdma") if i == 0 else []) + \
                      [st_po[i - 1] if i >= 1 else pv.get("po_last")]
            st_cg[i] = S.dma("sync", cg_deps,
                             ctxg[i % 2][:].rearrange("p (dt t) -> p dt t", dt=16),
                             scr3[:, :, i * 128:(i + 1) * 128])
            x2 = S.dma("sync", [st_out[i - 1] if i >= 1 else pv.get("out_last1"), st_norm[NT - 1]],
                       xst[:], d["x"][b * T + i * 128: b * T + (i + 1) * 128, :])
            deps = wo_lds + [st_cg[i], c_bo, st_radd[i - 1] if i >= 1 else pv.get("radd_last")]
            if i == 0:
                deps.append(prev_stt)
            def po_fn(e, i=i):
                for c in range(4):
                    e.matmul(ps_out4[c][:], ones_r[0:1, 0:128],
                             bo_row[0:1, c * 512:(c + 1) * 512], start=True, stop=False)
                last = None
                for dt in range(16):
                    for c in range(4):
                        last = e.matmul(ps_out4[c][:],
                                        ctxg[i % 2][:, dt * 128:(dt + 1) * 128],
                                        bigA[:, dt * 2048 + c * 512: dt * 2048 + (c + 1) * 512],
                                        start=False, stop=(dt == 15))
                return last
            po = S.step("tensor", deps, po_fn)
            po_last = po
            st_po[i] = po
            ra = S.step("vector", [po, x2],
                        lambda e: [e.tensor_tensor(xst[:, c * 512:(c + 1) * 512],
                                                   ps_out4[c][:], xst[:, c * 512:(c + 1) * 512],
                                                   ALU.add) for c in range(4)][-1])
            st_radd[i] = ra
            st_out[i] = S.dma("sync", [ra], d["out"][b * T + i * 128: b * T + (i + 1) * 128, :], xst[:])
            last_out.append(st_out[i])

        pv = dict(norm_last=st_norm[NT - 1], tr_last=st_tr[NT - 1], ag_last=st_ag[NT - 1],
                  po_last=po_last, psq_free=psq_prev, psv_free=psv_prev,
                  radd_last=st_radd[NT - 1],
                  stt_last=prev_stt, rec_last=prev_rec, bcp_last=prev_bcp,
                  avs_last=exp_slot_free, proj_last=proj_last, pm_last=prev_pm,
                  out_last1=st_out[NT - 1],
                  ctxd_m2=prev_ctx_dma[0], ctxd_m1=prev_ctx_dma[1],
                  ckv_free=psv_prev, kr_free=exp_slot_free)

    return last_out


D = 2048
LAT = 1024
SU = 4096
E = 32
NSU = SU // 128
NLT = LAT // 128
BIG = 30000.0


def emit_l1_ffn(S, NTOK, d, SLAB=1024, CH=1024, debug_stop=False):
    NSL = NTOK // SLAB
    NCH = NTOK // CH
    NT = SLAB // 128
    TC = SLAB // 512
    CT = CH // 128
    CC = CH // 512
    sb = S.sbuf
    G = S.global_psums()
    assert TC == 1
    ps_a = [G["q"], G["p1"]]
    ps_d0 = [G["sc1"], G["ctx"], G["bc"], G["d"]]
    ps_d1 = [G["q"], G["p1"], G["p2"], G["sc0"]]
    assert CC == 1
    ps_h = [G["q"], G["p1"]]
    ps_y = [G["sc1"], G["ctx"]]
    ps_l = G["p2"]
    ps_bcast = G["sc0"]

    xst0 = sb("l1f_xst0", [128, D], F32)
    xst = [xst0, xst0]
    xn = [sb(f"l1f_xn{i}", [128, D], BF16) for i in range(2)]
    xn32 = sb("l1f_xn32", [128, D], F32)
    xlo = sb("l1f_xlo", [128, D], BF16)
    xloT = sb("l1f_xloT", [128, D], BF16)
    stats = sb("l1f_stats", [128, 24], F32)
    mv = [sb(f"l1f_mv{i}", [128, 2], F32) for i in range(2)]
    rstd = [sb(f"l1f_rstd{i}", [128, 1], F32) for i in range(2)]
    eps_c = sb("l1f_eps", [128, 1], F32)
    ones_r = sb("l1f_ones_r", [1, 512], F32)
    ones8 = sb("l1f_ones8", [128, 8], F32)
    onesb = sb("l1f_onesb", [1, 128], BF16)
    AB1 = sb("l1f_AB1", [128, 32768], BF16)
    AB2 = sb("l1f_AB2", [128, 16384], BF16)
    WS = sb("l1f_WS", [128, 4096], BF16)
    YSB = sb("l1f_ysb", [128, 8 * 1024], F32)
    wbf = [sb(f"l1f_wb{i}", [128, D], BF16) for i in range(2)]
    gw32 = xst[0]
    wtp = [sb(f"l1f_wt{i}", [128, D], BF16) for i in range(2)]
    gbB = sb("l1f_gbB", [128, 32], F32)
    scores = sb("l1f_scores", [128, 32], F32)
    sbv = sb("l1f_sbv", [128, 32], F32)
    gs = sb("l1f_gs", [128, 4], F32)
    rt1 = sb("l1f_rt1", [128, 32], F32)
    rt2 = sb("l1f_rt2", [128, 32], F32)
    sel = sb("l1f_sel", [128, 32], F32)
    rcol = sb("l1f_rcol", [128, 4], F32)
    cwpad = sb("l1f_cwpad", [128, 128], BF16)
    rt2i = sb("l1f_rt2i", [128, 32], I32)
    cwrow = sb("l1f_cwrow", [1, 2 * CH], BF16)
    cwB = sb("l1f_cwB", [128, 2 * 512], BF16)
    gb_row = sb("l1f_gbrow", [1, 32], F32)
    negB = sb("l1f_negB", [128, 32], F32)

    def A_hsu(s_):
        return AB1[:, s_ * 1024:(s_ + 1) * 1024]

    def B_xlat(lk):
        return AB1[:, lk * 1024:(lk + 1) * 1024]

    def B_h(it):
        return AB1[:, 8192 + it * 1024:8192 + (it + 1) * 1024]

    def B_fc2(lt):
        return AB1[:, lt * 2048:(lt + 1) * 2048]

    def B_ybf(lt):
        return AB1[:, 16384 + lt * CH:16384 + (lt + 1) * CH]

    def A_xnT(dt):
        return AB2[:, dt * SLAB:(dt + 1) * SLAB]

    def B_xnTtc(dt):
        return AB2[:, dt * 512:(dt + 1) * 512]

    def B_gwT(dt):
        return AB2[:, 8192 + dt * 32:8192 + dt * 32 + 32]

    def B_gwTlo(dt):
        return AB2[:, 8832 + dt * 32:8832 + dt * 32 + 32]

    def B_xnlo(dt):
        return AB1[:, 24576 + dt * 512:24576 + (dt + 1) * 512]

    B_cwT = AB2[:, 9472:9472 + 128]

    def A_wbuf(par):
        return WS[:, par * 2048:(par + 1) * 2048]

    def B_fc1buf(par):
        return WS[:, par * 2048:(par + 1) * 2048]

    def B_w1(par):
        return WS[:, par * 1024:(par + 1) * 1024]

    def B_w2(par):
        return WS[:, 2048 + par * 1024:2048 + (par + 1) * 1024]

    scr_su, scr_sd, scr_fc1, scr_fc2, scr_w1, scr_w2 = (
        d["scr_su"], d["scr_sd"], d["scr_fc1"], d["scr_fc2"], d["scr_w1"], d["scr_w2"])
    xnT_scr = d["xnT_scr"]
    xnT_lo_scr = d["xnT_lo_scr"]
    cw_scr = d["cw_scr"]

    c_ones = S.step("vector", [], lambda e: (e.memset(eps_c[:], 1e-5), e.memset(ones_r[:], 1.0),
                                             e.memset(ones8[:], 1.0), e.memset(onesb[:], 1.0),
                                             e.memset(cwpad[:], 0.0), e.memset(negB[:], -BIG))[-1])
    c_gb = S.dma("gdma", [], gb_row[:], d["gate_bias"].rearrange("(a m) -> a m", a=1))
    c_gbm = S.step("tensor", [c_gb, c_ones],
                   lambda e: e.matmul(G["d"][:, 0:32], ones_r[0:1, 0:128], gb_row[0:1, :],
                                      start=True, stop=True))
    c_gbB = S.step("scalar", [c_gbm], lambda e: e.copy(gbB[:], G["d"][:, 0:32]))

    prep_units = ([("su", i) for i in range(NSU)] + [("sd", i) for i in range(NSU)] +
                  [("fc1", i) for i in range(NLT)] + [("fc2", i) for i in range(NLT)] +
                  [("w1", i) for i in range(E * NLT)] + [("w2", i) for i in range(E * NLT)])
    SRC = {"su": (d["shared_up_w"], "row", 2048), "sd": (d["shared_down_w"], "col", 2048),
           "fc1": (d["fc1_w"], "row", 2048), "fc2": (d["fc2_w"], "col", 2048),
           "w1": (d["w1f"], "row", 1024), "w2": (d["w2f"], "row", 1024)}
    SCRS = {"su": scr_su, "sd": scr_sd, "fc1": scr_fc1, "fc2": scr_fc2, "w1": scr_w1, "w2": scr_w2}
    prep_done = {}
    pu_prev = {0: None, 1: None}
    dq_prev = {0: None, 1: None}
    tp_prev = {0: None, 1: None}
    for n, (k, i) in enumerate(prep_units):
        par = n % 2
        srcM, mode, W = SRC[k]
        nblk = W // 128
        if mode == "row":
            ld = S.dma("gdma", [tp_prev[par]], wbf[par][:, 0:W], srcM[i * 128:(i + 1) * 128, :])
        else:
            ld = S.dma("gdma", [tp_prev[par]],
                       wbf[par][:, 0:W].rearrange("p (dt m) -> p dt m", dt=nblk),
                       srcM[:, i * 128:(i + 1) * 128].rearrange("(dt p) m -> p dt m", p=128))
        dq = ld
        dq_prev[par] = dq
        tp = S.step("sync", [dq, pu_prev[par]],
                    lambda e, par=par, nblk=nblk: [e.dma_start_transpose(
                        wtp[par][:, dt * 128:(dt + 1) * 128],
                        wbf[par][:, dt * 128:(dt + 1) * 128]) for dt in range(nblk)],
                    weight=16 * nblk)
        tp_prev[par] = tp
        stc = S.dma("sdma", [tp], SCRS[k][i * 128:(i + 1) * 128, 0:W], wtp[par][:, 0:W])
        pu_prev[par] = stc
        prep_done[(k, i)] = stc

    last_out = []
    pv = {}
    a_out = {}

    for sl in range(NSL):
        base = sl * SLAB
        st_norm = [None] * NT
        st_tr = [None] * NT
        st_ag = [None] * NT
        st_xscr = [None] * NT
        prev_n32 = pv.get("n32_last")
        for i in range(NT):
            dep_war = prev_n32 if i >= 1 else pv.get("out_last1")
            ld = S.dma("sync", [dep_war], xst[i % 2][:],
                       d["x"][base + i * 128: base + (i + 1) * 128, :])
            bs = S.step("vector", [ld, st_ag[i - 1] if i >= 1 else pv.get("ag_last")],
                        lambda e, i=i: [e.bn_stats(stats[:, 6 * kk:6 * kk + 6],
                                                   xst[i % 2][:, 512 * kk:512 * (kk + 1)]) for kk in range(4)][-1])
            st_ag[i] = S.step("vector", [bs], lambda e, i=i: e.bn_aggr(mv[i % 2][:], stats[:]))
            ln1 = S.step("scalar", [st_ag[i], st_norm[i - 2] if i >= 2 else pv.get("norm_last"), c_ones],
                         lambda e, i=i: e.activation(rstd[i % 2][:], mv[i % 2][:, 1:2], AF.Ln, bias=eps_c[:, 0:1]))
            ln2 = S.step("scalar", [ln1], lambda e, i=i: e.activation(rstd[i % 2][:], rstd[i % 2][:], AF.Exp, scale=-0.5))
            n32 = S.step("vector", [ln2, pv.get(("lo", i - 1)) if i >= 1 else pv.get("lo_last")],
                         lambda e, i=i: e.tensor_scalar(xn32[:], xst[i % 2][:], mv[i % 2][:, 0:1],
                                                        rstd[i % 2][:, 0:1], ALU.subtract, ALU.mult))
            st_norm[i] = S.step("vector", [n32, st_tr[i - 2] if i >= 2 else pv.get("tr_last")],
                                lambda e, i=i: e.tensor_copy(xn[i % 2][:], xn32[:]))
            lo = S.step("vector", [st_norm[i], pv.get(("trlo", i - 1)) if i >= 1 else pv.get("trlo_last")],
                        lambda e, i=i: e.tensor_tensor(xlo[:], xn32[:], xn[i % 2][:], ALU.subtract))
            pv[("lo", i)] = lo
            prev_n32 = n32
            deps_tr = [st_norm[i]] + ([pv.get("xnT_free")] if i < 2 else [])
            prev_xscr = pv.get("xscr_list")
            if prev_xscr:
                deps_tr.append(prev_xscr[i])
            st_tr[i] = S.step("sync", deps_tr,
                              lambda e, i=i: [e.dma_start_transpose(
                                  A_xnT(dt)[:, i * 128:(i + 1) * 128],
                                  xn[i % 2][:, dt * 128:(dt + 1) * 128]) for dt in range(16)],
                              weight=256)
            st_xscr[i] = S.dma("sdma", [st_tr[i]],
                               xnT_scr.rearrange("p (dt t) -> p dt t", dt=16)[:, :, base + i * 128: base + (i + 1) * 128],
                               AB2[:, 0:16 * SLAB].rearrange("p (dt t) -> p dt t", dt=16)[:, :, i * 128:(i + 1) * 128])
            trlo = S.step("sync", [lo, pv.get("xlscr_last")],
                          lambda e: [e.dma_start_transpose(
                              xloT[:, dt * 128:(dt + 1) * 128],
                              xlo[:, dt * 128:(dt + 1) * 128]) for dt in range(16)],
                          weight=256)
            pv[("trlo", i)] = trlo
            pv["xlscr_last"] = S.dma("sdma", [trlo],
                                     xnT_lo_scr.rearrange("p (dt t) -> p dt t", dt=16)[:, :, base + i * 128: base + (i + 1) * 128],
                                     xloT[:].rearrange("p (dt m) -> p dt m", dt=16))

        pv["n32_last"] = prev_n32
        st_g = {}
        wld_g = {}

        def ensure_su(sidx):
            if sidx in wld_g or sidx >= NSU:
                return
            par = sidx % 2
            depg = st_g.get(sidx - 2, pv.get("dlast"))
            wld_g[sidx] = S.dma("gdma", [prep_done[("su", sidx)], depg], A_wbuf(par),
                                scr_su[sidx * 128:(sidx + 1) * 128, :])
        ensure_su(0)
        ensure_su(1)
        prev_gev = pv.get("gev_last")
        prev_sq = pv.get("sq_last")
        hread_prev = pv.get("hread", {})
        agev_bank = {0: pv.get("agev_b0"), 1: pv.get("agev_b1")}
        h_ready = [None] * NSU
        for s_ in range(NSU):
            par = s_ % 2
            bk = s_ % 2
            deps = [wld_g[s_], agev_bank[bk]] + [st_tr[j] for j in range(NT)]
            if s_ == 0:
                deps.append(pv.get("radd_last"))
            gp = S.step("tensor", deps,
                        lambda e, par=par, bk=bk: [e.matmul(ps_a[bk][:], A_wbuf(par)[:, dt * 128:(dt + 1) * 128],
                                                            A_xnT(dt)[:, 0:512],
                                                            start=(dt == 0), stop=(dt == 15))
                                                   for dt in range(16)][-1])
            st_g[s_] = gp
            gev = S.step("vector", [gp, prev_sq, hread_prev.get(s_), prev_gev],
                         lambda e, s_=s_, bk=bk: e.tensor_scalar_max(A_hsu(s_)[:, 0:512],
                                                                     ps_a[bk][:], 0.0))
            agev_bank[bk] = gev
            prev_gev = gev
            sq = S.step("scalar", [gev],
                        lambda e, s_=s_: e.activation(A_hsu(s_)[:, 0:SLAB], A_hsu(s_)[:, 0:SLAB], AF.Square))
            prev_sq = sq
            h_ready[s_] = sq
            ensure_su(s_ + 2)

        st_d = {}
        wldd = {}
        st_radd = {}
        st_out_l = {}

        def ensure_sd(j, ht):
            if (j, ht) in wldd or ht >= NSU or j >= NT // 2:
                return
            par = (j * NSU + ht) % 2
            dep = (st_d.get((j, ht - 2)) if ht >= 2 else
                   (st_d.get((j - 1, NSU - 2 + ht)) or st_g.get(NSU - 2 + ht) or pv.get("dlast")))
            wldd[(j, ht)] = S.dma("gdma", [prep_done[("sd", ht)], dep], A_wbuf(par),
                                  scr_sd[ht * 128:(ht + 1) * 128, :])
        ensure_sd(0, 0)
        ensure_sd(0, 1)
        for j in range(NT // 2):
            po = None
            for ht in range(NSU):
                par = (j * NSU + ht) % 2
                deps = [wldd[(j, ht)], h_ready[ht]]
                if ht == 0:
                    deps += [st_radd.get(j - 1, pv.get("radd_last")), prev_gev, prev_sq]
                def dp_fn(e, j=j, par=par, ht=ht, first=(ht == 0), last=(ht == NSU - 1)):
                    out = None
                    for tt in range(2):
                        pset = ps_d0 if tt == 0 else ps_d1
                        for c in range(4):
                            out = e.matmul(pset[c][:],
                                           A_hsu(ht)[:, (2 * j + tt) * 128:(2 * j + tt + 1) * 128],
                                           A_wbuf(par)[:, c * 512:(c + 1) * 512],
                                           start=first, stop=last)
                    return out
                po = S.step("tensor", deps, dp_fn)
                st_d[(j, ht)] = po
                if ht + 2 < NSU:
                    ensure_sd(j, ht + 2)
                else:
                    ensure_sd(j + 1, ht + 2 - NSU)
            for tt in range(2):
                i = 2 * j + tt
                x2 = S.dma("sync", [pv.get("n32_last"),
                                    st_out_l.get(i - 1, pv.get("out_last1"))],
                           xst[i % 2][:], d["x"][base + i * 128: base + (i + 1) * 128, :])
                pset = ps_d0 if tt == 0 else ps_d1
                ra = S.step("vector", [po, x2],
                            lambda e, i=i, pset=pset: [
                                e.tensor_tensor(xst[i % 2][:, c * 512:(c + 1) * 512], pset[c][:],
                                                xst[i % 2][:, c * 512:(c + 1) * 512], ALU.add)
                                for c in range(4)][-1])
                st_radd[j] = ra
                so = S.dma("sync", [ra], d["out"][base + i * 128: base + (i + 1) * 128, :], xst[i % 2][:])
                st_out_l[i] = so
                a_out[base // 128 + i] = so
                last_out.append(so)

        pv = dict(
            norm_last=st_norm[NT - 1], tr_last=st_tr[NT - 1], ag_last=st_ag[NT - 1],
            n32_last=prev_n32,
            gev_last=prev_gev, sq_last=prev_sq, radd_last=st_radd[NT // 2 - 1],
            agev_b0=agev_bank[0], agev_b1=agev_bank[1],
            out_last2=st_out_l[NT - 2], out_last1=st_out_l[NT - 1],
            xnT_free=st_g[NSU - 1], dlast=st_d[(NT // 2 - 1, NSU - 1)],
            xscr_last=st_xscr[NT - 1], xscr_list=st_xscr,
            lo_last=pv.get(("lo", NT - 1)), trlo_last=pv.get(("trlo", NT - 1)),
            xlscr_last=pv.get("xlscr_last"),
            hread={s_: st_d[(NT // 2 - 1, s_)] for s_ in range(NSU)},
        )

    a_radd_last = pv["radd_last"]
    a_dlast = pv["dlast"]
    a_xnT_free = pv["xnT_free"]
    a_gev = pv["gev_last"]
    a_sq = pv["sq_last"]

    b_fence = S.dma_fence("sdma")
    gld = S.dma("gdma", [pv.get("out_last1")], gw32[0:32, :], d["gate_w"])
    gcv = S.step("vector", [gld, tp_prev[0]], lambda e: e.tensor_copy(wbf[0][0:32, :], gw32[0:32, :]))
    gcl = S.step("vector", [gcv, tp_prev[1]],
                 lambda e: e.tensor_tensor(wbf[1][0:32, :], gw32[0:32, :], wbf[0][0:32, :], ALU.subtract))
    gtp = S.step("sync", [gcl, tp_prev[0], tp_prev[1], a_xnT_free] + b_fence,
                 lambda e: [e.dma_start_transpose(AB2[:, 8192 + dt * 32:8192 + dt * 32 + 32],
                                                  wbf[0][0:32, dt * 128:(dt + 1) * 128]) for dt in range(16)] +
                           [e.dma_start_transpose(AB2[:, 8832 + dt * 32:8832 + dt * 32 + 32],
                                                  wbf[1][0:32, dt * 128:(dt + 1) * 128]) for dt in range(16)],
                 weight=512)

    st_b_out = {}
    bpv = {}
    for ch in range(NCH):
        cbase = ch * CH
        wfc1 = {}

        def ensure_fc1(lt):
            if lt in wfc1 or lt >= NLT:
                return
            par = lt % 2
            dep = bpv.get(("fp", lt - 2)) or (bpv.get("fp_last") if ch > 0 else a_dlast)
            wfc1[lt] = S.dma("gdma", [prep_done[("fc1", lt)], dep], B_fc1buf(par),
                             scr_fc1[lt * 128:(lt + 1) * 128, :])
        ensure_fc1(0)
        ensure_fc1(1)
        prev_xl_ev = bpv.get("xl_ev")
        prev_logit_ev = bpv.get("sg_last")
        prev_rt = bpv.get("rt_last")
        st_xnld = {}
        rt_done = {}
        fp = None
        slab_readers = [bpv.get("xn_free0"), bpv.get("xn_free1")]
        for cc in range(CC):
            st_xnld[cc] = S.dma("sync", [gtp if (ch == 0 and cc == 0) else None]
                                + (b_fence if (ch == 0 and cc == 0) else [])
                                + slab_readers,
                                AB2[:, 0:8192].rearrange("p (dt t) -> p dt t", dt=16),
                                xnT_scr.rearrange("p (dt t) -> p dt t", dt=16)[:, :, cbase + cc * 512: cbase + (cc + 1) * 512])
            lo_deps = [pv["xlscr_last"] if (ch == 0 and cc == 0) else None] + slab_readers + \
                      ([bpv.get("hread_all")] if True else [])
            st_xnlo = S.dma("sync", lo_deps,
                            AB1[:, 24576:24576 + 8192].rearrange("p (dt t) -> p dt t", dt=16),
                            xnT_lo_scr.rearrange("p (dt t) -> p dt t", dt=16)[:, :, cbase + cc * 512: cbase + (cc + 1) * 512])
            for lt in range(NLT):
                par = lt % 2
                deps = [st_xnld[cc], wfc1[lt], prev_xl_ev]
                if ch == 0 and cc == 0 and lt == 0:
                    deps += [a_radd_last, a_gev, a_sq]
                fp = S.step("tensor", deps,
                            lambda e, par=par, cc=cc: [e.matmul(ps_h[cc % len(ps_h)][:],
                                                                B_fc1buf(par)[:, dt * 128:(dt + 1) * 128],
                                                                B_xnTtc(dt), start=(dt == 0), stop=(dt == 15))
                                                       for dt in range(16)][-1])
                bpv[("fp", lt)] = fp
                if cc == CC - 1:
                    ensure_fc1(lt + 2)
                ev = S.step("scalar", [fp, prev_logit_ev],
                            lambda e, lt=lt, cc=cc: e.copy(B_xlat(lt)[:, cc * 512:(cc + 1) * 512],
                                                           ps_h[cc % len(ps_h)][:]))
                prev_xl_ev = ev
            for w in range(4):
                i = cc * 4 + w
                lg = S.step("tensor", [st_xnld[cc], st_xnlo, gtp, prev_logit_ev, c_gbm],
                            lambda e, w=w: [e.matmul(ps_l[:, 0:32],
                                                     (B_xnTtc(dt), B_xnTtc(dt), B_xnlo(dt))[z][:, w * 128:(w + 1) * 128],
                                                     (B_gwT(dt), B_gwTlo(dt), B_gwT(dt))[z],
                                                     start=(dt == 0 and z == 0), stop=(dt == 15 and z == 2))
                                            for dt in range(16) for z in range(3)][-1])
                sg = S.step("scalar", [lg, prev_rt, c_gbB],
                            lambda e: e.activation(scores[:], ps_l[:, 0:32], AF.Sigmoid))
                prev_logit_ev = sg
                fns = [
                    lambda e: e.tensor_tensor(sbv[:], scores[:], gbB[:], ALU.add),
                    lambda e: [e.reduce_max(rcol[:, g:g + 1], sbv[:, 8 * g:8 * g + 8], axis=AX.X) for g in range(4)][-1],
                    lambda e: [e.tensor_scalar(rt2[:, 8 * g:8 * g + 8], sbv[:, 8 * g:8 * g + 8],
                                               rcol[:, g:g + 1], None, ALU.is_equal) for g in range(4)][-1],
                    lambda e: e.scalar_tensor_tensor(rt1[:], rt2[:], -BIG, sbv[:], ALU.mult, ALU.add),
                    lambda e: [e.reduce_max(gs[:, g:g + 1], rt1[:, 8 * g:8 * g + 8], axis=AX.X) for g in range(4)][-1],
                    lambda e: e.tensor_tensor(gs[:], gs[:], rcol[:, 0:4], ALU.add),
                    lambda e: e.reduce_max(rcol[:, 0:1], gs[:], axis=AX.X),
                    lambda e: e.tensor_scalar(rt2[:, 0:4], gs[:], rcol[:, 0:1], None, ALU.is_equal),
                    lambda e: e.scalar_tensor_tensor(rt1[:, 0:4], rt2[:, 0:4], -BIG, gs[:], ALU.mult, ALU.add),
                    lambda e: e.reduce_max(rcol[:, 1:2], rt1[:, 0:4], axis=AX.X),
                    lambda e: e.tensor_scalar(rcol[:, 0:4], gs[:], rcol[:, 1:2], None, ALU.is_ge),
                    lambda e: [e.tensor_scalar(rt2[:, 8 * g:8 * g + 8], ones8[:], rcol[:, g:g + 1],
                                               None, ALU.mult) for g in range(4)][-1],
                    lambda e: e.tensor_copy(rt1[:], negB[:]),
                    lambda e: e.tensor_copy(rt2i[:], rt2[:]),
                    lambda e: e.copy_predicated(rt1[:], rt2i[:], sbv[:]),
                    lambda e: e.memset(sel[:], 0.0),
                ]
                for _ in range(8):
                    fns += [
                        lambda e: e.reduce_max(rcol[:, 0:1], rt1[:], axis=AX.X),
                        lambda e: e.tensor_scalar(rt2[:], rt1[:], rcol[:, 0:1], None, ALU.is_equal),
                        lambda e: e.scalar_tensor_tensor(rt1[:], rt2[:], -BIG, rt1[:], ALU.mult, ALU.add),
                        lambda e: e.tensor_tensor(sel[:], sel[:], rt2[:], ALU.add),
                    ]
                fns += [
                    lambda e: e.tensor_tensor(rt2[:], sel[:], scores[:], ALU.mult),
                    lambda e: e.reduce_sum(rcol[:, 0:1], rt2[:], axis=AX.X),
                    lambda e: e.reciprocal(rcol[:, 1:2], rcol[:, 0:1]),
                    lambda e: e.tensor_scalar(rcol[:, 1:2], rcol[:, 1:2], 2.5, None, ALU.mult),
                    lambda e: e.tensor_scalar(rt2[:], rt2[:], rcol[:, 1:2], None, ALU.mult),
                ]
                prev = None
                for fn in fns:
                    prev = S.step("vector", [sg, c_gbB, c_ones, prev_rt] if prev is None else [prev], fn)
                rt = prev
                sqr = S.step("scalar", [rt], lambda e: e.activation(cwpad[:, 0:32], rt2[:], AF.Sqrt))
                prev_rt = sqr
                tr = S.step("sync", [sqr, bpv.get("cw_free")] +
                            (b_fence if ch == 0 else []),
                            lambda e: e.dma_start_transpose(B_cwT, cwpad[:]), weight=16)
                cwst = S.dma("sdma", [tr], cw_scr[:, cbase + i * 128: cbase + (i + 1) * 128],
                             B_cwT[0:32, 0:128])
                bpv["cw_free"] = cwst
                rt_done[i] = cwst
            slab_readers = [fp, lg]
            bpv["hread_all"] = None
        bpv["fp_last"] = fp
        bpv["xl_ev"] = prev_xl_ev
        bpv["sg_last"] = prev_logit_ev
        bpv["rt_last"] = prev_rt
        bpv["xn_free0"] = fp
        bpv["xn_free1"] = lg

        wld1 = {}
        wld2 = {}

        def ensure_w(kind, eidx, it):
            if eidx >= E:
                return
            dic = wld1 if kind == "w1" else wld2
            if (eidx, it) in dic:
                return
            n = eidx * NLT + it
            par = n % 2
            buf = B_w1(par) if kind == "w1" else B_w2(par)
            scr = scr_w1 if kind == "w1" else scr_w2
            dic[(eidx, it)] = S.dma("gdma", [prep_done[(kind, n)],
                                             bpv.get(("wread", kind, par)), bpv.get("fp_last")],
                                    buf, scr[n * 128:(n + 1) * 128, :])
        for it0 in range(2):
            ensure_w("w1", 0, it0)
            ensure_w("w2", 0, it0)
        prev_hev = bpv.get("hev")
        prev_sqh = bpv.get("sqh")
        prev_yadd = bpv.get("yadd")
        cwld_prev = bpv.get("cwld")
        prev_bv = bpv.get("bv_last")
        hev_bank = {0: bpv.get("hev_b0"), 1: bpv.get("hev_b1")}
        ya_bank = {0: bpv.get("ya_b0"), 1: bpv.get("ya_b1")}
        cw_steps = {}

        def ensure_cw(eidx):
            if eidx in cw_steps or eidx >= E:
                return
            nonlocal cwld_prev, prev_bv
            half = eidx % 2
            cwld = S.dma("gdma", [rt_done[CT - 1] if eidx == 0 else None,
                                  bpv.get(("cwrow_read", half)), cwld_prev],
                         cwrow[0:1, half * CH:(half + 1) * CH],
                         cw_scr[eidx:eidx + 1, cbase:cbase + CH])
            cwld_prev = cwld
            bm = S.step("tensor", [cwld, prev_bv],
                        lambda e, half=half: e.matmul(ps_bcast[:], onesb[0:1, :],
                                                      cwrow[0:1, half * CH:half * CH + 512],
                                                      start=True, stop=True))
            bpv[("cwrow_read", half)] = bm
            bv = S.step("scalar", [bm, bpv.get(("cwB_read", half))],
                        lambda e, half=half: e.copy(cwB[:, half * 512:(half + 1) * 512], ps_bcast[:]))
            prev_bv = bv
            cw_steps[eidx] = bv
        ensure_cw(0)
        ensure_cw(1)
        for eidx in range(E):
            half = eidx % 2
            cwB_ev = cw_steps[eidx]
            for it in range(NLT):
                bk = it % 2
                deps = [wld1[(eidx, it)], bpv["xl_ev"], hev_bank[bk]]
                def h_fn(e, par=(eidx * NLT + it) % 2, bk=bk):
                    out = None
                    for lk in range(NLT):
                        out = e.matmul(ps_h[bk][:], B_w1(par)[:, lk * 128:(lk + 1) * 128],
                                       B_xlat(lk)[:, 0:512],
                                       start=(lk == 0), stop=(lk == NLT - 1))
                    return out
                hp = S.step("tensor", deps, h_fn)
                bpv[("wread", "w1", (eidx * NLT + it) % 2)] = hp
                if it + 2 < NLT:
                    ensure_w("w1", eidx, it + 2)
                else:
                    ensure_w("w1", eidx + 1, it + 2 - NLT)
                hev = S.step("vector", [hp, cwB_ev, prev_sqh],
                             lambda e, it=it, bk=bk, half=half: e.scalar_tensor_tensor(
                                 B_h(it)[:, 0:512], ps_h[bk][:], 0.0,
                                 cwB[:, half * 512:(half + 1) * 512], ALU.max, ALU.mult))
                hev_bank[bk] = hev
                prev_hev = hev
                sqh = S.step("scalar", [hev],
                             lambda e, it=it: e.activation(B_h(it)[:, 0:CH], B_h(it)[:, 0:CH], AF.Square))
                prev_sqh = sqh
                bpv[("hready", it)] = sqh
            bpv[("cwB_read", half)] = prev_hev
            ensure_cw(eidx + 1)
            ensure_cw(eidx + 2)
            for lt in range(NLT):
                bk = lt % 2
                deps = [wld2[(eidx, lt)], ya_bank[bk]] + [bpv[("hready", k2)] for k2 in range(NLT)]
                def y_fn(e, par=(eidx * NLT + lt) % 2, bk=bk):
                    out = None
                    for it in range(NLT):
                        out = e.matmul(ps_y[bk][:], B_w2(par)[:, it * 128:(it + 1) * 128],
                                       B_h(it)[:, 0:512],
                                       start=(it == 0), stop=(it == NLT - 1))
                    return out
                yp = S.step("tensor", deps, y_fn)
                bpv[("wread", "w2", (eidx * NLT + lt) % 2)] = yp
                if lt + 2 < NLT:
                    ensure_w("w2", eidx, lt + 2)
                else:
                    ensure_w("w2", eidx + 1, lt + 2 - NLT)
                if eidx == 0:
                    ya = S.step("vector", [yp, bpv.get("ysb_read"), prev_yadd],
                                lambda e, lt=lt, bk=bk: e.tensor_copy(
                                    YSB[:, lt * CH:lt * CH + 512], ps_y[bk][:]))
                else:
                    ya = S.step("vector", [yp, prev_yadd],
                                lambda e, lt=lt, bk=bk: e.tensor_tensor(
                                    YSB[:, lt * CH:lt * CH + 512], ps_y[bk][:],
                                    YSB[:, lt * CH:lt * CH + 512], ALU.add))
                ya_bank[bk] = ya
                prev_yadd = ya
                bpv["hev"] = prev_hev
        bpv["sqh"] = prev_sqh
        bpv["cwld"] = cwld_prev
        bpv["bv_last"] = prev_bv
        bpv["hev_b0"] = hev_bank[0]
        bpv["hev_b1"] = hev_bank[1]
        bpv["ya_b0"] = ya_bank[0]
        bpv["ya_b1"] = ya_bank[1]

        ybf = {}
        for lt in range(NLT):
            ybf[lt] = S.step("vector", [prev_yadd, prev_sqh, bpv.get(("hread2", lt))],
                             lambda e, lt=lt: e.tensor_copy(B_ybf(lt)[:, 0:CH], YSB[:, lt * CH:(lt + 1) * CH]))
        if debug_stop:
            return last_out
        wfc2 = {}
        for lt in range(NLT):
            wfc2[lt] = S.dma("gdma", [prep_done[("fc2", lt)],
                                      bpv[("wread", "w1", 0)], bpv[("wread", "w1", 1)],
                                      bpv[("wread", "w2", 0)], bpv[("wread", "w2", 1)],
                                      bpv.get(("fc2read", lt))],
                             B_fc2(lt), scr_fc2[lt * 128:(lt + 1) * 128, :])
        prev_po = None
        prev_racc = bpv.get("racc")
        for i in range(CT):
            gi = cbase // 128 + i
            acc = S.dma("sync", [a_out[gi], gcl if (ch == 0 and i == 0) else None,
                                 st_b_out.get(gi - 1, bpv.get("b_out_m1") or pv.get("out_last1"))],
                        xst[i % 2][:], d["out"][gi * 128:(gi + 1) * 128, :])
            deps = [ybf[lt] for lt in range(NLT)] + list(wfc2.values()) + \
                   [prev_racc, bpv.get("rt_last")]
            def o_fn(e, i=i):
                out = None
                for lt in range(NLT):
                    for c in range(4):
                        out = e.matmul(ps_d0[c][:], B_ybf(lt)[:, i * 128:(i + 1) * 128],
                                       B_fc2(lt)[:, c * 512:(c + 1) * 512],
                                       start=(lt == 0), stop=(lt == NLT - 1))
                return out
            po = S.step("tensor", deps, o_fn)
            prev_po = po
            ra = S.step("vector", [po, acc],
                        lambda e, i=i: [e.tensor_tensor(xst[i % 2][:, c * 512:(c + 1) * 512], ps_d0[c][:],
                                                        xst[i % 2][:, c * 512:(c + 1) * 512], ALU.add)
                                        for c in range(4)][-1])
            prev_racc = ra
            so = S.dma("sync", [ra], d["out"][gi * 128:(gi + 1) * 128, :], xst[i % 2][:])
            st_b_out[gi] = so
            last_out.append(so)
        bpv["racc"] = prev_racc
        bpv["yadd"] = prev_po
        bpv["ysb_read"] = ybf[NLT - 1]
        for lt in range(NLT):
            bpv[("hread2", lt)] = prev_po
            bpv[("fc2read", lt)] = prev_po
        bpv["b_out_m2"] = st_b_out.get(cbase // 128 + CT - 2)
        bpv["b_out_m1"] = st_b_out.get(cbase // 128 + CT - 1)
        bpv["xl_ev"] = None
        bpv["fp_last"] = prev_po
        prev_xl_ev = None

    return last_out


AR_F32 = 13312
AR_BF16 = 77056
AR_I32 = 32


def build_fused(T, NB, SLAB=512, CH=512, debug_mid=False):
    NTOK = NB * T
    nc = bass.Bass("TRN2", target_bir_lowering=False)
    ins = {}

    def din(name, shape, dt=F32):
        ins[name] = (shape, dt)
        return nc.dram_tensor(name, shape, dt, kind="ExternalInput").ap()

    def dint(name, shape, dt=BF16):
        return nc.dram_tensor(name, shape, dt, kind="Internal").ap()

    x_in = din("x", [NTOK, 2048])
    out = nc.dram_tensor("out", [NTOK, 2048], F32, kind="ExternalOutput").ap()
    mid_kind = "ExternalOutput" if debug_mid else "Internal"
    x1 = nc.dram_tensor("x1", [NTOK, 2048], F32, kind=mid_kind).ap()
    x2 = nc.dram_tensor("x2", [NTOK, 2048], F32, kind=mid_kind).ap()
    x3 = nc.dram_tensor("x3", [NTOK, 2048], F32, kind=mid_kind).ap()
    ctx_scr = dint("ctx_scr", [128, 16 * T])

    d1 = dict(x=x_in, out=x1, ctx_scr=ctx_scr,
              wq=din("l0a_wq", [2048, 2048]), bq=din("l0a_bq", [2048]),
              wk=din("l0a_wk", [2048, 512]), bk=din("l0a_bk", [512]),
              wv=din("l0a_wv", [2048, 512]), bv=din("l0a_bv", [512]),
              wo=din("l0a_wo", [2048, 2048]), bo=din("l0a_bo", [2048]),
              ct=din("l0a_ct", [128, T]), st=din("l0a_st", [128, T]),
              p1t=din("l0a_p1t", [128, 128]), p2t=din("l0a_p2t", [128, 128]),
              dmask=din("l0a_dmask", [128, 2048]))
    d2 = dict(x=x1, out=x2,
              wq_gate=din("l0f_wq_gate", [5632, 2048], I32),
              wq_up=din("l0f_wq_up", [5632, 2048], I32),
              wq_down=din("l0f_wq_down", [2048, 5632], I32),
              s_gate_t=din("l0f_s_gate_t", [128, 44]), s_up_t=din("l0f_s_up_t", [128, 44]),
              s_down_t=din("l0f_s_down_t", [128, 16]),
              scr_g=dint("scr_g", [5632, 2048]), scr_u=dint("scr_u", [5632, 2048]),
              scr_d=dint("scr_d", [5632, 2048]))
    d3 = dict(x=x2, out=x3, ctx_scr=ctx_scr,
              w_dkv=din("l1a_w_dkv", [2048, 512]), w_dq=din("l1a_w_dq", [2048, 512]),
              w_uk=din("l1a_w_uk", [512, 2048]), w_uv=din("l1a_w_uv", [512, 2048]),
              w_uq=din("l1a_w_uq", [512, 2048]), w_kr=din("l1a_w_kr", [2048, 64]),
              w_qr=din("l1a_w_qr", [512, 1024]), w_o=din("l1a_w_o", [2048, 2048]),
              b_o=din("l1a_b_o", [2048]),
              ct=din("l1a_ct", [128, T]), st=din("l1a_st", [128, T]),
              p1t=din("l1a_p1t", [128, 128]), p2t=din("l1a_p2t", [128, 128]))
    d4 = dict(x=x3, out=out,
              gate_w=din("l1f_gate_w", [32, 2048]), gate_bias=din("l1f_gate_bias", [32]),
              fc1_w=din("l1f_fc1_w", [1024, 2048]), fc2_w=din("l1f_fc2_w", [2048, 1024]),
              w1f=din("l1f_w1f", [32768, 1024]), w2f=din("l1f_w2f", [32768, 1024]),
              shared_up_w=din("l1f_shared_up_w", [4096, 2048]),
              shared_down_w=din("l1f_shared_down_w", [2048, 4096]),
              scr_su=dint("scr_su", [4096, 2048]), scr_sd=dint("scr_sd", [4096, 2048]),
              scr_fc1=dint("scr_fc1", [1024, 2048]), scr_fc2=dint("scr_fc2", [1024, 2048]),
              scr_w1=dint("scr_w1", [32768, 1024]), scr_w2=dint("scr_w2", [32768, 1024]),
              xnT_scr=dint("xnT_scr", [128, 16 * NTOK]),
              xnT_lo_scr=dint("xnT_lo_scr", [128, 16 * NTOK]),
              cw_scr=dint("cw_scr", [32, NTOK]))

    S = Sched(nc)
    S.use_arenas(AR_BF16, AR_F32, AR_I32)
    S.sbuf_reset()
    o1 = emit_l0_attn(S, T, NB, d1)
    S.barrier(o1)
    S.sbuf_reset()
    o2 = emit_l0_ffn(S, NTOK, d2, SLAB=SLAB)
    S.barrier(o2)
    S.sbuf_reset()
    o3 = emit_l1_attn(S, T, NB, d3)
    S.barrier(o3)
    S.sbuf_reset()
    o4 = emit_l1_ffn(S, NTOK, d4, SLAB=SLAB, CH=CH)
    S.finish_interleaved(final_waits=o4)
    return nc, ins


def make_inputs(T, weights):
    C0, S20, P1T0, P2T0 = rope_consts_l0(T)
    C1, S21, P1T1, P2T1 = rope_consts_l1(T)
    w = weights
    return dict(
        l0a_wq=w["L0_attn_Wq"], l0a_bq=w["L0_attn_bq"], l0a_wk=w["L0_attn_Wk"],
        l0a_bk=w["L0_attn_bk"], l0a_wv=w["L0_attn_Wv"], l0a_bv=w["L0_attn_bv"],
        l0a_wo=w["L0_attn_Wo"], l0a_bo=w["L0_attn_bo"],
        l0a_ct=C0, l0a_st=S20, l0a_p1t=P1T0, l0a_p2t=P2T0, l0a_dmask=causal_masks(512),
        l0f_wq_gate=w["L0_ffn_wq_gate"], l0f_wq_up=w["L0_ffn_wq_up"],
        l0f_wq_down=w["L0_ffn_wq_down"],
        l0f_s_gate_t=np.ascontiguousarray(w["L0_ffn_s_gate"].reshape(44, 128).T),
        l0f_s_up_t=np.ascontiguousarray(w["L0_ffn_s_up"].reshape(44, 128).T),
        l0f_s_down_t=np.ascontiguousarray(w["L0_ffn_s_down"].reshape(16, 128).T),
        l1a_w_dkv=w["L1_attn_W_DKV"], l1a_w_dq=w["L1_attn_W_DQ"], l1a_w_uk=w["L1_attn_W_UK"],
        l1a_w_uv=w["L1_attn_W_UV"], l1a_w_uq=w["L1_attn_W_UQ"], l1a_w_kr=w["L1_attn_W_KR"],
        l1a_w_qr=w["L1_attn_W_QR"], l1a_w_o=w["L1_attn_W_O"], l1a_b_o=w["L1_attn_b_O"],
        l1a_ct=C1, l1a_st=S21, l1a_p1t=P1T1, l1a_p2t=P2T1,
        l1f_gate_w=w["L1_ffn_gate_w"], l1f_gate_bias=w["L1_ffn_gate_bias"],
        l1f_fc1_w=w["L1_ffn_fc1_w"], l1f_fc2_w=w["L1_ffn_fc2_w"],
        l1f_w1f=np.ascontiguousarray(w["L1_ffn_w1"].reshape(32768, 1024)),
        l1f_w2f=np.ascontiguousarray(w["L1_ffn_w2"].reshape(32768, 1024)),
        l1f_shared_up_w=w["L1_ffn_shared_up_w"], l1f_shared_down_w=w["L1_ffn_shared_down_w"],
    )


from concourse.bass_utils import run_bass_kernel_spmd

NCORES = 8
TT = 2048
DD = 2048


def kernel(L0_attn_x, **kw):
    kw["L0_attn_x"] = L0_attn_x
    x = np.ascontiguousarray(kw["L0_attn_x"].astype(np.float32).reshape(16, TT, DD))
    xs = [np.ascontiguousarray(x[2 * c:2 * c + 2].reshape(2 * TT, DD)) for c in range(NCORES)]
    nc, ins_spec = build_fused(TT, 2, SLAB=512, CH=512)
    shared = make_inputs(TT, kw)
    in_maps = []
    for c in range(NCORES):
        m = dict(shared)
        m["x"] = xs[c]
        in_maps.append(m)
    res = run_bass_kernel_spmd(nc, in_maps, core_ids=list(range(NCORES)))
    out = np.concatenate([res.results[c]["out"].reshape(2, TT, DD) for c in range(NCORES)], axis=0)
    return out.astype(np.float32)


# version 05
# speedup vs baseline: 1.1440x; 1.1213x over previous
from contextlib import ExitStack
import numpy as np
import concourse.bass as bass
import concourse.mybir as mybir

F32 = mybir.dt.float32
F32R = mybir.dt.float32r
BF16 = mybir.dt.bfloat16
I32 = mybir.dt.int32
AF = mybir.ActivationFunctionType
ALU = mybir.AluOpType
AX = mybir.AxisListType

STREAMS = {
    "tensor": ("tensor", False),
    "vector": ("vector", False),
    "scalar": ("scalar", False),
    "gpsimd": ("gpsimd", False),
    "sync": ("sync", True),
    "gdma": ("gpsimd", True),
    "sdma": ("scalar", True),
}
SHARED_QUEUE = {"gpsimd": "gp", "gdma": "gp", "scalar": "sc", "sdma": "sc"}


class Step:
    __slots__ = ("stream", "count_after", "sid", "slot", "target", "epoch")

    def __init__(self, stream, count_after, sid, slot=None, target=0, epoch=0):
        self.stream = stream
        self.count_after = count_after
        self.sid = sid
        self.slot = slot
        self.target = target
        self.epoch = epoch


class Sched:
    DMA_POOL = 6

    def __init__(self, nc):
        self.nc = nc
        self.ctx = ExitStack()
        self.steps = []
        self._semcount = 0
        self._sid = 0
        self.epoch = -1
        self.esems = []
        self.edma_sems = []
        self.ecounts = []
        self.eslot_counts = []
        self._new_epoch()

    def _new_epoch(self):
        self.epoch += 1
        self.esems.append({k: self.sem(f"e{self.epoch}_{k}") for k in STREAMS if not STREAMS[k][1]})
        self.edma_sems.append({k: [self.sem(f"e{self.epoch}_{k}{j}") for j in range(self.DMA_POOL)]
                               for k in STREAMS if STREAMS[k][1]})
        self.ecounts.append({k: 0 for k in STREAMS})
        self.eslot_counts.append({k: [0] * self.DMA_POOL for k in STREAMS if STREAMS[k][1]})
        self.dma_rr = {k: 0 for k in STREAMS if STREAMS[k][1]}

    def sem(self, name):
        self._semcount += 1
        return self.ctx.enter_context(self.nc.semaphore(f"{name}_{self._semcount}"))

    def use_arenas(self, bf16_cols, f32_cols, i32_cols=0):
        self._arena = {}
        for dt, cols, nm in ((BF16, bf16_cols, "bf16"), (F32, f32_cols, "f32"), (I32, i32_cols, "i32")):
            if cols:
                t = self.ctx.enter_context(self.nc.sbuf_tensor(f"arena_{nm}", [128, cols], dt))
                self._arena[dt] = [t, cols, 0]

    def sbuf_reset(self):
        for v in self._arena.values():
            v[2] = 0

    def sbuf(self, name, shape, dtype, **kw):
        ar = getattr(self, "_arena", None)
        if ar and dtype in ar:
            t, cap, off = ar[dtype]
            bpe = 2 if dtype == BF16 else 4
            align = 64 // bpe
            off = (off + align - 1) // align * align
            need = shape[1]
            assert off + need <= cap, f"arena {dtype} overflow at {name}: {off}+{need}>{cap}"
            ar[dtype][2] = off + need
            return t[0:shape[0], off:off + need]
        return self.ctx.enter_context(self.nc.sbuf_tensor(name, shape, dtype, **kw))

    def barrier(self, dep_steps):
        self._pending_bar = {k: list(dep_steps) for k in STREAMS}
        self._new_epoch()

    def psum(self, name, shape, dtype=F32):
        return self.ctx.enter_context(self.nc.psum_tensor(name, shape, dtype))

    def global_psums(self):
        if not hasattr(self, "_gps"):
            self._gps = {k: self.psum("gps_" + k, [128, 512]) for k in
                         ["q", "p1", "p2", "sc0", "sc1", "ctx", "bc", "d"]}
            self._gps["rs"] = self._gps["bc"]
        return self._gps

    def psum_open(self, name, shape, dtype=F32):
        cm = self.nc.psum_tensor(name, shape, dtype)
        h = cm.__enter__()
        return h, (lambda: cm.__exit__(None, None, None))

    def step(self, stream, deps, fn, weight=None):
        if weight is None:
            weight = 16 if STREAMS[stream][1] else 1
        deps = [d for d in deps if d is not None]
        pb = getattr(self, "_pending_bar", None)
        if pb and pb.get(stream):
            deps = deps + pb.pop(stream)
        self._sid += 1
        ep = self.epoch
        if STREAMS[stream][1]:
            slot = self.dma_rr[stream]
            self.dma_rr[stream] = (slot + 1) % self.DMA_POOL
            self.eslot_counts[ep][stream][slot] += weight
            st = Step(stream, 0, self._sid, slot,
                      self.eslot_counts[ep][stream][slot], ep)
        else:
            self.ecounts[ep][stream] += weight
            st = Step(stream, self.ecounts[ep][stream], self._sid, epoch=ep)
        self.steps.append((stream, deps, fn, weight, st))
        return st

    def dma_fence(self, stream):
        ep = self.epoch
        return [Step(stream, 0, 0, slot=j, target=self.eslot_counts[ep][stream][j], epoch=ep)
                for j in range(self.DMA_POOL) if self.eslot_counts[ep][stream][j] > 0]

    def dma(self, stream, deps, out, in_, **kw):
        return self.step(stream, deps, lambda e: e.dma_start(out=out, in_=in_, **kw), 16)

    def dmaT(self, stream, deps, out, in_):
        return self.step(stream, deps, lambda e: e.dma_start_transpose(out, in_), 16)

    def finish(self, final_waits=()):
        for k in STREAMS:
            self.sems[k] = self.sem("st_" + k)
        waited = {k: {j: 0 for j in STREAMS} for k in STREAMS}
        by_stream = {k: [] for k in STREAMS}
        for stream, deps, fn, weight, st in self.steps:
            by_stream[stream].append((deps, fn, weight, st))

        def run_stream(stream, eng):
            w = waited[stream]
            for deps, fn, weight, st in by_stream[stream]:
                for dp in deps:
                    if dp.stream == stream:
                        if dp.count_after > w[stream]:
                            eng.wait_ge(self.sems[stream], dp.count_after)
                            w[stream] = dp.count_after
                    else:
                        if dp.count_after > w[dp.stream]:
                            eng.wait_ge(self.sems[dp.stream], dp.count_after)
                            w[dp.stream] = dp.count_after
                ins = fn(eng)
                assert ins is not None, f"step fn returned None (stream {stream})"
                ins.then_inc(self.sems[stream], weight)

        with self.nc.Block() as block:
            @block.sync
            def _(e):
                run_stream("sync", e)

            @block.scalar
            def _(e):
                run_stream("scalar", e)
                run_stream("sdma", e)

            @block.vector
            def _(e):
                run_stream("vector", e)

            @block.tensor
            def _(e):
                run_stream("tensor", e)

            @block.gpsimd
            def _(e):
                run_stream("gpsimd", e)
                run_stream("gdma", e)
                for st in final_waits:
                    e.wait_ge(self.sems[st.stream], st.count_after)

    def finish_interleaved(self, final_waits=()):
        waited = {}
        for k in set(list(STREAMS) + list(SHARED_QUEUE.values())):
            waited[k] = {}
            waited[k, "dma"] = {}

        def run(eng, streams_here):
            wkey = SHARED_QUEUE.get(streams_here[0], streams_here[0])
            w = waited[wkey]
            wd = waited[wkey, "dma"]
            own = set(streams_here)
            for stream, deps, fn, weight, st in self.steps:
                if stream not in own:
                    continue
                for dp in deps:
                    if STREAMS[dp.stream][1]:
                        key = (dp.epoch, dp.stream, dp.slot)
                        if dp.target > wd.get(key, 0):
                            eng.wait_ge(self.edma_sems[dp.epoch][dp.stream][dp.slot], dp.target)
                            wd[key] = dp.target
                    else:
                        key = (dp.epoch, dp.stream)
                        if dp.count_after > w.get(key, 0):
                            eng.wait_ge(self.esems[dp.epoch][dp.stream], dp.count_after)
                            w[key] = dp.count_after
                if STREAMS[stream][1]:
                    prev_target = st.target - weight
                    key = (st.epoch, stream, st.slot)
                    if prev_target > 0 and prev_target > wd.get(key, 0):
                        eng.wait_ge(self.edma_sems[st.epoch][stream][st.slot], prev_target)
                        wd[key] = prev_target
                ins = fn(eng)
                assert ins is not None
                if STREAMS[stream][1]:
                    if not isinstance(ins, (list, tuple)):
                        ins = [ins]
                    assert weight == 16 * len(ins), f"dma weight {weight} != 16*{len(ins)}"
                    for one in ins:
                        one.then_inc(self.edma_sems[st.epoch][stream][st.slot], 16)
                else:
                    if isinstance(ins, (list, tuple)):
                        ins = ins[-1]
                    ins.then_inc(self.esems[st.epoch][stream], weight)

        with self.nc.Block() as block:
            @block.sync
            def _(e):
                run(e, ["sync"])

            @block.scalar
            def _(e):
                run(e, ["scalar", "sdma"])

            @block.vector
            def _(e):
                run(e, ["vector"])

            @block.tensor
            def _(e):
                run(e, ["tensor"])

            @block.gpsimd
            def _(e):
                run(e, ["gpsimd", "gdma"])
                done = {}
                for st in final_waits:
                    key = (st.epoch, st.stream, st.slot)
                    tgt = st.target if STREAMS[st.stream][1] else st.count_after
                    if done.get(key, 0) >= tgt:
                        continue
                    done[key] = tgt
                    if STREAMS[st.stream][1]:
                        e.wait_ge(self.edma_sems[st.epoch][st.stream][st.slot], tgt)
                    else:
                        e.wait_ge(self.esems[st.epoch][st.stream], st.count_after)

        self.ctx.close()



def rope_consts_l0(T):
    f = np.arange(64, dtype=np.float64)
    freqs = 10000.0 ** (-2.0 * f / 2048.0)
    t = np.arange(T, dtype=np.float64)
    ang = np.outer(t, freqs)
    C = np.zeros((128, T), np.float32)
    S2 = np.zeros((128, T), np.float32)
    for i in range(64):
        C[2 * i] = np.cos(ang[:, i])
        C[2 * i + 1] = np.sin(ang[:, i])
        S2[2 * i] = -np.sin(ang[:, i])
        S2[2 * i + 1] = np.cos(ang[:, i])
    P1T = np.zeros((128, 128), np.float32)
    P2T = np.zeros((128, 128), np.float32)
    for i in range(64):
        P1T[2 * i, 2 * i] = 1.0
        P1T[2 * i, 2 * i + 1] = 1.0
        P2T[2 * i + 1, 2 * i] = 1.0
        P2T[2 * i + 1, 2 * i + 1] = 1.0
    return C, S2, P1T, P2T


def rope_consts_l1(T):
    half = 32
    inv_freq = 1.0 / 10000.0 ** (np.arange(half, dtype=np.float64) / half)
    t = np.arange(T, dtype=np.float64)
    fr = np.outer(t, inv_freq)
    C = np.zeros((128, T), np.float32)
    S2 = np.zeros((128, T), np.float32)
    P1T = np.zeros((128, 128), np.float32)
    P2T = np.zeros((128, 128), np.float32)
    for blk in range(2):
        b = blk * 64
        for i in range(half):
            C[b + 2 * i] = np.cos(fr[:, i])
            C[b + 2 * i + 1] = np.sin(fr[:, i])
            S2[b + 2 * i] = -np.sin(fr[:, i])
            S2[b + 2 * i + 1] = np.cos(fr[:, i])
            P1T[b + i, b + 2 * i] = 1.0
            P1T[b + i, b + 2 * i + 1] = 1.0
            P2T[b + i + 32, b + 2 * i] = 1.0
            P2T[b + i + 32, b + 2 * i + 1] = 1.0
    return C, S2, P1T, P2T


def causal_masks(TCH=512):
    m = np.zeros((128, 4 * TCH), np.float32)
    for r in range(4):
        for sl in range(128):
            lo = r * 128 + sl
            if lo < TCH:
                m[sl, r * TCH + lo:(r + 1) * TCH] = 1.0
    return m


NHEAD = 16
NKV = 4
GRP = 4
D = 2048
SCALE_OUT = 2048.0 ** -0.5


def emit_l0_attn(S, T, NB, d):
    NT = T // 128
    TC = T // 512
    sb = S.sbuf

    xst0 = sb("l0a_xst0", [128, D], F32)
    xst = [xst0, xst0]
    xn = [sb(f"l0a_xn{i}", [128, D], BF16) for i in range(2)]
    stats = sb("l0a_stats", [128, 24], F32)
    mv = [sb(f"l0a_mv{i}", [128, 2], F32) for i in range(2)]
    rstd = [sb(f"l0a_rstd{i}", [128, 1], F32) for i in range(2)]
    bigA = sb("l0a_bigA", [128, 16 * 2048], BF16)
    ct = sb("l0a_ct", [128, T], F32)
    st = sb("l0a_st", [128, T], F32)
    p1t = sb("l0a_p1t", [128, 128], F32)
    p2t = sb("l0a_p2t", [128, 128], F32)
    qf32 = sb("l0a_qf32", [128, 512], F32)
    dmask = sb("l0a_dmask", [128, 4 * 512], BF16)
    w_sl = [sb(f"l0a_w{i}", [128, D], BF16) for i in range(4)]
    bqo_row = sb("l0a_bqo", [1, D], F32)
    bk_row = sb("l0a_bk", [1, 512], F32)
    bv_row = sb("l0a_bv", [1, 512], F32)
    ones_r = sb("l0a_ones_r", [1, 512], F32)
    ones_c = sb("l0a_ones_c", [128, 1], BF16)
    eps_c = sb("l0a_eps", [128, 1], F32)
    qT = sb("l0a_qT", [128, T], BF16)
    kT = sb("l0a_kT", [128, T], BF16)
    v_tm = sb("l0a_vtm", [128, NT * 512], BF16)
    tmp1 = sb("l0a_tmp1", [128, 512], F32)
    tmp2 = sb("l0a_tmp2", [128, 512], F32)
    expT = sb("l0a_expT", [128, NT * 512], BF16)
    recip = sb("l0a_recip", [1, 512], F32)
    recipB = sb("l0a_recipB", [128, 512], F32)
    ctx_sb = [sb(f"l0a_ctxsb{i}", [128, 512], BF16) for i in range(2)]
    ctxg = [sb(f"l0a_ctxg{i}", [128, 2048], BF16) for i in range(2)]

    scr = d["ctx_scr"]

    c_tab = S.dma("gdma", [], ct[:], d["ct"])
    S.dma("gdma", [], st[:], d["st"])
    S.dma("gdma", [], p1t[:], d["p1t"])
    S.dma("gdma", [], p2t[:], d["p2t"])
    c_msk = S.dma("gdma", [], dmask[:], d["dmask"])
    c_bk = S.dma("gdma", [], bk_row[:], d["bk"].rearrange("(a m) -> a m", a=1))
    c_bv = S.dma("gdma", [], bv_row[:], d["bv"].rearrange("(a m) -> a m", a=1))
    c_ones = S.step("vector", [], lambda e: (e.memset(ones_r[:], 1.0), e.memset(ones_c[:], 1.0),
                                             e.memset(eps_c[:], 1e-5))[-1])

    last_out = []
    pv = {}

    for b in range(NB):
        c_bq = S.dma("gdma", [pv.get("po_last")], bqo_row[:], d["bq"].rearrange("(a m) -> a m", a=1))

        G = S.global_psums()
        ps_q, ps_p1, ps_p2 = G["q"], G["p1"], G["p2"]
        ps_ctx, ps_bc = G["ctx"], G["bc"]
        ps_rs = ps_bc
        ps_sc = [G["sc0"], G["sc1"]]
        ps_out4 = [G["sc0"], G["sc1"], G["ctx"], G["bc"]]

        st_norm = [None] * NT
        st_tr = [None] * NT
        st_ag = [None] * NT
        for i in range(NT):
            dep_war = st_norm[i - 1] if i >= 1 else pv.get("out_last1")
            ld = S.dma("sync", [dep_war], xst[i % 2][:],
                       d["x"][b * T + i * 128: b * T + (i + 1) * 128, :])
            bs = S.step("vector", [ld, st_ag[i - 1] if i >= 1 else pv.get("ag_last")],
                        lambda e, i=i: [e.bn_stats(stats[:, 6 * k:6 * k + 6],
                                                   xst[i % 2][:, 512 * k:512 * (k + 1)]) for k in range(4)][-1])
            st_ag[i] = S.step("vector", [bs], lambda e, i=i: e.bn_aggr(mv[i % 2][:], stats[:]))
            ln1 = S.step("scalar", [st_ag[i], st_norm[i - 2] if i >= 2 else pv.get("norm_last"), c_ones],
                         lambda e, i=i: e.activation(rstd[i % 2][:], mv[i % 2][:, 1:2], AF.Ln, bias=eps_c[:, 0:1]))
            ln2 = S.step("scalar", [ln1], lambda e, i=i: e.activation(rstd[i % 2][:], rstd[i % 2][:], AF.Exp, scale=-0.5))
            st_norm[i] = S.step("vector", [ln2, st_tr[i - 2] if i >= 2 else pv.get("tr_last")],
                                lambda e, i=i: e.tensor_scalar(xn[i % 2][:], xst[i % 2][:], mv[i % 2][:, 0:1],
                                                               rstd[i % 2][:, 0:1], ALU.subtract, ALU.mult))
            deps_tr = [st_norm[i]] + ([pv.get("po_last")] if (b > 0 and i == 0) else [])
            st_tr[i] = S.step("sync", deps_tr,
                              lambda e, i=i: [e.dma_start_transpose(
                                  bigA[:, dt * T + i * 128: dt * T + (i + 1) * 128],
                                  xn[i % 2][:, dt * 128:(dt + 1) * 128]) for dt in range(16)],
                              weight=16 * 16)

        units = [("v", m) for m in range(4)]
        for h in range(NHEAD):
            if h % GRP == 0:
                units.append(("k", h // GRP))
            units.append(("q", h))

        def bufof(kind, idx):
            return w_sl[(2 + idx % 2) if kind != "q" else (idx % 2)]

        wsrc_of = {"q": d["wq"], "k": d["wk"], "v": d["wv"]}
        wld = {}
        last_proj_of = {}

        def ensure_wld(ui):
            if ui >= len(units) or units[ui] in wld:
                return
            kind, idx = units[ui]
            buf = bufof(kind, idx)
            wsrc = wsrc_of[kind][:, idx * 128:(idx + 1) * 128]
            key = ("q", idx % 2) if kind == "q" else ("kv", idx % 2)
            dep = last_proj_of.get(key, pv.get("proj_last") if b > 0 else None)
            wld[(kind, idx)] = S.dma("gdma", [dep],
                                     buf[:].rearrange("p (dt m) -> p dt m", dt=16),
                                     wsrc.rearrange("(dt p) m -> p dt m", p=128))
        ensure_wld(0)
        ensure_wld(1)

        st_ropeB = {}
        prev_pm = pv.get("pm_last")
        prev_ropeA = None
        prev_ropeB = None
        last_reader_of = {}
        prev_stt = pv.get("stt_last")
        prev_rec = pv.get("rec_last")
        prev_bcp = pv.get("bcp_last")
        prev_ctx_dma = {0: pv.get("ctxd_m2"), 1: pv.get("ctxd_m1")}
        st_ctx_dma = []
        psq_prev = pv.get("psq_free")
        exp_slot_free = pv.get("avs_last")
        sc_bank_prev = [pv.get("avs_last"), pv.get("avs_last")]
        proj_last = None
        vT_done = {}
        prev_vtr = pv.get("vtr_last")

        for ui, (kind, idx) in enumerate(units):
            ensure_wld(ui)
            wbuf = bufof(kind, idx)
            if kind == "q":
                brow, boff = bqo_row, idx * 128
                bdep = c_bq
            elif kind == "k":
                brow, boff = bk_row, idx * 128
                bdep = c_bk
            else:
                brow, boff = bv_row, idx * 128
                bdep = c_bv
            dstT = qT if kind == "q" else kT
            for tc in range(TC):
                deps = [wld[(kind, idx)], psq_prev, bdep, c_ones] + [st_tr[min(4 * tc + j, NT - 1)] for j in range(4)]
                if b > 0 and ui == 0 and tc == 0:
                    deps.append(pv.get("radd_last"))
                def pr_fn(e, wbuf=wbuf, brow=brow, boff=boff, tc=tc):
                    e.matmul(ps_q[:], brow[0:1, boff:boff + 128], ones_r[0:1, :], start=True, stop=False)
                    last = None
                    for dt in range(16):
                        last = e.matmul(ps_q[:], wbuf[:, dt * 128:(dt + 1) * 128],
                                        bigA[:, dt * T + tc * 512: dt * T + tc * 512 + 512],
                                        start=False, stop=(dt == 15))
                    return last
                stp = S.step("tensor", deps, pr_fn)
                proj_last = stp
                if tc == TC - 1:
                    last_proj_of[("q", idx % 2) if kind == "q" else ("kv", idx % 2)] = stp
                    ensure_wld(ui + 1)
                    ensure_wld(ui + 2)
                if kind == "v":
                    ev_war = vT_done.get(idx - 1, pv.get("avs_last"))
                    ev = S.step("scalar", [stp, ev_war] if ev_war else [stp],
                                lambda e, dstT=dstT, tc=tc: e.copy(dstT[:, tc * 512:(tc + 1) * 512], ps_q[:]))
                    psq_prev = ev
                    continue
                ev = S.step("scalar", [stp] + ([prev_pm] if prev_pm else []),
                            lambda e: e.copy(qf32[:], ps_q[:]))
                psq_prev = ev
                pm = S.step("tensor", [ev, prev_ropeA],
                            lambda e: (
                                e.matmul(ps_p1[:], p1t[:], qf32[:], start=True, stop=True),
                                e.matmul(ps_p2[:], p2t[:], qf32[:], start=True, stop=True))[1])
                prev_pm = pm
                ra = S.step("vector", [pm, prev_ropeB, c_tab],
                            lambda e, tc=tc: (
                                e.tensor_tensor(tmp1[:], ps_p1[:], ct[:, tc * 512:(tc + 1) * 512], ALU.mult),
                                e.tensor_tensor(tmp2[:], ps_p2[:], st[:, tc * 512:(tc + 1) * 512], ALU.mult))[1])
                prev_ropeA = ra
                if kind == "k":
                    rb_war = last_reader_of.get(("k", idx - 1), vT_done.get(3))
                else:
                    rb_war = last_reader_of.get(("q", idx - 1), pv.get("avs_last"))
                rb_ = S.step("vector", [ra] + ([rb_war] if rb_war else []),
                             lambda e, dstT=dstT, tc=tc: e.tensor_tensor(
                                 dstT[:, tc * 512:(tc + 1) * 512], tmp1[:], tmp2[:], ALU.add))
                prev_ropeB = rb_
                st_ropeB[(kind, idx, tc)] = rb_

            if kind == "v":
                war_vtm = pv.get("avs_last") if (b > 0 and idx == 0) else None
                vt = S.step("sync", [psq_prev] + ([war_vtm] if war_vtm else []),
                            lambda e, m=idx: [e.dma_start_transpose(
                                v_tm[:, i * 512 + m * 128: i * 512 + (m + 1) * 128],
                                kT[:, i * 128:(i + 1) * 128]) for i in range(NT)],
                            weight=16 * NT)
                vT_done[idx] = vt
                continue
            if kind == "k":
                continue

            h = idx
            g = h // GRP
            for tc in range(TC):
                smax = min(4 * (tc + 1), NT)
                rdy = [None] * smax
                for s_ in range(smax):
                    deps = [st_ropeB[("q", h, tc)], st_ropeB[("k", g, s_ // 4)],
                            sc_bank_prev[s_ % 2]]
                    scs = S.step("tensor", deps,
                                 lambda e, s_=s_, tc=tc: e.matmul(
                                     ps_sc[s_ % 2][:], kT[:, s_ * 128:(s_ + 1) * 128],
                                     qT[:, tc * 512:(tc + 1) * 512], start=True, stop=True))
                    ex = S.step("scalar", [scs, exp_slot_free],
                                lambda e, s_=s_: e.activation(expT[:, s_ * 512:(s_ + 1) * 512],
                                                              ps_sc[s_ % 2][:], AF.Exp))
                    sc_bank_prev[s_ % 2] = ex
                    if s_ >= 4 * tc:
                        r = s_ - 4 * tc
                        rdy[s_] = S.step("vector", [ex, c_msk],
                                         lambda e, s_=s_, r=r: e.tensor_tensor(
                                             expT[:, s_ * 512:(s_ + 1) * 512], expT[:, s_ * 512:(s_ + 1) * 512],
                                             dmask[:, r * 512:(r + 1) * 512], ALU.mult))
                    else:
                        rdy[s_] = ex
                avs = None
                for s_ in range(smax):
                    deps = [rdy[s_], vT_done[g]]
                    if s_ == 0:
                        deps += [prev_rec, prev_stt, prev_bcp, c_ones]
                    def av_fn(e, s_=s_, g=g, first=(s_ == 0), last=(s_ == smax - 1)):
                        e.matmul(ps_rs[0:1, :], ones_c[:], expT[:, s_ * 512:(s_ + 1) * 512],
                                 start=first, stop=last)
                        return e.matmul(ps_ctx[:], v_tm[:, s_ * 512 + g * 128: s_ * 512 + g * 128 + 128],
                                        expT[:, s_ * 512:(s_ + 1) * 512], start=first, stop=last)
                    avs = S.step("tensor", deps, av_fn)
                exp_slot_free = avs
                last_reader_of[("q", h)] = avs
                if h % GRP == GRP - 1:
                    last_reader_of[("k", g)] = avs
                rec = S.step("vector", [avs], lambda e: e.reciprocal(recip[:], ps_rs[0:1, :]))
                prev_rec = rec
                bcm = S.step("tensor", [rec, prev_bcp],
                             lambda e: e.matmul(ps_bc[:], ones_r[0:1, 0:128], recip[0:1, :],
                                                start=True, stop=True))
                rbc = S.step("scalar", [bcm, prev_stt],
                             lambda e: e.copy(recipB[:], ps_bc[:]))
                prev_bcp = rbc
                j = (h * TC + tc) % 2
                stt = S.step("vector", [rbc, avs, prev_ctx_dma[j]],
                             lambda e, j=j: e.scalar_tensor_tensor(
                                 ctx_sb[j][:], ps_ctx[:], SCALE_OUT, recipB[:], ALU.mult, ALU.mult))
                prev_stt = stt
                cd = S.dma("sdma", [stt], scr[:, h * T + tc * 512: h * T + (tc + 1) * 512], ctx_sb[j][:])
                prev_ctx_dma[j] = cd
                st_ctx_dma.append(cd)

        xnT_free = psq_prev
        wo_lds = []
        for dt in range(16):
            wo_lds.append(S.dma("gdma", [xnT_free] if dt == 0 else [],
                                bigA[:, dt * 2048:(dt + 1) * 2048],
                                d["wo"][dt * 128:(dt + 1) * 128, :]))
        c_bo = S.dma("gdma", [proj_last], bqo_row[:], d["bo"].rearrange("(a m) -> a m", a=1))
        st_radd = [None] * NT
        st_out = [None] * NT
        st_cg = [None] * NT
        st_po = [None] * NT
        po_last = None
        scr3 = scr.rearrange("p (dt t) -> p dt t", dt=16)
        for i in range(NT):
            cg_deps = (S.dma_fence("sdma") if i == 0 else []) + \
                      [st_po[i - 2] if i >= 2 else pv.get("po_last")]
            st_cg[i] = S.dma("sync", cg_deps,
                             ctxg[i % 2][:].rearrange("p (dt t) -> p dt t", dt=16),
                             scr3[:, :, i * 128:(i + 1) * 128])
            x2 = S.dma("sync", [st_out[i - 1] if i >= 1 else pv.get("out_last1"),
                                st_norm[NT - 1] if i < 2 else None],
                       xst[i % 2][:], d["x"][b * T + i * 128: b * T + (i + 1) * 128, :])
            deps = wo_lds + [st_cg[i], c_bo, st_radd[i - 1] if i >= 1 else pv.get("radd_last")]
            if i == 0:
                deps.append(prev_stt)
            def po_fn(e, i=i):
                for c in range(4):
                    e.matmul(ps_out4[c][:], ones_r[0:1, 0:128],
                             bqo_row[0:1, c * 512:(c + 1) * 512], start=True, stop=False)
                last = None
                for dt in range(16):
                    for c in range(4):
                        last = e.matmul(ps_out4[c][:],
                                        ctxg[i % 2][:, dt * 128:(dt + 1) * 128],
                                        bigA[:, dt * 2048 + c * 512: dt * 2048 + (c + 1) * 512],
                                        start=False, stop=(dt == 15))
                return last
            po = S.step("tensor", deps, po_fn)
            po_last = po
            st_po[i] = po_last
            ra = S.step("vector", [po_last, x2],
                        lambda e, i=i: [e.tensor_tensor(xst[i % 2][:, c * 512:(c + 1) * 512],
                                                        ps_out4[c][:], xst[i % 2][:, c * 512:(c + 1) * 512],
                                                        ALU.add) for c in range(4)][-1])
            st_radd[i] = ra
            st_out[i] = S.dma("sync", [ra], d["out"][b * T + i * 128: b * T + (i + 1) * 128, :], xst[i % 2][:])
            last_out.append(st_out[i])

        pv = dict(norm_last=st_norm[NT - 1], tr_last=st_tr[NT - 1], ag_last=st_ag[NT - 1],
                  po_last=po_last, psq_free=psq_prev, radd_last=st_radd[NT - 1],
                  stt_last=prev_stt, rec_last=prev_rec, avs_last=exp_slot_free,
                  proj_last=proj_last, bcp_last=prev_bcp, pm_last=prev_pm,
                  out_last2=st_out[NT - 2] if NT >= 2 else None,
                  out_last1=st_out[NT - 1], ctxd_m2=prev_ctx_dma[0], ctxd_m1=prev_ctx_dma[1],
                  vtr_last=vT_done.get(3))

    return last_out


D = 2048
HID = 5632
NH = HID // 128


def emit_l0_ffn(S, NTOK, d, SLAB=1024):
    NSL = NTOK // SLAB
    NT = SLAB // 128
    TC = SLAB // 512
    sb = S.sbuf
    G = S.global_psums()
    assert TC == 1
    ps_g = [G["q"], G["p1"]]
    ps_u = [G["p2"], G["sc0"]]
    ps_d0 = [G["sc1"], G["ctx"], G["bc"], G["d"]]
    ps_d1 = [G["q"], G["p1"], G["p2"], G["sc0"]]

    xst = [sb(f"l0f_xst{i}", [128, D], F32) for i in range(2)]
    xn = [sb(f"l0f_xn{i}", [128, D], BF16) for i in range(2)]
    stats = sb("l0f_stats", [128, 24], F32)
    mv = [sb(f"l0f_mv{i}", [128, 2], F32) for i in range(2)]
    rstd = [sb(f"l0f_rstd{i}", [128, 1], F32) for i in range(2)]
    eps_c = sb("l0f_eps", [128, 1], F32)
    xnT = sb("l0f_xnT", [128, 16 * SLAB], BF16)
    wbf = [sb(f"l0f_wbf{i}", [128, D], BF16) for i in range(2)]
    sgu = sb("l0f_sgu", [128, 2 * NH], F32)
    sd_row = sb("l0f_sdrow", [1, D], F32)
    sdB = sb("l0f_sdB", [128, D], F32)
    rtmp = sb("l0f_rtmp", [128, 2048], F32)
    wg = [sb(f"l0f_wg{i}", [128, D], BF16) for i in range(2)]
    wu = [sb(f"l0f_wu{i}", [128, D], BF16) for i in range(2)]
    wd = [sb(f"l0f_wd{i}", [128, D], BF16) for i in range(2)]
    g_sb = sb("l0f_gsb", [128, 1024], BF16)
    hT = sb("l0f_hT", [128, NH * SLAB], BF16)

    scr_g, scr_u, scr_d = d["scr_g"], d["scr_u"], d["scr_d"]

    c_eps = S.step("vector", [], lambda e: e.memset(eps_c[:], 1e-5))
    ones_rr = sb("l0f_ones_rr", [1, 128], F32)
    c_o2 = S.step("vector", [c_eps], lambda e: e.memset(ones_rr[:], 1.0))
    c_sg = S.dma("gdma", [], sgu[:, 0:NH], d["s_gate_t"])
    c_su = S.dma("gdma", [], sgu[:, NH:2 * NH], d["s_up_t"])
    c_sd = S.dma("gdma", [], sd_row[:], d["s_down_row"])
    sdb_prev = None
    for c in range(4):
        mm = S.step("tensor", [c_sd, c_o2, sdb_prev],
                    lambda e, c=c: e.matmul(G["bc"][:], ones_rr[0:1, :],
                                            sd_row[0:1, c * 512:(c + 1) * 512], start=True, stop=True))
        sdb_prev = S.step("scalar", [mm], lambda e, c=c: e.copy(sdB[:, c * 512:(c + 1) * 512], G["bc"][:]))
    c_sdb = sdb_prev

    prep_units = [(k, ht) for k in ("g", "u") for ht in range(NH)] + [("d", ht) for ht in range(NH)]
    prep_done = {}
    pu_prev = {0: None, 1: None}
    dq_prev = {0: None, 1: None}
    for n, (k, ht) in enumerate(prep_units):
        par = n % 2
        srcM = {"g": d["wq_gate_t"], "u": d["wq_up_t"], "d": d["wq_down_t"]}[k]
        scr = {"g": scr_g, "u": scr_u, "d": scr_d}[k]
        ld = S.dma("gdma", [pu_prev[par]], wbf[par][:], srcM[ht * 128:(ht + 1) * 128, :])
        dq = S.step("vector", [ld, dq_prev[par]],
                    lambda e, par=par: e.tensor_scalar(wbf[par][:], wbf[par][:], -128.0, None, ALU.add))
        dq_prev[par] = dq
        stc = S.dma("sdma", [dq], scr[ht * 128:(ht + 1) * 128, :], wbf[par][:])
        pu_prev[par] = stc
        prep_done[(k, ht)] = stc

    last_out = []
    pv = {}

    for sl in range(NSL):
        base = sl * SLAB
        st_norm = [None] * NT
        st_tr = [None] * NT
        st_ag = [None] * NT
        for i in range(NT):
            dep_war = st_norm[i - 2] if i >= 2 else (pv.get("out_last2") if i == 0 else pv.get("out_last1"))
            ld = S.dma("sync", [dep_war], xst[i % 2][:],
                       d["x"][base + i * 128: base + (i + 1) * 128, :])
            bs = S.step("vector", [ld, st_ag[i - 1] if i >= 1 else pv.get("ag_last")],
                        lambda e, i=i: [e.bn_stats(stats[:, 6 * kk:6 * kk + 6],
                                                   xst[i % 2][:, 512 * kk:512 * (kk + 1)]) for kk in range(4)][-1])
            st_ag[i] = S.step("vector", [bs], lambda e, i=i: e.bn_aggr(mv[i % 2][:], stats[:]))
            ln1 = S.step("scalar", [st_ag[i], st_norm[i - 2] if i >= 2 else pv.get("norm_last"), c_eps],
                         lambda e, i=i: e.activation(rstd[i % 2][:], mv[i % 2][:, 1:2], AF.Ln, bias=eps_c[:, 0:1]))
            ln2 = S.step("scalar", [ln1], lambda e, i=i: e.activation(rstd[i % 2][:], rstd[i % 2][:], AF.Exp, scale=-0.5))
            st_norm[i] = S.step("vector", [ln2, st_tr[i - 2] if i >= 2 else pv.get("tr_last")],
                                lambda e, i=i: e.tensor_scalar(xn[i % 2][:], xst[i % 2][:], mv[i % 2][:, 0:1],
                                                               rstd[i % 2][:, 0:1], ALU.subtract, ALU.mult))
            deps_tr = [st_norm[i]] + ([pv.get("xnT_free")] if i < 2 else [])
            st_tr[i] = S.step("sync", deps_tr,
                              lambda e, i=i: [e.dma_start_transpose(
                                  xnT[:, dt * SLAB + i * 128: dt * SLAB + (i + 1) * 128],
                                  xn[i % 2][:, dt * 128:(dt + 1) * 128]) for dt in range(16)],
                              weight=256)

        st_g = {}
        st_u = {}
        wld_g = {}
        wld_u = {}
        hread_prev = pv.get("hread", {})

        def ensure_gu(ht):
            if ht in wld_g or ht >= NH:
                return
            par = ht % 2
            depg = st_g.get(ht - 2, pv.get("xnT_free"))
            depu = st_u.get(ht - 2, pv.get("xnT_free"))
            wld_g[ht] = S.dma("gdma", [prep_done[("g", ht)], depg], wg[par][:],
                              scr_g[ht * 128:(ht + 1) * 128, :])
            wld_u[ht] = S.dma("gdma", [prep_done[("u", ht)], depu], wu[par][:],
                              scr_u[ht * 128:(ht + 1) * 128, :])
        ensure_gu(0)
        ensure_gu(1)

        prev_gev = pv.get("gev_last")
        prev_hvv = pv.get("hvv_last")
        gev_bank = {0: pv.get("gev_b0"), 1: pv.get("gev_b1")}
        hv_bank = {0: pv.get("hv_b0"), 1: pv.get("hv_b1")}
        h_ready = [None] * NH
        for ht in range(NH):
            par = ht % 2
            bk = ht % 2
            deps = [wld_g[ht], gev_bank[bk]] + [st_tr[j] for j in range(NT)]
            if ht == 0:
                deps.append(pv.get("radd_last"))
            gp = S.step("tensor", deps,
                        lambda e, par=par, bk=bk: [e.matmul(ps_g[bk][:], wg[par][:, dt * 128:(dt + 1) * 128],
                                                            xnT[:, dt * SLAB: dt * SLAB + 512],
                                                            start=(dt == 0), stop=(dt == 15))
                                                   for dt in range(16)][-1])
            st_g[ht] = gp
            gev = S.step("scalar", [gp, hv_bank[bk], prev_gev, c_sg],
                         lambda e, bk=bk, ht=ht: e.activation(g_sb[:, bk * 512:(bk + 1) * 512], ps_g[bk][:],
                                                              AF.Silu, scale=sgu[:, ht:ht + 1]))
            gev_bank[bk] = gev
            prev_gev = gev
            up = S.step("tensor", [wld_u[ht], hv_bank[bk]],
                        lambda e, par=par, bk=bk: [e.matmul(ps_u[bk][:], wu[par][:, dt * 128:(dt + 1) * 128],
                                                            xnT[:, dt * SLAB: dt * SLAB + 512],
                                                            start=(dt == 0), stop=(dt == 15))
                                                   for dt in range(16)][-1])
            st_u[ht] = up
            hv = S.step("vector", [up, gev, hread_prev.get(ht), prev_hvv, c_su],
                        lambda e, ht=ht, bk=bk: e.scalar_tensor_tensor(
                            hT[:, ht * SLAB: ht * SLAB + 512], ps_u[bk][:], sgu[:, NH + ht:NH + ht + 1],
                            g_sb[:, bk * 512:(bk + 1) * 512], ALU.mult, ALU.mult))
            hv_bank[bk] = hv
            prev_hvv = hv
            h_ready[ht] = hv
            ensure_gu(ht + 2)

        st_d = {}
        wldd = {}
        st_radd = {}
        st_out_l = {}
        prev_ra = pv.get("radd_last")

        def ensure_d(j, ht):
            if (j, ht) in wldd or ht >= NH:
                return
            if j >= NT // 2:
                return
            par = (j * NH + ht) % 2
            dep = st_d.get((j, ht - 2)) if ht >= 2 else (st_d.get((j - 1, NH - 2 + ht)) or pv.get("dlast"))
            wldd[(j, ht)] = S.dma("gdma", [prep_done[("d", ht)], dep], wd[par][:],
                                  scr_d[ht * 128:(ht + 1) * 128, :])
        ensure_d(0, 0)
        ensure_d(0, 1)
        for j in range(NT // 2):
            po = None
            for ht in range(NH):
                par = (j * NH + ht) % 2
                deps = [wldd[(j, ht)], h_ready[ht]]
                if ht == 0:
                    deps += [st_radd.get(j - 1, pv.get("radd_last")), h_ready[NH - 1], prev_gev]
                def dp_fn(e, j=j, ht=ht, par=par, first=(ht == 0), last=(ht == NH - 1)):
                    out = None
                    for tt in range(2):
                        pset = ps_d0 if tt == 0 else ps_d1
                        for c in range(4):
                            out = e.matmul(pset[c][:],
                                           hT[:, ht * SLAB + (2 * j + tt) * 128: ht * SLAB + (2 * j + tt + 1) * 128],
                                           wd[par][:, c * 512:(c + 1) * 512],
                                           start=first, stop=last)
                    return out
                po = S.step("tensor", deps, dp_fn)
                st_d[(j, ht)] = po
                if ht + 2 < NH:
                    ensure_d(j, ht + 2)
                else:
                    ensure_d(j + 1, ht + 2 - NH)
            for tt in range(2):
                i = 2 * j + tt
                x2 = S.dma("sync", [st_norm[NT - 1],
                                    st_out_l.get(i - 2, pv.get("out_last2") if tt == 0 else pv.get("out_last1"))],
                           xst[i % 2][:], d["x"][base + i * 128: base + (i + 1) * 128, :])
                pset = ps_d0 if tt == 0 else ps_d1
                ram = S.step("vector", [po, c_sdb, prev_ra],
                             lambda e, pset=pset: [
                                 e.tensor_tensor(rtmp[:, c * 512:(c + 1) * 512], pset[c][:],
                                                 sdB[:, c * 512:(c + 1) * 512], ALU.mult)
                                 for c in range(4)][-1])
                ra = S.step("vector", [ram, x2],
                            lambda e, i=i: [
                                e.tensor_tensor(xst[i % 2][:, c * 512:(c + 1) * 512],
                                                rtmp[:, c * 512:(c + 1) * 512],
                                                xst[i % 2][:, c * 512:(c + 1) * 512], ALU.add)
                                for c in range(4)][-1])
                st_radd[j] = ra
                prev_ra = ra
                so = S.dma("sync", [ra], d["out"][base + i * 128: base + (i + 1) * 128, :], xst[i % 2][:])
                st_out_l[i] = so
                last_out.append(so)

        pv = dict(
            norm_last=st_norm[NT - 1], tr_last=st_tr[NT - 1], ag_last=st_ag[NT - 1],
            gev_last=prev_gev, hvv_last=prev_hvv, radd_last=st_radd[NT // 2 - 1],
            gev_b0=gev_bank[0], gev_b1=gev_bank[1], hv_b0=hv_bank[0], hv_b1=hv_bank[1],
            out_last2=st_out_l[NT - 2], out_last1=st_out_l[NT - 1],
            xnT_free=st_u[NH - 1], dlast=st_d[(NT // 2 - 1, NH - 1)],
            hread={ht: st_d[(NT // 2 - 1, ht)] for ht in range(NH)},
        )

    return last_out


NHEAD = 16
D = 2048
DC = 512
DR = 64
SCL = 1.0 / np.sqrt(192.0)


def emit_l1_attn(S, T, NB, d):
    NT = T // 128
    TC = T // 512
    sb = S.sbuf
    G = S.global_psums()
    ps_q, ps_p1, ps_p2 = G["q"], G["p1"], G["p2"]
    ps_ctx, ps_bc, ps_v = G["ctx"], G["bc"], G["d"]
    ps_rs = ps_bc
    ps_sc = [G["sc0"], G["sc1"]]
    ps_out4 = [G["sc0"], G["sc1"], G["ctx"], G["d"]]

    xst = sb("l1a_xst", [128, D], F32)
    xn = sb("l1a_xn", [128, D], BF16)
    stats = sb("l1a_stats", [128, 24], F32)
    mv = [sb(f"l1a_mv{i}", [128, 2], F32) for i in range(2)]
    rstd = [sb(f"l1a_rstd{i}", [128, 1], F32) for i in range(2)]
    eps_c = sb("l1a_eps", [128, 1], F32)
    bigA = sb("l1a_bigA", [128, 16 * 2048], BF16)
    ct = sb("l1a_ct", [128, T], F32)
    st = sb("l1a_st", [128, T], F32)
    p1t = sb("l1a_p1t", [128, 128], F32)
    p2t = sb("l1a_p2t", [128, 128], F32)
    qf32 = sb("l1a_qf32", [128, 512], F32)
    w_sl = [sb(f"l1a_w{i}", [128, 2048], BF16) for i in range(2)]
    bo_row = sb("l1a_bo", [1, D], F32)
    ones_r = sb("l1a_ones_r", [1, 512], F32)
    ones_c = sb("l1a_ones_c", [128, 1], BF16)
    ckvT = sb("l1a_ckvT", [128, 4 * T], BF16)
    cqT = sb("l1a_cqT", [128, 4 * T], BF16)
    kRT = sb("l1a_kRT", [128, T], BF16)
    qRT = sb("l1a_qRT", [128, T], BF16)
    qT = sb("l1a_qT", [128, T], BF16)
    kT = sb("l1a_kT", [128, T], BF16)
    v_tm = sb("l1a_vtm", [128, NT * 128], BF16)
    tmp1 = sb("l1a_tmp1", [128, 512], F32)
    tmp2 = sb("l1a_tmp2", [128, 512], F32)
    expT = sb("l1a_expT", [128, NT * 512], BF16)
    recip = sb("l1a_recip", [1, 512], F32)
    recipB = sb("l1a_recipB", [128, 512], F32)
    ctx_sb = [sb(f"l1a_ctxsb{i}", [128, 512], BF16) for i in range(2)]
    ctxg0 = sb("l1a_ctxg0", [128, 2048], BF16)
    ctxg = [ctxg0, ctxg0]

    scr = d["ctx_scr"]

    c_tab = S.dma("gdma", [], ct[:], d["ct"])
    S.dma("gdma", [], st[:], d["st"])
    S.dma("gdma", [], p1t[:], d["p1t"])
    c_p2 = S.dma("gdma", [], p2t[:], d["p2t"])
    c_ones = S.step("vector", [], lambda e: (e.memset(ones_r[:], 1.0), e.memset(ones_c[:], 1.0),
                                             e.memset(eps_c[:], 1e-5))[-1])

    last_out = []
    pv = {}

    for b in range(NB):
        c_bo = S.dma("gdma", [pv.get("po_last")], bo_row[:], d["b_o"].rearrange("(a m) -> a m", a=1))

        st_norm = [None] * NT
        st_tr = [None] * NT
        st_ag = [None] * NT
        for i in range(NT):
            dep_war = st_norm[i - 1] if i >= 1 else pv.get("out_last1")
            ld = S.dma("sync", [dep_war], xst[:],
                       d["x"][b * T + i * 128: b * T + (i + 1) * 128, :])
            bs = S.step("vector", [ld, st_ag[i - 1] if i >= 1 else pv.get("ag_last")],
                        lambda e: [e.bn_stats(stats[:, 6 * kk:6 * kk + 6],
                                              xst[:, 512 * kk:512 * (kk + 1)]) for kk in range(4)][-1])
            st_ag[i] = S.step("vector", [bs], lambda e, i=i: e.bn_aggr(mv[i % 2][:], stats[:]))
            ln1 = S.step("scalar", [st_ag[i], st_norm[i - 2] if i >= 2 else pv.get("norm_last"), c_ones],
                         lambda e, i=i: e.activation(rstd[i % 2][:], mv[i % 2][:, 1:2], AF.Ln, bias=eps_c[:, 0:1]))
            ln2 = S.step("scalar", [ln1], lambda e, i=i: e.activation(rstd[i % 2][:], rstd[i % 2][:], AF.Exp, scale=-0.5))
            st_norm[i] = S.step("vector", [ln2, st_tr[i - 1] if i >= 1 else pv.get("tr_last")],
                                lambda e, i=i: e.tensor_scalar(xn[:], xst[:], mv[i % 2][:, 0:1],
                                                               rstd[i % 2][:, 0:1], ALU.subtract, ALU.mult))
            deps_tr = [st_norm[i]] + ([pv.get("po_last")] if (b > 0 and i == 0) else [])
            st_tr[i] = S.step("sync", deps_tr,
                              lambda e, i=i: [e.dma_start_transpose(
                                  bigA[:, dt * T + i * 128: dt * T + (i + 1) * 128],
                                  xn[:, dt * 128:(dt + 1) * 128]) for dt in range(16)],
                              weight=256)
        tr_all = st_tr[NT - 1]

        units = [("ckv", m) for m in range(4)] + [("cq", m) for m in range(4)] + [("kr", 0)]
        for h in range(NHEAD):
            if h % 2 == 0:
                units.append(("qr", h // 2))
            units.append(("k", h))
            units.append(("q", h))
            units.append(("v", h))

        def wspec(kind, idx):
            if kind == "ckv":
                return d["w_dkv"][:, idx * 128:(idx + 1) * 128], 16, 128
            if kind == "cq":
                return d["w_dq"][:, idx * 128:(idx + 1) * 128], 16, 128
            if kind == "kr":
                return d["w_kr"], 16, 128
            if kind == "qr":
                return d["w_qr"][:, idx * 128:(idx + 1) * 128], 4, 128
            if kind == "k":
                return d["w_uk"][:, idx * 128:(idx + 1) * 128], 4, 128
            if kind == "q":
                return d["w_uq"][:, idx * 128:(idx + 1) * 128], 4, 128
            if kind == "v":
                return d["w_uv"][:, idx * 128:(idx + 1) * 128], 4, 128

        wld = {}
        wld_reader = {}

        def ensure_wld(ui):
            if ui >= len(units) or units[ui] in wld:
                return
            kind, idx = units[ui]
            wsrc, nk, mc = wspec(kind, idx)
            par = ui % 2
            if kind == "kr":
                v3 = w_sl[par][:].rearrange("p (dt m) -> p dt m", m=128)[:, 0:nk, :]
                d1 = S.dma("gdma", [wld_reader.get(par)], v3[:, :, 0:64],
                           wsrc.rearrange("(dt p) m -> p dt m", p=128))
                wld[(kind, idx)] = S.dma("gdma", [d1], v3[:, :, 64:128],
                                         wsrc.rearrange("(dt p) m -> p dt m", p=128))
            else:
                wld[(kind, idx)] = S.dma("gdma", [wld_reader.get(par)],
                                         w_sl[par][:].rearrange("p (dt m) -> p dt m", m=mc)[:, 0:nk, :],
                                         wsrc.rearrange("(dt p) m -> p dt m", p=128))
        ensure_wld(0)
        ensure_wld(1)

        st_ropeB = {}
        prev_pm = pv.get("pm_last")
        prev_ropeA = None
        prev_ropeB = None
        last_reader_of = {}
        prev_stt = pv.get("stt_last")
        prev_rec = pv.get("rec_last")
        prev_bcp = pv.get("bcp_last")
        prev_ctx_dma = {0: pv.get("ctxd_m2"), 1: pv.get("ctxd_m1")}
        psq_prev = pv.get("psq_free")
        psv_prev = pv.get("psv_free")
        exp_slot_free = pv.get("avs_last")
        sc_bank_prev = [pv.get("avs_last"), pv.get("avs_last")]
        proj_last = None
        kr_done = None

        for ui, (kind, idx) in enumerate(units):
            ensure_wld(ui)
            par = ui % 2
            wbuf = w_sl[par]
            wsrc, nk, mc = wspec(kind, idx)

            if kind == "v":
                h = idx
                deps = [wld[(kind, idx)], tr_all, psv_prev,
                        last_reader_of.get(("v", h - 1), pv.get("avs_last"))]
                vev = None
                for i in range(NT):
                    vp = S.step("tensor", deps if i == 0 else [vev],
                                lambda e, wbuf=wbuf, i=i: [e.matmul(
                                    ps_v[:, 0:128], ckvT[:, kt * T + i * 128: kt * T + (i + 1) * 128],
                                    wbuf[:, kt * 128:(kt + 1) * 128], start=(kt == 0), stop=(kt == 3))
                                    for kt in range(4)][-1])
                    vev = S.step("scalar", [vp],
                                 lambda e, i=i: e.copy(v_tm[:, i * 128:(i + 1) * 128], ps_v[:, 0:128]))
                wld_reader[par] = vp
                psv_prev = vev
                vT_ready = vev
                ensure_wld(ui + 1)
                ensure_wld(ui + 2)
                g2 = h % 2
                for tc in range(TC):
                    rdy = [None] * NT
                    st_sc = None
                    for s_ in range(NT):
                        deps = [st_ropeB[("q", h)], st_ropeB[("k", h)],
                                st_ropeB[("qr", h // 2)], kr_done,
                                sc_bank_prev[s_ % 2]]
                        st_sc = S.step("tensor", deps,
                                       lambda e, s_=s_, tc=tc, g2=g2: (
                                           e.matmul(ps_sc[s_ % 2][:], kT[:, s_ * 128:(s_ + 1) * 128],
                                                    qT[:, tc * 512:(tc + 1) * 512], start=True, stop=False),
                                           e.matmul(ps_sc[s_ % 2][:], kRT[64 * g2:64 * g2 + 64, s_ * 128:(s_ + 1) * 128],
                                                    qRT[64 * g2:64 * g2 + 64, tc * 512:(tc + 1) * 512],
                                                    start=False, stop=True))[1])
                        ex = S.step("scalar", [st_sc, exp_slot_free],
                                    lambda e, s_=s_: e.activation(expT[:, s_ * 512:(s_ + 1) * 512],
                                                                  ps_sc[s_ % 2][:], AF.Exp, scale=SCL))
                        sc_bank_prev[s_ % 2] = ex
                        rdy[s_] = ex
                    avs = None
                    for s_ in range(NT):
                        deps = [rdy[s_], vT_ready]
                        if s_ == 0:
                            deps += [prev_rec, prev_stt, prev_bcp, c_ones]
                        def av_fn(e, s_=s_, first=(s_ == 0), last=(s_ == NT - 1)):
                            e.matmul(ps_rs[0:1, :], ones_c[:], expT[:, s_ * 512:(s_ + 1) * 512],
                                     start=first, stop=last)
                            return e.matmul(ps_ctx[:], v_tm[:, s_ * 128:(s_ + 1) * 128],
                                            expT[:, s_ * 512:(s_ + 1) * 512], start=first, stop=last)
                        avs = S.step("tensor", deps, av_fn)
                    exp_slot_free = avs
                    last_reader_of[("v", h)] = avs
                    last_reader_of[("q", h)] = avs
                    last_reader_of[("k", h)] = avs
                    if h % 2 == 1:
                        last_reader_of[("qr", h // 2)] = avs
                    rec = S.step("vector", [avs], lambda e: e.reciprocal(recip[:], ps_rs[0:1, :]))
                    prev_rec = rec
                    bcm = S.step("tensor", [rec, prev_bcp],
                                 lambda e: e.matmul(ps_bc[:], ones_r[0:1, 0:128], recip[0:1, :],
                                                    start=True, stop=True))
                    rbc = S.step("scalar", [bcm, prev_stt],
                                 lambda e: e.copy(recipB[:], ps_bc[:]))
                    prev_bcp = rbc
                    j = (h * TC + tc) % 2
                    stt = S.step("vector", [rbc, avs, prev_ctx_dma[j]],
                                 lambda e, j=j: e.scalar_tensor_tensor(
                                     ctx_sb[j][:], ps_ctx[:], 1.0, recipB[:], ALU.mult, ALU.mult))
                    prev_stt = stt
                    cd = S.dma("sdma", [stt], scr[:, h * T + tc * 512: h * T + (tc + 1) * 512], ctx_sb[j][:])
                    prev_ctx_dma[j] = cd
                continue

            if kind in ("ckv", "cq", "kr"):
                mov, movT = bigA, T
                mov_nk = 16
            else:
                mov, movT = (cqT, T) if kind in ("qr",) else (ckvT, T) if kind == "k" else (cqT, T)
                mov_nk = 4
            for tc in range(TC):
                deps = [wld[(kind, idx)], psq_prev, tr_all]
                if b > 0 and ui == 0 and tc == 0:
                    deps.append(pv.get("radd_last"))
                mrows = mc
                def pr_fn(e, wbuf=wbuf, tc=tc, nk=nk, mov=mov, movT=movT, mc=mc):
                    last = None
                    for kt in range(nk):
                        last = e.matmul(ps_q[0:mc, :], wbuf[:, kt * mc:(kt + 1) * mc],
                                        mov[:, kt * movT + tc * 512: kt * movT + tc * 512 + 512],
                                        start=(kt == 0), stop=(kt == nk - 1))
                    return last
                stp = S.step("tensor", deps, pr_fn)
                proj_last = stp
                wld_reader[par] = stp
                if tc == TC - 1:
                    ensure_wld(ui + 1)
                    ensure_wld(ui + 2)
                if kind in ("ckv", "cq", "q", "k"):
                    dst, doff = {
                        "ckv": (ckvT, idx * T), "cq": (cqT, idx * T),
                        "q": (qT, 0), "k": (kT, 0)}[kind]
                    ev_war = None
                    if kind == "q":
                        ev_war = last_reader_of.get(("q", idx - 1), pv.get("avs_last"))
                    elif kind == "k":
                        ev_war = last_reader_of.get(("k", idx - 1), pv.get("avs_last"))
                    elif b > 0 and kind in ("ckv", "cq"):
                        ev_war = pv.get("avs_last" if False else "ckv_free")
                    ev = S.step("scalar", [stp, ev_war],
                                lambda e, dst=dst, doff=doff, tc=tc, mc=mc: e.copy(
                                    dst[:, doff + tc * 512: doff + (tc + 1) * 512], ps_q[:]))
                    psq_prev = ev
                    if kind in ("ckv", "cq"):
                        st_ropeB[(kind, idx)] = ev
                    else:
                        st_ropeB[(kind, idx)] = ev
                else:
                    ev = S.step("scalar", [stp, prev_pm],
                                lambda e, mc=mc: e.copy(qf32[0:mc, :], ps_q[0:mc, :]))
                    psq_prev = ev
                    pm = S.step("tensor", [ev, prev_ropeA],
                                lambda e, mc=mc: (
                                    e.matmul(ps_p1[0:mc, :], p1t[0:mc, 0:mc], qf32[0:mc, :], start=True, stop=True),
                                    e.matmul(ps_p2[0:mc, :], p2t[0:mc, 0:mc], qf32[0:mc, :], start=True, stop=True))[1])
                    prev_pm = pm
                    ra = S.step("vector", [pm, prev_ropeB, c_tab],
                                lambda e, tc=tc, mc=mc: (
                                    e.tensor_tensor(tmp1[0:mc, :], ps_p1[0:mc, :], ct[0:mc, tc * 512:(tc + 1) * 512], ALU.mult),
                                    e.tensor_tensor(tmp2[0:mc, :], ps_p2[0:mc, :], st[0:mc, tc * 512:(tc + 1) * 512], ALU.mult))[1])
                    prev_ropeA = ra
                    if kind == "kr":
                        rb_war = pv.get("kr_free") if b > 0 else None
                        dstT, doff2 = kRT, 0
                    else:
                        rb_war = last_reader_of.get(("qr", idx - 1), pv.get("avs_last"))
                        dstT, doff2 = qRT, 0
                    rb_ = S.step("vector", [ra] + ([rb_war] if rb_war else []),
                                 lambda e, dstT=dstT, tc=tc, mc=mc: e.tensor_tensor(
                                     dstT[0:mc, tc * 512:(tc + 1) * 512], tmp1[0:mc, :], tmp2[0:mc, :], ALU.add))
                    prev_ropeB = rb_
                    if kind == "kr":
                        kr_done = rb_ if tc == TC - 1 else kr_done
                    else:
                        st_ropeB[("qr", idx)] = rb_

        xnT_free = psq_prev
        wo_lds = []
        for dt in range(16):
            wo_lds.append(S.dma("gdma", [xnT_free, proj_last] if dt == 0 else [],
                                bigA[:, dt * 2048:(dt + 1) * 2048],
                                d["w_o"][dt * 128:(dt + 1) * 128, :]))
        st_radd = [None] * NT
        st_out = [None] * NT
        st_po = [None] * NT
        st_cg = [None] * NT
        po_last = None
        scr3 = scr.rearrange("p (dt t) -> p dt t", dt=16)
        for i in range(NT):
            cg_deps = (S.dma_fence("sdma") if i == 0 else []) + \
                      [st_po[i - 1] if i >= 1 else pv.get("po_last")]
            st_cg[i] = S.dma("sync", cg_deps,
                             ctxg[i % 2][:].rearrange("p (dt t) -> p dt t", dt=16),
                             scr3[:, :, i * 128:(i + 1) * 128])
            x2 = S.dma("sync", [st_out[i - 1] if i >= 1 else pv.get("out_last1"), st_norm[NT - 1]],
                       xst[:], d["x"][b * T + i * 128: b * T + (i + 1) * 128, :])
            deps = wo_lds + [st_cg[i], c_bo, st_radd[i - 1] if i >= 1 else pv.get("radd_last")]
            if i == 0:
                deps.append(prev_stt)
            def po_fn(e, i=i):
                for c in range(4):
                    e.matmul(ps_out4[c][:], ones_r[0:1, 0:128],
                             bo_row[0:1, c * 512:(c + 1) * 512], start=True, stop=False)
                last = None
                for dt in range(16):
                    for c in range(4):
                        last = e.matmul(ps_out4[c][:],
                                        ctxg[i % 2][:, dt * 128:(dt + 1) * 128],
                                        bigA[:, dt * 2048 + c * 512: dt * 2048 + (c + 1) * 512],
                                        start=False, stop=(dt == 15))
                return last
            po = S.step("tensor", deps, po_fn)
            po_last = po
            st_po[i] = po
            ra = S.step("vector", [po, x2],
                        lambda e: [e.tensor_tensor(xst[:, c * 512:(c + 1) * 512],
                                                   ps_out4[c][:], xst[:, c * 512:(c + 1) * 512],
                                                   ALU.add) for c in range(4)][-1])
            st_radd[i] = ra
            st_out[i] = S.dma("sync", [ra], d["out"][b * T + i * 128: b * T + (i + 1) * 128, :], xst[:])
            last_out.append(st_out[i])

        pv = dict(norm_last=st_norm[NT - 1], tr_last=st_tr[NT - 1], ag_last=st_ag[NT - 1],
                  po_last=po_last, psq_free=psq_prev, psv_free=psv_prev,
                  radd_last=st_radd[NT - 1],
                  stt_last=prev_stt, rec_last=prev_rec, bcp_last=prev_bcp,
                  avs_last=exp_slot_free, proj_last=proj_last, pm_last=prev_pm,
                  out_last1=st_out[NT - 1],
                  ctxd_m2=prev_ctx_dma[0], ctxd_m1=prev_ctx_dma[1],
                  ckv_free=psv_prev, kr_free=exp_slot_free)

    return last_out


D = 2048
LAT = 1024
SU = 4096
E = 32
NSU = SU // 128
NLT = LAT // 128
BIG = 30000.0


def emit_l1_ffn(S, NTOK, d, SLAB=1024, CH=1024, debug_stop=False):
    NSL = NTOK // SLAB
    NCH = NTOK // CH
    NT = SLAB // 128
    TC = SLAB // 512
    CT = CH // 128
    CC = CH // 512
    sb = S.sbuf
    G = S.global_psums()
    assert TC == 1
    ps_a = [G["q"], G["p1"]]
    ps_d0 = [G["sc1"], G["ctx"], G["bc"], G["d"]]
    ps_d1 = [G["q"], G["p1"], G["p2"], G["sc0"]]
    assert CC == 1
    ps_h = [G["q"], G["p1"]]
    ps_y = [G["sc1"], G["ctx"]]
    ps_l = G["p2"]
    ps_bcast = G["sc0"]

    xst0 = sb("l1f_xst0", [128, D], F32)
    xst = [xst0, xst0]
    xn = [sb(f"l1f_xn{i}", [128, D], BF16) for i in range(2)]
    xn32 = sb("l1f_xn32", [128, D], F32)
    xlo = sb("l1f_xlo", [128, D], BF16)
    xloT = sb("l1f_xloT", [128, D], BF16)
    stats = sb("l1f_stats", [128, 24], F32)
    mv = [sb(f"l1f_mv{i}", [128, 2], F32) for i in range(2)]
    rstd = [sb(f"l1f_rstd{i}", [128, 1], F32) for i in range(2)]
    eps_c = sb("l1f_eps", [128, 1], F32)
    ones_r = sb("l1f_ones_r", [1, 512], F32)
    ones8 = sb("l1f_ones8", [128, 8], F32)
    onesb = sb("l1f_onesb", [1, 128], BF16)
    AB1 = sb("l1f_AB1", [128, 32768], BF16)
    AB2 = sb("l1f_AB2", [128, 16384], BF16)
    WS = sb("l1f_WS", [128, 4096], BF16)
    YSB = sb("l1f_ysb", [128, 8 * 1024], F32)
    wbf = [sb(f"l1f_wb{i}", [128, D], BF16) for i in range(2)]
    gw32 = xst[0]
    gbB = sb("l1f_gbB", [128, 32], F32)
    scores = sb("l1f_scores", [128, 32], F32)
    sbv = sb("l1f_sbv", [128, 32], F32)
    gs = sb("l1f_gs", [128, 4], F32)
    rt1 = sb("l1f_rt1", [128, 32], F32)
    rt2 = sb("l1f_rt2", [128, 32], F32)
    sel = sb("l1f_sel", [128, 32], F32)
    rcol = sb("l1f_rcol", [128, 4], F32)
    cwpad = sb("l1f_cwpad", [128, 128], BF16)
    rt2i = sb("l1f_rt2i", [128, 32], I32)
    cwrow = sb("l1f_cwrow", [1, 2 * CH], BF16)
    cwB = sb("l1f_cwB", [128, 2 * 512], BF16)
    gb_row = sb("l1f_gbrow", [1, 32], F32)
    negB = sb("l1f_negB", [128, 32], F32)

    def A_hsu(s_):
        return AB1[:, s_ * 1024:(s_ + 1) * 1024]

    def B_xlat(lk):
        return AB1[:, lk * 1024:(lk + 1) * 1024]

    def B_h(it):
        return AB1[:, 8192 + it * 1024:8192 + (it + 1) * 1024]

    def B_fc2(lt):
        return AB1[:, lt * 2048:(lt + 1) * 2048]

    def B_ybf(lt):
        return AB1[:, 16384 + lt * CH:16384 + (lt + 1) * CH]

    def A_xnT(dt):
        return AB2[:, dt * SLAB:(dt + 1) * SLAB]

    def B_xnTtc(dt):
        return AB2[:, dt * 512:(dt + 1) * 512]

    def B_gwT(dt):
        return AB2[:, 8192 + dt * 32:8192 + dt * 32 + 32]

    def B_gwTlo(dt):
        return AB2[:, 8832 + dt * 32:8832 + dt * 32 + 32]

    def B_xnlo(dt):
        return AB1[:, 24576 + dt * 512:24576 + (dt + 1) * 512]

    B_cwT = AB2[:, 9472:9472 + 128]

    def A_wbuf(par):
        return WS[:, par * 2048:(par + 1) * 2048]

    def B_fc1buf(par):
        return WS[:, par * 2048:(par + 1) * 2048]

    def B_w1(par):
        return WS[:, par * 1024:(par + 1) * 1024]

    def B_w2(par):
        return WS[:, 2048 + par * 1024:2048 + (par + 1) * 1024]

    xnT_scr = d["xnT_scr"]
    xnT_lo_scr = d["xnT_lo_scr"]
    cw_scr = d["cw_scr"]

    c_ones = S.step("vector", [], lambda e: (e.memset(eps_c[:], 1e-5), e.memset(ones_r[:], 1.0),
                                             e.memset(ones8[:], 1.0), e.memset(onesb[:], 1.0),
                                             e.memset(cwpad[:], 0.0), e.memset(negB[:], -BIG))[-1])
    c_gb = S.dma("gdma", [], gb_row[:], d["gate_bias"].rearrange("(a m) -> a m", a=1))
    c_gbm = S.step("tensor", [c_gb, c_ones],
                   lambda e: e.matmul(G["d"][:, 0:32], ones_r[0:1, 0:128], gb_row[0:1, :],
                                      start=True, stop=True))
    c_gbB = S.step("scalar", [c_gbm], lambda e: e.copy(gbB[:], G["d"][:, 0:32]))


    last_out = []
    pv = {}
    a_out = {}

    for sl in range(NSL):
        base = sl * SLAB
        st_norm = [None] * NT
        st_tr = [None] * NT
        st_ag = [None] * NT
        st_xscr = [None] * NT
        prev_n32 = pv.get("n32_last")
        for i in range(NT):
            dep_war = prev_n32 if i >= 1 else pv.get("out_last1")
            ld = S.dma("sync", [dep_war], xst[i % 2][:],
                       d["x"][base + i * 128: base + (i + 1) * 128, :])
            bs = S.step("vector", [ld, st_ag[i - 1] if i >= 1 else pv.get("ag_last")],
                        lambda e, i=i: [e.bn_stats(stats[:, 6 * kk:6 * kk + 6],
                                                   xst[i % 2][:, 512 * kk:512 * (kk + 1)]) for kk in range(4)][-1])
            st_ag[i] = S.step("vector", [bs], lambda e, i=i: e.bn_aggr(mv[i % 2][:], stats[:]))
            ln1 = S.step("scalar", [st_ag[i], st_norm[i - 2] if i >= 2 else pv.get("norm_last"), c_ones],
                         lambda e, i=i: e.activation(rstd[i % 2][:], mv[i % 2][:, 1:2], AF.Ln, bias=eps_c[:, 0:1]))
            ln2 = S.step("scalar", [ln1], lambda e, i=i: e.activation(rstd[i % 2][:], rstd[i % 2][:], AF.Exp, scale=-0.5))
            n32 = S.step("vector", [ln2, pv.get(("lo", i - 1)) if i >= 1 else pv.get("lo_last")],
                         lambda e, i=i: e.tensor_scalar(xn32[:], xst[i % 2][:], mv[i % 2][:, 0:1],
                                                        rstd[i % 2][:, 0:1], ALU.subtract, ALU.mult))
            st_norm[i] = S.step("vector", [n32, st_tr[i - 2] if i >= 2 else pv.get("tr_last")],
                                lambda e, i=i: e.tensor_copy(xn[i % 2][:], xn32[:]))
            lo = S.step("vector", [st_norm[i], pv.get(("trlo", i - 1)) if i >= 1 else pv.get("trlo_last")],
                        lambda e, i=i: e.tensor_tensor(xlo[:], xn32[:], xn[i % 2][:], ALU.subtract))
            pv[("lo", i)] = lo
            prev_n32 = n32
            deps_tr = [st_norm[i]] + ([pv.get("xnT_free")] if i < 2 else [])
            prev_xscr = pv.get("xscr_list")
            if prev_xscr:
                deps_tr.append(prev_xscr[i])
            st_tr[i] = S.step("sync", deps_tr,
                              lambda e, i=i: [e.dma_start_transpose(
                                  A_xnT(dt)[:, i * 128:(i + 1) * 128],
                                  xn[i % 2][:, dt * 128:(dt + 1) * 128]) for dt in range(16)],
                              weight=256)
            st_xscr[i] = S.dma("sdma", [st_tr[i]],
                               xnT_scr.rearrange("p (dt t) -> p dt t", dt=16)[:, :, base + i * 128: base + (i + 1) * 128],
                               AB2[:, 0:16 * SLAB].rearrange("p (dt t) -> p dt t", dt=16)[:, :, i * 128:(i + 1) * 128])
            trlo = S.step("sync", [lo, pv.get("xlscr_last")],
                          lambda e: [e.dma_start_transpose(
                              xloT[:, dt * 128:(dt + 1) * 128],
                              xlo[:, dt * 128:(dt + 1) * 128]) for dt in range(16)],
                          weight=256)
            pv[("trlo", i)] = trlo
            pv["xlscr_last"] = S.dma("sdma", [trlo],
                                     xnT_lo_scr.rearrange("p (dt t) -> p dt t", dt=16)[:, :, base + i * 128: base + (i + 1) * 128],
                                     xloT[:].rearrange("p (dt m) -> p dt m", dt=16))

        pv["n32_last"] = prev_n32
        st_g = {}
        wld_g = {}

        def ensure_su(sidx):
            if sidx in wld_g or sidx >= NSU:
                return
            par = sidx % 2
            depg = st_g.get(sidx - 2, pv.get("dlast"))
            wld_g[sidx] = S.dma("gdma", [depg], A_wbuf(par),
                                d["sut"][sidx * 128:(sidx + 1) * 128, :])
        ensure_su(0)
        ensure_su(1)
        prev_gev = pv.get("gev_last")
        prev_sq = pv.get("sq_last")
        hread_prev = pv.get("hread", {})
        agev_bank = {0: pv.get("agev_b0"), 1: pv.get("agev_b1")}
        h_ready = [None] * NSU
        for s_ in range(NSU):
            par = s_ % 2
            bk = s_ % 2
            deps = [wld_g[s_], agev_bank[bk]] + [st_tr[j] for j in range(NT)]
            if s_ == 0:
                deps.append(pv.get("radd_last"))
            gp = S.step("tensor", deps,
                        lambda e, par=par, bk=bk: [e.matmul(ps_a[bk][:], A_wbuf(par)[:, dt * 128:(dt + 1) * 128],
                                                            A_xnT(dt)[:, 0:512],
                                                            start=(dt == 0), stop=(dt == 15))
                                                   for dt in range(16)][-1])
            st_g[s_] = gp
            gev = S.step("vector", [gp, prev_sq, hread_prev.get(s_), prev_gev],
                         lambda e, s_=s_, bk=bk: e.tensor_scalar_max(A_hsu(s_)[:, 0:512],
                                                                     ps_a[bk][:], 0.0))
            agev_bank[bk] = gev
            prev_gev = gev
            sq = S.step("scalar", [gev],
                        lambda e, s_=s_: e.activation(A_hsu(s_)[:, 0:SLAB], A_hsu(s_)[:, 0:SLAB], AF.Square))
            prev_sq = sq
            h_ready[s_] = sq
            ensure_su(s_ + 2)

        st_d = {}
        wldd = {}
        st_radd = {}
        st_out_l = {}

        def ensure_sd(j, ht):
            if (j, ht) in wldd or ht >= NSU or j >= NT // 2:
                return
            par = (j * NSU + ht) % 2
            dep = (st_d.get((j, ht - 2)) if ht >= 2 else
                   (st_d.get((j - 1, NSU - 2 + ht)) or st_g.get(NSU - 2 + ht) or pv.get("dlast")))
            wldd[(j, ht)] = S.dma("gdma", [dep], A_wbuf(par),
                                  d["sdt"][ht * 128:(ht + 1) * 128, :])
        ensure_sd(0, 0)
        ensure_sd(0, 1)
        for j in range(NT // 2):
            po = None
            for ht in range(NSU):
                par = (j * NSU + ht) % 2
                deps = [wldd[(j, ht)], h_ready[ht]]
                if ht == 0:
                    deps += [st_radd.get(j - 1, pv.get("radd_last")), prev_gev, prev_sq]
                def dp_fn(e, j=j, par=par, ht=ht, first=(ht == 0), last=(ht == NSU - 1)):
                    out = None
                    for tt in range(2):
                        pset = ps_d0 if tt == 0 else ps_d1
                        for c in range(4):
                            out = e.matmul(pset[c][:],
                                           A_hsu(ht)[:, (2 * j + tt) * 128:(2 * j + tt + 1) * 128],
                                           A_wbuf(par)[:, c * 512:(c + 1) * 512],
                                           start=first, stop=last)
                    return out
                po = S.step("tensor", deps, dp_fn)
                st_d[(j, ht)] = po
                if ht + 2 < NSU:
                    ensure_sd(j, ht + 2)
                else:
                    ensure_sd(j + 1, ht + 2 - NSU)
            for tt in range(2):
                i = 2 * j + tt
                x2 = S.dma("sync", [pv.get("n32_last"),
                                    st_out_l.get(i - 1, pv.get("out_last1"))],
                           xst[i % 2][:], d["x"][base + i * 128: base + (i + 1) * 128, :])
                pset = ps_d0 if tt == 0 else ps_d1
                ra = S.step("vector", [po, x2],
                            lambda e, i=i, pset=pset: [
                                e.tensor_tensor(xst[i % 2][:, c * 512:(c + 1) * 512], pset[c][:],
                                                xst[i % 2][:, c * 512:(c + 1) * 512], ALU.add)
                                for c in range(4)][-1])
                st_radd[j] = ra
                so = S.dma("sync", [ra], d["out"][base + i * 128: base + (i + 1) * 128, :], xst[i % 2][:])
                st_out_l[i] = so
                a_out[base // 128 + i] = so
                last_out.append(so)

        pv = dict(
            norm_last=st_norm[NT - 1], tr_last=st_tr[NT - 1], ag_last=st_ag[NT - 1],
            n32_last=prev_n32,
            gev_last=prev_gev, sq_last=prev_sq, radd_last=st_radd[NT // 2 - 1],
            agev_b0=agev_bank[0], agev_b1=agev_bank[1],
            out_last2=st_out_l[NT - 2], out_last1=st_out_l[NT - 1],
            xnT_free=st_g[NSU - 1], dlast=st_d[(NT // 2 - 1, NSU - 1)],
            xscr_last=st_xscr[NT - 1], xscr_list=st_xscr,
            lo_last=pv.get(("lo", NT - 1)), trlo_last=pv.get(("trlo", NT - 1)),
            xlscr_last=pv.get("xlscr_last"),
            hread={s_: st_d[(NT // 2 - 1, s_)] for s_ in range(NSU)},
        )

    a_radd_last = pv["radd_last"]
    a_dlast = pv["dlast"]
    a_xnT_free = pv["xnT_free"]
    a_gev = pv["gev_last"]
    a_sq = pv["sq_last"]

    b_fence = S.dma_fence("sdma")
    gld = S.dma("gdma", [pv.get("out_last1")], gw32[0:32, :], d["gate_w"])
    gcv = S.step("vector", [gld], lambda e: e.tensor_copy(wbf[0][0:32, :], gw32[0:32, :]))
    gcl = S.step("vector", [gcv],
                 lambda e: e.tensor_tensor(wbf[1][0:32, :], gw32[0:32, :], wbf[0][0:32, :], ALU.subtract))
    gtp = S.step("sync", [gcl, a_xnT_free] + b_fence,
                 lambda e: [e.dma_start_transpose(AB2[:, 8192 + dt * 32:8192 + dt * 32 + 32],
                                                  wbf[0][0:32, dt * 128:(dt + 1) * 128]) for dt in range(16)] +
                           [e.dma_start_transpose(AB2[:, 8832 + dt * 32:8832 + dt * 32 + 32],
                                                  wbf[1][0:32, dt * 128:(dt + 1) * 128]) for dt in range(16)],
                 weight=512)

    st_b_out = {}
    bpv = {}
    for ch in range(NCH):
        cbase = ch * CH
        wfc1 = {}

        def ensure_fc1(lt):
            if lt in wfc1 or lt >= NLT:
                return
            par = lt % 2
            dep = bpv.get(("fp", lt - 2)) or (bpv.get("fp_last") if ch > 0 else a_dlast)
            wfc1[lt] = S.dma("gdma", [dep], B_fc1buf(par),
                             d["fc1t"][lt * 128:(lt + 1) * 128, :])
        ensure_fc1(0)
        ensure_fc1(1)
        prev_xl_ev = bpv.get("xl_ev")
        prev_logit_ev = bpv.get("sg_last")
        prev_rt = bpv.get("rt_last")
        st_xnld = {}
        rt_done = {}
        fp = None
        slab_readers = [bpv.get("xn_free0"), bpv.get("xn_free1")]
        for cc in range(CC):
            st_xnld[cc] = S.dma("sync", [gtp if (ch == 0 and cc == 0) else None]
                                + (b_fence if (ch == 0 and cc == 0) else [])
                                + slab_readers,
                                AB2[:, 0:8192].rearrange("p (dt t) -> p dt t", dt=16),
                                xnT_scr.rearrange("p (dt t) -> p dt t", dt=16)[:, :, cbase + cc * 512: cbase + (cc + 1) * 512])
            lo_deps = [pv["xlscr_last"] if (ch == 0 and cc == 0) else None] + slab_readers + \
                      ([bpv.get("hread_all")] if True else [])
            st_xnlo = S.dma("sync", lo_deps,
                            AB1[:, 24576:24576 + 8192].rearrange("p (dt t) -> p dt t", dt=16),
                            xnT_lo_scr.rearrange("p (dt t) -> p dt t", dt=16)[:, :, cbase + cc * 512: cbase + (cc + 1) * 512])
            for lt in range(NLT):
                par = lt % 2
                deps = [st_xnld[cc], wfc1[lt], prev_xl_ev]
                if ch == 0 and cc == 0 and lt == 0:
                    deps += [a_radd_last, a_gev, a_sq]
                fp = S.step("tensor", deps,
                            lambda e, par=par, cc=cc: [e.matmul(ps_h[cc % len(ps_h)][:],
                                                                B_fc1buf(par)[:, dt * 128:(dt + 1) * 128],
                                                                B_xnTtc(dt), start=(dt == 0), stop=(dt == 15))
                                                       for dt in range(16)][-1])
                bpv[("fp", lt)] = fp
                if cc == CC - 1:
                    ensure_fc1(lt + 2)
                ev = S.step("scalar", [fp, prev_logit_ev],
                            lambda e, lt=lt, cc=cc: e.copy(B_xlat(lt)[:, cc * 512:(cc + 1) * 512],
                                                           ps_h[cc % len(ps_h)][:]))
                prev_xl_ev = ev
            for w in range(4):
                i = cc * 4 + w
                lg = S.step("tensor", [st_xnld[cc], st_xnlo, gtp, prev_logit_ev, c_gbm],
                            lambda e, w=w: [e.matmul(ps_l[:, 0:32],
                                                     (B_xnTtc(dt), B_xnTtc(dt), B_xnlo(dt))[z][:, w * 128:(w + 1) * 128],
                                                     (B_gwT(dt), B_gwTlo(dt), B_gwT(dt))[z],
                                                     start=(dt == 0 and z == 0), stop=(dt == 15 and z == 2))
                                            for dt in range(16) for z in range(3)][-1])
                sg = S.step("scalar", [lg, prev_rt, c_gbB],
                            lambda e: e.activation(scores[:], ps_l[:, 0:32], AF.Sigmoid))
                prev_logit_ev = sg
                fns = [
                    lambda e: e.tensor_tensor(sbv[:], scores[:], gbB[:], ALU.add),
                    lambda e: [e.reduce_max(rcol[:, g:g + 1], sbv[:, 8 * g:8 * g + 8], axis=AX.X) for g in range(4)][-1],
                    lambda e: [e.tensor_scalar(rt2[:, 8 * g:8 * g + 8], sbv[:, 8 * g:8 * g + 8],
                                               rcol[:, g:g + 1], None, ALU.is_equal) for g in range(4)][-1],
                    lambda e: e.scalar_tensor_tensor(rt1[:], rt2[:], -BIG, sbv[:], ALU.mult, ALU.add),
                    lambda e: [e.reduce_max(gs[:, g:g + 1], rt1[:, 8 * g:8 * g + 8], axis=AX.X) for g in range(4)][-1],
                    lambda e: e.tensor_tensor(gs[:], gs[:], rcol[:, 0:4], ALU.add),
                    lambda e: e.reduce_max(rcol[:, 0:1], gs[:], axis=AX.X),
                    lambda e: e.tensor_scalar(rt2[:, 0:4], gs[:], rcol[:, 0:1], None, ALU.is_equal),
                    lambda e: e.scalar_tensor_tensor(rt1[:, 0:4], rt2[:, 0:4], -BIG, gs[:], ALU.mult, ALU.add),
                    lambda e: e.reduce_max(rcol[:, 1:2], rt1[:, 0:4], axis=AX.X),
                    lambda e: e.tensor_scalar(rcol[:, 0:4], gs[:], rcol[:, 1:2], None, ALU.is_ge),
                    lambda e: [e.tensor_scalar(rt2[:, 8 * g:8 * g + 8], ones8[:], rcol[:, g:g + 1],
                                               None, ALU.mult) for g in range(4)][-1],
                    lambda e: e.tensor_copy(rt1[:], negB[:]),
                    lambda e: e.tensor_copy(rt2i[:], rt2[:]),
                    lambda e: e.copy_predicated(rt1[:], rt2i[:], sbv[:]),
                    lambda e: e.memset(sel[:], 0.0),
                ]
                for _ in range(8):
                    fns += [
                        lambda e: e.reduce_max(rcol[:, 0:1], rt1[:], axis=AX.X),
                        lambda e: e.tensor_scalar(rt2[:], rt1[:], rcol[:, 0:1], None, ALU.is_equal),
                        lambda e: e.scalar_tensor_tensor(rt1[:], rt2[:], -BIG, rt1[:], ALU.mult, ALU.add),
                        lambda e: e.tensor_tensor(sel[:], sel[:], rt2[:], ALU.add),
                    ]
                fns += [
                    lambda e: e.tensor_tensor(rt2[:], sel[:], scores[:], ALU.mult),
                    lambda e: e.reduce_sum(rcol[:, 0:1], rt2[:], axis=AX.X),
                    lambda e: e.reciprocal(rcol[:, 1:2], rcol[:, 0:1]),
                    lambda e: e.tensor_scalar(rcol[:, 1:2], rcol[:, 1:2], 2.5, None, ALU.mult),
                    lambda e: e.tensor_scalar(rt2[:], rt2[:], rcol[:, 1:2], None, ALU.mult),
                ]
                prev = None
                for fn in fns:
                    prev = S.step("vector", [sg, c_gbB, c_ones, prev_rt] if prev is None else [prev], fn)
                rt = prev
                sqr = S.step("scalar", [rt], lambda e: e.activation(cwpad[:, 0:32], rt2[:], AF.Sqrt))
                prev_rt = sqr
                tr = S.step("sync", [sqr, bpv.get("cw_free")] +
                            (b_fence if ch == 0 else []),
                            lambda e: e.dma_start_transpose(B_cwT, cwpad[:]), weight=16)
                cwst = S.dma("sdma", [tr], cw_scr[:, cbase + i * 128: cbase + (i + 1) * 128],
                             B_cwT[0:32, 0:128])
                bpv["cw_free"] = cwst
                rt_done[i] = cwst
            slab_readers = [fp, lg]
            bpv["hread_all"] = None
        bpv["fp_last"] = fp
        bpv["xl_ev"] = prev_xl_ev
        bpv["sg_last"] = prev_logit_ev
        bpv["rt_last"] = prev_rt
        bpv["xn_free0"] = fp
        bpv["xn_free1"] = lg

        wld1 = {}
        wld2 = {}

        def ensure_w(kind, eidx, it):
            if eidx >= E:
                return
            dic = wld1 if kind == "w1" else wld2
            if (eidx, it) in dic:
                return
            n = eidx * NLT + it
            par = n % 2
            buf = B_w1(par) if kind == "w1" else B_w2(par)
            scr = d["w1t"] if kind == "w1" else d["w2t"]
            dic[(eidx, it)] = S.dma("gdma", [bpv.get(("wread", kind, par)), bpv.get("fp_last")],
                                    buf, scr[n * 128:(n + 1) * 128, :])
        for it0 in range(2):
            ensure_w("w1", 0, it0)
            ensure_w("w2", 0, it0)
        prev_hev = bpv.get("hev")
        prev_sqh = bpv.get("sqh")
        prev_yadd = bpv.get("yadd")
        cwld_prev = bpv.get("cwld")
        prev_bv = bpv.get("bv_last")
        hev_bank = {0: bpv.get("hev_b0"), 1: bpv.get("hev_b1")}
        ya_bank = {0: bpv.get("ya_b0"), 1: bpv.get("ya_b1")}
        cw_steps = {}

        def ensure_cw(eidx):
            if eidx in cw_steps or eidx >= E:
                return
            nonlocal cwld_prev, prev_bv
            half = eidx % 2
            cwld = S.dma("gdma", [rt_done[CT - 1] if eidx == 0 else None,
                                  bpv.get(("cwrow_read", half)), cwld_prev],
                         cwrow[0:1, half * CH:(half + 1) * CH],
                         cw_scr[eidx:eidx + 1, cbase:cbase + CH])
            cwld_prev = cwld
            bm = S.step("tensor", [cwld, prev_bv],
                        lambda e, half=half: e.matmul(ps_bcast[:], onesb[0:1, :],
                                                      cwrow[0:1, half * CH:half * CH + 512],
                                                      start=True, stop=True))
            bpv[("cwrow_read", half)] = bm
            bv = S.step("scalar", [bm, bpv.get(("cwB_read", half))],
                        lambda e, half=half: e.copy(cwB[:, half * 512:(half + 1) * 512], ps_bcast[:]))
            prev_bv = bv
            cw_steps[eidx] = bv
        ensure_cw(0)
        ensure_cw(1)
        for eidx in range(E):
            half = eidx % 2
            cwB_ev = cw_steps[eidx]
            for it in range(NLT):
                bk = it % 2
                deps = [wld1[(eidx, it)], bpv["xl_ev"], hev_bank[bk]]
                def h_fn(e, par=(eidx * NLT + it) % 2, bk=bk):
                    out = None
                    for lk in range(NLT):
                        out = e.matmul(ps_h[bk][:], B_w1(par)[:, lk * 128:(lk + 1) * 128],
                                       B_xlat(lk)[:, 0:512],
                                       start=(lk == 0), stop=(lk == NLT - 1))
                    return out
                hp = S.step("tensor", deps, h_fn)
                bpv[("wread", "w1", (eidx * NLT + it) % 2)] = hp
                if it + 2 < NLT:
                    ensure_w("w1", eidx, it + 2)
                else:
                    ensure_w("w1", eidx + 1, it + 2 - NLT)
                hev = S.step("vector", [hp, cwB_ev, prev_sqh],
                             lambda e, it=it, bk=bk, half=half: e.scalar_tensor_tensor(
                                 B_h(it)[:, 0:512], ps_h[bk][:], 0.0,
                                 cwB[:, half * 512:(half + 1) * 512], ALU.max, ALU.mult))
                hev_bank[bk] = hev
                prev_hev = hev
                sqh = S.step("scalar", [hev],
                             lambda e, it=it: e.activation(B_h(it)[:, 0:CH], B_h(it)[:, 0:CH], AF.Square))
                prev_sqh = sqh
                bpv[("hready", it)] = sqh
            bpv[("cwB_read", half)] = prev_hev
            ensure_cw(eidx + 1)
            ensure_cw(eidx + 2)
            for lt in range(NLT):
                bk = lt % 2
                deps = [wld2[(eidx, lt)], ya_bank[bk]] + [bpv[("hready", k2)] for k2 in range(NLT)]
                def y_fn(e, par=(eidx * NLT + lt) % 2, bk=bk):
                    out = None
                    for it in range(NLT):
                        out = e.matmul(ps_y[bk][:], B_w2(par)[:, it * 128:(it + 1) * 128],
                                       B_h(it)[:, 0:512],
                                       start=(it == 0), stop=(it == NLT - 1))
                    return out
                yp = S.step("tensor", deps, y_fn)
                bpv[("wread", "w2", (eidx * NLT + lt) % 2)] = yp
                if lt + 2 < NLT:
                    ensure_w("w2", eidx, lt + 2)
                else:
                    ensure_w("w2", eidx + 1, lt + 2 - NLT)
                if eidx == 0:
                    ya = S.step("vector", [yp, bpv.get("ysb_read"), prev_yadd],
                                lambda e, lt=lt, bk=bk: e.tensor_copy(
                                    YSB[:, lt * CH:lt * CH + 512], ps_y[bk][:]))
                else:
                    ya = S.step("vector", [yp, prev_yadd],
                                lambda e, lt=lt, bk=bk: e.tensor_tensor(
                                    YSB[:, lt * CH:lt * CH + 512], ps_y[bk][:],
                                    YSB[:, lt * CH:lt * CH + 512], ALU.add))
                ya_bank[bk] = ya
                prev_yadd = ya
                bpv["hev"] = prev_hev
        bpv["sqh"] = prev_sqh
        bpv["cwld"] = cwld_prev
        bpv["bv_last"] = prev_bv
        bpv["hev_b0"] = hev_bank[0]
        bpv["hev_b1"] = hev_bank[1]
        bpv["ya_b0"] = ya_bank[0]
        bpv["ya_b1"] = ya_bank[1]

        ybf = {}
        for lt in range(NLT):
            ybf[lt] = S.step("vector", [prev_yadd, prev_sqh, bpv.get(("hread2", lt))],
                             lambda e, lt=lt: e.tensor_copy(B_ybf(lt)[:, 0:CH], YSB[:, lt * CH:(lt + 1) * CH]))
        if debug_stop:
            return last_out
        wfc2 = {}
        for lt in range(NLT):
            wfc2[lt] = S.dma("gdma", [bpv[("wread", "w1", 0)], bpv[("wread", "w1", 1)],
                                      bpv[("wread", "w2", 0)], bpv[("wread", "w2", 1)],
                                      bpv.get(("fc2read", lt))],
                             B_fc2(lt), d["fc2t"][lt * 128:(lt + 1) * 128, :])
        prev_po = None
        prev_racc = bpv.get("racc")
        for i in range(CT):
            gi = cbase // 128 + i
            acc = S.dma("sync", [a_out[gi], gcl if (ch == 0 and i == 0) else None,
                                 st_b_out.get(gi - 1, bpv.get("b_out_m1") or pv.get("out_last1"))],
                        xst[i % 2][:], d["out"][gi * 128:(gi + 1) * 128, :])
            deps = [ybf[lt] for lt in range(NLT)] + list(wfc2.values()) + \
                   [prev_racc, bpv.get("rt_last")]
            def o_fn(e, i=i):
                out = None
                for lt in range(NLT):
                    for c in range(4):
                        out = e.matmul(ps_d0[c][:], B_ybf(lt)[:, i * 128:(i + 1) * 128],
                                       B_fc2(lt)[:, c * 512:(c + 1) * 512],
                                       start=(lt == 0), stop=(lt == NLT - 1))
                return out
            po = S.step("tensor", deps, o_fn)
            prev_po = po
            ra = S.step("vector", [po, acc],
                        lambda e, i=i: [e.tensor_tensor(xst[i % 2][:, c * 512:(c + 1) * 512], ps_d0[c][:],
                                                        xst[i % 2][:, c * 512:(c + 1) * 512], ALU.add)
                                        for c in range(4)][-1])
            prev_racc = ra
            so = S.dma("sync", [ra], d["out"][gi * 128:(gi + 1) * 128, :], xst[i % 2][:])
            st_b_out[gi] = so
            last_out.append(so)
        bpv["racc"] = prev_racc
        bpv["yadd"] = prev_po
        bpv["ysb_read"] = ybf[NLT - 1]
        for lt in range(NLT):
            bpv[("hread2", lt)] = prev_po
            bpv[("fc2read", lt)] = prev_po
        bpv["b_out_m2"] = st_b_out.get(cbase // 128 + CT - 2)
        bpv["b_out_m1"] = st_b_out.get(cbase // 128 + CT - 1)
        bpv["xl_ev"] = None
        bpv["fp_last"] = prev_po
        prev_xl_ev = None

    return last_out


AR_F32 = 13312
AR_BF16 = 77056
AR_I32 = 32


def build_fused(T, NB, SLAB=512, CH=512, debug_mid=False):
    NTOK = NB * T
    nc = bass.Bass("TRN2", target_bir_lowering=False)
    ins = {}

    def din(name, shape, dt=F32):
        ins[name] = (shape, dt)
        return nc.dram_tensor(name, shape, dt, kind="ExternalInput").ap()

    def dint(name, shape, dt=BF16):
        return nc.dram_tensor(name, shape, dt, kind="Internal").ap()

    x_in = din("x", [NTOK, 2048])
    out = nc.dram_tensor("out", [NTOK, 2048], F32, kind="ExternalOutput").ap()
    mid_kind = "ExternalOutput" if debug_mid else "Internal"
    x1 = nc.dram_tensor("x1", [NTOK, 2048], F32, kind=mid_kind).ap()
    x2 = nc.dram_tensor("x2", [NTOK, 2048], F32, kind=mid_kind).ap()
    x3 = nc.dram_tensor("x3", [NTOK, 2048], F32, kind=mid_kind).ap()
    ctx_scr = dint("ctx_scr", [128, 16 * T])

    d1 = dict(x=x_in, out=x1, ctx_scr=ctx_scr,
              wq=din("l0a_wq", [2048, 2048]), bq=din("l0a_bq", [2048]),
              wk=din("l0a_wk", [2048, 512]), bk=din("l0a_bk", [512]),
              wv=din("l0a_wv", [2048, 512]), bv=din("l0a_bv", [512]),
              wo=din("l0a_wo", [2048, 2048]), bo=din("l0a_bo", [2048]),
              ct=din("l0a_ct", [128, T]), st=din("l0a_st", [128, T]),
              p1t=din("l0a_p1t", [128, 128]), p2t=din("l0a_p2t", [128, 128]),
              dmask=din("l0a_dmask", [128, 2048]))
    d2 = dict(x=x1, out=x2,
              wq_gate_t=din("l0f_wq_gate_t", [5632, 2048], I32),
              wq_up_t=din("l0f_wq_up_t", [5632, 2048], I32),
              wq_down_t=din("l0f_wq_down_t", [5632, 2048], I32),
              s_gate_t=din("l0f_s_gate_t", [128, 44]), s_up_t=din("l0f_s_up_t", [128, 44]),
              s_down_row=din("l0f_s_down_row", [1, 2048]),
              scr_g=dint("scr_g", [5632, 2048]), scr_u=dint("scr_u", [5632, 2048]),
              scr_d=dint("scr_d", [5632, 2048]))
    d3 = dict(x=x2, out=x3, ctx_scr=ctx_scr,
              w_dkv=din("l1a_w_dkv", [2048, 512]), w_dq=din("l1a_w_dq", [2048, 512]),
              w_uk=din("l1a_w_uk", [512, 2048]), w_uv=din("l1a_w_uv", [512, 2048]),
              w_uq=din("l1a_w_uq", [512, 2048]), w_kr=din("l1a_w_kr", [2048, 64]),
              w_qr=din("l1a_w_qr", [512, 1024]), w_o=din("l1a_w_o", [2048, 2048]),
              b_o=din("l1a_b_o", [2048]),
              ct=din("l1a_ct", [128, T]), st=din("l1a_st", [128, T]),
              p1t=din("l1a_p1t", [128, 128]), p2t=din("l1a_p2t", [128, 128]))
    d4 = dict(x=x3, out=out,
              gate_w=din("l1f_gate_w", [32, 2048]), gate_bias=din("l1f_gate_bias", [32]),
              fc1t=din("l1f_fc1t", [1024, 2048]), fc2t=din("l1f_fc2t", [1024, 2048]),
              w1t=din("l1f_w1t", [32768, 1024]), w2t=din("l1f_w2t", [32768, 1024]),
              sut=din("l1f_sut", [4096, 2048]), sdt=din("l1f_sdt", [4096, 2048]),
              xnT_scr=dint("xnT_scr", [128, 16 * NTOK]),
              xnT_lo_scr=dint("xnT_lo_scr", [128, 16 * NTOK]),
              cw_scr=dint("cw_scr", [32, NTOK]))

    S = Sched(nc)
    S.use_arenas(AR_BF16, AR_F32, AR_I32)
    S.sbuf_reset()
    o1 = emit_l0_attn(S, T, NB, d1)
    S.barrier(o1)
    S.sbuf_reset()
    o2 = emit_l0_ffn(S, NTOK, d2, SLAB=SLAB)
    S.barrier(o2)
    S.sbuf_reset()
    o3 = emit_l1_attn(S, T, NB, d3)
    S.barrier(o3)
    S.sbuf_reset()
    o4 = emit_l1_ffn(S, NTOK, d4, SLAB=SLAB, CH=CH)
    S.finish_interleaved(final_waits=o4)
    return nc, ins


def make_inputs(T, weights):
    C0, S20, P1T0, P2T0 = rope_consts_l0(T)
    C1, S21, P1T1, P2T1 = rope_consts_l1(T)
    w = weights
    return dict(
        l0a_wq=w["L0_attn_Wq"], l0a_bq=w["L0_attn_bq"], l0a_wk=w["L0_attn_Wk"],
        l0a_bk=w["L0_attn_bk"], l0a_wv=w["L0_attn_Wv"], l0a_bv=w["L0_attn_bv"],
        l0a_wo=w["L0_attn_Wo"], l0a_bo=w["L0_attn_bo"],
        l0a_ct=C0, l0a_st=S20, l0a_p1t=P1T0, l0a_p2t=P2T0, l0a_dmask=causal_masks(512),
        l0f_wq_gate_t=_tq(w["L0_ffn_wq_gate"]), l0f_wq_up_t=_tq(w["L0_ffn_wq_up"]),
        l0f_wq_down_t=np.ascontiguousarray(w["L0_ffn_wq_down"].T).reshape(5632, 2048),
        l0f_s_gate_t=np.ascontiguousarray(w["L0_ffn_s_gate"].reshape(44, 128).T),
        l0f_s_up_t=np.ascontiguousarray(w["L0_ffn_s_up"].reshape(44, 128).T),
        l0f_s_down_row=np.ascontiguousarray(w["L0_ffn_s_down"].reshape(1, 2048)),
        l1a_w_dkv=w["L1_attn_W_DKV"], l1a_w_dq=w["L1_attn_W_DQ"], l1a_w_uk=w["L1_attn_W_UK"],
        l1a_w_uv=w["L1_attn_W_UV"], l1a_w_uq=w["L1_attn_W_UQ"], l1a_w_kr=w["L1_attn_W_KR"],
        l1a_w_qr=w["L1_attn_W_QR"], l1a_w_o=w["L1_attn_W_O"], l1a_b_o=w["L1_attn_b_O"],
        l1a_ct=C1, l1a_st=S21, l1a_p1t=P1T1, l1a_p2t=P2T1,
        l1f_gate_w=w["L1_ffn_gate_w"], l1f_gate_bias=w["L1_ffn_gate_bias"],
        l1f_fc1t=_t4(w["L1_ffn_fc1_w"], 8, 16).reshape(1024, 2048),
        l1f_fc2t=_t4(w["L1_ffn_fc2_w"], 16, 8, swap=True).reshape(1024, 2048),
        l1f_w1t=_tw(w["L1_ffn_w1"]), l1f_w2t=_tw(w["L1_ffn_w2"]),
        l1f_sut=_t4(w["L1_ffn_shared_up_w"], 32, 16).reshape(4096, 2048),
        l1f_sdt=np.ascontiguousarray(w["L1_ffn_shared_down_w"].T).reshape(4096, 2048),
    )


def _t4(W, a, b, swap=False):
    r = W.reshape(a, 128, b, 128)
    if swap:
        return np.ascontiguousarray(r.transpose(2, 3, 0, 1))
    return np.ascontiguousarray(r.transpose(0, 3, 2, 1))


def _tw(W):
    r = W.reshape(32, 8, 128, 8, 128)
    return np.ascontiguousarray(r.transpose(0, 1, 4, 3, 2)).reshape(32768, 1024)


def _tq(W):
    r = W.reshape(44, 128, 16, 128)
    return np.ascontiguousarray(r.transpose(0, 3, 2, 1)).reshape(5632, 2048)


from concourse.bass_utils import run_bass_kernel_spmd

NCORES = 8
TT = 2048
DD = 2048


def kernel(L0_attn_x, **kw):
    kw["L0_attn_x"] = L0_attn_x
    x = np.ascontiguousarray(kw["L0_attn_x"].astype(np.float32).reshape(16, TT, DD))
    xs = [np.ascontiguousarray(x[2 * c:2 * c + 2].reshape(2 * TT, DD)) for c in range(NCORES)]
    nc, ins_spec = build_fused(TT, 2, SLAB=512, CH=512)
    shared = make_inputs(TT, kw)
    in_maps = []
    for c in range(NCORES):
        m = dict(shared)
        m["x"] = xs[c]
        in_maps.append(m)
    res = run_bass_kernel_spmd(nc, in_maps, core_ids=list(range(NCORES)))
    out = np.concatenate([res.results[c]["out"].reshape(2, TT, DD) for c in range(NCORES)], axis=0)
    return out.astype(np.float32)


# version 07 (final)
# speedup vs baseline: 1.3126x; 1.1474x over previous
from contextlib import ExitStack
import numpy as np
import concourse.bass as bass
import concourse.mybir as mybir

F32 = mybir.dt.float32
F32R = mybir.dt.float32r
BF16 = mybir.dt.bfloat16
I32 = mybir.dt.int32
AF = mybir.ActivationFunctionType
ALU = mybir.AluOpType
AX = mybir.AxisListType

STREAMS = {
    "tensor": ("tensor", False),
    "vector": ("vector", False),
    "scalar": ("scalar", False),
    "gpsimd": ("gpsimd", False),
    "sync": ("sync", True),
    "gdma": ("gpsimd", True),
    "sdma": ("scalar", True),
}
SHARED_QUEUE = {"gpsimd": "gp", "gdma": "gp", "scalar": "sc", "sdma": "sc"}


class Step:
    __slots__ = ("stream", "count_after", "sid", "slot", "target", "epoch")

    def __init__(self, stream, count_after, sid, slot=None, target=0, epoch=0):
        self.stream = stream
        self.count_after = count_after
        self.sid = sid
        self.slot = slot
        self.target = target
        self.epoch = epoch


class Sched:
    DMA_POOL = 6

    def __init__(self, nc):
        self.nc = nc
        self.ctx = ExitStack()
        self.steps = []
        self._semcount = 0
        self._sid = 0
        self.epoch = -1
        self.esems = []
        self.edma_sems = []
        self.ecounts = []
        self.eslot_counts = []
        self._new_epoch()

    def _new_epoch(self):
        self.epoch += 1
        self.esems.append({k: self.sem(f"e{self.epoch}_{k}") for k in STREAMS if not STREAMS[k][1]})
        self.edma_sems.append({k: [self.sem(f"e{self.epoch}_{k}{j}") for j in range(self.DMA_POOL)]
                               for k in STREAMS if STREAMS[k][1]})
        self.ecounts.append({k: 0 for k in STREAMS})
        self.eslot_counts.append({k: [0] * self.DMA_POOL for k in STREAMS if STREAMS[k][1]})
        self.dma_rr = {k: 0 for k in STREAMS if STREAMS[k][1]}

    def sem(self, name):
        self._semcount += 1
        return self.ctx.enter_context(self.nc.semaphore(f"{name}_{self._semcount}"))

    def use_arenas(self, bf16_cols, f32_cols, i32_cols=0):
        self._arena = {}
        for dt, cols, nm in ((BF16, bf16_cols, "bf16"), (F32, f32_cols, "f32"), (I32, i32_cols, "i32")):
            if cols:
                t = self.ctx.enter_context(self.nc.sbuf_tensor(f"arena_{nm}", [128, cols], dt))
                self._arena[dt] = [t, cols, 0]

    def sbuf_reset(self):
        for v in self._arena.values():
            v[2] = 0

    def sbuf(self, name, shape, dtype, **kw):
        ar = getattr(self, "_arena", None)
        if ar and dtype in ar:
            t, cap, off = ar[dtype]
            bpe = 2 if dtype == BF16 else 4
            align = 64 // bpe
            off = (off + align - 1) // align * align
            need = shape[1]
            assert off + need <= cap, f"arena {dtype} overflow at {name}: {off}+{need}>{cap}"
            ar[dtype][2] = off + need
            return t[0:shape[0], off:off + need]
        return self.ctx.enter_context(self.nc.sbuf_tensor(name, shape, dtype, **kw))

    def barrier(self, dep_steps):
        self._pending_bar = {k: list(dep_steps) for k in STREAMS}
        self._new_epoch()

    def psum(self, name, shape, dtype=F32):
        return self.ctx.enter_context(self.nc.psum_tensor(name, shape, dtype))

    def global_psums(self):
        if not hasattr(self, "_gps"):
            self._gps = {k: self.psum("gps_" + k, [128, 512]) for k in
                         ["q", "p1", "p2", "sc0", "sc1", "ctx", "bc", "d"]}
            self._gps["rs"] = self._gps["bc"]
        return self._gps

    def psum_open(self, name, shape, dtype=F32):
        cm = self.nc.psum_tensor(name, shape, dtype)
        h = cm.__enter__()
        return h, (lambda: cm.__exit__(None, None, None))

    def step(self, stream, deps, fn, weight=None):
        if weight is None:
            weight = 16 if STREAMS[stream][1] else 1
        deps = [d for d in deps if d is not None]
        pb = getattr(self, "_pending_bar", None)
        if pb and pb.get(stream):
            deps = deps + pb.pop(stream)
        self._sid += 1
        ep = self.epoch
        if STREAMS[stream][1]:
            slot = self.dma_rr[stream]
            self.dma_rr[stream] = (slot + 1) % self.DMA_POOL
            self.eslot_counts[ep][stream][slot] += weight
            st = Step(stream, 0, self._sid, slot,
                      self.eslot_counts[ep][stream][slot], ep)
        else:
            self.ecounts[ep][stream] += weight
            st = Step(stream, self.ecounts[ep][stream], self._sid, epoch=ep)
        self.steps.append((stream, deps, fn, weight, st))
        return st

    def dma_fence(self, stream):
        ep = self.epoch
        return [Step(stream, 0, 0, slot=j, target=self.eslot_counts[ep][stream][j], epoch=ep)
                for j in range(self.DMA_POOL) if self.eslot_counts[ep][stream][j] > 0]

    def dma(self, stream, deps, out, in_, **kw):
        return self.step(stream, deps, lambda e: e.dma_start(out=out, in_=in_, **kw), 16)

    def dmaT(self, stream, deps, out, in_):
        return self.step(stream, deps, lambda e: e.dma_start_transpose(out, in_), 16)

    def finish(self, final_waits=()):
        for k in STREAMS:
            self.sems[k] = self.sem("st_" + k)
        waited = {k: {j: 0 for j in STREAMS} for k in STREAMS}
        by_stream = {k: [] for k in STREAMS}
        for stream, deps, fn, weight, st in self.steps:
            by_stream[stream].append((deps, fn, weight, st))

        def run_stream(stream, eng):
            w = waited[stream]
            for deps, fn, weight, st in by_stream[stream]:
                for dp in deps:
                    if dp.stream == stream:
                        if dp.count_after > w[stream]:
                            eng.wait_ge(self.sems[stream], dp.count_after)
                            w[stream] = dp.count_after
                    else:
                        if dp.count_after > w[dp.stream]:
                            eng.wait_ge(self.sems[dp.stream], dp.count_after)
                            w[dp.stream] = dp.count_after
                ins = fn(eng)
                assert ins is not None, f"step fn returned None (stream {stream})"
                ins.then_inc(self.sems[stream], weight)

        with self.nc.Block() as block:
            @block.sync
            def _(e):
                run_stream("sync", e)

            @block.scalar
            def _(e):
                run_stream("scalar", e)
                run_stream("sdma", e)

            @block.vector
            def _(e):
                run_stream("vector", e)

            @block.tensor
            def _(e):
                run_stream("tensor", e)

            @block.gpsimd
            def _(e):
                run_stream("gpsimd", e)
                run_stream("gdma", e)
                for st in final_waits:
                    e.wait_ge(self.sems[st.stream], st.count_after)

    def finish_interleaved(self, final_waits=()):
        waited = {}
        for k in set(list(STREAMS) + list(SHARED_QUEUE.values())):
            waited[k] = {}
            waited[k, "dma"] = {}

        def run(eng, streams_here):
            wkey = SHARED_QUEUE.get(streams_here[0], streams_here[0])
            w = waited[wkey]
            wd = waited[wkey, "dma"]
            own = set(streams_here)
            for stream, deps, fn, weight, st in self.steps:
                if stream not in own:
                    continue
                for dp in deps:
                    if STREAMS[dp.stream][1]:
                        key = (dp.epoch, dp.stream, dp.slot)
                        if dp.target > wd.get(key, 0):
                            eng.wait_ge(self.edma_sems[dp.epoch][dp.stream][dp.slot], dp.target)
                            wd[key] = dp.target
                    else:
                        key = (dp.epoch, dp.stream)
                        if dp.count_after > w.get(key, 0):
                            eng.wait_ge(self.esems[dp.epoch][dp.stream], dp.count_after)
                            w[key] = dp.count_after
                if STREAMS[stream][1]:
                    prev_target = st.target - weight
                    key = (st.epoch, stream, st.slot)
                    if prev_target > 0 and prev_target > wd.get(key, 0):
                        eng.wait_ge(self.edma_sems[st.epoch][stream][st.slot], prev_target)
                        wd[key] = prev_target
                ins = fn(eng)
                assert ins is not None
                if STREAMS[stream][1]:
                    if not isinstance(ins, (list, tuple)):
                        ins = [ins]
                    assert weight == 16 * len(ins), f"dma weight {weight} != 16*{len(ins)}"
                    for one in ins:
                        one.then_inc(self.edma_sems[st.epoch][stream][st.slot], 16)
                else:
                    if isinstance(ins, (list, tuple)):
                        ins = ins[-1]
                    ins.then_inc(self.esems[st.epoch][stream], weight)

        with self.nc.Block() as block:
            @block.sync
            def _(e):
                run(e, ["sync"])

            @block.scalar
            def _(e):
                run(e, ["scalar", "sdma"])

            @block.vector
            def _(e):
                run(e, ["vector"])

            @block.tensor
            def _(e):
                run(e, ["tensor"])

            @block.gpsimd
            def _(e):
                run(e, ["gpsimd", "gdma"])
                done = {}
                for st in final_waits:
                    key = (st.epoch, st.stream, st.slot)
                    tgt = st.target if STREAMS[st.stream][1] else st.count_after
                    if done.get(key, 0) >= tgt:
                        continue
                    done[key] = tgt
                    if STREAMS[st.stream][1]:
                        e.wait_ge(self.edma_sems[st.epoch][st.stream][st.slot], tgt)
                    else:
                        e.wait_ge(self.esems[st.epoch][st.stream], st.count_after)

        self.ctx.close()



def rope_consts_l0(T):
    f = np.arange(64, dtype=np.float64)
    freqs = 10000.0 ** (-2.0 * f / 2048.0)
    t = np.arange(T, dtype=np.float64)
    ang = np.outer(t, freqs)
    C = np.zeros((128, T), np.float32)
    S2 = np.zeros((128, T), np.float32)
    for i in range(64):
        C[2 * i] = np.cos(ang[:, i])
        C[2 * i + 1] = np.sin(ang[:, i])
        S2[2 * i] = -np.sin(ang[:, i])
        S2[2 * i + 1] = np.cos(ang[:, i])
    P1T = np.zeros((128, 128), np.float32)
    P2T = np.zeros((128, 128), np.float32)
    for i in range(64):
        P1T[2 * i, 2 * i] = 1.0
        P1T[2 * i, 2 * i + 1] = 1.0
        P2T[2 * i + 1, 2 * i] = 1.0
        P2T[2 * i + 1, 2 * i + 1] = 1.0
    return C, S2, P1T, P2T


def rope_consts_l1(T):
    half = 32
    inv_freq = 1.0 / 10000.0 ** (np.arange(half, dtype=np.float64) / half)
    t = np.arange(T, dtype=np.float64)
    fr = np.outer(t, inv_freq)
    C = np.zeros((128, T), np.float32)
    S2 = np.zeros((128, T), np.float32)
    P1T = np.zeros((128, 128), np.float32)
    P2T = np.zeros((128, 128), np.float32)
    for blk in range(2):
        b = blk * 64
        for i in range(half):
            C[b + 2 * i] = np.cos(fr[:, i])
            C[b + 2 * i + 1] = np.sin(fr[:, i])
            S2[b + 2 * i] = -np.sin(fr[:, i])
            S2[b + 2 * i + 1] = np.cos(fr[:, i])
            P1T[b + i, b + 2 * i] = 1.0
            P1T[b + i, b + 2 * i + 1] = 1.0
            P2T[b + i + 32, b + 2 * i] = 1.0
            P2T[b + i + 32, b + 2 * i + 1] = 1.0
    return C, S2, P1T, P2T


def causal_masks(TCH=512):
    m = np.zeros((128, 4 * TCH), np.float32)
    for r in range(4):
        for sl in range(128):
            lo = r * 128 + sl
            if lo < TCH:
                m[sl, r * TCH + lo:(r + 1) * TCH] = 1.0
    return m


NHEAD = 16
NKV = 4
GRP = 4
D = 2048
SCALE_OUT = 2048.0 ** -0.5


def emit_l0_attn(S, T, NB, d):
    NT = T // 128
    TC = T // 512
    sb = S.sbuf

    xst0 = sb("l0a_xst0", [128, D], F32)
    xst = [xst0, xst0]
    xn = [sb(f"l0a_xn{i}", [128, D], BF16) for i in range(2)]
    stats = sb("l0a_stats", [128, 24], F32)
    mv = [sb(f"l0a_mv{i}", [128, 2], F32) for i in range(2)]
    rstd = [sb(f"l0a_rstd{i}", [128, 1], F32) for i in range(2)]
    bigA = sb("l0a_bigA", [128, 16 * 2048], BF16)
    ct = sb("l0a_ct", [128, T], F32)
    st = sb("l0a_st", [128, T], F32)
    p1t = sb("l0a_p1t", [128, 128], F32)
    p2t = sb("l0a_p2t", [128, 128], F32)
    qf32 = sb("l0a_qf32", [128, 512], F32)
    dmask = sb("l0a_dmask", [128, 4 * 512], BF16)
    w_sl = [sb(f"l0a_w{i}", [128, D], BF16) for i in range(4)]
    bqo_row = sb("l0a_bqo", [1, D], F32)
    bk_row = sb("l0a_bk", [1, 512], F32)
    bv_row = sb("l0a_bv", [1, 512], F32)
    ones_r = sb("l0a_ones_r", [1, 512], F32)
    ones_c = sb("l0a_ones_c", [128, 1], BF16)
    eps_c = sb("l0a_eps", [128, 1], F32)
    qT = sb("l0a_qT", [128, T], BF16)
    kT = sb("l0a_kT", [128, T], BF16)
    v_tm = sb("l0a_vtm", [128, NT * 512], BF16)
    tmp1 = sb("l0a_tmp1", [128, 512], F32)
    tmp2 = sb("l0a_tmp2", [128, 512], F32)
    expT = sb("l0a_expT", [128, NT * 512], BF16)
    recip = sb("l0a_recip", [1, 512], F32)
    recipB = sb("l0a_recipB", [128, 512], F32)
    ctx_sb = [sb(f"l0a_ctxsb{i}", [128, 512], BF16) for i in range(2)]
    ctxg = [sb(f"l0a_ctxg{i}", [128, 2048], BF16) for i in range(2)]

    scr = d["ctx_scr"]

    c_tab = S.dma("gdma", [], ct[:], d["ct"])
    S.dma("gdma", [], st[:], d["st"])
    S.dma("gdma", [], p1t[:], d["p1t"])
    S.dma("gdma", [], p2t[:], d["p2t"])
    c_msk = S.dma("gdma", [], dmask[:], d["dmask"])
    c_bk = S.dma("gdma", [], bk_row[:], d["bk"].rearrange("(a m) -> a m", a=1))
    c_bv = S.dma("gdma", [], bv_row[:], d["bv"].rearrange("(a m) -> a m", a=1))
    c_ones = S.step("vector", [], lambda e: (e.memset(ones_r[:], 1.0), e.memset(ones_c[:], 1.0),
                                             e.memset(eps_c[:], 1e-5))[-1])

    last_out = []
    pv = {}

    for b in range(NB):
        c_bq = S.dma("gdma", [pv.get("po_last")], bqo_row[:], d["bq"].rearrange("(a m) -> a m", a=1))

        G = S.global_psums()
        ps_q, ps_p1, ps_p2 = G["q"], G["p1"], G["p2"]
        ps_ctx, ps_bc = G["ctx"], G["bc"]
        ps_rs = ps_bc
        ps_sc = [G["sc0"], G["sc1"]]
        ps_out4 = [G["sc0"], G["sc1"], G["ctx"], G["bc"]]

        st_norm = [None] * NT
        st_tr = [None] * NT
        st_ag = [None] * NT
        for i in range(NT):
            dep_war = st_norm[i - 1] if i >= 1 else pv.get("out_last1")
            ld = S.dma("sync", [dep_war], xst[i % 2][:],
                       d["x"][b * T + i * 128: b * T + (i + 1) * 128, :])
            bs = S.step("vector", [ld, st_ag[i - 1] if i >= 1 else pv.get("ag_last")],
                        lambda e, i=i: [e.bn_stats(stats[:, 6 * k:6 * k + 6],
                                                   xst[i % 2][:, 512 * k:512 * (k + 1)]) for k in range(4)][-1])
            st_ag[i] = S.step("vector", [bs], lambda e, i=i: e.bn_aggr(mv[i % 2][:], stats[:]))
            ln1 = S.step("scalar", [st_ag[i], st_norm[i - 2] if i >= 2 else pv.get("norm_last"), c_ones],
                         lambda e, i=i: e.activation(rstd[i % 2][:], mv[i % 2][:, 1:2], AF.Ln, bias=eps_c[:, 0:1]))
            ln2 = S.step("scalar", [ln1], lambda e, i=i: e.activation(rstd[i % 2][:], rstd[i % 2][:], AF.Exp, scale=-0.5))
            st_norm[i] = S.step("vector", [ln2, st_tr[i - 2] if i >= 2 else pv.get("tr_last")],
                                lambda e, i=i: e.tensor_scalar(xn[i % 2][:], xst[i % 2][:], mv[i % 2][:, 0:1],
                                                               rstd[i % 2][:, 0:1], ALU.subtract, ALU.mult))
            deps_tr = [st_norm[i]] + ([pv.get("po_last")] if (b > 0 and i == 0) else [])
            st_tr[i] = S.step("sync", deps_tr,
                              lambda e, i=i: [e.dma_start_transpose(
                                  bigA[:, dt * T + i * 128: dt * T + (i + 1) * 128],
                                  xn[i % 2][:, dt * 128:(dt + 1) * 128]) for dt in range(16)],
                              weight=16 * 16)

        units = [("v", m) for m in range(4)]
        for h in range(NHEAD):
            if h % GRP == 0:
                units.append(("k", h // GRP))
            units.append(("q", h))

        def bufof(kind, idx):
            return w_sl[(2 + idx % 2) if kind != "q" else (idx % 2)]

        wsrc_of = {"q": d["wq"], "k": d["wk"], "v": d["wv"]}
        wld = {}
        last_proj_of = {}

        def ensure_wld(ui):
            if ui >= len(units) or units[ui] in wld:
                return
            kind, idx = units[ui]
            buf = bufof(kind, idx)
            wsrc = wsrc_of[kind][:, idx * 128:(idx + 1) * 128]
            key = ("q", idx % 2) if kind == "q" else ("kv", idx % 2)
            dep = last_proj_of.get(key, pv.get("proj_last") if b > 0 else None)
            wld[(kind, idx)] = S.dma("gdma", [dep],
                                     buf[:].rearrange("p (dt m) -> p dt m", dt=16),
                                     wsrc.rearrange("(dt p) m -> p dt m", p=128))
        ensure_wld(0)
        ensure_wld(1)

        st_ropeB = {}
        prev_pm = pv.get("pm_last")
        prev_ropeA = None
        prev_ropeB = None
        last_reader_of = {}
        prev_stt = pv.get("stt_last")
        prev_rec = pv.get("rec_last")
        prev_bcp = pv.get("bcp_last")
        prev_ctx_dma = {0: pv.get("ctxd_m2"), 1: pv.get("ctxd_m1")}
        st_ctx_dma = []
        psq_prev = pv.get("psq_free")
        exp_slot_free = pv.get("avs_last")
        sc_bank_prev = [pv.get("avs_last"), pv.get("avs_last")]
        proj_last = None
        vT_done = {}
        prev_vtr = pv.get("vtr_last")

        for ui, (kind, idx) in enumerate(units):
            ensure_wld(ui)
            wbuf = bufof(kind, idx)
            if kind == "q":
                brow, boff = bqo_row, idx * 128
                bdep = c_bq
            elif kind == "k":
                brow, boff = bk_row, idx * 128
                bdep = c_bk
            else:
                brow, boff = bv_row, idx * 128
                bdep = c_bv
            dstT = qT if kind == "q" else kT
            for tc in range(TC):
                deps = [wld[(kind, idx)], psq_prev, bdep, c_ones] + [st_tr[min(4 * tc + j, NT - 1)] for j in range(4)]
                if b > 0 and ui == 0 and tc == 0:
                    deps.append(pv.get("radd_last"))
                def pr_fn(e, wbuf=wbuf, brow=brow, boff=boff, tc=tc):
                    e.matmul(ps_q[:], brow[0:1, boff:boff + 128], ones_r[0:1, :], start=True, stop=False)
                    last = None
                    for dt in range(16):
                        last = e.matmul(ps_q[:], wbuf[:, dt * 128:(dt + 1) * 128],
                                        bigA[:, dt * T + tc * 512: dt * T + tc * 512 + 512],
                                        start=False, stop=(dt == 15))
                    return last
                stp = S.step("tensor", deps, pr_fn)
                proj_last = stp
                if tc == TC - 1:
                    last_proj_of[("q", idx % 2) if kind == "q" else ("kv", idx % 2)] = stp
                    ensure_wld(ui + 1)
                    ensure_wld(ui + 2)
                if kind == "v":
                    ev_war = vT_done.get(idx - 1, pv.get("avs_last"))
                    ev = S.step("scalar", [stp, ev_war] if ev_war else [stp],
                                lambda e, dstT=dstT, tc=tc: e.copy(dstT[:, tc * 512:(tc + 1) * 512], ps_q[:]))
                    psq_prev = ev
                    continue
                ev = S.step("scalar", [stp] + ([prev_pm] if prev_pm else []),
                            lambda e: e.copy(qf32[:], ps_q[:]))
                psq_prev = ev
                pm = S.step("tensor", [ev, prev_ropeA],
                            lambda e: (
                                e.matmul(ps_p1[:], p1t[:], qf32[:], start=True, stop=True),
                                e.matmul(ps_p2[:], p2t[:], qf32[:], start=True, stop=True))[1])
                prev_pm = pm
                ra = S.step("vector", [pm, prev_ropeB, c_tab],
                            lambda e, tc=tc: (
                                e.tensor_tensor(tmp1[:], ps_p1[:], ct[:, tc * 512:(tc + 1) * 512], ALU.mult),
                                e.tensor_tensor(tmp2[:], ps_p2[:], st[:, tc * 512:(tc + 1) * 512], ALU.mult))[1])
                prev_ropeA = ra
                if kind == "k":
                    rb_war = last_reader_of.get(("k", idx - 1), vT_done.get(3))
                else:
                    rb_war = last_reader_of.get(("q", idx - 1), pv.get("avs_last"))
                rb_ = S.step("vector", [ra] + ([rb_war] if rb_war else []),
                             lambda e, dstT=dstT, tc=tc: e.tensor_tensor(
                                 dstT[:, tc * 512:(tc + 1) * 512], tmp1[:], tmp2[:], ALU.add))
                prev_ropeB = rb_
                st_ropeB[(kind, idx, tc)] = rb_

            if kind == "v":
                war_vtm = pv.get("avs_last") if (b > 0 and idx == 0) else None
                vt = S.step("sync", [psq_prev] + ([war_vtm] if war_vtm else []),
                            lambda e, m=idx: [e.dma_start_transpose(
                                v_tm[:, i * 512 + m * 128: i * 512 + (m + 1) * 128],
                                kT[:, i * 128:(i + 1) * 128]) for i in range(NT)],
                            weight=16 * NT)
                vT_done[idx] = vt
                continue
            if kind == "k":
                continue

            h = idx
            g = h // GRP
            for tc in range(TC):
                smax = min(4 * (tc + 1), NT)
                rdy = [None] * smax
                for s_ in range(smax):
                    deps = [st_ropeB[("q", h, tc)], st_ropeB[("k", g, s_ // 4)],
                            sc_bank_prev[s_ % 2]]
                    scs = S.step("tensor", deps,
                                 lambda e, s_=s_, tc=tc: e.matmul(
                                     ps_sc[s_ % 2][:], kT[:, s_ * 128:(s_ + 1) * 128],
                                     qT[:, tc * 512:(tc + 1) * 512], start=True, stop=True))
                    ex = S.step("scalar", [scs, exp_slot_free],
                                lambda e, s_=s_: e.activation(expT[:, s_ * 512:(s_ + 1) * 512],
                                                              ps_sc[s_ % 2][:], AF.Exp))
                    sc_bank_prev[s_ % 2] = ex
                    if s_ >= 4 * tc:
                        r = s_ - 4 * tc
                        rdy[s_] = S.step("vector", [ex, c_msk],
                                         lambda e, s_=s_, r=r: e.tensor_tensor(
                                             expT[:, s_ * 512:(s_ + 1) * 512], expT[:, s_ * 512:(s_ + 1) * 512],
                                             dmask[:, r * 512:(r + 1) * 512], ALU.mult))
                    else:
                        rdy[s_] = ex
                avs = None
                for s_ in range(smax):
                    deps = [rdy[s_], vT_done[g]]
                    if s_ == 0:
                        deps += [prev_rec, prev_stt, prev_bcp, c_ones]
                    def av_fn(e, s_=s_, g=g, first=(s_ == 0), last=(s_ == smax - 1)):
                        e.matmul(ps_rs[0:1, :], ones_c[:], expT[:, s_ * 512:(s_ + 1) * 512],
                                 start=first, stop=last)
                        return e.matmul(ps_ctx[:], v_tm[:, s_ * 512 + g * 128: s_ * 512 + g * 128 + 128],
                                        expT[:, s_ * 512:(s_ + 1) * 512], start=first, stop=last)
                    avs = S.step("tensor", deps, av_fn)
                exp_slot_free = avs
                last_reader_of[("q", h)] = avs
                if h % GRP == GRP - 1:
                    last_reader_of[("k", g)] = avs
                rec = S.step("vector", [avs], lambda e: e.reciprocal(recip[:], ps_rs[0:1, :]))
                prev_rec = rec
                bcm = S.step("tensor", [rec, prev_bcp],
                             lambda e: e.matmul(ps_bc[:], ones_r[0:1, 0:128], recip[0:1, :],
                                                start=True, stop=True))
                rbc = S.step("scalar", [bcm, prev_stt],
                             lambda e: e.copy(recipB[:], ps_bc[:]))
                prev_bcp = rbc
                j = (h * TC + tc) % 2
                stt = S.step("vector", [rbc, avs, prev_ctx_dma[j]],
                             lambda e, j=j: e.scalar_tensor_tensor(
                                 ctx_sb[j][:], ps_ctx[:], SCALE_OUT, recipB[:], ALU.mult, ALU.mult))
                prev_stt = stt
                cd = S.dma("sdma", [stt], scr[:, h * T + tc * 512: h * T + (tc + 1) * 512], ctx_sb[j][:])
                prev_ctx_dma[j] = cd
                st_ctx_dma.append(cd)

        xnT_free = psq_prev
        wo_lds = []
        for dt in range(16):
            wo_lds.append(S.dma("gdma", [xnT_free] if dt == 0 else [],
                                bigA[:, dt * 2048:(dt + 1) * 2048],
                                d["wo"][dt * 128:(dt + 1) * 128, :]))
        c_bo = S.dma("gdma", [proj_last], bqo_row[:], d["bo"].rearrange("(a m) -> a m", a=1))
        st_radd = [None] * NT
        st_out = [None] * NT
        st_cg = [None] * NT
        st_po = [None] * NT
        po_last = None
        scr3 = scr.rearrange("p (dt t) -> p dt t", dt=16)
        for i in range(NT):
            cg_deps = (S.dma_fence("sdma") if i == 0 else []) + \
                      [st_po[i - 2] if i >= 2 else pv.get("po_last")]
            st_cg[i] = S.dma("sync", cg_deps,
                             ctxg[i % 2][:].rearrange("p (dt t) -> p dt t", dt=16),
                             scr3[:, :, i * 128:(i + 1) * 128])
            x2 = S.dma("sync", [st_out[i - 1] if i >= 1 else pv.get("out_last1"),
                                st_norm[NT - 1] if i < 2 else None],
                       xst[i % 2][:], d["x"][b * T + i * 128: b * T + (i + 1) * 128, :])
            deps = wo_lds + [st_cg[i], c_bo, st_radd[i - 1] if i >= 1 else pv.get("radd_last")]
            if i == 0:
                deps.append(prev_stt)
            def po_fn(e, i=i):
                for c in range(4):
                    e.matmul(ps_out4[c][:], ones_r[0:1, 0:128],
                             bqo_row[0:1, c * 512:(c + 1) * 512], start=True, stop=False)
                last = None
                for dt in range(16):
                    for c in range(4):
                        last = e.matmul(ps_out4[c][:],
                                        ctxg[i % 2][:, dt * 128:(dt + 1) * 128],
                                        bigA[:, dt * 2048 + c * 512: dt * 2048 + (c + 1) * 512],
                                        start=False, stop=(dt == 15))
                return last
            po = S.step("tensor", deps, po_fn)
            po_last = po
            st_po[i] = po_last
            ra = S.step("vector", [po_last, x2],
                        lambda e, i=i: [e.tensor_tensor(xst[i % 2][:, c * 512:(c + 1) * 512],
                                                        ps_out4[c][:], xst[i % 2][:, c * 512:(c + 1) * 512],
                                                        ALU.add) for c in range(4)][-1])
            st_radd[i] = ra
            st_out[i] = S.dma("sync", [ra], d["out"][b * T + i * 128: b * T + (i + 1) * 128, :], xst[i % 2][:])
            last_out.append(st_out[i])

        pv = dict(norm_last=st_norm[NT - 1], tr_last=st_tr[NT - 1], ag_last=st_ag[NT - 1],
                  po_last=po_last, psq_free=psq_prev, radd_last=st_radd[NT - 1],
                  stt_last=prev_stt, rec_last=prev_rec, avs_last=exp_slot_free,
                  proj_last=proj_last, bcp_last=prev_bcp, pm_last=prev_pm,
                  out_last2=st_out[NT - 2] if NT >= 2 else None,
                  out_last1=st_out[NT - 1], ctxd_m2=prev_ctx_dma[0], ctxd_m1=prev_ctx_dma[1],
                  vtr_last=vT_done.get(3))

    return last_out


D = 2048
HID = 5632
NH = HID // 128


def emit_l0_ffn(S, NTOK, d, SLAB=1024):
    NSL = NTOK // SLAB
    NT = SLAB // 128
    TC = SLAB // 512
    sb = S.sbuf
    G = S.global_psums()
    assert TC == 1
    ps_g = [G["q"], G["p1"]]
    ps_u = [G["p2"], G["sc0"]]
    ps_d0 = [G["sc1"], G["ctx"], G["bc"], G["d"]]
    ps_d1 = [G["q"], G["p1"], G["p2"], G["sc0"]]

    xst = [sb(f"l0f_xst{i}", [128, D], F32) for i in range(2)]
    xn = [sb(f"l0f_xn{i}", [128, D], BF16) for i in range(2)]
    stats = sb("l0f_stats", [128, 24], F32)
    mv = [sb(f"l0f_mv{i}", [128, 2], F32) for i in range(2)]
    rstd = [sb(f"l0f_rstd{i}", [128, 1], F32) for i in range(2)]
    eps_c = sb("l0f_eps", [128, 1], F32)
    xnT = sb("l0f_xnT", [128, 16 * SLAB], BF16)
    wbf = [sb(f"l0f_wbf{i}", [128, D], BF16) for i in range(4)]
    sgu = sb("l0f_sgu", [128, 2 * NH], F32)
    sd_row = sb("l0f_sdrow", [1, D], F32)
    sdB = sb("l0f_sdB", [128, D], F32)
    rtmp = sb("l0f_rtmp", [128, 2048], F32)
    wg = [sb(f"l0f_wg{i}", [128, D], BF16) for i in range(4)]
    wu = [sb(f"l0f_wu{i}", [128, D], BF16) for i in range(4)]
    wd = [sb(f"l0f_wd{i}", [128, D], BF16) for i in range(4)]
    g_sb = sb("l0f_gsb", [128, 1024], BF16)
    hT = sb("l0f_hT", [128, NH * SLAB], BF16)

    scr_g, scr_u, scr_d = d["scr_g"], d["scr_u"], d["scr_d"]

    c_eps = S.step("vector", [], lambda e: e.memset(eps_c[:], 1e-5))
    ones_rr = sb("l0f_ones_rr", [1, 128], F32)
    c_o2 = S.step("vector", [c_eps], lambda e: e.memset(ones_rr[:], 1.0))
    c_sg = S.dma("gdma", [], sgu[:, 0:NH], d["s_gate_t"])
    c_su = S.dma("gdma", [], sgu[:, NH:2 * NH], d["s_up_t"])
    c_sd = S.dma("gdma", [], sd_row[:], d["s_down_row"])
    sdb_prev = None
    for c in range(4):
        mm = S.step("tensor", [c_sd, c_o2, sdb_prev],
                    lambda e, c=c: e.matmul(G["bc"][:], ones_rr[0:1, :],
                                            sd_row[0:1, c * 512:(c + 1) * 512], start=True, stop=True))
        sdb_prev = S.step("scalar", [mm], lambda e, c=c: e.copy(sdB[:, c * 512:(c + 1) * 512], G["bc"][:]))
    c_sdb = sdb_prev

    prep_units = [(k, ht) for k in ("g", "u") for ht in range(NH)] + [("d", ht) for ht in range(NH)]
    prep_done = {}
    pu_prev = {0: None, 1: None, 2: None, 3: None}
    dq_prev = {0: None, 1: None, 2: None, 3: None}
    for n, (k, ht) in enumerate(prep_units):
        par = n % 4
        srcM = {"g": d["wq_gate_t"], "u": d["wq_up_t"], "d": d["wq_down_t"]}[k]
        scr = {"g": scr_g, "u": scr_u, "d": scr_d}[k]
        ld = S.dma("gdma", [pu_prev[par]], wbf[par][:], srcM[ht * 128:(ht + 1) * 128, :])
        dq = S.step("vector", [ld, dq_prev[par]],
                    lambda e, par=par: e.tensor_scalar(wbf[par][:], wbf[par][:], -128.0, None, ALU.add))
        dq_prev[par] = dq
        stc = S.dma("sdma", [dq], scr[ht * 128:(ht + 1) * 128, :], wbf[par][:])
        pu_prev[par] = stc
        prep_done[(k, ht)] = stc

    last_out = []
    pv = {}

    for sl in range(NSL):
        base = sl * SLAB
        st_norm = [None] * NT
        st_tr = [None] * NT
        st_ag = [None] * NT
        for i in range(NT):
            dep_war = st_norm[i - 2] if i >= 2 else (pv.get("out_last2") if i == 0 else pv.get("out_last1"))
            ld = S.dma("sync", [dep_war], xst[i % 2][:],
                       d["x"][base + i * 128: base + (i + 1) * 128, :])
            bs = S.step("vector", [ld, st_ag[i - 1] if i >= 1 else pv.get("ag_last")],
                        lambda e, i=i: [e.bn_stats(stats[:, 6 * kk:6 * kk + 6],
                                                   xst[i % 2][:, 512 * kk:512 * (kk + 1)]) for kk in range(4)][-1])
            st_ag[i] = S.step("vector", [bs], lambda e, i=i: e.bn_aggr(mv[i % 2][:], stats[:]))
            ln1 = S.step("scalar", [st_ag[i], st_norm[i - 2] if i >= 2 else pv.get("norm_last"), c_eps],
                         lambda e, i=i: e.activation(rstd[i % 2][:], mv[i % 2][:, 1:2], AF.Ln, bias=eps_c[:, 0:1]))
            ln2 = S.step("scalar", [ln1], lambda e, i=i: e.activation(rstd[i % 2][:], rstd[i % 2][:], AF.Exp, scale=-0.5))
            st_norm[i] = S.step("vector", [ln2, st_tr[i - 2] if i >= 2 else pv.get("tr_last")],
                                lambda e, i=i: e.tensor_scalar(xn[i % 2][:], xst[i % 2][:], mv[i % 2][:, 0:1],
                                                               rstd[i % 2][:, 0:1], ALU.subtract, ALU.mult))
            deps_tr = [st_norm[i]] + ([pv.get("xnT_free")] if i < 2 else [])
            st_tr[i] = S.step("sync", deps_tr,
                              lambda e, i=i: [e.dma_start_transpose(
                                  xnT[:, dt * SLAB + i * 128: dt * SLAB + (i + 1) * 128],
                                  xn[i % 2][:, dt * 128:(dt + 1) * 128]) for dt in range(16)],
                              weight=256)

        st_g = {}
        st_u = {}
        wld_g = {}
        wld_u = {}
        hread_prev = pv.get("hread", {})

        def ensure_gu(ht):
            if ht in wld_g or ht >= NH:
                return
            par = ht % 4
            depg = st_g.get(ht - 4, pv.get("xnT_free"))
            depu = st_u.get(ht - 4, pv.get("xnT_free"))
            wld_g[ht] = S.dma("gdma", [prep_done[("g", ht)], depg], wg[par][:],
                              scr_g[ht * 128:(ht + 1) * 128, :])
            wld_u[ht] = S.dma("sdma", [prep_done[("u", ht)], depu], wu[par][:],
                              scr_u[ht * 128:(ht + 1) * 128, :])
        for _k in range(4):
            ensure_gu(_k)

        prev_gev = pv.get("gev_last")
        prev_hvv = pv.get("hvv_last")
        gev_bank = {0: pv.get("gev_b0"), 1: pv.get("gev_b1")}
        hv_bank = {0: pv.get("hv_b0"), 1: pv.get("hv_b1")}
        h_ready = [None] * NH
        for ht in range(NH):
            par = ht % 4
            bk = ht % 2
            deps = [wld_g[ht], gev_bank[bk]] + [st_tr[j] for j in range(NT)]
            if ht == 0:
                deps.append(pv.get("radd_last"))
            gp = S.step("tensor", deps,
                        lambda e, par=par, bk=bk: [e.matmul(ps_g[bk][:], wg[par][:, dt * 128:(dt + 1) * 128],
                                                            xnT[:, dt * SLAB: dt * SLAB + 512],
                                                            start=(dt == 0), stop=(dt == 15))
                                                   for dt in range(16)][-1])
            st_g[ht] = gp
            gev = S.step("scalar", [gp, hv_bank[bk], prev_gev, c_sg],
                         lambda e, bk=bk, ht=ht: e.activation(g_sb[:, bk * 512:(bk + 1) * 512], ps_g[bk][:],
                                                              AF.Silu, scale=sgu[:, ht:ht + 1]))
            gev_bank[bk] = gev
            prev_gev = gev
            up = S.step("tensor", [wld_u[ht], hv_bank[bk]],
                        lambda e, par=par, bk=bk: [e.matmul(ps_u[bk][:], wu[par][:, dt * 128:(dt + 1) * 128],
                                                            xnT[:, dt * SLAB: dt * SLAB + 512],
                                                            start=(dt == 0), stop=(dt == 15))
                                                   for dt in range(16)][-1])
            st_u[ht] = up
            hv = S.step("vector", [up, gev, hread_prev.get(ht), prev_hvv, c_su],
                        lambda e, ht=ht, bk=bk: e.scalar_tensor_tensor(
                            hT[:, ht * SLAB: ht * SLAB + 512], ps_u[bk][:], sgu[:, NH + ht:NH + ht + 1],
                            g_sb[:, bk * 512:(bk + 1) * 512], ALU.mult, ALU.mult))
            hv_bank[bk] = hv
            prev_hvv = hv
            h_ready[ht] = hv
            ensure_gu(ht + 4)

        st_d = {}
        wldd = {}
        st_radd = {}
        st_out_l = {}
        prev_ra = pv.get("radd_last")

        def ensure_d(j, ht):
            if (j, ht) in wldd or ht >= NH:
                return
            if j >= NT // 2:
                return
            par = (j * NH + ht) % 4
            dep = (st_d.get((j, ht - 4)) if ht >= 4 else
                   (st_d.get((j - 1, NH - 4 + ht)) or pv.get("dlast")))
            wldd[(j, ht)] = S.dma("gdma" if ht % 2 == 0 else "sdma",
                                  [prep_done[("d", ht)], dep], wd[par][:],
                                  scr_d[ht * 128:(ht + 1) * 128, :])
        for _k in range(4):
            ensure_d(0, _k)
        for j in range(NT // 2):
            po = None
            for ht in range(NH):
                par = (j * NH + ht) % 4
                deps = [wldd[(j, ht)], h_ready[ht]]
                if ht == 0:
                    deps += [st_radd.get(j - 1, pv.get("radd_last")), h_ready[NH - 1], prev_gev]
                def dp_fn(e, j=j, ht=ht, par=par, first=(ht == 0), last=(ht == NH - 1)):
                    out = None
                    for tt in range(2):
                        pset = ps_d0 if tt == 0 else ps_d1
                        for c in range(4):
                            out = e.matmul(pset[c][:],
                                           hT[:, ht * SLAB + (2 * j + tt) * 128: ht * SLAB + (2 * j + tt + 1) * 128],
                                           wd[par][:, c * 512:(c + 1) * 512],
                                           start=first, stop=last)
                    return out
                po = S.step("tensor", deps, dp_fn)
                st_d[(j, ht)] = po
                if ht + 4 < NH:
                    ensure_d(j, ht + 4)
                else:
                    ensure_d(j + 1, ht + 4 - NH)
            for tt in range(2):
                i = 2 * j + tt
                x2 = S.dma("sync", [st_norm[NT - 1],
                                    st_out_l.get(i - 2, pv.get("out_last2") if tt == 0 else pv.get("out_last1"))],
                           xst[i % 2][:], d["x"][base + i * 128: base + (i + 1) * 128, :])
                pset = ps_d0 if tt == 0 else ps_d1
                ram = S.step("vector", [po, c_sdb, prev_ra],
                             lambda e, pset=pset: [
                                 e.tensor_tensor(rtmp[:, c * 512:(c + 1) * 512], pset[c][:],
                                                 sdB[:, c * 512:(c + 1) * 512], ALU.mult)
                                 for c in range(4)][-1])
                ra = S.step("vector", [ram, x2],
                            lambda e, i=i: [
                                e.tensor_tensor(xst[i % 2][:, c * 512:(c + 1) * 512],
                                                rtmp[:, c * 512:(c + 1) * 512],
                                                xst[i % 2][:, c * 512:(c + 1) * 512], ALU.add)
                                for c in range(4)][-1])
                st_radd[j] = ra
                prev_ra = ra
                so = S.dma("sync", [ra], d["out"][base + i * 128: base + (i + 1) * 128, :], xst[i % 2][:])
                st_out_l[i] = so
                last_out.append(so)

        pv = dict(
            norm_last=st_norm[NT - 1], tr_last=st_tr[NT - 1], ag_last=st_ag[NT - 1],
            gev_last=prev_gev, hvv_last=prev_hvv, radd_last=st_radd[NT // 2 - 1],
            gev_b0=gev_bank[0], gev_b1=gev_bank[1], hv_b0=hv_bank[0], hv_b1=hv_bank[1],
            out_last2=st_out_l[NT - 2], out_last1=st_out_l[NT - 1],
            xnT_free=st_u[NH - 1], dlast=st_d[(NT // 2 - 1, NH - 1)],
            hread={ht: st_d[(NT // 2 - 1, ht)] for ht in range(NH)},
        )

    return last_out


NHEAD = 16
D = 2048
DC = 512
DR = 64
SCL = 1.0 / np.sqrt(192.0)


def emit_l1_attn(S, T, NB, d):
    NT = T // 128
    TC = T // 512
    sb = S.sbuf
    G = S.global_psums()
    ps_q, ps_p1, ps_p2 = G["q"], G["p1"], G["p2"]
    ps_ctx, ps_bc, ps_v = G["ctx"], G["bc"], G["d"]
    ps_rs = ps_bc
    ps_sc = [G["sc0"], G["sc1"]]
    ps_out4 = [G["sc0"], G["sc1"], G["ctx"], G["d"]]

    xst = sb("l1a_xst", [128, D], F32)
    xn = sb("l1a_xn", [128, D], BF16)
    stats = sb("l1a_stats", [128, 24], F32)
    mv = [sb(f"l1a_mv{i}", [128, 2], F32) for i in range(2)]
    rstd = [sb(f"l1a_rstd{i}", [128, 1], F32) for i in range(2)]
    eps_c = sb("l1a_eps", [128, 1], F32)
    bigA = sb("l1a_bigA", [128, 16 * 2048], BF16)
    ct = sb("l1a_ct", [128, T], F32)
    st = sb("l1a_st", [128, T], F32)
    p1t = sb("l1a_p1t", [128, 128], F32)
    p2t = sb("l1a_p2t", [128, 128], F32)
    qf32 = sb("l1a_qf32", [128, 512], F32)
    w_sl = [sb(f"l1a_w{i}", [128, 2048], BF16) for i in range(2)]
    bo_row = sb("l1a_bo", [1, D], F32)
    ones_r = sb("l1a_ones_r", [1, 512], F32)
    ones_c = sb("l1a_ones_c", [128, 1], BF16)
    ckvT = sb("l1a_ckvT", [128, 4 * T], BF16)
    cqT = sb("l1a_cqT", [128, 4 * T], BF16)
    kRT = sb("l1a_kRT", [128, T], BF16)
    qRT = sb("l1a_qRT", [128, T], BF16)
    qT = sb("l1a_qT", [128, T], BF16)
    kT = sb("l1a_kT", [128, T], BF16)
    v_tm = sb("l1a_vtm", [128, NT * 128], BF16)
    tmp1 = sb("l1a_tmp1", [128, 512], F32)
    tmp2 = sb("l1a_tmp2", [128, 512], F32)
    expT = sb("l1a_expT", [128, NT * 512], BF16)
    recip = sb("l1a_recip", [1, 512], F32)
    recipB = sb("l1a_recipB", [128, 512], F32)
    ctx_sb = [sb(f"l1a_ctxsb{i}", [128, 512], BF16) for i in range(2)]
    ctxg0 = sb("l1a_ctxg0", [128, 2048], BF16)
    ctxg = [ctxg0, ctxg0]

    scr = d["ctx_scr"]

    c_tab = S.dma("gdma", [], ct[:], d["ct"])
    S.dma("gdma", [], st[:], d["st"])
    S.dma("gdma", [], p1t[:], d["p1t"])
    c_p2 = S.dma("gdma", [], p2t[:], d["p2t"])
    c_ones = S.step("vector", [], lambda e: (e.memset(ones_r[:], 1.0), e.memset(ones_c[:], 1.0),
                                             e.memset(eps_c[:], 1e-5))[-1])

    last_out = []
    pv = {}

    for b in range(NB):
        c_bo = S.dma("gdma", [pv.get("po_last")], bo_row[:], d["b_o"].rearrange("(a m) -> a m", a=1))

        st_norm = [None] * NT
        st_tr = [None] * NT
        st_ag = [None] * NT
        for i in range(NT):
            dep_war = st_norm[i - 1] if i >= 1 else pv.get("out_last1")
            ld = S.dma("sync", [dep_war], xst[:],
                       d["x"][b * T + i * 128: b * T + (i + 1) * 128, :])
            bs = S.step("vector", [ld, st_ag[i - 1] if i >= 1 else pv.get("ag_last")],
                        lambda e: [e.bn_stats(stats[:, 6 * kk:6 * kk + 6],
                                              xst[:, 512 * kk:512 * (kk + 1)]) for kk in range(4)][-1])
            st_ag[i] = S.step("vector", [bs], lambda e, i=i: e.bn_aggr(mv[i % 2][:], stats[:]))
            ln1 = S.step("scalar", [st_ag[i], st_norm[i - 2] if i >= 2 else pv.get("norm_last"), c_ones],
                         lambda e, i=i: e.activation(rstd[i % 2][:], mv[i % 2][:, 1:2], AF.Ln, bias=eps_c[:, 0:1]))
            ln2 = S.step("scalar", [ln1], lambda e, i=i: e.activation(rstd[i % 2][:], rstd[i % 2][:], AF.Exp, scale=-0.5))
            st_norm[i] = S.step("vector", [ln2, st_tr[i - 1] if i >= 1 else pv.get("tr_last")],
                                lambda e, i=i: e.tensor_scalar(xn[:], xst[:], mv[i % 2][:, 0:1],
                                                               rstd[i % 2][:, 0:1], ALU.subtract, ALU.mult))
            deps_tr = [st_norm[i]] + ([pv.get("po_last")] if (b > 0 and i == 0) else [])
            st_tr[i] = S.step("sync", deps_tr,
                              lambda e, i=i: [e.dma_start_transpose(
                                  bigA[:, dt * T + i * 128: dt * T + (i + 1) * 128],
                                  xn[:, dt * 128:(dt + 1) * 128]) for dt in range(16)],
                              weight=256)
        tr_all = st_tr[NT - 1]

        units = [("ckv", m) for m in range(4)] + [("cq", m) for m in range(4)] + [("kr", 0)]
        for h in range(NHEAD):
            if h % 2 == 0:
                units.append(("qr", h // 2))
            units.append(("k", h))
            units.append(("q", h))
            units.append(("v", h))

        def wspec(kind, idx):
            if kind == "ckv":
                return d["w_dkv"][:, idx * 128:(idx + 1) * 128], 16, 128
            if kind == "cq":
                return d["w_dq"][:, idx * 128:(idx + 1) * 128], 16, 128
            if kind == "kr":
                return d["w_kr"], 16, 128
            if kind == "qr":
                return d["w_qr"][:, idx * 128:(idx + 1) * 128], 4, 128
            if kind == "k":
                return d["w_uk"][:, idx * 128:(idx + 1) * 128], 4, 128
            if kind == "q":
                return d["w_uq"][:, idx * 128:(idx + 1) * 128], 4, 128
            if kind == "v":
                return d["w_uv"][:, idx * 128:(idx + 1) * 128], 4, 128

        wld = {}
        wld_reader = {}

        def ensure_wld(ui):
            if ui >= len(units) or units[ui] in wld:
                return
            kind, idx = units[ui]
            wsrc, nk, mc = wspec(kind, idx)
            par = ui % 2
            if kind == "kr":
                v3 = w_sl[par][:].rearrange("p (dt m) -> p dt m", m=128)[:, 0:nk, :]
                d1 = S.dma("gdma", [wld_reader.get(par)], v3[:, :, 0:64],
                           wsrc.rearrange("(dt p) m -> p dt m", p=128))
                wld[(kind, idx)] = S.dma("gdma", [d1], v3[:, :, 64:128],
                                         wsrc.rearrange("(dt p) m -> p dt m", p=128))
            else:
                wld[(kind, idx)] = S.dma("gdma", [wld_reader.get(par)],
                                         w_sl[par][:].rearrange("p (dt m) -> p dt m", m=mc)[:, 0:nk, :],
                                         wsrc.rearrange("(dt p) m -> p dt m", p=128))
        ensure_wld(0)
        ensure_wld(1)

        st_ropeB = {}
        prev_pm = pv.get("pm_last")
        prev_ropeA = None
        prev_ropeB = None
        last_reader_of = {}
        prev_stt = pv.get("stt_last")
        prev_rec = pv.get("rec_last")
        prev_bcp = pv.get("bcp_last")
        prev_ctx_dma = {0: pv.get("ctxd_m2"), 1: pv.get("ctxd_m1")}
        psq_prev = pv.get("psq_free")
        psv_prev = pv.get("psv_free")
        exp_slot_free = pv.get("avs_last")
        sc_bank_prev = [pv.get("avs_last"), pv.get("avs_last")]
        proj_last = None
        kr_done = None

        for ui, (kind, idx) in enumerate(units):
            ensure_wld(ui)
            par = ui % 2
            wbuf = w_sl[par]
            wsrc, nk, mc = wspec(kind, idx)

            if kind == "v":
                h = idx
                deps = [wld[(kind, idx)], tr_all, psv_prev,
                        last_reader_of.get(("v", h - 1), pv.get("avs_last"))]
                vev = None
                for i in range(NT):
                    vp = S.step("tensor", deps if i == 0 else [vev],
                                lambda e, wbuf=wbuf, i=i: [e.matmul(
                                    ps_v[:, 0:128], ckvT[:, kt * T + i * 128: kt * T + (i + 1) * 128],
                                    wbuf[:, kt * 128:(kt + 1) * 128], start=(kt == 0), stop=(kt == 3))
                                    for kt in range(4)][-1])
                    vev = S.step("scalar", [vp],
                                 lambda e, i=i: e.copy(v_tm[:, i * 128:(i + 1) * 128], ps_v[:, 0:128]))
                wld_reader[par] = vp
                psv_prev = vev
                vT_ready = vev
                ensure_wld(ui + 1)
                ensure_wld(ui + 2)
                g2 = h % 2
                for tc in range(TC):
                    rdy = [None] * NT
                    st_sc = None
                    for s_ in range(NT):
                        deps = [st_ropeB[("q", h)], st_ropeB[("k", h)],
                                st_ropeB[("qr", h // 2)], kr_done,
                                sc_bank_prev[s_ % 2]]
                        st_sc = S.step("tensor", deps,
                                       lambda e, s_=s_, tc=tc, g2=g2: (
                                           e.matmul(ps_sc[s_ % 2][:], kT[:, s_ * 128:(s_ + 1) * 128],
                                                    qT[:, tc * 512:(tc + 1) * 512], start=True, stop=False),
                                           e.matmul(ps_sc[s_ % 2][:], kRT[64 * g2:64 * g2 + 64, s_ * 128:(s_ + 1) * 128],
                                                    qRT[64 * g2:64 * g2 + 64, tc * 512:(tc + 1) * 512],
                                                    start=False, stop=True))[1])
                        ex = S.step("scalar", [st_sc, exp_slot_free],
                                    lambda e, s_=s_: e.activation(expT[:, s_ * 512:(s_ + 1) * 512],
                                                                  ps_sc[s_ % 2][:], AF.Exp, scale=SCL))
                        sc_bank_prev[s_ % 2] = ex
                        rdy[s_] = ex
                    avs = None
                    for s_ in range(NT):
                        deps = [rdy[s_], vT_ready]
                        if s_ == 0:
                            deps += [prev_rec, prev_stt, prev_bcp, c_ones]
                        def av_fn(e, s_=s_, first=(s_ == 0), last=(s_ == NT - 1)):
                            e.matmul(ps_rs[0:1, :], ones_c[:], expT[:, s_ * 512:(s_ + 1) * 512],
                                     start=first, stop=last)
                            return e.matmul(ps_ctx[:], v_tm[:, s_ * 128:(s_ + 1) * 128],
                                            expT[:, s_ * 512:(s_ + 1) * 512], start=first, stop=last)
                        avs = S.step("tensor", deps, av_fn)
                    exp_slot_free = avs
                    last_reader_of[("v", h)] = avs
                    last_reader_of[("q", h)] = avs
                    last_reader_of[("k", h)] = avs
                    if h % 2 == 1:
                        last_reader_of[("qr", h // 2)] = avs
                    rec = S.step("vector", [avs], lambda e: e.reciprocal(recip[:], ps_rs[0:1, :]))
                    prev_rec = rec
                    bcm = S.step("tensor", [rec, prev_bcp],
                                 lambda e: e.matmul(ps_bc[:], ones_r[0:1, 0:128], recip[0:1, :],
                                                    start=True, stop=True))
                    rbc = S.step("scalar", [bcm, prev_stt],
                                 lambda e: e.copy(recipB[:], ps_bc[:]))
                    prev_bcp = rbc
                    j = (h * TC + tc) % 2
                    stt = S.step("vector", [rbc, avs, prev_ctx_dma[j]],
                                 lambda e, j=j: e.scalar_tensor_tensor(
                                     ctx_sb[j][:], ps_ctx[:], 1.0, recipB[:], ALU.mult, ALU.mult))
                    prev_stt = stt
                    cd = S.dma("sdma", [stt], scr[:, h * T + tc * 512: h * T + (tc + 1) * 512], ctx_sb[j][:])
                    prev_ctx_dma[j] = cd
                continue

            if kind in ("ckv", "cq", "kr"):
                mov, movT = bigA, T
                mov_nk = 16
            else:
                mov, movT = (cqT, T) if kind in ("qr",) else (ckvT, T) if kind == "k" else (cqT, T)
                mov_nk = 4
            for tc in range(TC):
                deps = [wld[(kind, idx)], psq_prev, tr_all]
                if b > 0 and ui == 0 and tc == 0:
                    deps.append(pv.get("radd_last"))
                mrows = mc
                def pr_fn(e, wbuf=wbuf, tc=tc, nk=nk, mov=mov, movT=movT, mc=mc):
                    last = None
                    for kt in range(nk):
                        last = e.matmul(ps_q[0:mc, :], wbuf[:, kt * mc:(kt + 1) * mc],
                                        mov[:, kt * movT + tc * 512: kt * movT + tc * 512 + 512],
                                        start=(kt == 0), stop=(kt == nk - 1))
                    return last
                stp = S.step("tensor", deps, pr_fn)
                proj_last = stp
                wld_reader[par] = stp
                if tc == TC - 1:
                    ensure_wld(ui + 1)
                    ensure_wld(ui + 2)
                if kind in ("ckv", "cq", "q", "k"):
                    dst, doff = {
                        "ckv": (ckvT, idx * T), "cq": (cqT, idx * T),
                        "q": (qT, 0), "k": (kT, 0)}[kind]
                    ev_war = None
                    if kind == "q":
                        ev_war = last_reader_of.get(("q", idx - 1), pv.get("avs_last"))
                    elif kind == "k":
                        ev_war = last_reader_of.get(("k", idx - 1), pv.get("avs_last"))
                    elif b > 0 and kind in ("ckv", "cq"):
                        ev_war = pv.get("avs_last" if False else "ckv_free")
                    ev = S.step("scalar", [stp, ev_war],
                                lambda e, dst=dst, doff=doff, tc=tc, mc=mc: e.copy(
                                    dst[:, doff + tc * 512: doff + (tc + 1) * 512], ps_q[:]))
                    psq_prev = ev
                    if kind in ("ckv", "cq"):
                        st_ropeB[(kind, idx)] = ev
                    else:
                        st_ropeB[(kind, idx)] = ev
                else:
                    ev = S.step("scalar", [stp, prev_pm],
                                lambda e, mc=mc: e.copy(qf32[0:mc, :], ps_q[0:mc, :]))
                    psq_prev = ev
                    pm = S.step("tensor", [ev, prev_ropeA],
                                lambda e, mc=mc: (
                                    e.matmul(ps_p1[0:mc, :], p1t[0:mc, 0:mc], qf32[0:mc, :], start=True, stop=True),
                                    e.matmul(ps_p2[0:mc, :], p2t[0:mc, 0:mc], qf32[0:mc, :], start=True, stop=True))[1])
                    prev_pm = pm
                    ra = S.step("vector", [pm, prev_ropeB, c_tab],
                                lambda e, tc=tc, mc=mc: (
                                    e.tensor_tensor(tmp1[0:mc, :], ps_p1[0:mc, :], ct[0:mc, tc * 512:(tc + 1) * 512], ALU.mult),
                                    e.tensor_tensor(tmp2[0:mc, :], ps_p2[0:mc, :], st[0:mc, tc * 512:(tc + 1) * 512], ALU.mult))[1])
                    prev_ropeA = ra
                    if kind == "kr":
                        rb_war = pv.get("kr_free") if b > 0 else None
                        dstT, doff2 = kRT, 0
                    else:
                        rb_war = last_reader_of.get(("qr", idx - 1), pv.get("avs_last"))
                        dstT, doff2 = qRT, 0
                    rb_ = S.step("vector", [ra] + ([rb_war] if rb_war else []),
                                 lambda e, dstT=dstT, tc=tc, mc=mc: e.tensor_tensor(
                                     dstT[0:mc, tc * 512:(tc + 1) * 512], tmp1[0:mc, :], tmp2[0:mc, :], ALU.add))
                    prev_ropeB = rb_
                    if kind == "kr":
                        kr_done = rb_ if tc == TC - 1 else kr_done
                    else:
                        st_ropeB[("qr", idx)] = rb_

        xnT_free = psq_prev
        wo_lds = []
        for dt in range(16):
            wo_lds.append(S.dma("gdma", [xnT_free, proj_last] if dt == 0 else [],
                                bigA[:, dt * 2048:(dt + 1) * 2048],
                                d["w_o"][dt * 128:(dt + 1) * 128, :]))
        st_radd = [None] * NT
        st_out = [None] * NT
        st_po = [None] * NT
        st_cg = [None] * NT
        po_last = None
        scr3 = scr.rearrange("p (dt t) -> p dt t", dt=16)
        for i in range(NT):
            cg_deps = (S.dma_fence("sdma") if i == 0 else []) + \
                      [st_po[i - 1] if i >= 1 else pv.get("po_last")]
            st_cg[i] = S.dma("sync", cg_deps,
                             ctxg[i % 2][:].rearrange("p (dt t) -> p dt t", dt=16),
                             scr3[:, :, i * 128:(i + 1) * 128])
            x2 = S.dma("sync", [st_out[i - 1] if i >= 1 else pv.get("out_last1"), st_norm[NT - 1]],
                       xst[:], d["x"][b * T + i * 128: b * T + (i + 1) * 128, :])
            deps = wo_lds + [st_cg[i], c_bo, st_radd[i - 1] if i >= 1 else pv.get("radd_last")]
            if i == 0:
                deps.append(prev_stt)
            def po_fn(e, i=i):
                for c in range(4):
                    e.matmul(ps_out4[c][:], ones_r[0:1, 0:128],
                             bo_row[0:1, c * 512:(c + 1) * 512], start=True, stop=False)
                last = None
                for dt in range(16):
                    for c in range(4):
                        last = e.matmul(ps_out4[c][:],
                                        ctxg[i % 2][:, dt * 128:(dt + 1) * 128],
                                        bigA[:, dt * 2048 + c * 512: dt * 2048 + (c + 1) * 512],
                                        start=False, stop=(dt == 15))
                return last
            po = S.step("tensor", deps, po_fn)
            po_last = po
            st_po[i] = po
            ra = S.step("vector", [po, x2],
                        lambda e: [e.tensor_tensor(xst[:, c * 512:(c + 1) * 512],
                                                   ps_out4[c][:], xst[:, c * 512:(c + 1) * 512],
                                                   ALU.add) for c in range(4)][-1])
            st_radd[i] = ra
            st_out[i] = S.dma("sync", [ra], d["out"][b * T + i * 128: b * T + (i + 1) * 128, :], xst[:])
            last_out.append(st_out[i])

        pv = dict(norm_last=st_norm[NT - 1], tr_last=st_tr[NT - 1], ag_last=st_ag[NT - 1],
                  po_last=po_last, psq_free=psq_prev, psv_free=psv_prev,
                  radd_last=st_radd[NT - 1],
                  stt_last=prev_stt, rec_last=prev_rec, bcp_last=prev_bcp,
                  avs_last=exp_slot_free, proj_last=proj_last, pm_last=prev_pm,
                  out_last1=st_out[NT - 1],
                  ctxd_m2=prev_ctx_dma[0], ctxd_m1=prev_ctx_dma[1],
                  ckv_free=psv_prev, kr_free=exp_slot_free)

    return last_out


D = 2048
LAT = 1024
SU = 4096
E = 32
NSU = SU // 128
NLT = LAT // 128
BIG = 30000.0


def emit_l1_ffn(S, NTOK, d, SLAB=1024, CH=1024, debug_stop=False):
    NSL = NTOK // SLAB
    NCH = NTOK // CH
    NT = SLAB // 128
    TC = SLAB // 512
    CT = CH // 128
    CC = CH // 512
    sb = S.sbuf
    G = S.global_psums()
    assert TC == 1
    ps_a = [G["q"], G["p1"]]
    ps_d0 = [G["sc1"], G["ctx"], G["bc"], G["d"]]
    ps_d1 = [G["q"], G["p1"], G["p2"], G["sc0"]]
    assert CC == 1
    ps_h = [G["q"], G["p1"]]
    ps_y = [G["sc1"], G["ctx"]]
    ps_l = G["p2"]
    ps_bcast = G["sc0"]

    xst0 = sb("l1f_xst0", [128, D], F32)
    xst = [xst0, xst0]
    xn = [sb(f"l1f_xn{i}", [128, D], BF16) for i in range(2)]
    xn32 = sb("l1f_xn32", [128, D], F32)
    xlo = sb("l1f_xlo", [128, D], BF16)
    xloT = sb("l1f_xloT", [128, D], BF16)
    stats = sb("l1f_stats", [128, 24], F32)
    mv = [sb(f"l1f_mv{i}", [128, 2], F32) for i in range(2)]
    rstd = [sb(f"l1f_rstd{i}", [128, 1], F32) for i in range(2)]
    eps_c = sb("l1f_eps", [128, 1], F32)
    ones_r = sb("l1f_ones_r", [1, 512], F32)
    ones8 = sb("l1f_ones8", [128, 8], F32)
    onesb = sb("l1f_onesb", [1, 128], BF16)
    AB1 = sb("l1f_AB1", [128, 32768], BF16)
    AB2 = sb("l1f_AB2", [128, 16384], BF16)
    WS = sb("l1f_WS", [128, 4096], BF16)
    YSB = sb("l1f_ysb", [128, 8 * 1024], F32)
    wbf = [sb(f"l1f_wb{i}", [128, D], BF16) for i in range(2)]
    gw32 = xst[0]
    gbB = sb("l1f_gbB", [128, 32], F32)
    scores = sb("l1f_scores", [128, 32], F32)
    sbv = sb("l1f_sbv", [128, 32], F32)
    gs = sb("l1f_gs", [128, 4], F32)
    rt1 = sb("l1f_rt1", [128, 32], F32)
    rt2 = sb("l1f_rt2", [128, 32], F32)
    sel = sb("l1f_sel", [128, 32], F32)
    rcol = sb("l1f_rcol", [128, 4], F32)
    cwpad = sb("l1f_cwpad", [128, 128], BF16)
    rt2i = sb("l1f_rt2i", [128, 32], I32)
    cwrow = sb("l1f_cwrow", [1, 2 * CH], BF16)
    cwB = sb("l1f_cwB", [128, 2 * 512], BF16)
    gb_row = sb("l1f_gbrow", [1, 32], F32)
    negB = sb("l1f_negB", [128, 32], F32)

    def A_hsu(s_):
        return AB1[:, s_ * 512:(s_ + 1) * 512]

    def B_xlat(lk):
        return AB1[:, lk * 512:(lk + 1) * 512]

    def B_h(it):
        return AB1[:, 4096 + it * 512:4096 + (it + 1) * 512]

    def B_ybf(lt):
        return AB1[:, 8192 + lt * 512:8192 + (lt + 1) * 512]

    def B_fc2(lt):
        return AB1[:, 12288 + lt * 2048:12288 + (lt + 1) * 2048]

    def A_xnT(dt):
        return AB2[:, dt * SLAB:(dt + 1) * SLAB]

    def B_xnTtc(dt):
        return AB2[:, dt * 512:(dt + 1) * 512]

    def B_gwT(dt):
        return AB2[:, 8192 + dt * 32:8192 + dt * 32 + 32]

    def B_gwTlo(dt):
        return AB2[:, 8832 + dt * 32:8832 + dt * 32 + 32]

    def B_xnlo(dt):
        return AB1[:, 12288 + dt * 512:12288 + (dt + 1) * 512]

    B_cwT = AB2[:, 9472:9472 + 128]

    def A_wbuf(par):
        return AB1[:, 16384 + par * 2048:16384 + (par + 1) * 2048]

    def B_fc1buf(par):
        return WS[:, par * 2048:(par + 1) * 2048]

    def B_w1(par):
        return AB1[:, 28672 + par * 1024:28672 + (par + 1) * 1024]

    def B_w2(par):
        return WS[:, par * 1024:(par + 1) * 1024]

    xnT_scr = d["xnT_scr"]
    xnT_lo_scr = d["xnT_lo_scr"]
    cw_scr = d["cw_scr"]

    c_ones = S.step("vector", [], lambda e: (e.memset(eps_c[:], 1e-5), e.memset(ones_r[:], 1.0),
                                             e.memset(ones8[:], 1.0), e.memset(onesb[:], 1.0),
                                             e.memset(cwpad[:], 0.0), e.memset(negB[:], -BIG))[-1])
    c_gb = S.dma("gdma", [], gb_row[:], d["gate_bias"].rearrange("(a m) -> a m", a=1))
    c_gbm = S.step("tensor", [c_gb, c_ones],
                   lambda e: e.matmul(G["d"][:, 0:32], ones_r[0:1, 0:128], gb_row[0:1, :],
                                      start=True, stop=True))
    c_gbB = S.step("scalar", [c_gbm], lambda e: e.copy(gbB[:], G["d"][:, 0:32]))


    last_out = []
    pv = {}
    a_out = {}

    for sl in range(NSL):
        base = sl * SLAB
        st_norm = [None] * NT
        st_tr = [None] * NT
        st_ag = [None] * NT
        st_xscr = [None] * NT
        prev_n32 = pv.get("n32_last")
        for i in range(NT):
            dep_war = prev_n32 if i >= 1 else pv.get("out_last1")
            ld = S.dma("sync", [dep_war], xst[i % 2][:],
                       d["x"][base + i * 128: base + (i + 1) * 128, :])
            bs = S.step("vector", [ld, st_ag[i - 1] if i >= 1 else pv.get("ag_last")],
                        lambda e, i=i: [e.bn_stats(stats[:, 6 * kk:6 * kk + 6],
                                                   xst[i % 2][:, 512 * kk:512 * (kk + 1)]) for kk in range(4)][-1])
            st_ag[i] = S.step("vector", [bs], lambda e, i=i: e.bn_aggr(mv[i % 2][:], stats[:]))
            ln1 = S.step("scalar", [st_ag[i], st_norm[i - 2] if i >= 2 else pv.get("norm_last"), c_ones],
                         lambda e, i=i: e.activation(rstd[i % 2][:], mv[i % 2][:, 1:2], AF.Ln, bias=eps_c[:, 0:1]))
            ln2 = S.step("scalar", [ln1], lambda e, i=i: e.activation(rstd[i % 2][:], rstd[i % 2][:], AF.Exp, scale=-0.5))
            n32 = S.step("vector", [ln2, pv.get(("lo", i - 1)) if i >= 1 else pv.get("lo_last")],
                         lambda e, i=i: e.tensor_scalar(xn32[:], xst[i % 2][:], mv[i % 2][:, 0:1],
                                                        rstd[i % 2][:, 0:1], ALU.subtract, ALU.mult))
            st_norm[i] = S.step("vector", [n32, st_tr[i - 2] if i >= 2 else pv.get("tr_last")],
                                lambda e, i=i: e.tensor_copy(xn[i % 2][:], xn32[:]))
            lo = S.step("vector", [st_norm[i], pv.get(("trlo", i - 1)) if i >= 1 else pv.get("trlo_last")],
                        lambda e, i=i: e.tensor_tensor(xlo[:], xn32[:], xn[i % 2][:], ALU.subtract))
            pv[("lo", i)] = lo
            prev_n32 = n32
            deps_tr = [st_norm[i]] + ([pv.get("xnT_free")] if i < 2 else [])
            prev_xscr = pv.get("xscr_list")
            if prev_xscr:
                deps_tr.append(prev_xscr[i])
            st_tr[i] = S.step("sync", deps_tr,
                              lambda e, i=i: [e.dma_start_transpose(
                                  A_xnT(dt)[:, i * 128:(i + 1) * 128],
                                  xn[i % 2][:, dt * 128:(dt + 1) * 128]) for dt in range(16)],
                              weight=256)
            st_xscr[i] = S.dma("sdma", [st_tr[i]],
                               xnT_scr.rearrange("p (dt t) -> p dt t", dt=16)[:, :, base + i * 128: base + (i + 1) * 128],
                               AB2[:, 0:16 * SLAB].rearrange("p (dt t) -> p dt t", dt=16)[:, :, i * 128:(i + 1) * 128])
            trlo = S.step("sync", [lo, pv.get("xlscr_last")],
                          lambda e: [e.dma_start_transpose(
                              xloT[:, dt * 128:(dt + 1) * 128],
                              xlo[:, dt * 128:(dt + 1) * 128]) for dt in range(16)],
                          weight=256)
            pv[("trlo", i)] = trlo
            pv["xlscr_last"] = S.dma("sdma", [trlo],
                                     xnT_lo_scr.rearrange("p (dt t) -> p dt t", dt=16)[:, :, base + i * 128: base + (i + 1) * 128],
                                     xloT[:].rearrange("p (dt m) -> p dt m", dt=16))

        pv["n32_last"] = prev_n32
        st_g = {}
        wld_g = {}

        def ensure_su(sidx):
            if sidx in wld_g or sidx >= NSU:
                return
            par = sidx % 4
            depg = st_g.get(sidx - 4, pv.get("dlast"))
            wld_g[sidx] = S.dma("gdma", [depg], A_wbuf(par),
                                d["sut"][sidx * 128:(sidx + 1) * 128, :])
        for _k in range(4):
            ensure_su(_k)
        prev_gev = pv.get("gev_last")
        prev_sq = pv.get("sq_last")
        hread_prev = pv.get("hread", {})
        agev_bank = {0: pv.get("agev_b0"), 1: pv.get("agev_b1")}
        h_ready = [None] * NSU
        for s_ in range(NSU):
            par = s_ % 4
            bk = s_ % 2
            deps = [wld_g[s_], agev_bank[bk]] + [st_tr[j] for j in range(NT)]
            if s_ == 0:
                deps.append(pv.get("radd_last"))
            gp = S.step("tensor", deps,
                        lambda e, par=par, bk=bk: [e.matmul(ps_a[bk][:], A_wbuf(par)[:, dt * 128:(dt + 1) * 128],
                                                            A_xnT(dt)[:, 0:512],
                                                            start=(dt == 0), stop=(dt == 15))
                                                   for dt in range(16)][-1])
            st_g[s_] = gp
            gev = S.step("vector", [gp, prev_sq, hread_prev.get(s_), prev_gev],
                         lambda e, s_=s_, bk=bk: e.tensor_scalar_max(A_hsu(s_)[:, 0:512],
                                                                     ps_a[bk][:], 0.0))
            agev_bank[bk] = gev
            prev_gev = gev
            sq = S.step("scalar", [gev],
                        lambda e, s_=s_: e.activation(A_hsu(s_)[:, 0:SLAB], A_hsu(s_)[:, 0:SLAB], AF.Square))
            prev_sq = sq
            h_ready[s_] = sq
            ensure_su(s_ + 4)

        st_d = {}
        wldd = {}
        st_radd = {}
        st_out_l = {}

        def ensure_sd(j, ht):
            if (j, ht) in wldd or ht >= NSU or j >= NT // 2:
                return
            par = (j * NSU + ht) % 4
            dep = (st_d.get((j, ht - 4)) if ht >= 4 else
                   (st_d.get((j - 1, NSU - 4 + ht)) or st_g.get(NSU - 4 + ht) or pv.get("dlast")))
            wldd[(j, ht)] = S.dma("gdma", [dep], A_wbuf(par),
                                  d["sdt"][ht * 128:(ht + 1) * 128, :])
        for _k in range(4):
            ensure_sd(0, _k)
        for j in range(NT // 2):
            po = None
            for ht in range(NSU):
                par = (j * NSU + ht) % 4
                deps = [wldd[(j, ht)], h_ready[ht]]
                if ht == 0:
                    deps += [st_radd.get(j - 1, pv.get("radd_last")), prev_gev, prev_sq]
                def dp_fn(e, j=j, par=par, ht=ht, first=(ht == 0), last=(ht == NSU - 1)):
                    out = None
                    for tt in range(2):
                        pset = ps_d0 if tt == 0 else ps_d1
                        for c in range(4):
                            out = e.matmul(pset[c][:],
                                           A_hsu(ht)[:, (2 * j + tt) * 128:(2 * j + tt + 1) * 128],
                                           A_wbuf(par)[:, c * 512:(c + 1) * 512],
                                           start=first, stop=last)
                    return out
                po = S.step("tensor", deps, dp_fn)
                st_d[(j, ht)] = po
                if ht + 4 < NSU:
                    ensure_sd(j, ht + 4)
                else:
                    ensure_sd(j + 1, ht + 4 - NSU)
            for tt in range(2):
                i = 2 * j + tt
                x2 = S.dma("sync", [pv.get("n32_last"),
                                    st_out_l.get(i - 1, pv.get("out_last1"))],
                           xst[i % 2][:], d["x"][base + i * 128: base + (i + 1) * 128, :])
                pset = ps_d0 if tt == 0 else ps_d1
                ra = S.step("vector", [po, x2],
                            lambda e, i=i, pset=pset: [
                                e.tensor_tensor(xst[i % 2][:, c * 512:(c + 1) * 512], pset[c][:],
                                                xst[i % 2][:, c * 512:(c + 1) * 512], ALU.add)
                                for c in range(4)][-1])
                st_radd[j] = ra
                so = S.dma("sync", [ra], d["out"][base + i * 128: base + (i + 1) * 128, :], xst[i % 2][:])
                st_out_l[i] = so
                a_out[base // 128 + i] = so
                last_out.append(so)

        pv = dict(
            norm_last=st_norm[NT - 1], tr_last=st_tr[NT - 1], ag_last=st_ag[NT - 1],
            n32_last=prev_n32,
            gev_last=prev_gev, sq_last=prev_sq, radd_last=st_radd[NT // 2 - 1],
            agev_b0=agev_bank[0], agev_b1=agev_bank[1],
            out_last2=st_out_l[NT - 2], out_last1=st_out_l[NT - 1],
            xnT_free=st_g[NSU - 1], dlast=st_d[(NT // 2 - 1, NSU - 1)],
            xscr_last=st_xscr[NT - 1], xscr_list=st_xscr,
            lo_last=pv.get(("lo", NT - 1)), trlo_last=pv.get(("trlo", NT - 1)),
            xlscr_last=pv.get("xlscr_last"),
            hread={s_: st_d[(NT // 2 - 1, s_)] for s_ in range(NSU)},
        )

    a_radd_last = pv["radd_last"]
    a_dlast = pv["dlast"]
    a_xnT_free = pv["xnT_free"]
    a_gev = pv["gev_last"]
    a_sq = pv["sq_last"]

    b_fence = S.dma_fence("sdma")
    gld = S.dma("gdma", [pv.get("out_last1")], gw32[0:32, :], d["gate_w"])
    gcv = S.step("vector", [gld], lambda e: e.tensor_copy(wbf[0][0:32, :], gw32[0:32, :]))
    gcl = S.step("vector", [gcv],
                 lambda e: e.tensor_tensor(wbf[1][0:32, :], gw32[0:32, :], wbf[0][0:32, :], ALU.subtract))
    gtp = S.step("sync", [gcl, a_xnT_free] + b_fence,
                 lambda e: [e.dma_start_transpose(AB2[:, 8192 + dt * 32:8192 + dt * 32 + 32],
                                                  wbf[0][0:32, dt * 128:(dt + 1) * 128]) for dt in range(16)] +
                           [e.dma_start_transpose(AB2[:, 8832 + dt * 32:8832 + dt * 32 + 32],
                                                  wbf[1][0:32, dt * 128:(dt + 1) * 128]) for dt in range(16)],
                 weight=512)

    st_b_out = {}
    bpv = {}
    for ch in range(NCH):
        cbase = ch * CH
        wfc1 = {}

        def ensure_fc1(lt):
            if lt in wfc1 or lt >= NLT:
                return
            par = lt % 2
            dep = bpv.get(("fp", lt - 2)) or (bpv.get("fp_last") if ch > 0 else a_dlast)
            wfc1[lt] = S.dma("gdma", [dep], B_fc1buf(par),
                             d["fc1t"][lt * 128:(lt + 1) * 128, :])
        ensure_fc1(0)
        ensure_fc1(1)
        prev_xl_ev = bpv.get("xl_ev")
        prev_logit_ev = bpv.get("sg_last")
        prev_rt = bpv.get("rt_last")
        st_xnld = {}
        rt_done = {}
        fp = None
        slab_readers = [bpv.get("xn_free0"), bpv.get("xn_free1")]
        for cc in range(CC):
            st_xnld[cc] = S.dma("sync", [gtp if (ch == 0 and cc == 0) else None]
                                + (b_fence if (ch == 0 and cc == 0) else [])
                                + slab_readers,
                                AB2[:, 0:8192].rearrange("p (dt t) -> p dt t", dt=16),
                                xnT_scr.rearrange("p (dt t) -> p dt t", dt=16)[:, :, cbase + cc * 512: cbase + (cc + 1) * 512])
            lo_deps = ([pv["xlscr_last"], a_sq, a_dlast] if (ch == 0 and cc == 0) else [None]) + slab_readers + \
                      [bpv.get("fp_last")]
            st_xnlo = S.dma("sync", lo_deps,
                            AB1[:, 12288:12288 + 8192].rearrange("p (dt t) -> p dt t", dt=16),
                            xnT_lo_scr.rearrange("p (dt t) -> p dt t", dt=16)[:, :, cbase + cc * 512: cbase + (cc + 1) * 512])
            for lt in range(NLT):
                par = lt % 2
                deps = [st_xnld[cc], wfc1[lt], prev_xl_ev]
                if ch == 0 and cc == 0 and lt == 0:
                    deps += [a_radd_last, a_gev, a_sq]
                fp = S.step("tensor", deps,
                            lambda e, par=par, cc=cc: [e.matmul(ps_h[cc % len(ps_h)][:],
                                                                B_fc1buf(par)[:, dt * 128:(dt + 1) * 128],
                                                                B_xnTtc(dt), start=(dt == 0), stop=(dt == 15))
                                                       for dt in range(16)][-1])
                bpv[("fp", lt)] = fp
                if cc == CC - 1:
                    ensure_fc1(lt + 2)
                ev = S.step("scalar", [fp, prev_logit_ev],
                            lambda e, lt=lt, cc=cc: e.copy(B_xlat(lt)[:, cc * 512:(cc + 1) * 512],
                                                           ps_h[cc % len(ps_h)][:]))
                prev_xl_ev = ev
            for w in range(4):
                i = cc * 4 + w
                lg = S.step("tensor", [st_xnld[cc], st_xnlo, gtp, prev_logit_ev, c_gbm],
                            lambda e, w=w: [e.matmul(ps_l[:, 0:32],
                                                     (B_xnTtc(dt), B_xnTtc(dt), B_xnlo(dt))[z][:, w * 128:(w + 1) * 128],
                                                     (B_gwT(dt), B_gwTlo(dt), B_gwT(dt))[z],
                                                     start=(dt == 0 and z == 0), stop=(dt == 15 and z == 2))
                                            for dt in range(16) for z in range(3)][-1])
                sg = S.step("scalar", [lg, prev_rt, c_gbB],
                            lambda e: e.activation(scores[:], ps_l[:, 0:32], AF.Sigmoid))
                prev_logit_ev = sg
                fns = [
                    lambda e: e.tensor_tensor(sbv[:], scores[:], gbB[:], ALU.add),
                    lambda e: [e.reduce_max(rcol[:, g:g + 1], sbv[:, 8 * g:8 * g + 8], axis=AX.X) for g in range(4)][-1],
                    lambda e: [e.tensor_scalar(rt2[:, 8 * g:8 * g + 8], sbv[:, 8 * g:8 * g + 8],
                                               rcol[:, g:g + 1], None, ALU.is_equal) for g in range(4)][-1],
                    lambda e: e.scalar_tensor_tensor(rt1[:], rt2[:], -BIG, sbv[:], ALU.mult, ALU.add),
                    lambda e: [e.reduce_max(gs[:, g:g + 1], rt1[:, 8 * g:8 * g + 8], axis=AX.X) for g in range(4)][-1],
                    lambda e: e.tensor_tensor(gs[:], gs[:], rcol[:, 0:4], ALU.add),
                    lambda e: e.reduce_max(rcol[:, 0:1], gs[:], axis=AX.X),
                    lambda e: e.tensor_scalar(rt2[:, 0:4], gs[:], rcol[:, 0:1], None, ALU.is_equal),
                    lambda e: e.scalar_tensor_tensor(rt1[:, 0:4], rt2[:, 0:4], -BIG, gs[:], ALU.mult, ALU.add),
                    lambda e: e.reduce_max(rcol[:, 1:2], rt1[:, 0:4], axis=AX.X),
                    lambda e: e.tensor_scalar(rcol[:, 0:4], gs[:], rcol[:, 1:2], None, ALU.is_ge),
                    lambda e: [e.tensor_scalar(rt2[:, 8 * g:8 * g + 8], ones8[:], rcol[:, g:g + 1],
                                               None, ALU.mult) for g in range(4)][-1],
                    lambda e: e.tensor_copy(rt1[:], negB[:]),
                    lambda e: e.tensor_copy(rt2i[:], rt2[:]),
                    lambda e: e.copy_predicated(rt1[:], rt2i[:], sbv[:]),
                    lambda e: e.memset(sel[:], 0.0),
                ]
                for _ in range(8):
                    fns += [
                        lambda e: e.reduce_max(rcol[:, 0:1], rt1[:], axis=AX.X),
                        lambda e: e.tensor_scalar(rt2[:], rt1[:], rcol[:, 0:1], None, ALU.is_equal),
                        lambda e: e.scalar_tensor_tensor(rt1[:], rt2[:], -BIG, rt1[:], ALU.mult, ALU.add),
                        lambda e: e.tensor_tensor(sel[:], sel[:], rt2[:], ALU.add),
                    ]
                fns += [
                    lambda e: e.tensor_tensor(rt2[:], sel[:], scores[:], ALU.mult),
                    lambda e: e.reduce_sum(rcol[:, 0:1], rt2[:], axis=AX.X),
                    lambda e: e.reciprocal(rcol[:, 1:2], rcol[:, 0:1]),
                    lambda e: e.tensor_scalar(rcol[:, 1:2], rcol[:, 1:2], 2.5, None, ALU.mult),
                    lambda e: e.tensor_scalar(rt2[:], rt2[:], rcol[:, 1:2], None, ALU.mult),
                ]
                prev = None
                for fn in fns:
                    prev = S.step("vector", [sg, c_gbB, c_ones, prev_rt] if prev is None else [prev], fn)
                rt = prev
                sqr = S.step("scalar", [rt], lambda e: e.activation(cwpad[:, 0:32], rt2[:], AF.Sqrt))
                prev_rt = sqr
                tr = S.step("sync", [sqr, bpv.get("cw_free")] +
                            (b_fence if ch == 0 else []),
                            lambda e: e.dma_start_transpose(B_cwT, cwpad[:]), weight=16)
                cwst = S.dma("sdma", [tr], cw_scr[:, cbase + i * 128: cbase + (i + 1) * 128],
                             B_cwT[0:32, 0:128])
                bpv["cw_free"] = cwst
                rt_done[i] = cwst
            slab_readers = [fp, lg]
            bpv["hread_all"] = None
        bpv["fp_last"] = fp
        bpv["xl_ev"] = prev_xl_ev
        bpv["sg_last"] = prev_logit_ev
        bpv["rt_last"] = prev_rt
        bpv["xn_free0"] = fp
        bpv["xn_free1"] = lg

        wld1 = {}
        wld2 = {}

        def ensure_w(kind, eidx, it):
            if eidx >= E:
                return
            dic = wld1 if kind == "w1" else wld2
            if (eidx, it) in dic:
                return
            n = eidx * NLT + it
            par = n % 4
            buf = B_w1(par) if kind == "w1" else B_w2(par)
            scr = d["w1t"] if kind == "w1" else d["w2t"]
            dic[(eidx, it)] = S.dma("gdma", [bpv.get(("wread", kind, par)), bpv.get("fp_last")],
                                    buf, scr[n * 128:(n + 1) * 128, :])
        for it0 in range(4):
            ensure_w("w1", 0, it0)
            ensure_w("w2", 0, it0)
        prev_hev = bpv.get("hev")
        prev_sqh = bpv.get("sqh")
        prev_yadd = bpv.get("yadd")
        cwld_prev = bpv.get("cwld")
        prev_bv = bpv.get("bv_last")
        hev_bank = {0: bpv.get("hev_b0"), 1: bpv.get("hev_b1")}
        ya_bank = {0: bpv.get("ya_b0"), 1: bpv.get("ya_b1")}
        cw_steps = {}

        def ensure_cw(eidx):
            if eidx in cw_steps or eidx >= E:
                return
            nonlocal cwld_prev, prev_bv
            half = eidx % 2
            cwld = S.dma("sync", [rt_done[CT - 1] if eidx == 0 else None,
                                  bpv.get(("cwrow_read", half)), cwld_prev],
                         cwrow[0:1, half * CH:(half + 1) * CH],
                         cw_scr[eidx:eidx + 1, cbase:cbase + CH])
            cwld_prev = cwld
            bm = S.step("tensor", [cwld, prev_bv],
                        lambda e, half=half: e.matmul(ps_bcast[:], onesb[0:1, :],
                                                      cwrow[0:1, half * CH:half * CH + 512],
                                                      start=True, stop=True))
            bpv[("cwrow_read", half)] = bm
            bv = S.step("scalar", [bm, bpv.get(("cwB_read", half))],
                        lambda e, half=half: e.copy(cwB[:, half * 512:(half + 1) * 512], ps_bcast[:]))
            prev_bv = bv
            cw_steps[eidx] = bv
        ensure_cw(0)
        ensure_cw(1)
        for eidx in range(E):
            half = eidx % 2
            cwB_ev = cw_steps[eidx]
            for it in range(NLT):
                bk = it % 2
                deps = [wld1[(eidx, it)], bpv["xl_ev"], hev_bank[bk]]
                def h_fn(e, par=(eidx * NLT + it) % 4, bk=bk):
                    out = None
                    for lk in range(NLT):
                        out = e.matmul(ps_h[bk][:], B_w1(par)[:, lk * 128:(lk + 1) * 128],
                                       B_xlat(lk)[:, 0:512],
                                       start=(lk == 0), stop=(lk == NLT - 1))
                    return out
                hp = S.step("tensor", deps, h_fn)
                bpv[("wread", "w1", (eidx * NLT + it) % 4)] = hp
                if it + 4 < NLT:
                    ensure_w("w1", eidx, it + 4)
                else:
                    ensure_w("w1", eidx + 1, it + 4 - NLT)
                hev = S.step("vector", [hp, cwB_ev, prev_sqh],
                             lambda e, it=it, bk=bk, half=half: e.scalar_tensor_tensor(
                                 B_h(it)[:, 0:512], ps_h[bk][:], 0.0,
                                 cwB[:, half * 512:(half + 1) * 512], ALU.max, ALU.mult))
                hev_bank[bk] = hev
                prev_hev = hev
                sqh = S.step("scalar", [hev],
                             lambda e, it=it: e.activation(B_h(it)[:, 0:CH], B_h(it)[:, 0:CH], AF.Square))
                prev_sqh = sqh
                bpv[("hready", it)] = sqh
            bpv[("cwB_read", half)] = prev_hev
            ensure_cw(eidx + 1)
            ensure_cw(eidx + 2)
            for lt in range(NLT):
                bk = lt % 2
                deps = [wld2[(eidx, lt)], ya_bank[bk]] + [bpv[("hready", k2)] for k2 in range(NLT)]
                def y_fn(e, par=(eidx * NLT + lt) % 4, bk=bk):
                    out = None
                    for it in range(NLT):
                        out = e.matmul(ps_y[bk][:], B_w2(par)[:, it * 128:(it + 1) * 128],
                                       B_h(it)[:, 0:512],
                                       start=(it == 0), stop=(it == NLT - 1))
                    return out
                yp = S.step("tensor", deps, y_fn)
                bpv[("wread", "w2", (eidx * NLT + lt) % 4)] = yp
                if lt + 4 < NLT:
                    ensure_w("w2", eidx, lt + 4)
                else:
                    ensure_w("w2", eidx + 1, lt + 4 - NLT)
                if eidx == 0:
                    ya = S.step("vector", [yp, bpv.get("ysb_read"), prev_yadd],
                                lambda e, lt=lt, bk=bk: e.tensor_copy(
                                    YSB[:, lt * CH:lt * CH + 512], ps_y[bk][:]))
                else:
                    ya = S.step("vector", [yp, prev_yadd],
                                lambda e, lt=lt, bk=bk: e.tensor_tensor(
                                    YSB[:, lt * CH:lt * CH + 512], ps_y[bk][:],
                                    YSB[:, lt * CH:lt * CH + 512], ALU.add))
                ya_bank[bk] = ya
                prev_yadd = ya
                bpv["hev"] = prev_hev
        bpv["sqh"] = prev_sqh
        bpv["cwld"] = cwld_prev
        bpv["bv_last"] = prev_bv
        bpv["hev_b0"] = hev_bank[0]
        bpv["hev_b1"] = hev_bank[1]
        bpv["ya_b0"] = ya_bank[0]
        bpv["ya_b1"] = ya_bank[1]

        ybf = {}
        for lt in range(NLT):
            ybf[lt] = S.step("vector", [prev_yadd, prev_sqh, bpv.get(("hread2", lt))],
                             lambda e, lt=lt: e.tensor_copy(B_ybf(lt)[:, 0:CH], YSB[:, lt * CH:(lt + 1) * CH]))
        if debug_stop:
            return last_out
        wfc2 = {}
        for lt in range(NLT):
            wfc2[lt] = S.dma("gdma", [bpv[("wread", "w1", 0)], bpv[("wread", "w1", 1)],
                                      bpv[("wread", "w1", 2)], bpv[("wread", "w1", 3)],
                                      bpv[("wread", "w2", 0)], bpv[("wread", "w2", 1)],
                                      bpv[("wread", "w2", 2)], bpv[("wread", "w2", 3)],
                                      bpv.get("xn_free1"), bpv.get(("fc2read", lt))],
                             B_fc2(lt), d["fc2t"][lt * 128:(lt + 1) * 128, :])
        prev_po = None
        prev_racc = bpv.get("racc")
        for i in range(CT):
            gi = cbase // 128 + i
            acc = S.dma("sync", [a_out[gi], gcl if (ch == 0 and i == 0) else None,
                                 st_b_out.get(gi - 1, bpv.get("b_out_m1") or pv.get("out_last1"))],
                        xst[i % 2][:], d["out"][gi * 128:(gi + 1) * 128, :])
            deps = [ybf[lt] for lt in range(NLT)] + list(wfc2.values()) + \
                   [prev_racc, bpv.get("rt_last")]
            def o_fn(e, i=i):
                out = None
                for lt in range(NLT):
                    for c in range(4):
                        out = e.matmul(ps_d0[c][:], B_ybf(lt)[:, i * 128:(i + 1) * 128],
                                       B_fc2(lt)[:, c * 512:(c + 1) * 512],
                                       start=(lt == 0), stop=(lt == NLT - 1))
                return out
            po = S.step("tensor", deps, o_fn)
            prev_po = po
            ra = S.step("vector", [po, acc],
                        lambda e, i=i: [e.tensor_tensor(xst[i % 2][:, c * 512:(c + 1) * 512], ps_d0[c][:],
                                                        xst[i % 2][:, c * 512:(c + 1) * 512], ALU.add)
                                        for c in range(4)][-1])
            prev_racc = ra
            so = S.dma("sync", [ra], d["out"][gi * 128:(gi + 1) * 128, :], xst[i % 2][:])
            st_b_out[gi] = so
            last_out.append(so)
        bpv["racc"] = prev_racc
        bpv["yadd"] = prev_po
        bpv["ysb_read"] = ybf[NLT - 1]
        for lt in range(NLT):
            bpv[("hread2", lt)] = prev_po
            bpv[("fc2read", lt)] = prev_po
        bpv["b_out_m2"] = st_b_out.get(cbase // 128 + CT - 2)
        bpv["b_out_m1"] = st_b_out.get(cbase // 128 + CT - 1)
        bpv["xl_ev"] = None
        bpv["fp_last"] = prev_po
        prev_xl_ev = None

    return last_out


AR_F32 = 13312
AR_BF16 = 77056
AR_I32 = 32


def build_fused(T, NB, SLAB=512, CH=512, debug_mid=False):
    NTOK = NB * T
    nc = bass.Bass("TRN2", target_bir_lowering=False)
    ins = {}

    def din(name, shape, dt=F32):
        ins[name] = (shape, dt)
        return nc.dram_tensor(name, shape, dt, kind="ExternalInput").ap()

    def dint(name, shape, dt=BF16):
        return nc.dram_tensor(name, shape, dt, kind="Internal").ap()

    x_in = din("x", [NTOK, 2048])
    out = nc.dram_tensor("out", [NTOK, 2048], F32, kind="ExternalOutput").ap()
    mid_kind = "ExternalOutput" if debug_mid else "Internal"
    x1 = nc.dram_tensor("x1", [NTOK, 2048], F32, kind=mid_kind).ap()
    x2 = nc.dram_tensor("x2", [NTOK, 2048], F32, kind=mid_kind).ap()
    x3 = nc.dram_tensor("x3", [NTOK, 2048], F32, kind=mid_kind).ap()
    ctx_scr = dint("ctx_scr", [128, 16 * T])

    d1 = dict(x=x_in, out=x1, ctx_scr=ctx_scr,
              wq=din("l0a_wq", [2048, 2048]), bq=din("l0a_bq", [2048]),
              wk=din("l0a_wk", [2048, 512]), bk=din("l0a_bk", [512]),
              wv=din("l0a_wv", [2048, 512]), bv=din("l0a_bv", [512]),
              wo=din("l0a_wo", [2048, 2048]), bo=din("l0a_bo", [2048]),
              ct=din("l0a_ct", [128, T]), st=din("l0a_st", [128, T]),
              p1t=din("l0a_p1t", [128, 128]), p2t=din("l0a_p2t", [128, 128]),
              dmask=din("l0a_dmask", [128, 2048]))
    d2 = dict(x=x1, out=x2,
              wq_gate_t=din("l0f_wq_gate_t", [5632, 2048], I32),
              wq_up_t=din("l0f_wq_up_t", [5632, 2048], I32),
              wq_down_t=din("l0f_wq_down_t", [5632, 2048], I32),
              s_gate_t=din("l0f_s_gate_t", [128, 44]), s_up_t=din("l0f_s_up_t", [128, 44]),
              s_down_row=din("l0f_s_down_row", [1, 2048]),
              scr_g=dint("scr_g", [5632, 2048]), scr_u=dint("scr_u", [5632, 2048]),
              scr_d=dint("scr_d", [5632, 2048]))
    d3 = dict(x=x2, out=x3, ctx_scr=ctx_scr,
              w_dkv=din("l1a_w_dkv", [2048, 512]), w_dq=din("l1a_w_dq", [2048, 512]),
              w_uk=din("l1a_w_uk", [512, 2048]), w_uv=din("l1a_w_uv", [512, 2048]),
              w_uq=din("l1a_w_uq", [512, 2048]), w_kr=din("l1a_w_kr", [2048, 64]),
              w_qr=din("l1a_w_qr", [512, 1024]), w_o=din("l1a_w_o", [2048, 2048]),
              b_o=din("l1a_b_o", [2048]),
              ct=din("l1a_ct", [128, T]), st=din("l1a_st", [128, T]),
              p1t=din("l1a_p1t", [128, 128]), p2t=din("l1a_p2t", [128, 128]))
    d4 = dict(x=x3, out=out,
              gate_w=din("l1f_gate_w", [32, 2048]), gate_bias=din("l1f_gate_bias", [32]),
              fc1t=din("l1f_fc1t", [1024, 2048]), fc2t=din("l1f_fc2t", [1024, 2048]),
              w1t=din("l1f_w1t", [32768, 1024]), w2t=din("l1f_w2t", [32768, 1024]),
              sut=din("l1f_sut", [4096, 2048]), sdt=din("l1f_sdt", [4096, 2048]),
              xnT_scr=dint("xnT_scr", [128, 16 * NTOK]),
              xnT_lo_scr=dint("xnT_lo_scr", [128, 16 * NTOK]),
              cw_scr=dint("cw_scr", [32, NTOK]))

    S = Sched(nc)
    S.use_arenas(AR_BF16, AR_F32, AR_I32)
    S.sbuf_reset()
    o1 = emit_l0_attn(S, T, NB, d1)
    S.barrier(o1)
    S.sbuf_reset()
    o2 = emit_l0_ffn(S, NTOK, d2, SLAB=SLAB)
    S.barrier(o2)
    S.sbuf_reset()
    o3 = emit_l1_attn(S, T, NB, d3)
    S.barrier(o3)
    S.sbuf_reset()
    o4 = emit_l1_ffn(S, NTOK, d4, SLAB=SLAB, CH=CH)
    S.finish_interleaved(final_waits=o4)
    return nc, ins


def make_inputs(T, weights):
    C0, S20, P1T0, P2T0 = rope_consts_l0(T)
    C1, S21, P1T1, P2T1 = rope_consts_l1(T)
    w = weights
    return dict(
        l0a_wq=w["L0_attn_Wq"], l0a_bq=w["L0_attn_bq"], l0a_wk=w["L0_attn_Wk"],
        l0a_bk=w["L0_attn_bk"], l0a_wv=w["L0_attn_Wv"], l0a_bv=w["L0_attn_bv"],
        l0a_wo=w["L0_attn_Wo"], l0a_bo=w["L0_attn_bo"],
        l0a_ct=C0, l0a_st=S20, l0a_p1t=P1T0, l0a_p2t=P2T0, l0a_dmask=causal_masks(512),
        l0f_wq_gate_t=_tq(w["L0_ffn_wq_gate"]), l0f_wq_up_t=_tq(w["L0_ffn_wq_up"]),
        l0f_wq_down_t=np.ascontiguousarray(w["L0_ffn_wq_down"].T).reshape(5632, 2048),
        l0f_s_gate_t=np.ascontiguousarray(w["L0_ffn_s_gate"].reshape(44, 128).T),
        l0f_s_up_t=np.ascontiguousarray(w["L0_ffn_s_up"].reshape(44, 128).T),
        l0f_s_down_row=np.ascontiguousarray(w["L0_ffn_s_down"].reshape(1, 2048)),
        l1a_w_dkv=w["L1_attn_W_DKV"], l1a_w_dq=w["L1_attn_W_DQ"], l1a_w_uk=w["L1_attn_W_UK"],
        l1a_w_uv=w["L1_attn_W_UV"], l1a_w_uq=w["L1_attn_W_UQ"], l1a_w_kr=w["L1_attn_W_KR"],
        l1a_w_qr=w["L1_attn_W_QR"], l1a_w_o=w["L1_attn_W_O"], l1a_b_o=w["L1_attn_b_O"],
        l1a_ct=C1, l1a_st=S21, l1a_p1t=P1T1, l1a_p2t=P2T1,
        l1f_gate_w=w["L1_ffn_gate_w"], l1f_gate_bias=w["L1_ffn_gate_bias"],
        l1f_fc1t=_t4(w["L1_ffn_fc1_w"], 8, 16).reshape(1024, 2048),
        l1f_fc2t=_t4(w["L1_ffn_fc2_w"], 16, 8, swap=True).reshape(1024, 2048),
        l1f_w1t=_tw(w["L1_ffn_w1"]), l1f_w2t=_tw(w["L1_ffn_w2"]),
        l1f_sut=_t4(w["L1_ffn_shared_up_w"], 32, 16).reshape(4096, 2048),
        l1f_sdt=np.ascontiguousarray(w["L1_ffn_shared_down_w"].T).reshape(4096, 2048),
    )


def _t4(W, a, b, swap=False):
    r = W.reshape(a, 128, b, 128)
    if swap:
        return np.ascontiguousarray(r.transpose(2, 3, 0, 1))
    return np.ascontiguousarray(r.transpose(0, 3, 2, 1))


def _tw(W):
    r = W.reshape(32, 8, 128, 8, 128)
    return np.ascontiguousarray(r.transpose(0, 1, 4, 3, 2)).reshape(32768, 1024)


def _tq(W):
    r = W.reshape(44, 128, 16, 128)
    return np.ascontiguousarray(r.transpose(0, 3, 2, 1)).reshape(5632, 2048)


from concourse.bass_utils import run_bass_kernel_spmd

NCORES = 8
TT = 2048
DD = 2048


def kernel(L0_attn_x, **kw):
    kw["L0_attn_x"] = L0_attn_x
    x = np.ascontiguousarray(kw["L0_attn_x"].astype(np.float32).reshape(16, TT, DD))
    xs = [np.ascontiguousarray(x[2 * c:2 * c + 2].reshape(2 * TT, DD)) for c in range(NCORES)]
    nc, ins_spec = build_fused(TT, 2, SLAB=512, CH=512)
    shared = make_inputs(TT, kw)
    in_maps = []
    for c in range(NCORES):
        m = dict(shared)
        m["x"] = xs[c]
        in_maps.append(m)
    res = run_bass_kernel_spmd(nc, in_maps, core_ids=list(range(NCORES)))
    out = np.concatenate([res.results[c]["out"].reshape(2, TT, DD) for c in range(NCORES)], axis=0)
    return out.astype(np.float32)
